# version 22
# baseline (speedup 1.0000x reference)
import numpy as np
import ml_dtypes
import concourse.bass as bass
import concourse.mybir as mybir
from concourse.bass_utils import run_bass_kernel_spmd

F32 = mybir.dt.float32
BF16 = mybir.dt.bfloat16
AF = mybir.ActivationFunctionType
ALU = mybir.AluOpType
AX = mybir.AxisListType

T = 4096
C = 1024
NT = T // 128
NF = C // 128
CH = 128
NCH = T // CH
NEG = np.float32(-1e30)
FUSED = True
DMA_CHAIN = True
SCHED_W = 0
SCHED_ON = True
DMA_STRIPES = 4


def cdiv(a, b):
    return (a + b - 1) // b

ENGINES = ("t", "v", "s", "g", "y")
DMA_QUEUES = ("y", "g", "s")


class Op:
    __slots__ = ("eng", "fn", "reads", "writes", "dma", "idx",
                 "eseq", "milestone", "mcount", "tag")

    def __init__(self, eng, fn, reads, writes, dma, tag=""):
        self.eng = eng
        self.fn = fn
        self.reads = reads
        self.writes = writes
        self.dma = dma
        self.tag = tag
        self.milestone = False
        self.mcount = 0


class Prog:

    def __init__(self, p, name):
        self.p = p
        self.name = name
        self.ops = []

    def op(self, eng, fn, reads=(), writes=(), tag=""):
        o = Op(eng, fn, tuple(reads), tuple(writes), False, tag)
        o.idx = len(self.ops)
        self.ops.append(o)
        return o

    def dma(self, eng, out_ap, in_ap, reads=(), writes=(), tag=""):
        assert eng in DMA_QUEUES
        def fn(e, out_ap=out_ap, in_ap=in_ap):
            return e.dma_start(out=out_ap, in_=in_ap)
        o = Op(eng, fn, tuple(reads), tuple(writes), True, tag)
        o.idx = len(self.ops)
        self.ops.append(o)
        return o

    def _schedule(self, deps):
        import heapq
        N = len(self.ops)
        DUR = {"t": 250, "v": 450, "s": 420, "g": 450, "y": 100}
        succ = [[] for _ in range(N)]
        indeg = [0] * N
        for i, ds in enumerate(deps):
            for d in ds:
                succ[d].append(i)
                indeg[i] += 1
        finish = [0] * N
        engine_free = {}
        order = []
        heap = []

        def est(i):
            o = self.ops[i]
            key = ("q" + o.eng) if o.dma else o.eng
            s = engine_free.get(key, 0)
            for d in deps[i]:
                do = self.ops[d]
                lag = 150 if (do.dma or do.eng != o.eng) else 0
                s = max(s, finish[d] + lag)
            return s

        for i in range(N):
            if indeg[i] == 0:
                heapq.heappush(heap, (est(i), i))
        while heap:
            s0, i = heapq.heappop(heap)
            s = est(i)
            if s > s0 and heap and heap[0][0] < s:
                heapq.heappush(heap, (s, i))
                continue
            o = self.ops[i]
            key = ("q" + o.eng) if o.dma else o.eng
            if o.dma:
                engine_free[key] = s + 300
                finish[i] = s + 2500
            else:
                d = DUR[o.eng]
                engine_free[key] = s + d
                finish[i] = s + d
            order.append(i)
            for j in succ[i]:
                indeg[j] -= 1
                if indeg[j] == 0:
                    heapq.heappush(heap, (est(j), j))
        assert len(order) == N
        return order

    def build(self, schedule=True):
        nc = self.p.nc
        last_writer = {}
        readers = {}
        deps = [set() for _ in self.ops]
        for o in self.ops:
            for r in o.reads:
                w = last_writer.get(r)
                if w is not None:
                    deps[o.idx].add(w)
                rl = readers.setdefault(r, [])
                if r.startswith(("pf", "pbb")) and o.eng in ("v", "s"):
                    for rd in rl:
                        if self.ops[rd].eng in ("v", "s") and self.ops[rd].eng != o.eng:
                            deps[o.idx].add(rd)
                rl.append(o.idx)
            for w in o.writes:
                pw = last_writer.get(w)
                if pw is not None:
                    deps[o.idx].add(pw)
                for rd in readers.get(w, ()):
                    deps[o.idx].add(rd)
                readers[w] = []
                last_writer[w] = o.idx
            deps[o.idx].discard(o.idx)
        need = [set() for _ in self.ops]
        need_drain_after = [set() for _ in self.ops]
        for o in self.ops:
            for d in deps[o.idx]:
                do = self.ops[d]
                if do.dma or do.eng != o.eng:
                    need[o.idx].add(d)
                    do.milestone = True
                elif o.eng in ("v", "s", "g"):
                    need_drain_after[o.idx].add(d)
        prev_dma = {}
        dma_stripe = {}
        nq = {}
        for o in self.ops:
            if o.dma:
                o.milestone = True
                st_ = nq.get(o.eng, 0) % DMA_STRIPES
                nq[o.eng] = nq.get(o.eng, 0) + 1
                dma_stripe[o.idx] = st_
                key = (o.eng, st_)
                if DMA_CHAIN and key in prev_dma:
                    deps[o.idx].add(prev_dma[key])
                prev_dma[key] = o.idx
        self.dma_stripe = dma_stripe
        W = SCHED_W
        if W:
            for i in range(W, len(self.ops)):
                deps[i].add(i - W)
        order = self._schedule(deps) if (schedule and SCHED_ON) else list(range(len(self.ops)))
        eng_ops = {e: [] for e in ENGINES}
        counts = self.p.semcnt
        used_keys = set()

        def keyof(o):
            return (("dma", o.eng, dma_stripe[o.idx]) if o.dma
                    else ("c", o.eng))
        for idx in order:
            o = self.ops[idx]
            key = keyof(o)
            if o.milestone:
                self.p.get_sem(key)
                counts[key] = counts.get(key, 0) + (16 if o.dma else 1)
                o.mcount = counts[key]
                used_keys.add(key)
            eng_ops[o.eng].append(o)
        sems = self.p.semreg
        eng_seq = {}
        for e in ENGINES:
            for si, o in enumerate(eng_ops[e]):
                eng_seq[o.idx] = si

        def emit_for(e, eng):
            watermark = {}
            last_drain = -1
            for o in eng_ops[e]:
                if need_drain_after[o.idx]:
                    dep_seq = max(eng_seq[d] for d in need_drain_after[o.idx])
                    if dep_seq > last_drain:
                        eng.drain()
                        last_drain = eng_seq[o.idx] - 1
                for d in sorted(need[o.idx]):
                    do = self.ops[d]
                    key = keyof(do)
                    v = do.mcount
                    if watermark.get(key, 0) < v:
                        eng.wait_ge(sems[key], v)
                        watermark[key] = v
                inst = o.fn(eng)
                if o.milestone:
                    key = keyof(o)
                    assert inst is not None, f"milestone op {o.tag} returned no inst"
                    if o.dma or o.eng == "t":
                        inst.then_inc(sems[key], 16 if o.dma else 1)
                    else:
                        eng.drain().then_inc(sems[key], 1)
                        last_drain = eng_seq[o.idx]

        def emit_end_drain(e, eng):
            emit_for(e, eng)
            eng.drain()

        with nc.Block() as blk:
            if eng_ops["y"]:
                blk.sync(lambda e: emit_for("y", e))
            if eng_ops["g"]:
                blk.gpsimd(lambda e: emit_end_drain("g", e))
            if eng_ops["s"]:
                blk.scalar(lambda e: emit_end_drain("s", e))
            if eng_ops["v"]:
                blk.vector(lambda e: emit_end_drain("v", e))
            if eng_ops["t"]:
                blk.tensor(lambda e: emit_for("t", e))
            lasts = [(k, counts[k]) for k in sorted(used_keys) if k[0] == "dma"]
            if lasts:
                def fin(eng):
                    for key, v in lasts:
                        eng.wait_ge(sems[key], v)
                blk.gpsimd(fin)


class P:

    def __init__(self):
        import contextlib
        self.nc = bass.Bass("TRN2", target_bir_lowering=False)
        self.es = contextlib.ExitStack()
        self.dram = {}
        self.sb = {}
        self._psum = {}
        self._scopes = [self.es]
        self._contextlib = contextlib
        self.semreg = {}
        self.semcnt = {}

    def get_sem(self, key):
        if key not in self.semreg:
            nm = "sem_" + "_".join(str(k) for k in key)
            self.semreg[key] = self.es.enter_context(self.nc.semaphore(nm))
            self.semcnt[key] = 0
        return self.semreg[key]

    def scope(self):
        import contextlib as _cl

        @_cl.contextmanager
        def _s():
            es = _cl.ExitStack()
            self._scopes.append(es)
            es._names = []
            try:
                yield
            finally:
                self._scopes.pop()
                es.close()
                for n in es._names:
                    self.sb.pop(n, None)
        return _s()

    def dram_t(self, name, shape, dt=F32, kind="Internal"):
        ap = self.nc.dram_tensor(name, shape, dt, kind=kind).ap()
        self.dram[name] = ap
        return ap

    def sbuf(self, name, shape, dt=F32):
        sc = self._scopes[-1]
        t = sc.enter_context(self.nc.sbuf_tensor(name, shape, dt))
        self.sb[name] = t
        if hasattr(sc, "_names"):
            sc._names.append(name)
        return t

    def const_col(self, value):
        j = CONST_SCALARS.index(value)
        return self.sb["c_scal"][:, j:j + 1]

    def psum_bank(self, name, dt=F32):
        if name not in self._psum:
            shape = [128, 512] if dt == F32 else [128, 1024]
            self._psum[name] = self.es.enter_context(
                self.nc.psum_tensor(name, shape, dt))
        return self._psum[name]

    def stage(self, name):
        return Prog(self, name)

    def dbg_store(self, pr, name, ap, res, shape, dt=F32):
        if not getattr(self, "dbg", False):
            return
        d = self.dram_t("dbg_" + name, list(shape), dt, kind="ExternalOutput")
        pr.dma("y", d, ap, reads=(res,), tag=f"dbg {name}")



def host_layout(inputs):
    W = dict(inputs)
    out = {}
    qkvz = W["L0_attn_W_qkvz"]
    qk = qkvz.reshape(C, 8, 4, 128)
    out["l0_Wq"] = np.ascontiguousarray(qk[:, :, 0, :].reshape(C, 1024))
    out["l0_Wk"] = np.ascontiguousarray(qk[:, :, 1, :].reshape(C, 1024))
    out["l0_Wv"] = np.ascontiguousarray(qk[:, :, 2, :].reshape(C, 1024))
    out["l0_Wz"] = np.ascontiguousarray(qk[:, :, 3, :].reshape(C, 1024))
    ba = W["L0_attn_W_ba"]
    out["l0_Wb"] = np.ascontiguousarray(ba[:, 0::2])
    out["l0_Wa"] = np.ascontiguousarray(ba[:, 1::2])
    cw = W["L0_attn_conv_w"][:, 0, :]
    out["l0_cwq"] = np.ascontiguousarray(cw[:1024])
    out["l0_cwk"] = np.ascontiguousarray(cw[1024:2048])
    out["l0_cwv"] = np.ascontiguousarray(cw[2048:])
    out["l0_dtb"] = W["L0_attn_dt_bias"].reshape(8, 1)
    out["l0_Alog"] = W["L0_attn_A_log"].reshape(8, 1)
    out["l0_nw"] = W["L0_attn_norm_weight"].reshape(128, 1)
    out["l0_WoT"] = W["L0_attn_W_out"]
    def cols(v):
        return np.ascontiguousarray(v.reshape(-1, 128).T)
    out["f0_n1"] = cols(W["L0_ffn_norm1_w"])
    out["f0_n2"] = cols(W["L0_ffn_norm2_w"])
    out["f0_qkvT"] = np.ascontiguousarray(W["L0_ffn_qkv_w"].T)
    out["f0_qkvb"] = cols(W["L0_ffn_qkv_b"])
    out["f0_oT"] = np.ascontiguousarray(W["L0_ffn_out_w"].T)
    out["f0_ob"] = cols(W["L0_ffn_out_b"])
    out["f0_gT"] = np.ascontiguousarray(W["L0_ffn_gate_w"].T)
    out["f0_w1T"] = np.ascontiguousarray(W["L0_ffn_w1"].transpose(0, 2, 1))
    out["f0_w2T"] = np.ascontiguousarray(W["L0_ffn_w2"].transpose(0, 2, 1))
    out["f0_w3T"] = np.ascontiguousarray(W["L0_ffn_w3"].transpose(0, 2, 1))
    out["f0_b1"] = np.stack([cols(W["L0_ffn_b1"][e]) for e in range(8)])
    out["f0_b2"] = np.stack([cols(W["L0_ffn_b2"][e]) for e in range(8)])
    out["f0_b3"] = np.stack([cols(W["L0_ffn_b3"][e]) for e in range(8)])
    out["l1_Wq"] = W["L1_attn_Wq"]
    out["l1_Wk"] = W["L1_attn_Wk"]
    out["l1_Wv"] = W["L1_attn_Wv"]
    out["l1_Wal"] = W["L1_attn_Walpha"]
    out["l1_Wbe"] = W["L1_attn_Wbeta"]
    out["l1_Alog"] = W["L1_attn_A_log"].reshape(16, 1)
    out["l1_dtb"] = W["L1_attn_dt_bias"].reshape(16, 1)
    out["l1_Wg"] = W["L1_attn_Wgate"]
    out["l1_Wo"] = W["L1_attn_Wout"]
    out["l1_rmsw"] = np.ascontiguousarray(np.tile(W["L1_attn_rms_w"].reshape(64, 1), (2, 1)))
    out["f1_sg"] = W["L1_ffn_shared_gate"]
    out["f1_su"] = W["L1_ffn_shared_up"]
    out["f1_sd"] = W["L1_ffn_shared_down"]
    out["f1_rg"] = W["L1_ffn_routed_gate"]
    out["f1_ru"] = W["L1_ffn_routed_up"]
    out["f1_rd"] = W["L1_ffn_routed_down"]
    out["f1_Wr"] = W["L1_ffn_Wr"]
    out["f1_rb"] = np.ascontiguousarray(
        np.broadcast_to(W["L1_ffn_routing_bias"], (128, 7)))
    out["c_scal"] = np.ascontiguousarray(
        np.broadcast_to(np.array(CONST_SCALARS, np.float32), (128, len(CONST_SCALARS))))
    out["c_idf"] = np.eye(128, dtype=np.float32)
    out["c_idb"] = np.eye(128).astype(ml_dtypes.bfloat16)
    out["c_ones_k1f"] = np.ones((1, 128), np.float32)
    out["c_ones_k1b"] = np.ones((1, 128)).astype(ml_dtypes.bfloat16)
    out["c_ones_c1f"] = np.ones((128, 1), np.float32)
    out["c_ones_c1b"] = np.ones((128, 1)).astype(ml_dtypes.bfloat16)
    out["c_maskDT"] = np.where(np.arange(128)[:, None] >= np.arange(128)[None, :],
                               NEG, np.float32(0)).astype(np.float32)
    out["c_eyeb"] = np.eye(128).astype(ml_dtypes.bfloat16)
    inv = 1.0 / 10000.0 ** (np.arange(0, 64, 2, dtype=np.float32) / 64)
    tpos = np.arange(T, dtype=np.float32)
    fr = tpos[:, None] * inv[None, :]
    emb = np.concatenate([fr, fr], -1)
    cos = np.cos(emb).astype(np.float32).T
    sin = np.sin(emb).astype(np.float32).T
    sinS = np.concatenate([-sin[:32], sin[32:]], 0)
    out["c_cosR"] = np.ascontiguousarray(
        np.concatenate([cos, cos], 0)).astype(ml_dtypes.bfloat16)
    out["c_sinR"] = np.ascontiguousarray(
        np.concatenate([sinS, sinS], 0)).astype(ml_dtypes.bfloat16)
    return out


CONST_SCALARS = [1e-5, 1e-6, float(np.log(128.0 ** -0.5)), float(np.log(64.0 ** -0.5)),
                 1.0, 2.0, 0.5, -1.0, 1e-18]

CONST_DEFS = {
    "c_scal": ([128, len(CONST_SCALARS)], F32),
    "c_idf": ([128, 128], F32), "c_idb": ([128, 128], BF16),
    "c_ones_k1f": ([1, 128], F32), "c_ones_k1b": ([1, 128], BF16),
    "c_ones_c1f": ([128, 1], F32), "c_ones_c1b": ([128, 1], BF16),
    "c_maskDT": ([128, 128], F32), "c_eyeb": ([128, 128], BF16),
    "c_cosR": ([128, T], BF16), "c_sinR": ([128, T], BF16),
}


def alloc_consts(p):
    for name, (shape, dt) in CONST_DEFS.items():
        ap = p.dram_t(name, shape, dt, kind="ExternalInput")
        t = p._scopes[0].enter_context(p.nc.sbuf_tensor("sb_" + name, shape, dt))
        p.sb[name] = t


def load_consts(p, pr):
    if "c_idf" not in p.sb:
        alloc_consts(p)
    for name in CONST_DEFS:
        pr.dma("y", p.sb[name][:], p.dram[name], writes=(name,), tag=f"ld_{name}")



def emit_s0(p, pr, x_hbm, xA_hbm, xnA_hbm, nt=NT, dbg_hbm=None):
    DB = 3
    xt = [p.sbuf(f"s0_x{i}", [128, C], F32) for i in range(DB)]
    sq = [p.sbuf(f"s0_sq{i}", [128, C], F32) for i in range(2)]
    stt = [p.sbuf(f"s0_st{i}", [128, 6], F32) for i in range(2)]
    xnf = [p.sbuf(f"s0_xnf{i}", [128, C], F32) for i in range(2)]
    xnb = [p.sbuf(f"s0_xnb{i}", [128, C], BF16) for i in range(2)]
    evf = [p.sbuf(f"s0_evf{i}", [128, C], F32) for i in range(2)]
    evb = [p.sbuf(f"s0_evb{i}", [128, C], BF16) for i in range(2)]
    pf = [p.psum_bank("pf0"), p.psum_bank("pf1")]
    pb = [p.psum_bank("pbb0", BF16), p.psum_bank("pbb1", BF16)]
    idf = p.sb["c_idf"]
    idb = p.sb["c_idb"]

    for i in range(nt):
        b = i % DB
        d = i % 2
        X, SQ, ST, XF, XB, EF, EB = (f"s0x{b}", f"s0sq{d}", f"s0st{d}",
                                     f"s0xnf{d}", f"s0xnb{d}", f"s0evf{d}", f"s0evb{d}")
        pr.dma("y", xt[b][:], x_hbm[i * 128:(i + 1) * 128, :], writes=(X,), tag=f"s0 ldx{i}")
        pr.op("s", lambda e, b=b, d=d: e.activation(sq[d][:], xt[b][:], AF.Square),
              reads=(X,), writes=(SQ,))
        pr.op("v", lambda e, b=b, d=d: e.tensor_reduce(stt[d][:, 0:1], xt[b][:], AX.X, ALU.add),
              reads=(X,), writes=(ST,))
        pr.op("v", lambda e, d=d: e.tensor_reduce(stt[d][:, 1:2], sq[d][:], AX.X, ALU.add),
              reads=(SQ,), writes=(ST,))
        pr.op("v", lambda e, d=d: e.tensor_scalar(stt[d][:, 2:3], stt[d][:, 0:1], 1.0 / C, -1.0, ALU.mult, ALU.mult),
              reads=(ST,), writes=(ST,))
        pr.op("v", lambda e, d=d: e.tensor_scalar(stt[d][:, 1:2], stt[d][:, 1:2], 1.0 / C, None, ALU.mult),
              reads=(ST,), writes=(ST,))
        pr.op("v", lambda e, d=d: e.tensor_tensor(stt[d][:, 3:4], stt[d][:, 2:3], stt[d][:, 2:3], ALU.mult),
              reads=(ST,), writes=(ST,))
        pr.op("v", lambda e, d=d: e.tensor_tensor(stt[d][:, 3:4], stt[d][:, 1:2], stt[d][:, 3:4], ALU.subtract),
              reads=(ST,), writes=(ST,))
        pr.op("s", lambda e, d=d: e.activation(stt[d][:, 4:5], stt[d][:, 3:4], AF.Ln, bias=p.const_col(1e-5)),
              reads=(ST, "c_scal"), writes=(ST + "r",))
        pr.op("s", lambda e, d=d: e.activation(stt[d][:, 4:5], stt[d][:, 4:5], AF.Exp, scale=-0.5),
              reads=(ST + "r",), writes=(ST + "r",))
        pr.op("v", lambda e, b=b, d=d: e.tensor_scalar(
              xnf[d][:], xt[b][:], stt[d][:, 2:3], stt[d][:, 4:5], ALU.add, ALU.mult),
              reads=(X, ST, ST + "r"), writes=(XF,))
        pr.op("v", lambda e, d=d: e.tensor_copy(xnb[d][:], xnf[d][:]),
              reads=(XF,), writes=(XB,))
        if dbg_hbm is not None:
            pr.dma("y", dbg_hbm[i], stt[d][:], reads=(ST, ST + "r"), tag=f"dbg{i}")
        for f in range(NF):
            bk = f % 2
            sl = (f // 2) % 4
            BA, BB = f"pf{bk}", f"pbb{bk}"
            pr.op("t", lambda e, bk=bk, sl=sl, b=b, f=f: e.transpose(
                  pf[bk][:, sl * 128:(sl + 1) * 128], xt[b][:, f * 128:(f + 1) * 128], idf[:]),
                  reads=(X, "c_idf"), writes=(BA,), tag=f"s0 trf{i}_{f}")
            pr.op("v", lambda e, bk=bk, sl=sl, d=d, f=f: e.tensor_copy(
                  evf[d][:, f * 128:(f + 1) * 128], pf[bk][:, sl * 128:(sl + 1) * 128]),
                  reads=(BA,), writes=(EF,))
            pr.op("t", lambda e, bk=bk, sl=sl, d=d, f=f: e.transpose(
                  pb[bk][:, sl * 128:(sl + 1) * 128], xnb[d][:, f * 128:(f + 1) * 128], idb[:]),
                  reads=(XB, "c_idb"), writes=(BB,), tag=f"s0 trb{i}_{f}")
            pr.op("v", lambda e, bk=bk, sl=sl, d=d, f=f: e.tensor_copy(
                  evb[d][:, f * 128:(f + 1) * 128], pb[bk][:, sl * 128:(sl + 1) * 128]),
                  reads=(BB,), writes=(EB,))
        pr.dma("y", xA_hbm[:, :, i * 128:(i + 1) * 128],
               evf[d][:].rearrange("p (f t) -> p f t", t=128), reads=(EF,), tag=f"s0 stxA{i}")
        pr.dma("y", xnA_hbm[:, :, i * 128:(i + 1) * 128],
               evb[d][:].rearrange("p (f t) -> p f t", t=128), reads=(EB,), tag=f"s0 stxnA{i}")



def emit_gemm(p, pr, tag, W_hbm, Din, Dout, rhs_fn, evac_fn, banks, wname,
              nch=8, chw=512, wq="g"):
    NK = Din // 128
    NM = cdiv(Dout, 128)
    wbuf = [p.sbuf(f"{wname}{i}", [128, NK * 128], BF16) for i in range(2)]
    for m in range(NM):
        mw = min(128, Dout - m * 128)
        ws = m % 2
        src = W_hbm[:, m * 128:m * 128 + mw].rearrange("(nk r) c -> r nk c", r=128)
        pr.dma(wq, wbuf[ws][:].rearrange("r (nk c) -> r nk c", c=128)[:, :, :mw],
               src, writes=(f"{wname}{ws}",), tag=f"{tag} w{m}")
        for ch in range(nch):
            ba, bres = banks[ch % 2]
            for k in range(NK):
                pr.op("t", lambda e, ws=ws, k=k, ch=ch, mw=mw, ba=ba: e.matmul(
                      ba[:mw, :chw], wbuf[ws][:, k * 128:k * 128 + mw],
                      rhs_fn(k, ch)[0], start=(k == 0), stop=(k == NK - 1)),
                      reads=(f"{wname}{ws}", rhs_fn(k, ch)[1]),
                      writes=(bres,), tag=f"{tag} mm{m}_{ch}_{k}")
            evac_fn(m, ch, ba, bres)


def bcast_strip(p, pr, strip_ap, strip_res, w, bank_ap, bank_res, f32=True):
    ones = p.sb["c_ones_k1f" if f32 else "c_ones_k1b"]
    ores = "c_ones_k1f" if f32 else "c_ones_k1b"
    pr.op("t", lambda e: e.matmul(bank_ap[:, :w], ones[:], strip_ap,
                                  start=True, stop=True),
          reads=(strip_res, ores), writes=(bank_res,), tag="bcast")


def rstd_strip(p, pr, strip_ap, strip_res, ln_scale, eps, exp_bias=None, n=1):
    ec = p.const_col(eps)[:n, :]
    pr.op("s", lambda e: e.activation(strip_ap, strip_ap, AF.Ln,
                                      bias=ec, scale=ln_scale),
          reads=(strip_res, "c_scal"), writes=(strip_res,), tag="rstd_ln")
    if exp_bias is None:
        pr.op("s", lambda e: e.activation(strip_ap, strip_ap, AF.Exp, scale=-0.5),
              reads=(strip_res,), writes=(strip_res,), tag="rstd_exp")
    else:
        bc = p.const_col(exp_bias)[:n, :]
        pr.op("s", lambda e: e.activation(strip_ap, strip_ap, AF.Exp,
                                          bias=bc, scale=-0.5),
              reads=(strip_res, "c_scal"), writes=(strip_res,), tag="rstd_exp")


def softplus_block(p, pr, src_ap, src_res, n, w, tmp_a, tmp_r, ra, rr,
                   dtb_col, dtb_res, negA_col, negA_res, out_ap, out_res):
    pr.op("s", lambda e: e.activation(tmp_a, src_ap, AF.Abs, bias=dtb_col),
          reads=(src_res, dtb_res), writes=(tmp_r,), tag="sp_abs")
    pr.op("s", lambda e: e.activation(ra, src_ap, AF.Relu, bias=dtb_col),
          reads=(src_res, dtb_res), writes=(rr,), tag="sp_relu")
    pr.op("s", lambda e: e.activation(tmp_a, tmp_a, AF.Exp, scale=-1.0),
          reads=(tmp_r,), writes=(tmp_r,), tag="sp_exp")
    pr.op("s", lambda e: e.activation(tmp_a, tmp_a, AF.Ln, bias=1.0),
          reads=(tmp_r,), writes=(tmp_r,), tag="sp_ln")
    pr.op("v", lambda e: e.tensor_tensor(tmp_a, tmp_a, ra, ALU.add),
          reads=(tmp_r, rr), writes=(tmp_r,), tag="sp_add")
    pr.op("v", lambda e: e.tensor_scalar(out_ap, tmp_a, negA_col, None, ALU.mult),
          reads=(tmp_r, negA_res), writes=(out_res,), tag="sp_scale")



def emit_s1(p, pr, xnA, qcA, kcA, vcA, zsA, bg, win):
    nc = p.nc
    xn = p.sbuf("s1_xn", [128, NF * T], BF16)
    pr.dma("y", xn[:].rearrange("p (f t) -> p f t", t=T), xnA,
           writes=("s1xn",), tag="s1 ldxn")

    def rhs(k, ch):
        return xn[:, k * T + ch * 512: k * T + ch * 512 + 512], "s1xn"

    banks = [(p.psum_bank("pf0"), "pf0"), (p.psum_bank("pf1"), "pf1")]
    banks2 = [(p.psum_bank("pf2"), "pf2"), (p.psum_bank("pf3"), "pf3")]

    cw = {}
    for sec in ("q", "k", "v"):
        t = p.sbuf(f"s1_cw{sec}", [128, 32], F32)
        cw[sec] = t
        pr.dma("y", t[:].rearrange("r (h j) -> r h j", j=4),
               win[f"l0_cw{sec}"].rearrange("(h r) j -> r h j", r=128),
               writes=(f"s1cw{sec}",), tag=f"s1 ldcw{sec}")
    dtb = p.sbuf("s1_dtb", [8, 1], F32)
    pr.dma("y", dtb[:], win["l0_dtb"], writes=("s1dtb",))
    alog = p.sbuf("s1_alog", [8, 1], F32)
    pr.dma("y", alog[:], win["l0_Alog"], writes=("s1alog",))
    negA = p.sbuf("s1_negA", [8, 1], F32)
    pr.op("s", lambda e: e.activation(negA[:], alog[:], AF.Exp),
          reads=("s1alog",), writes=("s1negA",))
    pr.op("v", lambda e: e.tensor_scalar_mul(negA[:], negA[:], -1.0),
          reads=("s1negA",), writes=("s1negA",))

    raw = [p.sbuf(f"s1_raw{i}", [128, T + 3], BF16) for i in range(2)]
    acc = [p.sbuf(f"s1_acc{i}", [128, T], F32) for i in range(2)]
    outb = [p.sbuf(f"s1_out{i}", [128, 512], BF16) for i in range(2)]

    for sec, Wk, dst in (("q", "l0_Wq", qcA), ("k", "l0_Wk", kcA), ("v", "l0_Wv", vcA)):
        def evac(m, ch, ba, bres, sec=sec, dst=dst):
            ms = m % 2
            R = f"s1raw{ms}"
            pr.op("v", lambda e, ms=ms, ch=ch, ba=ba: e.tensor_copy(
                  raw[ms][:, 3 + ch * 512: 3 + (ch + 1) * 512], ba[:, :512]),
                  reads=(bres,), writes=(R,), tag=f"s1 {sec}ev{m}_{ch}")
            if ch == 0:
                pr.op("v", lambda e, ms=ms: e.memset(raw[ms][:, 0:3], 0.0),
                      writes=(R,), tag="s1 pad0")
            if ch == nchunks - 1:
                A = f"s1acc{ms}"
                pr.op("v", lambda e, ms=ms, sec=sec, m=m: e.tensor_scalar(
                      acc[ms][:], raw[ms][:, 0:T], cw[sec][:, 4 * m:4 * m + 1],
                      None, ALU.mult),
                      reads=(R, f"s1cw{sec}"), writes=(A,), tag="s1 conv0")
                for j in range(1, 4):
                    pr.op("v", lambda e, ms=ms, sec=sec, m=m, j=j: e.scalar_tensor_tensor(
                          acc[ms][:], raw[ms][:, j:j + T], cw[sec][:, 4 * m + j:4 * m + j + 1],
                          acc[ms][:], ALU.mult, ALU.add),
                          reads=(R, f"s1cw{sec}", A), writes=(A,), tag=f"s1 conv{j}")
                for oc in range(8):
                    O = f"s1out{oc % 2}"
                    pr.op("s", lambda e, ms=ms, oc=oc: e.activation(
                          outb[oc % 2][:], acc[ms][:, oc * 512:(oc + 1) * 512], AF.Silu),
                          reads=(A,), writes=(O,), tag="s1 silu")
                    pr.dma("y", dst[:, m, oc * 512:(oc + 1) * 512], outb[oc % 2][:],
                           reads=(O,), tag=f"s1 st{sec}{m}_{oc}")
        nchunks = 8
        emit_gemm(p, pr, f"s1{sec}", win[Wk], C, C, rhs, evac, banks, f"s1w{sec}")

    zst = [p.sbuf(f"s1_z{i}", [128, T], BF16) for i in range(2)]

    def evac_z(m, ch, ba, bres):
        ms = m % 2
        Z = f"s1z{ms}"
        pr.op("s", lambda e, ms=ms, ch=ch, ba=ba: e.activation(
              zst[ms][:, ch * 512:(ch + 1) * 512], ba[:, :512], AF.Silu),
              reads=(bres,), writes=(Z,), tag=f"s1 zev{m}_{ch}")
        if ch == 7:
            pr.dma("y", zsA[:, m, :], zst[ms][:], reads=(Z,), tag=f"s1 stz{m}")
    emit_gemm(p, pr, "s1z", win["l0_Wz"], C, C, rhs, evac_z, banks, "s1wz")

    wb = p.sbuf("s1_wb", [128, NF * 8], BF16)
    wa = p.sbuf("s1_wa", [128, NF * 8], BF16)
    pr.dma("g", wb[:].rearrange("r (nk c) -> r nk c", c=8),
           win["l0_Wb"].rearrange("(nk r) c -> r nk c", r=128), writes=("s1wb",))
    pr.dma("g", wa[:].rearrange("r (nk c) -> r nk c", c=8),
           win["l0_Wa"].rearrange("(nk r) c -> r nk c", r=128), writes=("s1wa",))
    bss = [p.sbuf(f"s1_bs{i}", [8, 512], F32) for i in range(2)]
    gss = [p.sbuf(f"s1_gs{i}", [8, 512], F32) for i in range(2)]
    gcs = [p.sbuf(f"s1_gc{i}", [8, 512], F32) for i in range(2)]
    tt1 = [p.sbuf(f"s1_t1{i}", [8, 512], F32) for i in range(2)]
    tt2 = [p.sbuf(f"s1_t2{i}", [8, 512], F32) for i in range(2)]
    egs = [p.sbuf(f"s1_eg{i}", [8, 512], F32) for i in range(2)]
    ers = [p.sbuf(f"s1_er{i}", [8, 512], F32) for i in range(2)]
    bes = [p.sbuf(f"s1_be{i}", [8, 512], F32) for i in range(2)]
    zcol = p.sbuf("s1_zcol", [8, 1], F32)
    pr.op("v", lambda e: e.memset(zcol[:], 0.0), writes=("s1zcol",))

    for ch in range(8):
        cs = ch % 2
        ba_b, bres_b = banks2[0]
        ba_a, bres_a = banks2[1]
        for k in range(NF):
            pr.op("t", lambda e, k=k, ch=ch, ba=ba_b: e.matmul(
                  ba[:8, :512], wb[:, k * 8:(k + 1) * 8], rhs(k, ch)[0],
                  start=(k == 0), stop=(k == NF - 1)),
                  reads=("s1wb", "s1xn"), writes=(bres_b,), tag=f"s1 bmm{ch}")
            pr.op("t", lambda e, k=k, ch=ch, ba=ba_a: e.matmul(
                  ba[:8, :512], wa[:, k * 8:(k + 1) * 8], rhs(k, ch)[0],
                  start=(k == 0), stop=(k == NF - 1)),
                  reads=("s1wa", "s1xn"), writes=(bres_a,), tag=f"s1 amm{ch}")
        B, G, GC = f"s1bs{cs}", f"s1gs{cs}", f"s1gc{cs}"
        pr.op("s", lambda e, cs=cs, ba=ba_b: e.activation(bss[cs][:], ba[:8, :512], AF.Sigmoid),
              reads=(bres_b,), writes=(B,), tag="s1 beta")
        softplus_block(p, pr, ba_a[:8, :512], bres_a, 8, 512,
                       tt1[cs][:], f"s1t1{cs}", tt2[cs][:], f"s1t2{cs}",
                       dtb[:], "s1dtb", negA[:], "s1negA",
                       gss[cs][:], G)
        for q in range(4):
            sl = slice(q * 128, (q + 1) * 128)
            pr.op("v", lambda e, cs=cs, sl=sl: e.tensor_tensor_scan(
                  gcs[cs][:, sl], gss[cs][:, sl], gss[cs][:, sl], zcol[:],
                  ALU.add, ALU.bypass),
                  reads=(G, "s1zcol"), writes=(GC,), tag="s1 cumsum")
        for q in range(4):
            sl = slice(q * 128, (q + 1) * 128)
            pr.op("v", lambda e, cs=cs, sl=sl, q=q: e.tensor_scalar(
                  gss[cs][:, sl], gcs[cs][:, sl], -1.0,
                  gcs[cs][:, q * 128 + 127: q * 128 + 128], ALU.mult, ALU.add),
                  reads=(GC,), writes=(G,), tag="s1 grev")
        EG, ER, BE = f"s1eg{cs}", f"s1er{cs}", f"s1be{cs}"
        pr.op("s", lambda e, cs=cs: e.activation(egs[cs][:], gcs[cs][:], AF.Exp),
              reads=(GC,), writes=(EG,), tag="s1 Eg")
        pr.op("s", lambda e, cs=cs: e.activation(ers[cs][:], gss[cs][:], AF.Exp),
              reads=(G,), writes=(ER,), tag="s1 Erev")
        pr.op("v", lambda e, cs=cs: e.tensor_tensor(bes[cs][:], bss[cs][:], egs[cs][:], ALU.mult),
              reads=(B, EG), writes=(BE,), tag="s1 bEg")
        sl5 = slice(ch * 512, (ch + 1) * 512)
        pr.dma("y", bg[0, :, sl5], gcs[cs][:], reads=(GC,), tag=f"s1 stg{ch}")
        pr.dma("y", bg[1, :, sl5], egs[cs][:], reads=(EG,), tag=f"s1 steg{ch}")
        pr.dma("y", bg[2, :, sl5], ers[cs][:], reads=(ER,), tag=f"s1 ster{ch}")
        pr.dma("y", bg[3, :, sl5], bss[cs][:], reads=(B,), tag=f"s1 stb{ch}")
        pr.dma("y", bg[4, :, sl5], bes[cs][:], reads=(BE,), tag=f"s1 stbe{ch}")



def emit_delta_head(p, pr, pfx, dk, dv, cstride, cofs, kA_ap, kA_res, qA_ap, qA_res,
                    vA_ap, vA_res, strips, tcols, eglast_bc, S_ap, S_res,
                    core_ap, core_res, nch=NCH, idsl=None):
    nc = p.nc
    idb = p.sb["c_idb"]
    if idsl is None:
        idsl = slice(0, dk)
    PF = [p.psum_bank(f"pf{i}") for i in range(6)]
    BB = [p.psum_bank("pbb0", BF16), p.psum_bank("pbb1", BF16)]
    BANKS = [dict(A=(PF[0], "pf0"), B=(PF[1], "pf1"), C=(PF[2], "pf2"),
                  D=(BB[0], "pbb0")),
             dict(A=(PF[3], "pf3"), B=(PF[4], "pf4"), C=(PF[5], "pf5"),
                  D=(BB[1], "pbb1"))]

    def W(n, par, sh=(128, 128), dt=BF16):
        nm = f"{pfx}_{n}{par}"
        return p.sb[nm] if nm in p.sb else p.sbuf(nm, list(sh), dt)

    gs, gs_r = strips["g"]
    egs, egs_r = strips["Eg"]
    bes, bes_r = strips["be"]
    gTn, gTn_r = tcols["gTn"]
    beT, beT_r = tcols["beT"]
    erT, erT_r = tcols["erT"]
    begT, begT_r = tcols["begT"]
    egl, egl_r = eglast_bc

    def ci(i):
        return slice(i * cstride + cofs, i * cstride + cofs + 1)

    for i in range(nch):
        par = i % 2
        bk = BANKS[par]
        bA, bAr = bk["A"]
        bN, bNr = bk["B"]
        bC, bCr = bk["C"]
        bD, bDr = bk["D"]
        R = lambda n: f"{pfx}_{n}{par}"
        argDT = W("argDT", par, (128, 128), F32)
        DstT = W("DstT", par); DTt = W("DTt", par); DbT = W("DbT", par)
        triT = W("triT", par); MTt = W("MTt", par)
        Xb = W("Xb", par); Zb = W("Zb", par); Pe = W("Pe", par); PeT = W("PeT", par)
        kT = W("kT", par); VbT = W("VbT", par); KtT = W("KtT", par)
        K2T = W("K2T", par); kcn = W("kcn", par); vnA = W("vnA", par)
        vnR = W("vnR", par); qt = W("qt", par); atT = W("atT", par)
        cs = slice(i * CH, (i + 1) * CH)
        bcast_strip(p, pr, gs[:, cs], gs_r, CH, bA, bAr, f32=True)
        pr.op("v", lambda e, i=i, argDT=argDT, bA=bA: e.scalar_tensor_tensor(
              argDT[:], bA[:, :CH], gTn[:, ci(i)], p.sb["c_maskDT"][:],
              ALU.add, ALU.add),
              reads=(bAr, gTn_r, "c_maskDT"), writes=(R("argDT"),), tag=f"{pfx} argDT{i}")
        pr.op("s", lambda e, DstT=DstT, argDT=argDT: e.activation(DstT[:], argDT[:], AF.Exp),
              reads=(R("argDT"),), writes=(R("DstT"),), tag="expD")
        pr.op("v", lambda e, DTt=DTt, DstT=DstT: e.tensor_tensor(
              DTt[:], DstT[:], p.sb["c_eyeb"][:], ALU.add),
              reads=(R("DstT"), "c_eyeb"), writes=(R("DTt"),), tag="DT")
        bcast_strip(p, pr, bes[:, cs], bes_r, CH, bA, bAr, f32=False)
        pr.op("v", lambda e, DbT=DbT, DstT=DstT, bA=bA: e.tensor_tensor(
              DbT[:], DstT[:], bA[:, :CH], ALU.mult),
              reads=(R("DstT"), bAr), writes=(R("DbT"),), tag="DbT")
        pr.op("t", lambda e, cs=cs, bA=bA: e.matmul(bA[:, :CH], kA_ap[:, cs],
                                                    kA_ap[:, cs], start=True, stop=True),
              reads=(kA_res,), writes=(bAr,), tag=f"{pfx} W1{i}")
        pr.op("v", lambda e, triT=triT, DbT=DbT, bA=bA: e.tensor_tensor(
              triT[:], bA[:, :CH], DbT[:], ALU.mult),
              reads=(bAr, R("DbT")), writes=(R("triT"),), tag="triT")
        pr.op("v", lambda e, MTt=MTt, triT=triT: e.tensor_tensor(
              MTt[:], triT[:], p.sb["c_eyeb"][:], ALU.add),
              reads=(R("triT"), "c_eyeb"), writes=(R("MTt"),), tag="MT")
        pr.op("v", lambda e, Xb=Xb, triT=triT: e.tensor_tensor(
              Xb[:], p.sb["c_eyeb"][:], triT[:], ALU.subtract),
              reads=(R("triT"), "c_eyeb"), writes=(R("Xb"),), tag="X0")
        for it in range(6):
            pr.op("t", lambda e, MTt=MTt, Xb=Xb, bN=bN: e.matmul(
                  bN[:, :CH], MTt[:], Xb[:], start=True, stop=True),
                  reads=(R("MTt"), R("Xb")), writes=(bNr,), tag=f"nwP{it}")
            pr.op("s", lambda e, Pe=Pe, bN=bN: e.copy(Pe[:], bN[:, :CH]),
                  reads=(bNr,), writes=(R("Pe"),), tag="nwPe")
            pr.op("t", lambda e, Pe=Pe, bD=bD: e.transpose(bD[:, 0:CH], Pe[:], idb[:]),
                  reads=(R("Pe"), "c_idb"), writes=(bDr,), tag="nwPT")
            pr.op("s", lambda e, PeT=PeT, bD=bD: e.copy(PeT[:], bD[:, 0:CH]),
                  reads=(bDr,), writes=(R("PeT"),), tag="nwPTe")
            pr.op("t", lambda e, PeT=PeT, Xb=Xb, bN=bN: e.matmul(
                  bN[:, :CH], PeT[:], Xb[:], start=True, stop=True),
                  reads=(R("PeT"), R("Xb")), writes=(bNr,), tag="nwXP")
            pr.op("v", lambda e, Xb=Xb, bN=bN: e.scalar_tensor_tensor(
                  Xb[:], Xb[:], 2.0, bN[:, :CH], ALU.mult, ALU.subtract),
                  reads=(R("Xb"), bNr), writes=(R("Xb"),), tag="nwX")
        pr.op("t", lambda e, Xb=Xb, bD=bD: e.transpose(bD[:, 0:CH], Xb[:], idb[:]),
              reads=(R("Xb"), "c_idb"), writes=(bDr,), tag="trTmT")
        pr.op("s", lambda e, Zb=Zb, bD=bD: e.copy(Zb[:], bD[:, 0:CH]),
              reads=(bDr,), writes=(R("Zb"),), tag="TmT")
        pr.op("t", lambda e, cs=cs, bD=bD: e.transpose(
              bD[:, 128:128 + dv], vA_ap[:dv, cs], idb[idsl, idsl]),
              reads=(vA_res, "c_idb"), writes=(bDr,), tag="trV")
        pr.op("v", lambda e, i=i, VbT=VbT, bD=bD: e.tensor_scalar(
              VbT[:, :dv], bD[:, 128:128 + dv], beT[:, ci(i)], None, ALU.mult),
              reads=(bDr, beT_r), writes=(R("VbT"),), tag="VbT")
        pr.op("t", lambda e, cs=cs, bD=bD: e.transpose(
              bD[:, 256:256 + dk], kA_ap[:dk, cs], idb[idsl, idsl]),
              reads=(kA_res, "c_idb"), writes=(bDr,), tag="trK")
        pr.op("s", lambda e, kT=kT, bD=bD: e.copy(kT[:, :dk], bD[:, 256:256 + dk]),
              reads=(bDr,), writes=(R("kT"),), tag="kT")
        pr.op("v", lambda e, i=i, KtT=KtT, kT=kT: e.tensor_scalar(
              KtT[:, :dk], kT[:, :dk], begT[:, ci(i)], None, ALU.mult),
              reads=(R("kT"), begT_r), writes=(R("KtT"),), tag="KtT")
        pr.op("v", lambda e, i=i, K2T=K2T, kT=kT: e.tensor_scalar(
              K2T[:, :dk], kT[:, :dk], erT[:, ci(i)], None, ALU.mult),
              reads=(R("kT"), erT_r), writes=(R("K2T"),), tag="K2T")
        pr.op("t", lambda e, KtT=KtT, Zb=Zb, bC=bC: e.matmul(
              bC[:dk, :CH], KtT[:, :dk], Zb[:], start=True, stop=True),
              reads=(R("KtT"), R("Zb")), writes=(bCr,), tag="kcum")
        pr.op("s", lambda e, kcn=kcn, bC=bC: e.mul(kcn[:dk, :], bC[:dk, :CH], -1.0),
              reads=(bCr,), writes=(R("kcn"),), tag="kcn")
        pr.op("t", lambda e, VbT=VbT, Zb=Zb, bC=bC: e.matmul(
              bC[:dv, :CH], VbT[:, :dv], Zb[:], start=True, stop=False),
              reads=(R("VbT"), R("Zb")), writes=(bCr,), tag="vn1")
        pr.op("t", lambda e, kcn=kcn, bC=bC: e.matmul(
              bC[:dv, :CH], S_ap, kcn[:dk, :], start=False, stop=True),
              reads=(S_res, R("kcn")), writes=(bCr,), tag="vn2")
        pr.op("s", lambda e, vnA=vnA, bC=bC: e.copy(vnA[:dv, :], bC[:dv, :CH]),
              reads=(bCr,), writes=(R("vnA"),), tag="vnA")
        pr.op("t", lambda e, vnA=vnA, bD=bD: e.transpose(
              bD[:, 384:384 + dv], vnA[:dv, :], idb[:dv, :dv]),
              reads=(R("vnA"), "c_idb"), writes=(bDr,), tag="trVn")
        pr.op("s", lambda e, vnR=vnR, bD=bD: e.copy(vnR[:, :dv], bD[:, 384:384 + dv]),
              reads=(bDr,), writes=(R("vnR"),), tag="vnR")
        bcast_strip(p, pr, egs[:, cs], egs_r, CH, bA, bAr, f32=True)
        pr.op("v", lambda e, cs=cs, qt=qt, bA=bA: e.tensor_tensor(
              qt[:dk, :], qA_ap[:, cs], bA[:dk, :CH], ALU.mult),
              reads=(qA_res, bAr), writes=(R("qt"),), tag="qt")
        pr.op("t", lambda e, cs=cs, bA=bA: e.matmul(
              bA[:, :CH], kA_ap[:, cs], qA_ap[:, cs], start=True, stop=True),
              reads=(kA_res, qA_res), writes=(bAr,), tag="attn")
        pr.op("v", lambda e, atT=atT, DTt=DTt, bA=bA: e.tensor_tensor(
              atT[:], bA[:, :CH], DTt[:], ALU.mult),
              reads=(bAr, R("DTt")), writes=(R("atT"),), tag="atT")
        pr.op("t", lambda e, qt=qt, bC=bC: e.matmul(
              bC[:dv, :CH], S_ap, qt[:dk, :], start=True, stop=False),
              reads=(S_res, R("qt")), writes=(bCr,), tag="out1")
        pr.op("t", lambda e, vnR=vnR, atT=atT, bC=bC: e.matmul(
              bC[:dv, :CH], vnR[:, :dv], atT[:], start=False, stop=True),
              reads=(R("vnR"), R("atT")), writes=(bCr,), tag="out2")
        pr.op("s", lambda e, cs=cs, bC=bC: e.copy(core_ap[:, cs], bC[:dv, :CH]),
              reads=(bCr,), writes=(core_res,), tag=f"{pfx} core{i}")
        pr.op("t", lambda e, K2T=K2T, vnR=vnR, bC=bC: e.matmul(
              bC[:dk, :dv], K2T[:, :dk], vnR[:, :dv], start=True, stop=True),
              reads=(R("K2T"), R("vnR")), writes=(bCr,), tag="U")
        pr.op("v", lambda e, i=i, bC=bC: e.scalar_tensor_tensor(
              S_ap, S_ap, egl[:dk, i:i + 1], bC[:dk, :dv], ALU.mult, ALU.add),
              reads=(S_res, egl_r, bCr), writes=(S_res,), tag="Supd")


def emit_s2a(p, pr, qcA, kcA, vcA, zsA, bg, gatedA, nw_hbm):
    nc = p.nc
    idf = p.sb["c_idf"]
    pf0 = p.psum_bank("pf0")
    pf1 = p.psum_bank("pf1")
    stage = p.sbuf("s2_stage", [8, T], F32)
    gT = p.sbuf("s2_gT", [128, NCH * 8], F32)
    gTn = p.sbuf("s2_gTn", [128, NCH * 8], F32)
    beT = p.sbuf("s2_beT", [128, NCH * 8], F32)
    begT = p.sbuf("s2_begT", [128, NCH * 8], F32)
    erT = p.sbuf("s2_erT", [128, NCH * 8], F32)
    for row, dst, res in ((0, gT, "s2gT"), (3, beT, "s2beT"),
                          (4, begT, "s2begT"), (2, erT, "s2erT")):
        pr.dma("y", stage[:], bg[row], writes=("s2stage",), tag=f"s2 ldbg{row}")
        for i in range(NCH):
            pr.op("t", lambda e, i=i: e.transpose(
                  pf0[:, (i % 4) * 128:(i % 4) * 128 + 8],
                  stage[:, i * CH:(i + 1) * CH], idf[:8, :8]),
                  reads=("s2stage", "c_idf"), writes=("pf0",), tag=f"s2 trbg{row}_{i}")
            pr.op("v", lambda e, i=i, dst=dst, res=res: e.tensor_copy(
                  dst[:, i * 8:(i + 1) * 8], pf0[:, (i % 4) * 128:(i % 4) * 128 + 8]),
                  reads=("pf0",), writes=(res,), tag=f"s2 evbg{row}_{i}")
    pr.op("v", lambda e: e.tensor_scalar_mul(gTn[:], gT[:], -1.0),
          reads=("s2gT",), writes=("s2gTn",))
    nw = p.sbuf("s2_nw", [128, 1], F32)
    pr.dma("y", nw[:], nw_hbm, writes=("s2nw",))

    qh = p.sbuf("s2_qh", [128, T], BF16)
    kh = p.sbuf("s2_kh", [128, T], BF16)
    vh = p.sbuf("s2_vh", [128, T], BF16)
    zh = p.sbuf("s2_zh", [128, T], BF16)
    sqb = p.sbuf("s2_sq", [128, T], BF16)
    core = p.sbuf("s2_core", [128, T], BF16)
    gcore = p.sbuf("s2_gcore", [128, T], BF16)
    qn = p.sbuf("s2_qn", [128, T], BF16)
    kn = p.sbuf("s2_kn", [128, T], BF16)
    ssf = p.sbuf("s2_ssf", [1, T], F32)
    g_st = p.sbuf("s2_gst", [1, T], F32)
    eg_st = p.sbuf("s2_egst", [1, T], F32)
    be_st = p.sbuf("s2_best", [1, T], BF16)
    egl = p.sbuf("s2_egl", [128, NCH], F32)
    Sst = p.sbuf("s2_S", [128, 128], BF16)

    def l2pass(src_ap, src_res, dst, dst_res, exp_bias):
        pr.op("s", lambda e: e.activation(sqb[:], src_ap, AF.Square),
              reads=(src_res,), writes=("s2sq",), tag="sq")
        for ch in range(8):
            pr.op("t", lambda e, ch=ch: e.matmul(
                  pf1[:1, :512],
                  p.sb["c_ones_c1b"][:], sqb[:, ch * 512:(ch + 1) * 512],
                  start=True, stop=True),
                  reads=("s2sq", "c_ones_c1b"), writes=("pf1",), tag=f"ssq{ch}")
            pr.op("v", lambda e, ch=ch: e.tensor_copy(
                  ssf[:, ch * 512:(ch + 1) * 512], pf1[:1, :512]),
                  reads=("pf1",), writes=("s2ssf",), tag=f"ssqe{ch}")
        rstd_strip(p, pr, ssf[:], "s2ssf", 1.0, 1e-6, exp_bias, n=1)
        for ch in range(8):
            sl = slice(ch * 512, (ch + 1) * 512)
            bcast_strip(p, pr, ssf[:, sl], "s2ssf", 512, pf0, "pf0", f32=True)
            pr.op("v", lambda e, sl=sl, src_ap=src_ap: e.tensor_tensor(
                  dst[:, sl], src_ap[:, sl], pf0[:, :512], ALU.mult),
                  reads=(src_res, "pf0"), writes=(dst_res,), tag="l2mul")

    LN128 = float(np.log(128.0 ** -0.5))
    for h in range(8):
        pr.dma("y", qh[:], qcA[:, h, :], writes=("s2qh",), tag=f"s2 ldq{h}")
        pr.dma("y", kh[:], kcA[:, h, :], writes=("s2kh",), tag=f"s2 ldk{h}")
        pr.dma("y", vh[:], vcA[:, h, :], writes=("s2vh",), tag=f"s2 ldv{h}")
        pr.dma("y", zh[:], zsA[:, h, :], writes=("s2zh",), tag=f"s2 ldz{h}")
        pr.dma("y", g_st[:], bg[0, h:h + 1, :], writes=("s2gst",), tag=f"s2 ldg{h}")
        pr.dma("y", eg_st[:], bg[1, h:h + 1, :], writes=("s2egst",), tag=f"s2 ldeg{h}")
        pr.dma("g", be_st[:], bg[3, h:h + 1, :], writes=("s2best",), tag=f"s2 ldbe{h}")
        bcast_strip(p, pr, eg_st[:, CH - 1::CH], "s2egst", NCH, pf0, "pf0", f32=True)
        pr.op("v", lambda e: e.tensor_copy(egl[:], pf0[:, :NCH]),
              reads=("pf0",), writes=("s2egl",), tag="egl")
        l2pass(qh[:], "s2qh", qn, "s2qn", LN128)
        l2pass(kh[:], "s2kh", kn, "s2kn", None)
        if h == 0:
            p.dbg_store(pr, "qn", qn[:], "s2qn", [128, T], BF16)
            p.dbg_store(pr, "kn", kn[:], "s2kn", [128, T], BF16)
            p.dbg_store(pr, "egl", egl[:], "s2egl", [128, NCH])
            p.dbg_store(pr, "gT", gT[:], "s2gT", [128, NCH * 8])
            p.dbg_store(pr, "beT", beT[:], "s2beT", [128, NCH * 8])
            p.dbg_store(pr, "begT", begT[:], "s2begT", [128, NCH * 8])
            p.dbg_store(pr, "erT", erT[:], "s2erT", [128, NCH * 8])
        pr.op("v", lambda e: e.memset(Sst[:], 0.0), writes=("s2S",), tag="S0")
        emit_delta_head(
            p, pr, "s2d", 128, 128, 8, h,
            kn[:], "s2kn", qn[:], "s2qn", vh[:], "s2vh",
            dict(g=(g_st[:], "s2gst"), Eg=(eg_st[:], "s2egst"),
                 be=(be_st[:], "s2best")),
            dict(gTn=(gTn[:, :], "s2gTn"), beT=(beT[:, :], "s2beT"),
                 erT=(erT[:, :], "s2erT"), begT=(begT[:, :], "s2begT")),
            (egl[:], "s2egl"), Sst[:], "s2S", core[:], "s2core")
        pr.op("s", lambda e: e.activation(sqb[:], core[:], AF.Square),
              reads=("s2core",), writes=("s2sq",), tag="csq")
        for ch in range(8):
            pr.op("t", lambda e, ch=ch: e.matmul(
                  pf1[:1, :512], p.sb["c_ones_c1b"][:],
                  sqb[:, ch * 512:(ch + 1) * 512], start=True, stop=True),
                  reads=("s2sq", "c_ones_c1b"), writes=("pf1",), tag=f"cs{ch}")
            pr.op("v", lambda e, ch=ch: e.tensor_copy(
                  ssf[:, ch * 512:(ch + 1) * 512], pf1[:1, :512]),
                  reads=("pf1",), writes=("s2ssf",), tag=f"cse{ch}")
        rstd_strip(p, pr, ssf[:], "s2ssf", 1.0 / 128.0, 1e-6, None, n=1)
        for ch in range(8):
            sl = slice(ch * 512, (ch + 1) * 512)
            bcast_strip(p, pr, ssf[:, sl], "s2ssf", 512, pf0, "pf0", f32=True)
            pr.op("v", lambda e, sl=sl: e.scalar_tensor_tensor(
                  gcore[:, sl], core[:, sl], nw[:], pf0[:, :512], ALU.mult, ALU.mult),
                  reads=("s2core", "s2nw", "pf0"), writes=("s2gcore",), tag="gc1")
            pr.op("v", lambda e, sl=sl: e.tensor_tensor(
                  gcore[:, sl], gcore[:, sl], zh[:, sl], ALU.mult),
                  reads=("s2gcore", "s2zh"), writes=("s2gcore",), tag="gc2")
        pr.dma("y", gatedA[:, h, :], gcore[:], reads=("s2gcore",), tag=f"s2 stg{h}")


def emit_s2b(p, pr, gatedA, WoT_hbm, s0A):
    gt = p.sbuf("s2b_g", [128, NF * T], BF16)
    pr.dma("y", gt[:].rearrange("p (f t) -> p f t", t=T), gatedA,
           writes=("s2bg",), tag="s2b ldg")

    def rhs(k, ch):
        return gt[:, k * T + ch * 512: k * T + ch * 512 + 512], "s2bg"

    banks = [(p.psum_bank("pf0"), "pf0"), (p.psum_bank("pf1"), "pf1")]
    ob = [p.sbuf(f"s2b_o{i}", [128, T], BF16) for i in range(2)]

    def evac(m, ch, ba, bres):
        ms = m % 2
        O = f"s2bo{ms}"
        pr.op("v", lambda e, ms=ms, ch=ch, ba=ba: e.tensor_copy(
              ob[ms][:, ch * 512:(ch + 1) * 512], ba[:, :512]),
              reads=(bres,), writes=(O,), tag=f"s2b ev{m}_{ch}")
        if ch == 7:
            pr.dma("y", s0A[:, m, :], ob[ms][:], reads=(O,), tag=f"s2b st{m}")
    emit_gemm(p, pr, "s2b", WoT_hbm, C, C, rhs, evac, banks, "s2bw")



def emit_rmsA(p, pr, pfx, src_rhs, n_w_col, xr, xr_res, eps=1e-6, pre_ch=None):
    pf0 = p.psum_bank("pf0")
    pf1 = p.psum_bank("pf1")
    sq = p.sbuf(f"{pfx}_sq", [128, 512], BF16)
    ssf = p.sbuf(f"{pfx}_ssf", [1, T], F32)
    SQ, SS = f"{pfx}sq", f"{pfx}ssf"
    for ch in range(8):
        if pre_ch is not None:
            pre_ch(ch, 0)
        for f in range(NF):
            ap, res = src_rhs(f, ch)
            pr.op("s", lambda e, ap=ap: e.activation(sq[:], ap, AF.Square),
                  reads=(res,), writes=(SQ,), tag=f"{pfx} sq{ch}_{f}")
            pr.op("t", lambda e, f=f: e.matmul(pf1[:1, :512], p.sb["c_ones_c1b"][:],
                                               sq[:], start=(f == 0), stop=(f == NF - 1)),
                  reads=(SQ, "c_ones_c1b"), writes=("pf1",), tag=f"{pfx} ssq{ch}_{f}")
        pr.op("v", lambda e, ch=ch: e.tensor_copy(ssf[:, ch * 512:(ch + 1) * 512],
                                                  pf1[:1, :512]),
              reads=("pf1",), writes=(SS,), tag=f"{pfx} ssqe{ch}")
    rstd_strip(p, pr, ssf[:], SS, 1.0 / C, eps, None, n=1)
    for ch in range(8):
        if pre_ch is not None:
            pre_ch(ch, 1)
        sl = slice(ch * 512, (ch + 1) * 512)
        bcast_strip(p, pr, ssf[:, sl], SS, 512, pf0, "pf0", f32=True)
        for f in range(NF):
            ap, res = src_rhs(f, ch)
            if n_w_col is None:
                pr.op("v", lambda e, ap=ap, f=f, sl=sl: e.tensor_tensor(
                      xr[:, f * T:][:, sl], ap, pf0[:, :512], ALU.mult),
                      reads=(res, "pf0"), writes=(xr_res,), tag=f"{pfx} xr{ch}_{f}")
            else:
                wc, wcr = n_w_col
                pr.op("v", lambda e, ap=ap, f=f, sl=sl, wc=wc: e.scalar_tensor_tensor(
                      xr[:, f * T:][:, sl], ap, wc[:, f:f + 1], pf0[:, :512],
                      ALU.mult, ALU.mult),
                      reads=(res, wcr, "pf0"), writes=(xr_res,), tag=f"{pfx} xr{ch}_{f}")


def emit_s3a(p, pr, xnA, qkvA, win):
    n1 = p.sbuf("s3_n1", [128, NF], F32)
    pr.dma("y", n1[:], win["f0_n1"], writes=("s3n1",))
    qb = p.sbuf("s3_qb", [128, 24], F32)
    pr.dma("y", qb[:], win["f0_qkvb"], writes=("s3qb",))
    xr = p.sbuf("s3_xr", [128, NF * T], BF16)
    xn = p.sbuf("s3_xnf", [128, NF * T], BF16)
    pr.dma("y", xn[:].rearrange("p (f t) -> p f t", t=T), xnA,
           writes=("s3xn",), tag="s3 ldxn")

    def src(f, ch):
        return xn[:, f * T + ch * 512: f * T + (ch + 1) * 512], "s3xn"
    emit_rmsA(p, pr, "s3r", src, (n1, "s3n1"), xr[:], "s3xr")

    def rhs(k, ch):
        return xr[:, k * T + ch * 512: k * T + ch * 512 + 512], "s3xr"

    banks = [(p.psum_bank("pf2"), "pf2"), (p.psum_bank("pf3"), "pf3")]
    raw = [p.sbuf(f"s3_raw{i}", [128, T], BF16) for i in range(2)]
    sw = [p.sbuf(f"s3_sw{i}", [128, T], BF16) for i in range(2)]
    cosR = p.sb["c_cosR"]
    sinR = p.sb["c_sinR"]

    def evac(m, ch, ba, bres):
        ms = m % 2
        R = f"s3raw{ms}"
        pr.op("v", lambda e, ms=ms, ch=ch, ba=ba, m=m: e.tensor_scalar(
              raw[ms][:, ch * 512:(ch + 1) * 512], ba[:, :512],
              qb[:, m:m + 1], None, ALU.add),
              reads=(bres, "s3qb"), writes=(R,), tag=f"s3 ev{m}_{ch}")
        if ch == 7:
            if m >= 16:
                pr.dma("y", qkvA[:, m, :], raw[ms][:], reads=(R,), tag=f"s3 stv{m}")
            else:
                S = f"s3sw{ms}"
                for blk in range(4):
                    srcr = [32, 0, 96, 64][blk]
                    pr.dma("y", sw[ms][blk * 32:(blk + 1) * 32, :],
                           raw[ms][srcr:srcr + 32, :], reads=(R,), writes=(S,),
                           tag=f"s3 sw{m}_{blk}")
                pr.op("v", lambda e, ms=ms: e.tensor_tensor(raw[ms][:], raw[ms][:],
                                                            cosR[:], ALU.mult),
                      reads=(R, "c_cosR"), writes=(R,), tag="rope1")
                pr.op("v", lambda e, ms=ms: e.tensor_tensor(sw[ms][:], sw[ms][:],
                                                            sinR[:], ALU.mult),
                      reads=(S, "c_sinR"), writes=(S,), tag="rope2")
                pr.op("v", lambda e, ms=ms: e.tensor_tensor(raw[ms][:], raw[ms][:],
                                                            sw[ms][:], ALU.add),
                      reads=(R, S), writes=(R,), tag="rope3")
                pr.dma("y", qkvA[:, m, :], raw[ms][:], reads=(R,), tag=f"s3 stqk{m}")
    emit_gemm(p, pr, "s3qkv", win["f0_qkvT"], C, 3 * C, rhs, evac, banks, "s3w")



def emit_s3b(p, pr, qkvA, oAh):
    pf_sc = [(p.psum_bank("pf0"), "pf0"), (p.psum_bank("pf1"), "pf1")]
    pf_o = p.psum_bank("pf2")
    pf_b = p.psum_bank("pf3")
    q2 = p.sbuf("s3b_q", [128, T], BF16)
    k2 = p.sbuf("s3b_k", [128, T], BF16)
    v2 = p.sbuf("s3b_v", [128, T], BF16)
    sq = p.sbuf("s3b_sq", [128, T], BF16)
    ssf = p.sbuf("s3b_ssf", [1, T], F32)
    mx = p.sbuf("s3b_mx", [1, 4], F32)
    negB = p.sbuf("s3b_negB", [128, 1], F32)
    vT = p.sbuf("s3b_vT", [128, 32 * 65], BF16)
    PT = [p.sbuf(f"s3b_PT{i}", [128, 512], BF16) for i in range(2)]
    lsb = p.sbuf("s3b_l", [1, 512], F32)
    rb = p.sbuf("s3b_rb", [64, 512], F32)
    oa = p.sbuf("s3b_oa", [128, T], BF16)
    idb = p.sb["c_idb"]

    for hp in range(8):
        pr.dma("y", q2[:], qkvA[:, hp, :], writes=("s3bq",), tag=f"s3b ldq{hp}")
        pr.dma("y", k2[:], qkvA[:, 8 + hp, :], writes=("s3bk",), tag=f"s3b ldk{hp}")
        pr.dma("y", v2[:], qkvA[:, 16 + hp, :], writes=("s3bv",), tag=f"s3b ldv{hp}")
        for hh in range(2):
            rws = slice(hh * 64, hh * 64 + 64)
            for t_, src_, res_ in ((0, q2, "s3bq"), (1, k2, "s3bk")):
                pr.op("s", lambda e, src_=src_, rws=rws: e.activation(
                      sq[rws, :], src_[rws, :], AF.Square),
                      reads=(res_,), writes=("s3bsq",), tag="Bsq")
                for ch in range(8):
                    pr.op("t", lambda e, ch=ch, rws=rws: e.matmul(
                          pf_b[:1, :512], p.sb["c_ones_c1b"][rws, :],
                          sq[rws, ch * 512:(ch + 1) * 512], start=True, stop=True),
                          reads=("s3bsq", "c_ones_c1b"), writes=("pf3",), tag="Bmm")
                    pr.op("v", lambda e, ch=ch: e.tensor_copy(
                          ssf[:, ch * 512:(ch + 1) * 512], pf_b[:1, :512]),
                          reads=("pf3",), writes=("s3bssf",), tag="Bev")
                pr.op("v", lambda e, t_=t_: e.tensor_reduce(
                      mx[:, t_:t_ + 1], ssf[:], AX.X, ALU.max),
                      reads=("s3bssf",), writes=("s3bmx",), tag="Bmax")
            pr.op("s", lambda e: e.activation(mx[:, 0:1], mx[:, 0:1], AF.Sqrt),
                  reads=("s3bmx",), writes=("s3bmx",), tag="Bsq1")
            pr.op("s", lambda e: e.activation(mx[:, 1:2], mx[:, 1:2], AF.Sqrt),
                  reads=("s3bmx",), writes=("s3bmx",), tag="Bsq2")
            pr.op("v", lambda e: e.tensor_tensor(mx[:, 2:3], mx[:, 0:1], mx[:, 1:2],
                                                 ALU.mult),
                  reads=("s3bmx",), writes=("s3bmx",), tag="Bmul")
            pr.op("v", lambda e: e.tensor_scalar_mul(mx[:, 2:3], mx[:, 2:3], -0.125),
                  reads=("s3bmx",), writes=("s3bmx",), tag="Bneg")
            bcast_strip(p, pr, mx[:, 2:3], "s3bmx", 1, pf_b, "pf3", f32=True)
            pr.op("v", lambda e: e.tensor_copy(negB[:], pf_b[:, :1]),
                  reads=("pf3",), writes=("s3bnegB",), tag="Bbc")
            pr.op("v", lambda e: e.memset(vT[:], 1.0), writes=("s3bvT",), tag="vT1")
            bb0 = p.psum_bank("pbb0", BF16)
            for kt in range(32):
                pr.op("t", lambda e, kt=kt, rws=rws: e.transpose(
                      bb0[:, (kt % 8) * 128:(kt % 8) * 128 + 64],
                      v2[rws, kt * 128:(kt + 1) * 128], idb[rws, rws]),
                      reads=("s3bv", "c_idb"), writes=("pbb0",), tag=f"vTt{kt}")
                pr.op("v", lambda e, kt=kt: e.tensor_copy(
                      vT[:, kt * 65:kt * 65 + 64], bb0[:, (kt % 8) * 128:(kt % 8) * 128 + 64]),
                      reads=("pbb0",), writes=("s3bvT",), tag=f"vTe{kt}")
            for qc in range(8):
                qsl = slice(qc * 512, (qc + 1) * 512)

                def sc_mm(kt):
                    sb, sbr = pf_sc[kt % 2]
                    pr.op("t", lambda e, kt=kt, rws=rws, qsl=qsl, sb=sb: e.matmul(
                          sb[:, :512], k2[rws, kt * 128:(kt + 1) * 128],
                          q2[rws, qsl], start=True, stop=True),
                          reads=("s3bk", "s3bq"), writes=(sbr,), tag=f"fs{qc}_{kt}")

                def exp_op(kt):
                    sb, sbr = pf_sc[kt % 2]
                    pt = PT[kt % 2]
                    pr.op("s", lambda e, sb=sb, pt=pt: e.activation(
                          pt[:], sb[:, :512], AF.Exp, bias=negB[:], scale=0.125),
                          reads=(sbr, "s3bnegB"), writes=(f"s3bPT{kt % 2}",),
                          tag=f"fe{qc}_{kt}")

                def pv_mm(kt):
                    pt = PT[kt % 2]
                    pr.op("t", lambda e, kt=kt, pt=pt: e.matmul(
                          pf_o[:65, :512], vT[:, kt * 65:(kt + 1) * 65], pt[:],
                          start=(kt == 0), stop=(kt == 31), skip_group_check=True),
                          reads=("s3bvT", f"s3bPT{kt % 2}"), writes=("pf2",),
                          tag=f"fpv{qc}_{kt}")
                sc_mm(0)
                exp_op(0)
                for kt in range(1, 32):
                    sc_mm(kt)
                    pv_mm(kt - 1)
                    exp_op(kt)
                pv_mm(31)
                pr.op("v", lambda e: e.tensor_copy(lsb[:], pf_o[64:65, :512]),
                      reads=("pf2",), writes=("s3bl",), tag="fl")
                pr.op("v", lambda e: e.reciprocal(lsb[:], lsb[:]),
                      reads=("s3bl",), writes=("s3bl",), tag="fr")
                bcast_strip(p, pr, lsb[:], "s3bl", 512, pf_b, "pf3", f32=True)
                pr.op("v", lambda e: e.tensor_copy(rb[:], pf_b[:64, :512]),
                      reads=("pf3",), writes=("s3brb",), tag="frb")
                pr.op("v", lambda e, rws=rws, qsl=qsl: e.tensor_tensor(
                      oa[rws, qsl], pf_o[:64, :512], rb[:], ALU.mult),
                      reads=("pf2", "s3brb"), writes=("s3boa",), tag="fo")
        pr.dma("y", oAh[:, hp, :], oa[:], reads=("s3boa",), tag=f"s3b st{hp}")



def emit_s3c(p, pr, oAh, xnA, x1fA, win):
    oh = p.sbuf("s3c_o", [128, NF * T], BF16)
    pr.dma("y", oh[:].rearrange("p (f t) -> p f t", t=T), oAh,
           writes=("s3co",), tag="s3c ldo")
    ob = p.sbuf("s3c_ob", [128, NF], F32)
    pr.dma("y", ob[:], win["f0_ob"], writes=("s3cob",))
    xnc = p.sbuf("s3c_xn", [128, 512], BF16)
    tmp = p.sbuf("s3c_tmp", [128, 512], F32)
    x1c = p.sbuf("s3c_x1", [128, 512], F32)

    def rhs(k, ch):
        return oh[:, k * T + ch * 512: k * T + ch * 512 + 512], "s3co"

    banks = [(p.psum_bank("pf0"), "pf0"), (p.psum_bank("pf1"), "pf1")]

    def evac(m, ch, ba, bres):
        pr.dma("y", xnc[:], xnA[:, m, ch * 512:(ch + 1) * 512],
               writes=("s3cxn",), tag=f"s3c ldxn{m}_{ch}")
        pr.op("v", lambda e, ba=ba, m=m: e.tensor_scalar(
              tmp[:], ba[:, :512], ob[:, m:m + 1], None, ALU.add),
              reads=(bres, "s3cob"), writes=("s3ctmp",), tag=f"s3c bias{m}_{ch}")
        pr.op("v", lambda e: e.tensor_tensor(x1c[:], tmp[:], xnc[:], ALU.add),
              reads=("s3ctmp", "s3cxn"), writes=("s3cx1",), tag=f"s3c add{m}_{ch}")
        pr.dma("y", x1fA[:, m, ch * 512:(ch + 1) * 512], x1c[:],
               reads=("s3cx1",), tag=f"s3c st{m}_{ch}")
    emit_gemm(p, pr, "s3c", win["f0_oT"], C, C, rhs, evac, banks, "s3cw")



def emit_top2(p, pr, pfx, lg_psum, psum_res, nexp, wts_ap, wts_res, mode):
    lg = p.sbuf(f"{pfx}_lg", [128, 8], F32) if f"{pfx}_lg" not in p.sb else p.sb[f"{pfx}_lg"]
    st = p.sbuf(f"{pfx}_st", [128, 8], F32) if f"{pfx}_st" not in p.sb else p.sb[f"{pfx}_st"]
    m_ = p.sbuf(f"{pfx}_m", [128, 6], F32) if f"{pfx}_m" not in p.sb else p.sb[f"{pfx}_m"]
    L, Sr, M = f"{pfx}lg", f"{pfx}st", f"{pfx}m"
    ne = nexp
    pr.op("v", lambda e: e.tensor_copy(lg[:, :ne], lg_psum), reads=(psum_res,),
          writes=(L,), tag=f"{pfx} cp")
    pr.op("v", lambda e: e.tensor_reduce(m_[:, 0:1], lg[:, :ne], AX.X, ALU.max),
          reads=(L,), writes=(M,), tag="m1")
    pr.op("v", lambda e: e.tensor_scalar(st[:, :ne], lg[:, :ne], m_[:, 0:1], None, ALU.is_ge),
          reads=(L, M), writes=(Sr,), tag="mask1")
    pr.op("v", lambda e: e.scalar_tensor_tensor(lg[:, :ne], st[:, :ne], float(NEG), lg[:, :ne],
                                                ALU.mult, ALU.add),
          reads=(Sr, L), writes=(L,), tag="lgm")
    pr.op("v", lambda e: e.tensor_reduce(m_[:, 1:2], lg[:, :ne], AX.X, ALU.max),
          reads=(L,), writes=(M,), tag="m2")
    pr.op("v", lambda e: e.tensor_scalar(lg[:, :ne], lg[:, :ne], m_[:, 1:2], None, ALU.is_ge),
          reads=(L, M), writes=(L,), tag="mask2")
    if mode == "softmax":
        pr.op("v", lambda e: e.tensor_tensor(m_[:, 2:3], m_[:, 0:1], m_[:, 1:2], ALU.subtract),
              reads=(M,), writes=(M,), tag="d12")
        pr.op("s", lambda e: e.activation(m_[:, 3:4], m_[:, 2:3], AF.Sigmoid),
              reads=(M,), writes=(M + "x",), tag="s1")
        pr.op("v", lambda e: e.tensor_scalar(m_[:, 4:5], m_[:, 3:4], -1.0, 1.0,
                                             ALU.mult, ALU.add),
              reads=(M + "x",), writes=(M + "y",), tag="s2")
    else:
        pr.op("v", lambda e: e.tensor_tensor(m_[:, 2:3], m_[:, 0:1], m_[:, 1:2], ALU.add),
              reads=(M,), writes=(M,), tag="dsum")
        pr.op("v", lambda e: e.reciprocal(m_[:, 5:6], m_[:, 2:3]),
              reads=(M,), writes=(M + "r",), tag="rec")
        pr.op("v", lambda e: e.tensor_tensor(m_[:, 3:4], m_[:, 0:1], m_[:, 5:6], ALU.mult),
              reads=(M, M + "r"), writes=(M + "x",), tag="w1")
        pr.op("v", lambda e: e.tensor_tensor(m_[:, 4:5], m_[:, 1:2], m_[:, 5:6], ALU.mult),
              reads=(M, M + "r"), writes=(M + "y",), tag="w2")
    pr.op("v", lambda e: e.tensor_scalar(st[:, :ne], st[:, :ne], m_[:, 3:4], None, ALU.mult),
          reads=(Sr, M + "x"), writes=(Sr,), tag="c1")
    pr.op("v", lambda e: e.scalar_tensor_tensor(wts_ap, lg[:, :ne], m_[:, 4:5],
                                                st[:, :ne], ALU.mult, ALU.add),
          reads=(L, M + "y", Sr), writes=(wts_res,), tag="comb")


def emit_s4(p, pr, x1fA, s0A, xA, xb1A, comb_hbm, win):
    idf = p.sb["c_idf"]
    pf0 = p.psum_bank("pf0")
    pf1 = p.psum_bank("pf1")
    pfA = p.psum_bank("pf2")
    pfB = p.psum_bank("pf3")
    pfC = p.psum_bank("pf4")
    pfD = p.psum_bank("pf5")
    n2 = p.sbuf("s4_n2", [128, NF], F32)
    pr.dma("y", n2[:], win["f0_n2"], writes=("s4n2",))
    xr = p.sbuf("s4_xr", [128, NF * T], BF16)
    x1c = p.sbuf("s4_x1c", [128, NF * 512], F32)

    def ldx1(ch, phase):
        pr.dma("y", x1c[:].rearrange("p (f t) -> p f t", t=512),
               x1fA[:, :, ch * 512:(ch + 1) * 512],
               writes=("s4x1c",), tag=f"s4 ldx1_{phase}_{ch}")

    def src(f, ch):
        return x1c[:, f * 512:(f + 1) * 512], "s4x1c"
    emit_rmsA(p, pr, "s4r", src, (n2, "s4n2"), xr[:], "s4xr", pre_ch=ldx1)

    if getattr(p, "dbg", False):
        p.dbg_store(pr, "xr", xr[:], "s4xr", [128, NF * T], BF16)
    gw = p.sbuf("s4_gw", [128, NF * 8], BF16)
    pr.dma("g", gw[:].rearrange("r (nk c) -> r nk c", c=8),
           win["f0_gT"].rearrange("(nk r) c -> r nk c", r=128), writes=("s4gw",))
    comb8 = p.sbuf("s4_comb8", [8, 128], F32)
    wts = p.sbuf("s4_wts", [128, 8], F32)
    for i in range(NT):
        for k in range(NF):
            pr.op("t", lambda e, i=i, k=k: e.matmul(
                  pf0[:, :8], xr[:, k * T + i * 128: k * T + (i + 1) * 128],
                  gw[:, k * 8:(k + 1) * 8], start=(k == 0), stop=(k == NF - 1)),
                  reads=("s4xr", "s4gw"), writes=("pf0",), tag=f"s4 rt{i}_{k}")
        emit_top2(p, pr, "s4t", pf0[:, :8], "pf0", 8, wts[:], "s4wts", "softmax")
        pr.op("t", lambda e, i=i: e.transpose(pf1[:8, :128], wts[:], idf[:]),
              reads=("s4wts", "c_idf"), writes=("pf1",), tag=f"s4 rtt{i}")
        pr.op("v", lambda e, i=i: e.tensor_copy(comb8[:], pf1[:8, :128]),
              reads=("pf1",), writes=("s4comb8",), tag=f"s4 rte{i}")
        pr.dma("y", comb_hbm[:, i * 128:(i + 1) * 128], comb8[:],
               reads=("s4comb8",), writes=("d_comb8",), tag=f"s4 stcomb{i}")

    TC = 1024
    hsb = p.sbuf("s4_h", [128, 16 * TC], BF16)
    acc = p.sbuf("s4_acc", [128, NF * TC], BF16)
    w12 = [p.sbuf(f"s4_w12{i}", [128, NF * 128], BF16) for i in range(4)]
    w3b = [p.sbuf(f"s4_w3{i}", [128, 16 * 128], BF16) for i in range(2)]
    b1c = p.sbuf("s4_b1", [128, 16], F32)
    b2c = p.sbuf("s4_b2", [128, 16], F32)
    b3c = p.sbuf("s4_b3", [128, 8], F32)
    cstr = p.sbuf("s4_cstr", [1, TC], BF16)
    s1t = [p.sbuf(f"s4_s1t{i}", [128, 512], BF16) for i in range(2)]
    u2t = [p.sbuf(f"s4_u2t{i}", [128, 512], BF16) for i in range(2)]
    yt = [p.sbuf(f"s4_yt{i}", [128, 512], F32) for i in range(2)]
    xbuf = [p.sbuf(f"s4_xb{i}", [128, 512], F32) for i in range(2)]
    gbuf = [p.sbuf(f"s4_gb{i}", [128, 512], BF16) for i in range(2)]
    sbuf0 = [p.sbuf(f"s4_sb{i}", [128, 512], BF16) for i in range(2)]
    xab = [p.sbuf(f"s4_xa{i}", [128, 512], F32) for i in range(2)]
    ot = [p.sbuf(f"s4_ot{i}", [128, 512], F32) for i in range(2)]

    for tc in range(4):
        tsl = slice(tc * TC, (tc + 1) * TC)
        if getattr(p, "dbg", False) and tc > 0:
            p.dbg_store(pr, f"acc{tc - 1}", acc[:], "s4acc", [128, NF * TC], BF16)
        pr.op("v", lambda e: e.memset(acc[:], 0.0), writes=("s4acc",), tag=f"s4 z{tc}")
        for ex in range(8):
            pr.dma("g", cstr[:], comb_hbm[ex:ex + 1, tsl],
                   reads=("d_comb8",), writes=("s4cstr",), tag=f"s4 ldc{tc}_{ex}")
            pr.dma("y", b1c[:], win["f0_b1"][ex], writes=("s4b1",))
            pr.dma("y", b2c[:], win["f0_b2"][ex], writes=("s4b2",))
            pr.dma("y", b3c[:], win["f0_b3"][ex], writes=("s4b3",))
            HB = [(pfA, "pf2", pfB, "pf3"), (pfC, "pf4", pfD, "pf5")]
            for m in range(16):
                wp = m % 2
                b1a, b1r, b2a, b2r = HB[wp]
                W1R, W2R = f"s4w12{2 * wp}", f"s4w12{2 * wp + 1}"
                pr.dma("g", w12[2 * wp][:].rearrange("r (nk c) -> r nk c", c=128),
                       win["f0_w1T"][ex][:, m * 128:(m + 1) * 128]
                       .rearrange("(nk r) c -> r nk c", r=128),
                       writes=(W1R,), tag=f"s4 w1{ex}_{m}")
                pr.dma("g", w12[2 * wp + 1][:].rearrange("r (nk c) -> r nk c", c=128),
                       win["f0_w2T"][ex][:, m * 128:(m + 1) * 128]
                       .rearrange("(nk r) c -> r nk c", r=128),
                       writes=(W2R,), tag=f"s4 w2{ex}_{m}")
                for sub in range(2):
                    ssl = slice(tc * TC + sub * 512, tc * TC + sub * 512 + 512)
                    for k in range(NF):
                        pr.op("t", lambda e, k=k, wp=wp, ssl=ssl, b1a=b1a: e.matmul(
                              b1a[:, :512], w12[2 * wp][:, k * 128:(k + 1) * 128],
                              xr[:, k * T:][:, ssl], start=(k == 0), stop=(k == NF - 1)),
                              reads=(W1R, "s4xr"), writes=(b1r,), tag="h1mm")
                        pr.op("t", lambda e, k=k, wp=wp, ssl=ssl, b2a=b2a: e.matmul(
                              b2a[:, :512], w12[2 * wp + 1][:, k * 128:(k + 1) * 128],
                              xr[:, k * T:][:, ssl], start=(k == 0), stop=(k == NF - 1)),
                              reads=(W2R, "s4xr"), writes=(b2r,), tag="h2mm")
                    pr.op("s", lambda e, m=m, wp=wp, b1a=b1a: e.activation(
                          s1t[wp][:], b1a[:, :512], AF.Silu, bias=b1c[:, m:m + 1]),
                          reads=(b1r, "s4b1"), writes=(f"s4s1t{wp}",), tag="silu1")
                    pr.op("v", lambda e, m=m, wp=wp, b2a=b2a: e.tensor_scalar(
                          u2t[wp][:], b2a[:, :512], b2c[:, m:m + 1], None, ALU.add),
                          reads=(b2r, "s4b2"), writes=(f"s4u2t{wp}",), tag="b2add")
                    pr.op("v", lambda e, m=m, sub=sub, wp=wp: e.tensor_tensor(
                          hsb[:, m * TC + sub * 512: m * TC + sub * 512 + 512],
                          s1t[wp][:], u2t[wp][:], ALU.mult),
                          reads=(f"s4s1t{wp}", f"s4u2t{wp}"), writes=("s4h",), tag="hmul")
            if getattr(p, "dbg", False) and tc == 0 and ex == 7:
                p.dbg_store(pr, "hsb", hsb[:], "s4h", [128, 16 * TC], BF16)
            for mo in range(8):
                wp = mo % 2
                ya, yr = (pfA, "pf2") if wp == 0 else (pfB, "pf3")
                W3R = f"s4w3{wp}"
                pr.dma("g", w3b[wp][:].rearrange("r (nk c) -> r nk c", c=128),
                       win["f0_w3T"][ex][:, mo * 128:(mo + 1) * 128]
                       .rearrange("(nk r) c -> r nk c", r=128),
                       writes=(W3R,), tag=f"s4 w3{ex}_{mo}")
                for sub in range(2):
                    cb, cbr = (pfC, "pf4") if sub == 0 else (pfD, "pf5")
                    bcast_strip(p, pr, cstr[:, sub * 512:(sub + 1) * 512], "s4cstr",
                                512, cb, cbr, f32=False)
                    for k in range(16):
                        pr.op("t", lambda e, k=k, wp=wp, sub=sub, ya=ya: e.matmul(
                              ya[:, :512], w3b[wp][:, k * 128:(k + 1) * 128],
                              hsb[:, k * TC + sub * 512: k * TC + sub * 512 + 512],
                              start=(k == 0), stop=(k == 15)),
                              reads=(W3R, "s4h"), writes=(yr,), tag="ymm")
                    pr.op("v", lambda e, mo=mo, ya=ya, wp=wp: e.tensor_scalar(
                          yt[wp][:], ya[:, :512], b3c[:, mo:mo + 1], None, ALU.add),
                          reads=(yr, "s4b3"), writes=(f"s4yt{wp}",), tag="b3add")
                    pr.op("v", lambda e, cb=cb, wp=wp: e.tensor_tensor(
                          yt[wp][:], yt[wp][:], cb[:, :512], ALU.mult),
                          reads=(f"s4yt{wp}", cbr), writes=(f"s4yt{wp}",), tag="cmul")
                    pr.op("v", lambda e, mo=mo, sub=sub, wp=wp: e.tensor_tensor(
                          acc[:, mo * TC + sub * 512: mo * TC + sub * 512 + 512],
                          acc[:, mo * TC + sub * 512: mo * TC + sub * 512 + 512],
                          yt[wp][:], ALU.add),
                          reads=("s4acc", f"s4yt{wp}"), writes=("s4acc",), tag="yacc")
        for f in range(NF):
            for sub in range(2):
                w = (2 * f + sub) % 2
                XB, GB, SB, XAB, OT = (f"s4xb{w}", f"s4gb{w}", f"s4sb{w}",
                                       f"s4xa{w}", f"s4ot{w}")
                ssl = slice(tc * TC + sub * 512, tc * TC + sub * 512 + 512)
                pr.dma("y", xbuf[w][:], x1fA[:, f, ssl], writes=(XB,), tag="ld x1")
                pr.op("v", lambda e, f=f, sub=sub, w=w: e.tensor_tensor(
                      xbuf[w][:], xbuf[w][:], acc[:, f * TC + sub * 512: f * TC + sub * 512 + 512],
                      ALU.add),
                      reads=(XB, "s4acc"), writes=(XB,), tag="x2f")
                pr.op("s", lambda e, w=w: e.activation(gbuf[w][:], xbuf[w][:], AF.Silu),
                      reads=(XB,), writes=(GB,), tag="g0")
                pr.dma("y", sbuf0[w][:], s0A[:, f, ssl], writes=(SB,), tag="ld s0")
                pr.dma("y", xab[w][:], xA[:, f, ssl], writes=(XAB,), tag="ld xA")
                pr.op("v", lambda e, w=w: e.tensor_tensor(ot[w][:], gbuf[w][:], sbuf0[w][:], ALU.mult),
                      reads=(GB, SB), writes=(OT,), tag="sg")
                pr.op("v", lambda e, w=w: e.tensor_tensor(ot[w][:], ot[w][:], xab[w][:], ALU.add),
                      reads=(OT, XAB), writes=(OT,), tag="xb1")
                pr.dma("y", xb1A[:, f, ssl], ot[w][:], reads=(OT,), tag="st xb1")
        if getattr(p, "dbg", False) and tc == 3:
            p.dbg_store(pr, "acc3", acc[:], "s4acc", [128, NF * TC], BF16)



def emit_s5a(p, pr, xb1A, xn1A, q1A, k1A, v1A, gt1A, bg1, win):
    pf0, pf1, pf2 = p.psum_bank("pf0"), p.psum_bank("pf1"), p.psum_bank("pf2")
    banks = [(p.psum_bank("pf3"), "pf3"), (p.psum_bank("pf4"), "pf4")]
    banks2 = [(p.psum_bank("pf5"), "pf5"), (p.psum_bank("pf2"), "pf2")]
    xc = p.sbuf("s5_xc", [128, NF * 512], F32)
    xcb = p.sbuf("s5_xcb", [128, 512], BF16)
    sq = p.sbuf("s5_sq", [128, 512], BF16)
    nm = p.sbuf("s5_nm", [1, T], BF16)
    rs = p.sbuf("s5_rs", [1, T], F32)
    tmpf = p.sbuf("s5_tmpf", [128, 512], F32)
    xn1 = p.sbuf("s5_xn1", [128, NF * T], BF16)

    for ch in range(8):
        sl = slice(ch * 512, (ch + 1) * 512)
        pr.dma("y", xc[:].rearrange("p (f t) -> p f t", t=512),
               xb1A[:, :, sl], writes=("s5xc",), tag=f"s5 ldx{ch}")
        for f in range(NF):
            pr.op("v", lambda e, f=f: e.tensor_copy(xcb[:], xc[:, f * 512:(f + 1) * 512]),
                  reads=("s5xc",), writes=("s5xcb",), tag="cast")
            pr.op("s", lambda e: e.activation(sq[:], xcb[:], AF.Square),
                  reads=("s5xcb",), writes=("s5sq",), tag="sq")
            pr.op("t", lambda e, f=f: e.matmul(pf1[:1, :512], p.sb["c_ones_c1b"][:],
                                               xcb[:], start=(f == 0), stop=(f == NF - 1)),
                  reads=("s5xcb", "c_ones_c1b"), writes=("pf1",), tag="sum")
            pr.op("t", lambda e, f=f: e.matmul(pf0[:1, :512], p.sb["c_ones_c1b"][:],
                                               sq[:], start=(f == 0), stop=(f == NF - 1)),
                  reads=("s5sq", "c_ones_c1b"), writes=("pf0",), tag="ssq")
        pr.op("v", lambda e, sl=sl: e.tensor_scalar(nm[:, sl], pf1[:1, :512],
                                                    -1.0 / C, None, ALU.mult),
              reads=("pf1",), writes=("s5nm",), tag="negmean")
        pr.op("v", lambda e, sl=sl: e.tensor_scalar(rs[:, sl], pf0[:1, :512],
                                                    1.0 / C, None, ALU.mult),
              reads=("pf0",), writes=("s5rs",), tag="msq")
    pr.op("v", lambda e: e.scalar_tensor_tensor(nm[:, T:T] if False else rs[:],
          nm[:], -1.0, rs[:], ALU.mult, ALU.bypass) if False else
          e.tensor_tensor(sq[:1, :1], nm[:1, :1], nm[:1, :1], ALU.mult),
          reads=(), writes=(), tag="dead") if False else None
    mm2 = p.sbuf("s5_mm2", [1, T], BF16)
    pr.op("v", lambda e: e.tensor_tensor(mm2[:], nm[:], nm[:], ALU.mult),
          reads=("s5nm",), writes=("s5mm2",), tag="m2")
    pr.op("v", lambda e: e.tensor_tensor(rs[:], rs[:], mm2[:], ALU.subtract),
          reads=("s5rs", "s5mm2"), writes=("s5rs",), tag="var")
    rstd_strip(p, pr, rs[:], "s5rs", 1.0, 1e-5, None, n=1)
    for ch in range(8):
        sl = slice(ch * 512, (ch + 1) * 512)
        pr.dma("y", xc[:].rearrange("p (f t) -> p f t", t=512),
               xb1A[:, :, sl], writes=("s5xc",), tag=f"s5 ldx2_{ch}")
        bcast_strip(p, pr, nm[:, sl], "s5nm", 512, pf0, "pf0", f32=False)
        bcast_strip(p, pr, rs[:, sl], "s5rs", 512, pf1, "pf1", f32=True)
        for f in range(NF):
            pr.op("v", lambda e, f=f: e.tensor_tensor(tmpf[:], xc[:, f * 512:(f + 1) * 512],
                                                      pf0[:, :512], ALU.add),
                  reads=("s5xc", "pf0"), writes=("s5tmpf",), tag="xmn")
            pr.op("v", lambda e, f=f, sl=sl: e.tensor_tensor(
                  xn1[:, f * T:][:, sl], tmpf[:], pf1[:, :512], ALU.mult),
                  reads=("s5tmpf", "pf1"), writes=("s5xn1",), tag="xn1")
        pr.dma("y", xn1A[:, :, sl],
               xn1[:].rearrange("p (f t) -> p f t", t=T)[:, :, sl],
               reads=("s5xn1",), tag=f"s5 stxn{ch}")

    def rhs(k, ch):
        return xn1[:, k * T + ch * 512: k * T + ch * 512 + 512], "s5xn1"

    ob = [p.sbuf(f"s5_o{i}", [128, T], BF16) for i in range(2)]

    def mk_evac(dst, act=None):
        def evac(m, ch, ba, bres):
            ms = m % 2
            O = f"s5o{ms}"
            if act is None:
                pr.op("v", lambda e, ms=ms, ch=ch, ba=ba: e.tensor_copy(
                      ob[ms][:, ch * 512:(ch + 1) * 512], ba[:, :512]),
                      reads=(bres,), writes=(O,), tag=f"s5 ev{m}_{ch}")
            else:
                pr.op("s", lambda e, ms=ms, ch=ch, ba=ba: e.activation(
                      ob[ms][:, ch * 512:(ch + 1) * 512], ba[:, :512], act),
                      reads=(bres,), writes=(O,), tag=f"s5 evg{m}_{ch}")
            if ch == 7:
                pr.dma("y", dst[:, m, :], ob[ms][:], reads=(O,), tag=f"s5 st{m}")
        return evac
    emit_gemm(p, pr, "s5q", win["l1_Wq"], C, C, rhs, mk_evac(q1A), banks, "s5wq")
    emit_gemm(p, pr, "s5k", win["l1_Wk"], C, C, rhs, mk_evac(k1A), banks, "s5wk")
    emit_gemm(p, pr, "s5v", win["l1_Wv"], C, C, rhs, mk_evac(v1A), banks, "s5wv")
    emit_gemm(p, pr, "s5g", win["l1_Wg"], C, C, rhs, mk_evac(gt1A, AF.Silu), banks, "s5wg")

    wal = p.sbuf("s5_wal", [128, NF * 16], BF16)
    wbe = p.sbuf("s5_wbe", [128, NF * 16], BF16)
    pr.dma("g", wal[:].rearrange("r (nk c) -> r nk c", c=16),
           win["l1_Wal"].rearrange("(nk r) c -> r nk c", r=128), writes=("s5wal",))
    pr.dma("g", wbe[:].rearrange("r (nk c) -> r nk c", c=16),
           win["l1_Wbe"].rearrange("(nk r) c -> r nk c", r=128), writes=("s5wbe",))
    dtb = p.sbuf("s5_dtb", [16, 1], F32)
    pr.dma("y", dtb[:], win["l1_dtb"], writes=("s5dtb",))
    alog = p.sbuf("s5_alog", [16, 1], F32)
    pr.dma("y", alog[:], win["l1_Alog"], writes=("s5alog",))
    negA = p.sbuf("s5_negA", [16, 1], F32)
    pr.op("s", lambda e: e.activation(negA[:], alog[:], AF.Exp),
          reads=("s5alog",), writes=("s5negA",))
    pr.op("v", lambda e: e.tensor_scalar_mul(negA[:], negA[:], -1.0),
          reads=("s5negA",), writes=("s5negA",))
    H1 = 16
    bss = [p.sbuf(f"s5_bs{i}", [H1, 512], F32) for i in range(2)]
    gss = [p.sbuf(f"s5_gs{i}", [H1, 512], F32) for i in range(2)]
    gcs = [p.sbuf(f"s5_gc{i}", [H1, 512], F32) for i in range(2)]
    tt1 = [p.sbuf(f"s5_t1{i}", [H1, 512], F32) for i in range(2)]
    tt2 = [p.sbuf(f"s5_t2{i}", [H1, 512], F32) for i in range(2)]
    egs = [p.sbuf(f"s5_eg{i}", [H1, 512], F32) for i in range(2)]
    ers = [p.sbuf(f"s5_er{i}", [H1, 512], F32) for i in range(2)]
    bes = [p.sbuf(f"s5_be{i}", [H1, 512], F32) for i in range(2)]
    zcol = p.sbuf("s5_zcol", [H1, 1], F32)
    pr.op("v", lambda e: e.memset(zcol[:], 0.0), writes=("s5zcol",))
    for ch in range(8):
        cs = ch % 2
        ba_b, bres_b = banks2[0]
        ba_a, bres_a = banks2[1]
        for k in range(NF):
            pr.op("t", lambda e, k=k, ch=ch, ba=ba_b: e.matmul(
                  ba[:H1, :512], wbe[:, k * 16:(k + 1) * 16], rhs(k, ch)[0],
                  start=(k == 0), stop=(k == NF - 1)),
                  reads=("s5wbe", "s5xn1"), writes=(bres_b,), tag=f"s5 bmm{ch}")
            pr.op("t", lambda e, k=k, ch=ch, ba=ba_a: e.matmul(
                  ba[:H1, :512], wal[:, k * 16:(k + 1) * 16], rhs(k, ch)[0],
                  start=(k == 0), stop=(k == NF - 1)),
                  reads=("s5wal", "s5xn1"), writes=(bres_a,), tag=f"s5 amm{ch}")
        B, G, GC = f"s5bs{cs}", f"s5gs{cs}", f"s5gc{cs}"
        pr.op("s", lambda e, cs=cs, ba=ba_b: e.activation(bss[cs][:], ba[:H1, :512], AF.Sigmoid),
              reads=(bres_b,), writes=(B,), tag="s5 beta")
        softplus_block(p, pr, ba_a[:H1, :512], bres_a, H1, 512,
                       tt1[cs][:], f"s5t1{cs}", tt2[cs][:], f"s5t2{cs}",
                       dtb[:], "s5dtb", negA[:], "s5negA",
                       gss[cs][:], G)
        for q in range(4):
            sl = slice(q * 128, (q + 1) * 128)
            pr.op("v", lambda e, cs=cs, sl=sl: e.tensor_tensor_scan(
                  gcs[cs][:, sl], gss[cs][:, sl], gss[cs][:, sl], zcol[:],
                  ALU.add, ALU.bypass),
                  reads=(G, "s5zcol"), writes=(GC,), tag="s5 cumsum")
        for q in range(4):
            sl = slice(q * 128, (q + 1) * 128)
            pr.op("v", lambda e, cs=cs, sl=sl, q=q: e.tensor_scalar(
                  gss[cs][:, sl], gcs[cs][:, sl], -1.0,
                  gcs[cs][:, q * 128 + 127: q * 128 + 128], ALU.mult, ALU.add),
                  reads=(GC,), writes=(G,), tag="s5 grev")
        EG, ER, BE = f"s5eg{cs}", f"s5er{cs}", f"s5be{cs}"
        pr.op("s", lambda e, cs=cs: e.activation(egs[cs][:], gcs[cs][:], AF.Exp),
              reads=(GC,), writes=(EG,), tag="s5 Eg")
        pr.op("s", lambda e, cs=cs: e.activation(ers[cs][:], gss[cs][:], AF.Exp),
              reads=(G,), writes=(ER,), tag="s5 Erev")
        pr.op("v", lambda e, cs=cs: e.tensor_tensor(bes[cs][:], bss[cs][:], egs[cs][:], ALU.mult),
              reads=(B, EG), writes=(BE,), tag="s5 bEg")
        sl5 = slice(ch * 512, (ch + 1) * 512)
        pr.dma("y", bg1[0, :, sl5], gcs[cs][:], reads=(GC,), tag=f"s5 stg{ch}")
        pr.dma("y", bg1[1, :, sl5], egs[cs][:], reads=(EG,), tag=f"s5 steg{ch}")
        pr.dma("y", bg1[2, :, sl5], ers[cs][:], reads=(ER,), tag=f"s5 ster{ch}")
        pr.dma("y", bg1[3, :, sl5], bss[cs][:], reads=(B,), tag=f"s5 stb{ch}")
        pr.dma("y", bg1[4, :, sl5], bes[cs][:], reads=(BE,), tag=f"s5 stbe{ch}")



def emit_s5b(p, pr, q1A, k1A, v1A, gt1A, bg1, gated1A, win):
    idf = p.sb["c_idf"]
    pf0 = p.psum_bank("pf0")
    pf1 = p.psum_bank("pf1")
    H1 = 16
    stage = p.sbuf("s5b_stage", [H1, T], F32)
    gT = p.sbuf("s5b_gT", [128, NCH * H1], F32)
    gTn = p.sbuf("s5b_gTn", [128, NCH * H1], F32)
    beT = p.sbuf("s5b_beT", [128, NCH * H1], F32)
    begT = p.sbuf("s5b_begT", [128, NCH * H1], F32)
    erT = p.sbuf("s5b_erT", [128, NCH * H1], F32)
    for row, dst, res in ((0, gT, "s5bgT"), (3, beT, "s5bbeT"),
                          (4, begT, "s5bbegT"), (2, erT, "s5berT")):
        pr.dma("y", stage[:], bg1[row], writes=("s5bstage",), tag=f"s5b ldbg{row}")
        for i in range(NCH):
            pr.op("t", lambda e, i=i: e.transpose(
                  pf0[:, (i % 4) * 128:(i % 4) * 128 + H1],
                  stage[:, i * CH:(i + 1) * CH], idf[:H1, :H1]),
                  reads=("s5bstage", "c_idf"), writes=("pf0",), tag=f"s5b tr{row}_{i}")
            pr.op("v", lambda e, i=i, dst=dst, res=res: e.tensor_copy(
                  dst[:, i * H1:(i + 1) * H1], pf0[:, (i % 4) * 128:(i % 4) * 128 + H1]),
                  reads=("pf0",), writes=(res,), tag=f"s5b ev{row}_{i}")
    pr.op("v", lambda e: e.tensor_scalar_mul(gTn[:], gT[:], -1.0),
          reads=("s5bgT",), writes=("s5bgTn",))
    rw = p.sbuf("s5b_rw", [128, 1], F32)
    pr.dma("y", rw[:], win["l1_rmsw"], writes=("s5brw",))

    qp = p.sbuf("s5b_qp", [128, T], BF16)
    kp = p.sbuf("s5b_kp", [128, T], BF16)
    vp = p.sbuf("s5b_vp", [128, T], BF16)
    gp = p.sbuf("s5b_gp", [128, T], BF16)
    sqb = p.sbuf("s5b_sq", [128, T], BF16)
    core = p.sbuf("s5b_core", [128, T], BF16)
    gcore = p.sbuf("s5b_gcore", [128, T], BF16)
    qn = p.sbuf("s5b_qn", [128, T], BF16)
    kn = p.sbuf("s5b_kn", [128, T], BF16)
    ssf = p.sbuf("s5b_ssf", [1, T], F32)
    g_st = p.sbuf("s5b_gst", [1, T], F32)
    eg_st = p.sbuf("s5b_egst", [1, T], F32)
    be_st = p.sbuf("s5b_best", [1, T], BF16)
    egl = p.sbuf("s5b_egl", [128, NCH], F32)
    Sst = p.sbuf("s5b_S", [64, 64], BF16)

    LN64 = float(np.log(64.0 ** -0.5))
    for hp in range(8):
        pr.dma("y", qp[:], q1A[:, hp, :], writes=("s5bqp",), tag=f"s5b ldq{hp}")
        pr.dma("y", kp[:], k1A[:, hp, :], writes=("s5bkp",), tag=f"s5b ldk{hp}")
        pr.dma("y", vp[:], v1A[:, hp, :], writes=("s5bvp",), tag=f"s5b ldv{hp}")
        pr.dma("y", gp[:], gt1A[:, hp, :], writes=("s5bgp",), tag=f"s5b ldg{hp}")
        for hh in range(2):
            h = hp * 2 + hh
            rws = slice(hh * 64, hh * 64 + 64)
            pr.dma("y", g_st[:], bg1[0, h:h + 1, :], writes=("s5bgst",), tag=f"ldgs{h}")
            pr.dma("y", eg_st[:], bg1[1, h:h + 1, :], writes=("s5begst",), tag=f"ldegs{h}")
            pr.dma("g", be_st[:], bg1[3, h:h + 1, :], writes=("s5bbest",), tag=f"ldbes{h}")
            bcast_strip(p, pr, eg_st[:, CH - 1::CH], "s5begst", NCH, pf0, "pf0", f32=True)
            pr.op("v", lambda e: e.tensor_copy(egl[:], pf0[:, :NCH]),
                  reads=("pf0",), writes=("s5begl",), tag="egl")
            for srcn, srcr, dstn, dstr, eb in ((qp, "s5bqp", qn, "s5bqn", LN64),
                                               (kp, "s5bkp", kn, "s5bkn", None)):
                pr.op("s", lambda e, srcn=srcn, rws=rws: e.activation(
                      sqb[rws, :], srcn[rws, :], AF.Square),
                      reads=(srcr,), writes=("s5bsq",), tag="sq")
                for ch in range(8):
                    pr.op("t", lambda e, ch=ch, rws=rws: e.matmul(
                          pf1[:1, :512], p.sb["c_ones_c1b"][rws, :],
                          sqb[rws, ch * 512:(ch + 1) * 512], start=True, stop=True),
                          reads=("s5bsq", "c_ones_c1b"), writes=("pf1",), tag=f"ssq{ch}")
                    pr.op("v", lambda e, ch=ch: e.tensor_copy(
                          ssf[:, ch * 512:(ch + 1) * 512], pf1[:1, :512]),
                          reads=("pf1",), writes=("s5bssf",), tag=f"ssqe{ch}")
                rstd_strip(p, pr, ssf[:], "s5bssf", 1.0, 1e-18, eb, n=1)
                for ch in range(8):
                    sl = slice(ch * 512, (ch + 1) * 512)
                    bcast_strip(p, pr, ssf[:, sl], "s5bssf", 512, pf0, "pf0", f32=True)
                    pr.op("v", lambda e, sl=sl, rws=rws, srcn=srcn, dstn=dstn:
                          e.tensor_tensor(dstn[rws, sl], srcn[rws, sl],
                                          pf0[rws, :512], ALU.mult),
                          reads=(srcr, "pf0"), writes=(dstr,), tag="l2mul")
            pr.op("v", lambda e: e.memset(Sst[:], 0.0), writes=("s5bS",), tag="S0")
            emit_delta_head(
                p, pr, "s5d", 64, 64, H1, h,
                kn[rws, :], "s5bkn", qn[rws, :], "s5bqn", vp[rws, :], "s5bvp",
                dict(g=(g_st[:], "s5bgst"), Eg=(eg_st[:], "s5begst"),
                     be=(be_st[:], "s5bbest")),
                dict(gTn=(gTn[:, :], "s5bgTn"), beT=(beT[:, :], "s5bbeT"),
                     erT=(erT[:, :], "s5berT"), begT=(begT[:, :], "s5bbegT")),
                (egl[:], "s5begl"), Sst[:], "s5bS", core[rws, :], "s5bcore",
                idsl=rws)
            pr.op("s", lambda e, rws=rws: e.activation(sqb[rws, :], core[rws, :], AF.Square),
                  reads=("s5bcore",), writes=("s5bsq",), tag="csq")
            for ch in range(8):
                pr.op("t", lambda e, ch=ch, rws=rws: e.matmul(
                      pf1[:1, :512], p.sb["c_ones_c1b"][rws, :],
                      sqb[rws, ch * 512:(ch + 1) * 512], start=True, stop=True),
                      reads=("s5bsq", "c_ones_c1b"), writes=("pf1",), tag=f"cs{ch}")
                pr.op("v", lambda e, ch=ch: e.tensor_copy(
                      ssf[:, ch * 512:(ch + 1) * 512], pf1[:1, :512]),
                      reads=("pf1",), writes=("s5bssf",), tag=f"cse{ch}")
            rstd_strip(p, pr, ssf[:], "s5bssf", 1.0 / 64.0, 1e-6, None, n=1)
            for ch in range(8):
                sl = slice(ch * 512, (ch + 1) * 512)
                bcast_strip(p, pr, ssf[:, sl], "s5bssf", 512, pf0, "pf0", f32=True)
                pr.op("v", lambda e, sl=sl, rws=rws: e.scalar_tensor_tensor(
                      gcore[rws, sl], core[rws, sl], rw[rws, :], pf0[rws, :512],
                      ALU.mult, ALU.mult),
                      reads=("s5bcore", "s5brw", "pf0"), writes=("s5bgcore",), tag="gc1")
                pr.op("v", lambda e, sl=sl, rws=rws: e.tensor_tensor(
                      gcore[rws, sl], gcore[rws, sl], gp[rws, sl], ALU.mult),
                      reads=("s5bgcore", "s5bgp"), writes=("s5bgcore",), tag="gc2")
        pr.dma("y", gated1A[:, hp, :], gcore[:], reads=("s5bgcore",), tag=f"s5b st{hp}")


def emit_s5c(p, pr, gated1A, Wo_hbm, s1A):
    gt = p.sbuf("s5c_g", [128, NF * T], BF16)
    pr.dma("y", gt[:].rearrange("p (f t) -> p f t", t=T), gated1A,
           writes=("s5cg",), tag="s5c ldg")

    def rhs(k, ch):
        return gt[:, k * T + ch * 512: k * T + ch * 512 + 512], "s5cg"

    banks = [(p.psum_bank("pf0"), "pf0"), (p.psum_bank("pf1"), "pf1")]
    ob = [p.sbuf(f"s5c_o{i}", [128, T], BF16) for i in range(2)]

    def evac(m, ch, ba, bres):
        ms = m % 2
        O = f"s5co{ms}"
        pr.op("v", lambda e, ms=ms, ch=ch, ba=ba: e.tensor_copy(
              ob[ms][:, ch * 512:(ch + 1) * 512], ba[:, :512]),
              reads=(bres,), writes=(O,), tag=f"s5c ev{m}_{ch}")
        if ch == 7:
            pr.dma("y", s1A[:, m, :], ob[ms][:], reads=(O,), tag=f"s5c st{m}")
    emit_gemm(p, pr, "s5c", Wo_hbm, C, C, rhs, evac, banks, "s5cw")



def emit_s6(p, pr, xn1A, xb1A, s1A, y_hbm, comb1_hbm, win):
    idf = p.sb["c_idf"]
    pf0 = p.psum_bank("pf0")
    pf1 = p.psum_bank("pf1")
    pfA = p.psum_bank("pf2")
    pfB = p.psum_bank("pf3")
    pfC = p.psum_bank("pf4")
    pfD = p.psum_bank("pf5")
    HID = 2816
    NH2 = HID // 128
    xn1 = p.sbuf("s6_xn1", [128, NF * T], BF16)
    pr.dma("y", xn1[:].rearrange("p (f t) -> p f t", t=T), xn1A,
           writes=("s6xn",), tag="s6 ldxn")

    wr = p.sbuf("s6_wr", [128, NF * 7], BF16)
    pr.dma("g", wr[:].rearrange("r (nk c) -> r nk c", c=7),
           win["f1_Wr"].rearrange("(nk r) c -> r nk c", r=128), writes=("s6wr",))
    rb = p.sbuf("s6_rb", [128, 7], F32)
    pr.dma("y", rb[:], win["f1_rb"], writes=("s6rb",))
    probs = p.sbuf("s6_probs", [128, 7], F32)
    wts = p.sbuf("s6_wts", [128, 7], F32)
    comb7 = p.sbuf("s6_comb7", [7, 128], F32)
    for i in range(NT):
        for k in range(NF):
            pr.op("t", lambda e, i=i, k=k: e.matmul(
                  pf0[:, :7], xn1[:, k * T + i * 128: k * T + (i + 1) * 128],
                  wr[:, k * 7:(k + 1) * 7], start=(k == 0), stop=(k == NF - 1)),
                  reads=("s6xn", "s6wr"), writes=("pf0",), tag=f"s6 rt{i}_{k}")
        pr.op("v", lambda e: e.tensor_tensor(probs[:], pf0[:, :7], rb[:], ALU.mult),
              reads=("pf0", "s6rb"), writes=("s6probs",), tag="lg")
        pr.op("s", lambda e: e.activation(probs[:], probs[:], AF.Sigmoid),
              reads=("s6probs",), writes=("s6probs",), tag="sig")
        emit_top2(p, pr, "s6t", probs[:], "s6probs", 7, wts[:], "s6wts", "sigmoid")
        pr.op("t", lambda e, i=i: e.transpose(pf1[:7, :128], wts[:], idf[:]),
              reads=("s6wts", "c_idf"), writes=("pf1",), tag=f"s6 rtt{i}")
        pr.op("v", lambda e, i=i: e.tensor_copy(comb7[:], pf1[:7, :128]),
              reads=("pf1",), writes=("s6comb7",), tag=f"s6 rte{i}")
        pr.dma("y", comb1_hbm[:, i * 128:(i + 1) * 128], comb7[:],
               reads=("s6comb7",), writes=("d_comb1",), tag=f"s6 stc{i}")

    TC = 1024
    hsb = p.sbuf("s6_h", [128, NH2 * TC], BF16)
    acc = p.sbuf("s6_acc", [128, NF * TC], BF16)
    w12 = [p.sbuf(f"s6_w12{i}", [128, NF * 128], BF16) for i in range(4)]
    w3b = [p.sbuf(f"s6_w3{i}", [128, NH2 * 128], BF16) for i in range(2)]
    cstr = p.sbuf("s6_cstr", [1, TC], BF16)
    s1t = p.sbuf("s6_s1t", [128, 512], BF16)
    yt = p.sbuf("s6_yt", [128, 512], F32)
    gbuf = p.sbuf("s6_gb", [128, 512], BF16)
    sbuf0 = p.sbuf("s6_sb", [128, 512], BF16)
    xab = p.sbuf("s6_xa", [128, 512], F32)
    ot = p.sbuf("s6_ot", [128, 512], F32)
    asm = p.sbuf("s6_asm", [128, 4 * 1024], F32)

    for tc in range(4):
        tsl = slice(tc * TC, (tc + 1) * TC)
        pr.op("v", lambda e: e.memset(acc[:], 0.0), writes=("s6acc",), tag=f"s6 z{tc}")
        for ex in range(8):
            routed = ex > 0
            if routed:
                wg_h = win["f1_rg"][ex - 1]
                wu_h = win["f1_ru"][ex - 1]
                wd_h = win["f1_rd"][ex - 1]
                pr.dma("g", cstr[:], comb1_hbm[ex - 1:ex, tsl],
                       reads=("d_comb1",), writes=("s6cstr",), tag=f"s6 ldc{tc}_{ex}")
            else:
                wg_h, wu_h, wd_h = win["f1_sg"], win["f1_su"], win["f1_sd"]
            for m in range(NH2):
                wp = m % 2
                W1R, W2R = f"s6w12{2 * wp}", f"s6w12{2 * wp + 1}"
                pr.dma("g", w12[2 * wp][:].rearrange("r (nk c) -> r nk c", c=128),
                       wg_h[:, m * 128:(m + 1) * 128].rearrange("(nk r) c -> r nk c", r=128),
                       writes=(W1R,), tag=f"s6 wg{ex}_{m}")
                pr.dma("g", w12[2 * wp + 1][:].rearrange("r (nk c) -> r nk c", c=128),
                       wu_h[:, m * 128:(m + 1) * 128].rearrange("(nk r) c -> r nk c", r=128),
                       writes=(W2R,), tag=f"s6 wu{ex}_{m}")
                for sub in range(2):
                    ssl = slice(tc * TC + sub * 512, tc * TC + sub * 512 + 512)
                    for k in range(NF):
                        pr.op("t", lambda e, k=k, wp=wp, ssl=ssl: e.matmul(
                              pfA[:, :512], w12[2 * wp][:, k * 128:(k + 1) * 128],
                              xn1[:, k * T:][:, ssl], start=(k == 0), stop=(k == NF - 1)),
                              reads=(W1R, "s6xn"), writes=("pf2",), tag="h1mm")
                        pr.op("t", lambda e, k=k, wp=wp, ssl=ssl: e.matmul(
                              pfB[:, :512], w12[2 * wp + 1][:, k * 128:(k + 1) * 128],
                              xn1[:, k * T:][:, ssl], start=(k == 0), stop=(k == NF - 1)),
                              reads=(W2R, "s6xn"), writes=("pf3",), tag="h2mm")
                    pr.op("s", lambda e: e.activation(s1t[:], pfA[:, :512], AF.Silu),
                          reads=("pf2",), writes=("s6s1t",), tag="silu1")
                    pr.op("v", lambda e, m=m, sub=sub: e.tensor_tensor(
                          hsb[:, m * TC + sub * 512: m * TC + sub * 512 + 512],
                          s1t[:], pfB[:, :512], ALU.mult),
                          reads=("s6s1t", "pf3"), writes=("s6h",), tag="hmul")
            for mo in range(NF):
                wp = mo % 2
                W3R = f"s6w3{wp}"
                pr.dma("g", w3b[wp][:].rearrange("r (nk c) -> r nk c", c=128),
                       wd_h[:, mo * 128:(mo + 1) * 128].rearrange("(nk r) c -> r nk c", r=128),
                       writes=(W3R,), tag=f"s6 wd{ex}_{mo}")
                for sub in range(2):
                    for k in range(NH2):
                        pr.op("t", lambda e, k=k, wp=wp, sub=sub: e.matmul(
                              pfC[:, :512], w3b[wp][:, k * 128:(k + 1) * 128],
                              hsb[:, k * TC + sub * 512: k * TC + sub * 512 + 512],
                              start=(k == 0), stop=(k == NH2 - 1)),
                              reads=(W3R, "s6h"), writes=("pf4",), tag="ymm")
                    if routed:
                        bcast_strip(p, pr, cstr[:, sub * 512:(sub + 1) * 512], "s6cstr",
                                    512, pfD, "pf5", f32=False)
                        pr.op("v", lambda e: e.tensor_tensor(yt[:], pfC[:, :512],
                                                             pfD[:, :512], ALU.mult),
                              reads=("pf4", "pf5"), writes=("s6yt",), tag="cmul") \
                            if False else None
                        pr.op("v", lambda e: e.tensor_copy(yt[:], pfC[:, :512]),
                              reads=("pf4",), writes=("s6yt",), tag="ycp")
                        pr.op("v", lambda e: e.tensor_tensor(yt[:], yt[:], pfD[:, :512],
                                                             ALU.mult),
                              reads=("s6yt", "pf5"), writes=("s6yt",), tag="cmul")
                    else:
                        pr.op("v", lambda e: e.tensor_copy(yt[:], pfC[:, :512]),
                              reads=("pf4",), writes=("s6yt",), tag="ycp")
                    pr.op("v", lambda e, mo=mo, sub=sub: e.tensor_tensor(
                          acc[:, mo * TC + sub * 512: mo * TC + sub * 512 + 512],
                          acc[:, mo * TC + sub * 512: mo * TC + sub * 512 + 512],
                          yt[:], ALU.add),
                          reads=("s6acc", "s6yt"), writes=("s6acc",), tag="yacc")
        for sub in range(2):
            for f in range(NF):
                ssl = slice(tc * TC + sub * 512, tc * TC + sub * 512 + 512)
                pr.op("s", lambda e, f=f, sub=sub: e.activation(
                      gbuf[:], acc[:, f * TC + sub * 512: f * TC + sub * 512 + 512],
                      AF.Silu),
                      reads=("s6acc",), writes=("s6gb",), tag="g1")
                pr.dma("y", sbuf0[:], s1A[:, f, ssl], writes=("s6sb",), tag="ld s1")
                pr.dma("y", xab[:], xb1A[:, f, ssl], writes=("s6xa",), tag="ld xb1")
                pr.op("v", lambda e: e.tensor_tensor(ot[:], gbuf[:], sbuf0[:], ALU.mult),
                      reads=("s6gb", "s6sb"), writes=("s6ot",), tag="sg")
                pr.op("v", lambda e: e.tensor_tensor(ot[:], ot[:], xab[:], ALU.add),
                      reads=("s6ot", "s6xa"), writes=("s6ot",), tag="out")
                for q in range(4):
                    pr.op("t", lambda e, q=q: e.transpose(
                          pf1[:, (q % 1) * 128:128] if False else pf1[:, :128],
                          ot[:, q * 128:(q + 1) * 128], idf[:]),
                          reads=("s6ot", "c_idf"), writes=("pf1",), tag=f"ftr{f}_{q}")
                    pr.op("v", lambda e, q=q, f=f: e.tensor_copy(
                          asm[:, q * 1024 + f * 128: q * 1024 + (f + 1) * 128],
                          pf1[:, :128]),
                          reads=("pf1",), writes=("s6asm",), tag=f"fev{f}_{q}")
            for q in range(4):
                t0 = tc * TC + sub * 512 + q * 128
                pr.dma("y", y_hbm[t0:t0 + 128, :], asm[:, q * 1024:(q + 1) * 1024],
                       reads=("s6asm",), tag=f"sty{t0}")



def _mk_inmaps(p, host, per_core):
    consts = {k: host[k] for k in CONST_DEFS if k in p.dram}
    weights = {k: host[k] for k in host if not k.startswith("c_") and k in p.dram}
    return [dict(consts, **weights, **pc) for pc in per_core]


def run_s0_only(inputs, nt=NT):
    host = host_layout(inputs)
    x = inputs["L0_attn_x"]
    p = P()
    x_hbm = p.dram_t("x", [T, C], F32, kind="ExternalInput")
    xA = p.dram_t("xA", [128, NF, T], F32, kind="ExternalOutput")
    xnA = p.dram_t("xnA", [128, NF, T], BF16, kind="ExternalOutput")
    pr = p.stage("s0")
    load_consts(p, pr)
    emit_s0(p, pr, x_hbm, xA, xnA, nt=nt)
    pr.build()
    in_maps = _mk_inmaps(p, host, [dict(x=np.ascontiguousarray(x[b])) for b in range(8)])
    return run_bass_kernel_spmd(p.nc, in_maps, core_ids=list(range(8)))


def run_s1_only(xnA_np, host):
    p = P()
    xnA = p.dram_t("xnA", [128, NF, T], BF16, kind="ExternalInput")
    qcA = p.dram_t("qcA", [128, 8, T], BF16, kind="ExternalOutput")
    kcA = p.dram_t("kcA", [128, 8, T], BF16, kind="ExternalOutput")
    vcA = p.dram_t("vcA", [128, 8, T], BF16, kind="ExternalOutput")
    zsA = p.dram_t("zsA", [128, 8, T], BF16, kind="ExternalOutput")
    bg = p.dram_t("bg", [5, 8, T], F32, kind="ExternalOutput")
    win = {k: p.dram_t(k, list(host[k].shape), F32, kind="ExternalInput")
           for k in ("l0_Wq", "l0_Wk", "l0_Wv", "l0_Wz", "l0_Wb", "l0_Wa",
                     "l0_cwq", "l0_cwk", "l0_cwv", "l0_dtb", "l0_Alog")}
    pr = p.stage("s1")
    load_consts(p, pr)
    emit_s1(p, pr, xnA, qcA, kcA, vcA, zsA, bg, win)
    pr.build()
    in_maps = _mk_inmaps(p, host, [dict(xnA=xnA_np[b]) for b in range(8)])
    return run_bass_kernel_spmd(p.nc, in_maps, core_ids=list(range(8)))


def run_s2_only(s1res, host, dbg=False):
    p = P()
    p.dbg = dbg
    p.dbg_chunk = (0, 0)
    qcA = p.dram_t("qcA", [128, 8, T], BF16, kind="ExternalInput")
    kcA = p.dram_t("kcA", [128, 8, T], BF16, kind="ExternalInput")
    vcA = p.dram_t("vcA", [128, 8, T], BF16, kind="ExternalInput")
    zsA = p.dram_t("zsA", [128, 8, T], BF16, kind="ExternalInput")
    bg = p.dram_t("bg", [5, 8, T], F32, kind="ExternalInput")
    nw = p.dram_t("l0_nw", [128, 1], F32, kind="ExternalInput")
    WoT = p.dram_t("l0_WoT", [C, C], F32, kind="ExternalInput")
    gatedA = p.dram_t("gatedA", [128, 8, T], BF16, kind="ExternalOutput")
    s0A = p.dram_t("s0A", [128, 8, T], BF16, kind="ExternalOutput")
    with p.scope():
        pr = p.stage("s2a")
        load_consts(p, pr)
        emit_s2a(p, pr, qcA, kcA, vcA, zsA, bg, gatedA, nw)
        pr.build()
    with p.scope():
        pr2 = p.stage("s2b")
        emit_s2b(p, pr2, gatedA, WoT, s0A)
        pr2.build()
    per_core = [dict(qcA=np.asarray(r["qcA"]), kcA=np.asarray(r["kcA"]),
                     vcA=np.asarray(r["vcA"]), zsA=np.asarray(r["zsA"]),
                     bg=np.asarray(r["bg"])) for r in s1res]
    in_maps = _mk_inmaps(p, host, per_core)
    return run_bass_kernel_spmd(p.nc, in_maps, core_ids=list(range(8)))


def run_s3_only(xnA_np, host, stages="abc"):
    p = P()
    xnA = p.dram_t("xnA", [128, NF, T], BF16, kind="ExternalInput")
    qkvA = p.dram_t("qkvA", [128, 24, T], BF16,
                    kind="Internal" if "b" in stages else "ExternalOutput")
    if "b" in stages or "c" in stages:
        oAh = p.dram_t("oAh", [128, 8, T], BF16,
                       kind="ExternalOutput" if "b" in stages else "ExternalInput")
    else:
        oAh = None
    x1fA = (p.dram_t("x1fA", [128, 8, T], F32, kind="ExternalOutput")
            if "c" in stages else None)
    win = {k: p.dram_t(k, list(host[k].shape), F32, kind="ExternalInput")
           for k in ("f0_n1", "f0_qkvT", "f0_qkvb", "f0_oT", "f0_ob")}
    first = [True]

    def st(name, fn):
        with p.scope():
            pr = p.stage(name)
            if first[0]:
                load_consts(p, pr)
                first[0] = False
            fn(pr)
            pr.build()
    if "a" in stages:
        st("s3a", lambda pr: emit_s3a(p, pr, xnA, qkvA, win))
    if "b" in stages:
        st("s3b", lambda pr: emit_s3b(p, pr, qkvA, oAh))
    if "c" in stages:
        st("s3c", lambda pr: emit_s3c(p, pr, oAh, xnA, x1fA, win))
    per = [dict(xnA=xnA_np[b]) for b in range(8)]
    in_maps = _mk_inmaps(p, host, per)
    return run_bass_kernel_spmd(p.nc, in_maps, core_ids=list(range(8)))


def run_s4_only(per_core, host, dbg=False):
    p = P()
    p.dbg = dbg
    x1fA = p.dram_t("x1fA", [128, 8, T], F32, kind="ExternalInput")
    s0A = p.dram_t("s0A", [128, 8, T], BF16, kind="ExternalInput")
    xA = p.dram_t("xA", [128, 8, T], F32, kind="ExternalInput")
    xb1A = p.dram_t("xb1A", [128, 8, T], F32, kind="ExternalOutput")
    comb = p.dram_t("comb8", [8, T], F32, kind="ExternalOutput")
    win = {k: p.dram_t(k, list(host[k].shape), F32, kind="ExternalInput")
           for k in ("f0_n2", "f0_gT", "f0_w1T", "f0_w2T", "f0_w3T",
                     "f0_b1", "f0_b2", "f0_b3")}
    with p.scope():
        pr = p.stage("s4")
        load_consts(p, pr)
        emit_s4(p, pr, x1fA, s0A, xA, xb1A, comb, win)
        pr.build()
    in_maps = _mk_inmaps(p, host, per_core)
    return run_bass_kernel_spmd(p.nc, in_maps, core_ids=list(range(8)))


BOUND = {
    "xA": ([128, NF, T], F32), "xnA": ([128, NF, T], BF16),
    "qcA": ([128, 8, T], BF16), "kcA": ([128, 8, T], BF16),
    "vcA": ([128, 8, T], BF16), "zsA": ([128, 8, T], BF16),
    "bg": ([5, 8, T], F32), "gatedA": ([128, 8, T], BF16),
    "s0A": ([128, 8, T], BF16), "qkvA": ([128, 24, T], BF16),
    "oAh": ([128, 8, T], BF16), "x1fA": ([128, 8, T], F32),
    "comb8": ([8, T], F32), "xb1A": ([128, 8, T], F32),
    "xn1A": ([128, 8, T], BF16), "q1A": ([128, 8, T], BF16),
    "k1A": ([128, 8, T], BF16), "v1A": ([128, 8, T], BF16),
    "gt1A": ([128, 8, T], BF16), "bg1": ([5, 16, T], F32),
    "gated1A": ([128, 8, T], BF16), "s1A": ([128, 8, T], BF16),
    "comb1": ([7, T], F32),
    "x": ([T, C], F32), "y": ([T, C], F32),
}

STAGE_DEFS = [
    ("s0", ["x"], ["xA", "xnA"],
     lambda p, pr, t, w: emit_s0(p, pr, t["x"], t["xA"], t["xnA"])),
    ("s1", ["xnA"], ["qcA", "kcA", "vcA", "zsA", "bg"],
     lambda p, pr, t, w: emit_s1(p, pr, t["xnA"], t["qcA"], t["kcA"], t["vcA"],
                                 t["zsA"], t["bg"], w)),
    ("s2a", ["qcA", "kcA", "vcA", "zsA", "bg"], ["gatedA"],
     lambda p, pr, t, w: emit_s2a(p, pr, t["qcA"], t["kcA"], t["vcA"], t["zsA"],
                                  t["bg"], t["gatedA"], w["l0_nw"])),
    ("s2b", ["gatedA"], ["s0A"],
     lambda p, pr, t, w: emit_s2b(p, pr, t["gatedA"], w["l0_WoT"], t["s0A"])),
    ("s3a", ["xnA"], ["qkvA"],
     lambda p, pr, t, w: emit_s3a(p, pr, t["xnA"], t["qkvA"], w)),
    ("s3b", ["qkvA"], ["oAh"],
     lambda p, pr, t, w: emit_s3b(p, pr, t["qkvA"], t["oAh"])),
    ("s3c", ["oAh", "xnA"], ["x1fA"],
     lambda p, pr, t, w: emit_s3c(p, pr, t["oAh"], t["xnA"], t["x1fA"], w)),
    ("s4", ["x1fA", "s0A", "xA"], ["xb1A", "comb8"],
     lambda p, pr, t, w: emit_s4(p, pr, t["x1fA"], t["s0A"], t["xA"],
                                 t["xb1A"], t["comb8"], w)),
    ("s5a", ["xb1A"], ["xn1A", "q1A", "k1A", "v1A", "gt1A", "bg1"],
     lambda p, pr, t, w: emit_s5a(p, pr, t["xb1A"], t["xn1A"], t["q1A"],
                                  t["k1A"], t["v1A"], t["gt1A"], t["bg1"], w)),
    ("s5b", ["q1A", "k1A", "v1A", "gt1A", "bg1"], ["gated1A"],
     lambda p, pr, t, w: emit_s5b(p, pr, t["q1A"], t["k1A"], t["v1A"],
                                  t["gt1A"], t["bg1"], t["gated1A"], w)),
    ("s5c", ["gated1A"], ["s1A"],
     lambda p, pr, t, w: emit_s5c(p, pr, t["gated1A"], w["l1_Wo"], t["s1A"])),
    ("s6", ["xn1A", "xb1A", "s1A"], ["y", "comb1"],
     lambda p, pr, t, w: emit_s6(p, pr, t["xn1A"], t["xb1A"], t["s1A"],
                                 t["y"], t["comb1"], w)),
]


class Win:
    def __init__(self, p, host):
        self.p = p
        self.host = host

    def __getitem__(self, k):
        if k not in self.p.dram:
            self.p.dram_t(k, list(self.host[k].shape), F32, kind="ExternalInput")
        return self.p.dram[k]


def _np16(a):
    return np.ascontiguousarray(a)


def kernel(**inputs):
    host = host_layout(inputs)
    x = inputs["L0_attn_x"]
    if FUSED:
        p = P()
        win = Win(p, host)
        tens = {}
        for n, (sh, dt) in BOUND.items():
            kind = ("ExternalInput" if n == "x" else
                    "ExternalOutput" if n == "y" else "Internal")
            tens[n] = p.dram_t(n, list(sh), dt, kind=kind)
        first = True
        for name, _ins, _outs, fn in STAGE_DEFS:
            with p.scope(), p.nc.named_scope(name):
                pr = p.stage(name)
                if first:
                    load_consts(p, pr)
                    first = False
                fn(p, pr, tens, win)
                pr.build()
        wnames = [k for k in p.dram if k in host and not k.startswith("c_")]
        consts = {k: host[k] for k in CONST_DEFS}
        in_maps = [dict(consts, **{k: host[k] for k in wnames},
                        x=np.ascontiguousarray(x[b])) for b in range(8)]
        res = run_bass_kernel_spmd(p.nc, in_maps, core_ids=list(range(8)))
        return np.stack([np.asarray(r["y"], dtype=np.float32)
                         for r in res.results])
    vals = [dict(x=np.ascontiguousarray(x[b])) for b in range(8)]
    for name, ins, outs, fn in STAGE_DEFS:
        p = P()
        win = Win(p, host)
        tens = {}
        for n in ins:
            tens[n] = p.dram_t(n, list(BOUND[n][0]), BOUND[n][1],
                               kind="ExternalInput")
        for n in outs:
            tens[n] = p.dram_t(n, list(BOUND[n][0]), BOUND[n][1],
                               kind="ExternalOutput")
        with p.scope():
            pr = p.stage(name)
            load_consts(p, pr)
            fn(p, pr, tens, win)
            pr.build()
        wnames = [k for k in p.dram if k in host and not k.startswith("c_")]
        consts = {k: host[k] for k in CONST_DEFS}
        in_maps = [dict(consts, **{k: host[k] for k in wnames},
                        **{n: vals[b][n] for n in ins}) for b in range(8)]
        res = run_bass_kernel_spmd(p.nc, in_maps, core_ids=list(range(8)))
        for b in range(8):
            for n in outs:
                vals[b][n] = np.asarray(res.results[b][n])
    return np.stack([vals[b]["y"].astype(np.float32) for b in range(8)])


# version 23
# speedup vs baseline: 1.1431x; 1.1431x over previous
import numpy as np
import ml_dtypes
import concourse.bass as bass
import concourse.mybir as mybir
from concourse.bass_utils import run_bass_kernel_spmd

F32 = mybir.dt.float32
BF16 = mybir.dt.bfloat16
AF = mybir.ActivationFunctionType
ALU = mybir.AluOpType
AX = mybir.AxisListType

T = 4096
C = 1024
NT = T // 128
NF = C // 128
CH = 128
NCH = T // CH
NEG = np.float32(-1e30)
FUSED = True
DMA_CHAIN = True
SCHED_W = 0
SCHED_ON = True
DMA_STRIPES = 6


def cdiv(a, b):
    return (a + b - 1) // b

ENGINES = ("t", "v", "s", "g", "y")
DMA_QUEUES = ("y", "g", "s")


class Op:
    __slots__ = ("eng", "fn", "reads", "writes", "dma", "idx",
                 "eseq", "milestone", "mcount", "tag")

    def __init__(self, eng, fn, reads, writes, dma, tag=""):
        self.eng = eng
        self.fn = fn
        self.reads = reads
        self.writes = writes
        self.dma = dma
        self.tag = tag
        self.milestone = False
        self.mcount = 0


class Prog:

    def __init__(self, p, name):
        self.p = p
        self.name = name
        self.ops = []

    def op(self, eng, fn, reads=(), writes=(), tag=""):
        o = Op(eng, fn, tuple(reads), tuple(writes), False, tag)
        o.idx = len(self.ops)
        self.ops.append(o)
        return o

    def dma(self, eng, out_ap, in_ap, reads=(), writes=(), tag=""):
        assert eng in DMA_QUEUES
        def fn(e, out_ap=out_ap, in_ap=in_ap):
            return e.dma_start(out=out_ap, in_=in_ap)
        o = Op(eng, fn, tuple(reads), tuple(writes), True, tag)
        o.idx = len(self.ops)
        self.ops.append(o)
        return o

    def _schedule(self, deps):
        import heapq
        N = len(self.ops)
        DUR = {"t": 250, "v": 450, "s": 420, "g": 450, "y": 100}
        succ = [[] for _ in range(N)]
        indeg = [0] * N
        for i, ds in enumerate(deps):
            for d in ds:
                succ[d].append(i)
                indeg[i] += 1
        finish = [0] * N
        engine_free = {}
        order = []
        heap = []

        def est(i):
            o = self.ops[i]
            key = ("q" + o.eng) if o.dma else o.eng
            s = engine_free.get(key, 0)
            for d in deps[i]:
                do = self.ops[d]
                lag = 150 if (do.dma or do.eng != o.eng) else 0
                s = max(s, finish[d] + lag)
            return s

        for i in range(N):
            if indeg[i] == 0:
                heapq.heappush(heap, (est(i), i))
        while heap:
            s0, i = heapq.heappop(heap)
            s = est(i)
            if s > s0 and heap and heap[0][0] < s:
                heapq.heappush(heap, (s, i))
                continue
            o = self.ops[i]
            key = ("q" + o.eng) if o.dma else o.eng
            if o.dma:
                engine_free[key] = s + 350
                finish[i] = s + 1400
            else:
                d = DUR[o.eng]
                engine_free[key] = s + d
                finish[i] = s + d
            order.append(i)
            for j in succ[i]:
                indeg[j] -= 1
                if indeg[j] == 0:
                    heapq.heappush(heap, (est(j), j))
        assert len(order) == N
        return order

    def build(self, schedule=True):
        nc = self.p.nc
        last_writer = {}
        readers = {}
        deps = [set() for _ in self.ops]
        for o in self.ops:
            for r in o.reads:
                w = last_writer.get(r)
                if w is not None:
                    deps[o.idx].add(w)
                rl = readers.setdefault(r, [])
                if r.startswith(("pf", "pbb")) and o.eng in ("v", "s"):
                    for rd in rl:
                        if self.ops[rd].eng in ("v", "s") and self.ops[rd].eng != o.eng:
                            deps[o.idx].add(rd)
                rl.append(o.idx)
            for w in o.writes:
                pw = last_writer.get(w)
                if pw is not None:
                    deps[o.idx].add(pw)
                for rd in readers.get(w, ()):
                    deps[o.idx].add(rd)
                readers[w] = []
                last_writer[w] = o.idx
            deps[o.idx].discard(o.idx)
        need = [set() for _ in self.ops]
        need_drain_after = [set() for _ in self.ops]
        for o in self.ops:
            for d in deps[o.idx]:
                do = self.ops[d]
                if do.dma or do.eng != o.eng:
                    need[o.idx].add(d)
                    do.milestone = True
                elif o.eng in ("v", "s", "g"):
                    need_drain_after[o.idx].add(d)
        prev_dma = {}
        dma_stripe = {}
        nq = {}
        for o in self.ops:
            if o.dma:
                o.milestone = True
                st_ = nq.get(o.eng, 0) % DMA_STRIPES
                nq[o.eng] = nq.get(o.eng, 0) + 1
                dma_stripe[o.idx] = st_
                key = (o.eng, st_)
                if DMA_CHAIN and key in prev_dma:
                    deps[o.idx].add(prev_dma[key])
                prev_dma[key] = o.idx
        self.dma_stripe = dma_stripe
        W = SCHED_W
        if W:
            for i in range(W, len(self.ops)):
                deps[i].add(i - W)
        order = self._schedule(deps) if (schedule and SCHED_ON) else list(range(len(self.ops)))
        eng_ops = {e: [] for e in ENGINES}
        counts = self.p.semcnt
        used_keys = set()

        def keyof(o):
            return (("dma", o.eng, dma_stripe[o.idx]) if o.dma
                    else ("c", o.eng))
        for idx in order:
            o = self.ops[idx]
            key = keyof(o)
            if o.milestone:
                self.p.get_sem(key)
                counts[key] = counts.get(key, 0) + (16 if o.dma else 1)
                o.mcount = counts[key]
                used_keys.add(key)
            eng_ops[o.eng].append(o)
        sems = self.p.semreg
        eng_seq = {}
        for e in ENGINES:
            for si, o in enumerate(eng_ops[e]):
                eng_seq[o.idx] = si

        def emit_for(e, eng):
            watermark = {}
            last_drain = -1
            for o in eng_ops[e]:
                if need_drain_after[o.idx]:
                    dep_seq = max(eng_seq[d] for d in need_drain_after[o.idx])
                    if dep_seq > last_drain:
                        eng.drain()
                        last_drain = eng_seq[o.idx] - 1
                for d in sorted(need[o.idx]):
                    do = self.ops[d]
                    key = keyof(do)
                    v = do.mcount
                    if watermark.get(key, 0) < v:
                        eng.wait_ge(sems[key], v)
                        watermark[key] = v
                inst = o.fn(eng)
                if o.milestone:
                    key = keyof(o)
                    assert inst is not None, f"milestone op {o.tag} returned no inst"
                    if o.dma or o.eng == "t":
                        inst.then_inc(sems[key], 16 if o.dma else 1)
                    else:
                        eng.drain().then_inc(sems[key], 1)
                        last_drain = eng_seq[o.idx]

        def emit_end_drain(e, eng):
            emit_for(e, eng)
            eng.drain()

        with nc.Block() as blk:
            if eng_ops["y"]:
                blk.sync(lambda e: emit_for("y", e))
            if eng_ops["g"]:
                blk.gpsimd(lambda e: emit_end_drain("g", e))
            if eng_ops["s"]:
                blk.scalar(lambda e: emit_end_drain("s", e))
            if eng_ops["v"]:
                blk.vector(lambda e: emit_end_drain("v", e))
            if eng_ops["t"]:
                blk.tensor(lambda e: emit_for("t", e))
            lasts = [(k, counts[k]) for k in sorted(used_keys) if k[0] == "dma"]
            if lasts:
                def fin(eng):
                    for key, v in lasts:
                        eng.wait_ge(sems[key], v)
                blk.gpsimd(fin)


class P:

    def __init__(self):
        import contextlib
        self.nc = bass.Bass("TRN2", target_bir_lowering=False)
        self.es = contextlib.ExitStack()
        self.dram = {}
        self.sb = {}
        self._psum = {}
        self._scopes = [self.es]
        self._contextlib = contextlib
        self.semreg = {}
        self.semcnt = {}

    def get_sem(self, key):
        if key not in self.semreg:
            nm = "sem_" + "_".join(str(k) for k in key)
            self.semreg[key] = self.es.enter_context(self.nc.semaphore(nm))
            self.semcnt[key] = 0
        return self.semreg[key]

    def scope(self):
        import contextlib as _cl

        @_cl.contextmanager
        def _s():
            es = _cl.ExitStack()
            self._scopes.append(es)
            es._names = []
            try:
                yield
            finally:
                self._scopes.pop()
                es.close()
                for n in es._names:
                    self.sb.pop(n, None)
        return _s()

    def dram_t(self, name, shape, dt=F32, kind="Internal"):
        ap = self.nc.dram_tensor(name, shape, dt, kind=kind).ap()
        self.dram[name] = ap
        return ap

    def sbuf(self, name, shape, dt=F32):
        sc = self._scopes[-1]
        t = sc.enter_context(self.nc.sbuf_tensor(name, shape, dt))
        self.sb[name] = t
        if hasattr(sc, "_names"):
            sc._names.append(name)
        return t

    def const_col(self, value):
        j = CONST_SCALARS.index(value)
        return self.sb["c_scal"][:, j:j + 1]

    def psum_bank(self, name, dt=F32):
        if name not in self._psum:
            shape = [128, 512] if dt == F32 else [128, 1024]
            self._psum[name] = self.es.enter_context(
                self.nc.psum_tensor(name, shape, dt))
        return self._psum[name]

    def stage(self, name):
        return Prog(self, name)

    def dbg_store(self, pr, name, ap, res, shape, dt=F32):
        if not getattr(self, "dbg", False):
            return
        d = self.dram_t("dbg_" + name, list(shape), dt, kind="ExternalOutput")
        pr.dma("y", d, ap, reads=(res,), tag=f"dbg {name}")



def host_layout(inputs):
    W = dict(inputs)
    out = {}
    qkvz = W["L0_attn_W_qkvz"]
    qk = qkvz.reshape(C, 8, 4, 128)
    out["l0_Wq"] = np.ascontiguousarray(qk[:, :, 0, :].reshape(C, 1024))
    out["l0_Wk"] = np.ascontiguousarray(qk[:, :, 1, :].reshape(C, 1024))
    out["l0_Wv"] = np.ascontiguousarray(qk[:, :, 2, :].reshape(C, 1024))
    out["l0_Wz"] = np.ascontiguousarray(qk[:, :, 3, :].reshape(C, 1024))
    ba = W["L0_attn_W_ba"]
    out["l0_Wb"] = np.ascontiguousarray(ba[:, 0::2])
    out["l0_Wa"] = np.ascontiguousarray(ba[:, 1::2])
    cw = W["L0_attn_conv_w"][:, 0, :]
    out["l0_cwq"] = np.ascontiguousarray(cw[:1024])
    out["l0_cwk"] = np.ascontiguousarray(cw[1024:2048])
    out["l0_cwv"] = np.ascontiguousarray(cw[2048:])
    out["l0_dtb"] = W["L0_attn_dt_bias"].reshape(8, 1)
    out["l0_Alog"] = W["L0_attn_A_log"].reshape(8, 1)
    out["l0_nw"] = W["L0_attn_norm_weight"].reshape(128, 1)
    out["l0_WoT"] = W["L0_attn_W_out"]
    def cols(v):
        return np.ascontiguousarray(v.reshape(-1, 128).T)
    out["f0_n1"] = cols(W["L0_ffn_norm1_w"])
    out["f0_n2"] = cols(W["L0_ffn_norm2_w"])
    out["f0_qkvT"] = np.ascontiguousarray(W["L0_ffn_qkv_w"].T)
    out["f0_qkvb"] = cols(W["L0_ffn_qkv_b"])
    out["f0_oT"] = np.ascontiguousarray(W["L0_ffn_out_w"].T)
    out["f0_ob"] = cols(W["L0_ffn_out_b"])
    out["f0_gT"] = np.ascontiguousarray(W["L0_ffn_gate_w"].T)
    out["f0_w1T"] = np.ascontiguousarray(W["L0_ffn_w1"].transpose(0, 2, 1))
    out["f0_w2T"] = np.ascontiguousarray(W["L0_ffn_w2"].transpose(0, 2, 1))
    out["f0_w3T"] = np.ascontiguousarray(W["L0_ffn_w3"].transpose(0, 2, 1))
    out["f0_b1"] = np.stack([cols(W["L0_ffn_b1"][e]) for e in range(8)])
    out["f0_b2"] = np.stack([cols(W["L0_ffn_b2"][e]) for e in range(8)])
    out["f0_b3"] = np.stack([cols(W["L0_ffn_b3"][e]) for e in range(8)])
    out["l1_Wq"] = W["L1_attn_Wq"]
    out["l1_Wk"] = W["L1_attn_Wk"]
    out["l1_Wv"] = W["L1_attn_Wv"]
    out["l1_Wal"] = W["L1_attn_Walpha"]
    out["l1_Wbe"] = W["L1_attn_Wbeta"]
    out["l1_Alog"] = W["L1_attn_A_log"].reshape(16, 1)
    out["l1_dtb"] = W["L1_attn_dt_bias"].reshape(16, 1)
    out["l1_Wg"] = W["L1_attn_Wgate"]
    out["l1_Wo"] = W["L1_attn_Wout"]
    out["l1_rmsw"] = np.ascontiguousarray(np.tile(W["L1_attn_rms_w"].reshape(64, 1), (2, 1)))
    out["f1_sg"] = W["L1_ffn_shared_gate"]
    out["f1_su"] = W["L1_ffn_shared_up"]
    out["f1_sd"] = W["L1_ffn_shared_down"]
    out["f1_rg"] = W["L1_ffn_routed_gate"]
    out["f1_ru"] = W["L1_ffn_routed_up"]
    out["f1_rd"] = W["L1_ffn_routed_down"]
    out["f1_Wr"] = W["L1_ffn_Wr"]
    out["f1_rb"] = np.ascontiguousarray(
        np.broadcast_to(W["L1_ffn_routing_bias"], (128, 7)))
    out["c_scal"] = np.ascontiguousarray(
        np.broadcast_to(np.array(CONST_SCALARS, np.float32), (128, len(CONST_SCALARS))))
    out["c_idf"] = np.eye(128, dtype=np.float32)
    out["c_idb"] = np.eye(128).astype(ml_dtypes.bfloat16)
    out["c_ones_k1f"] = np.ones((1, 128), np.float32)
    out["c_ones_k1b"] = np.ones((1, 128)).astype(ml_dtypes.bfloat16)
    out["c_ones_c1f"] = np.ones((128, 1), np.float32)
    out["c_ones_c1b"] = np.ones((128, 1)).astype(ml_dtypes.bfloat16)
    out["c_maskDT"] = np.where(np.arange(128)[:, None] >= np.arange(128)[None, :],
                               NEG, np.float32(0)).astype(np.float32)
    out["c_eyeb"] = np.eye(128).astype(ml_dtypes.bfloat16)
    inv = 1.0 / 10000.0 ** (np.arange(0, 64, 2, dtype=np.float32) / 64)
    tpos = np.arange(T, dtype=np.float32)
    fr = tpos[:, None] * inv[None, :]
    emb = np.concatenate([fr, fr], -1)
    cos = np.cos(emb).astype(np.float32).T
    sin = np.sin(emb).astype(np.float32).T
    sinS = np.concatenate([-sin[:32], sin[32:]], 0)
    out["c_cosR"] = np.ascontiguousarray(
        np.concatenate([cos, cos], 0)).astype(ml_dtypes.bfloat16)
    out["c_sinR"] = np.ascontiguousarray(
        np.concatenate([sinS, sinS], 0)).astype(ml_dtypes.bfloat16)
    return out


CONST_SCALARS = [1e-5, 1e-6, float(np.log(128.0 ** -0.5)), float(np.log(64.0 ** -0.5)),
                 1.0, 2.0, 0.5, -1.0, 1e-18]

CONST_DEFS = {
    "c_scal": ([128, len(CONST_SCALARS)], F32),
    "c_idf": ([128, 128], F32), "c_idb": ([128, 128], BF16),
    "c_ones_k1f": ([1, 128], F32), "c_ones_k1b": ([1, 128], BF16),
    "c_ones_c1f": ([128, 1], F32), "c_ones_c1b": ([128, 1], BF16),
    "c_maskDT": ([128, 128], F32), "c_eyeb": ([128, 128], BF16),
    "c_cosR": ([128, T], BF16), "c_sinR": ([128, T], BF16),
}


def alloc_consts(p):
    for name, (shape, dt) in CONST_DEFS.items():
        ap = p.dram_t(name, shape, dt, kind="ExternalInput")
        t = p._scopes[0].enter_context(p.nc.sbuf_tensor("sb_" + name, shape, dt))
        p.sb[name] = t


def load_consts(p, pr):
    if "c_idf" not in p.sb:
        alloc_consts(p)
    for name in CONST_DEFS:
        pr.dma("y", p.sb[name][:], p.dram[name], writes=(name,), tag=f"ld_{name}")



def emit_s0(p, pr, x_hbm, xA_hbm, xnA_hbm, nt=NT, dbg_hbm=None):
    DB = 3
    xt = [p.sbuf(f"s0_x{i}", [128, C], F32) for i in range(DB)]
    sq = [p.sbuf(f"s0_sq{i}", [128, C], F32) for i in range(2)]
    stt = [p.sbuf(f"s0_st{i}", [128, 6], F32) for i in range(2)]
    xnf = [p.sbuf(f"s0_xnf{i}", [128, C], F32) for i in range(2)]
    xnb = [p.sbuf(f"s0_xnb{i}", [128, C], BF16) for i in range(2)]
    evf = [p.sbuf(f"s0_evf{i}", [128, C], F32) for i in range(2)]
    evb = [p.sbuf(f"s0_evb{i}", [128, C], BF16) for i in range(2)]
    pf = [p.psum_bank("pf0"), p.psum_bank("pf1")]
    pb = [p.psum_bank("pbb0", BF16), p.psum_bank("pbb1", BF16)]
    idf = p.sb["c_idf"]
    idb = p.sb["c_idb"]

    for i in range(nt):
        b = i % DB
        d = i % 2
        X, SQ, ST, XF, XB, EF, EB = (f"s0x{b}", f"s0sq{d}", f"s0st{d}",
                                     f"s0xnf{d}", f"s0xnb{d}", f"s0evf{d}", f"s0evb{d}")
        pr.dma("y", xt[b][:], x_hbm[i * 128:(i + 1) * 128, :], writes=(X,), tag=f"s0 ldx{i}")
        pr.op("s", lambda e, b=b, d=d: e.activation(sq[d][:], xt[b][:], AF.Square),
              reads=(X,), writes=(SQ,))
        pr.op("v", lambda e, b=b, d=d: e.tensor_reduce(stt[d][:, 0:1], xt[b][:], AX.X, ALU.add),
              reads=(X,), writes=(ST,))
        pr.op("v", lambda e, d=d: e.tensor_reduce(stt[d][:, 1:2], sq[d][:], AX.X, ALU.add),
              reads=(SQ,), writes=(ST,))
        pr.op("v", lambda e, d=d: e.tensor_scalar(stt[d][:, 2:3], stt[d][:, 0:1], 1.0 / C, -1.0, ALU.mult, ALU.mult),
              reads=(ST,), writes=(ST,))
        pr.op("v", lambda e, d=d: e.tensor_scalar(stt[d][:, 1:2], stt[d][:, 1:2], 1.0 / C, None, ALU.mult),
              reads=(ST,), writes=(ST,))
        pr.op("v", lambda e, d=d: e.tensor_tensor(stt[d][:, 3:4], stt[d][:, 2:3], stt[d][:, 2:3], ALU.mult),
              reads=(ST,), writes=(ST,))
        pr.op("v", lambda e, d=d: e.tensor_tensor(stt[d][:, 3:4], stt[d][:, 1:2], stt[d][:, 3:4], ALU.subtract),
              reads=(ST,), writes=(ST,))
        pr.op("s", lambda e, d=d: e.activation(stt[d][:, 4:5], stt[d][:, 3:4], AF.Ln, bias=p.const_col(1e-5)),
              reads=(ST, "c_scal"), writes=(ST + "r",))
        pr.op("s", lambda e, d=d: e.activation(stt[d][:, 4:5], stt[d][:, 4:5], AF.Exp, scale=-0.5),
              reads=(ST + "r",), writes=(ST + "r",))
        pr.op("v", lambda e, b=b, d=d: e.tensor_scalar(
              xnf[d][:], xt[b][:], stt[d][:, 2:3], stt[d][:, 4:5], ALU.add, ALU.mult),
              reads=(X, ST, ST + "r"), writes=(XF,))
        pr.op("v", lambda e, d=d: e.tensor_copy(xnb[d][:], xnf[d][:]),
              reads=(XF,), writes=(XB,))
        if dbg_hbm is not None:
            pr.dma("y", dbg_hbm[i], stt[d][:], reads=(ST, ST + "r"), tag=f"dbg{i}")
        for f in range(NF):
            bk = f % 2
            sl = (f // 2) % 4
            BA, BB = f"pf{bk}", f"pbb{bk}"
            pr.op("t", lambda e, bk=bk, sl=sl, b=b, f=f: e.transpose(
                  pf[bk][:, sl * 128:(sl + 1) * 128], xt[b][:, f * 128:(f + 1) * 128], idf[:]),
                  reads=(X, "c_idf"), writes=(BA,), tag=f"s0 trf{i}_{f}")
            pr.op("v", lambda e, bk=bk, sl=sl, d=d, f=f: e.tensor_copy(
                  evf[d][:, f * 128:(f + 1) * 128], pf[bk][:, sl * 128:(sl + 1) * 128]),
                  reads=(BA,), writes=(EF,))
            pr.op("t", lambda e, bk=bk, sl=sl, d=d, f=f: e.transpose(
                  pb[bk][:, sl * 128:(sl + 1) * 128], xnb[d][:, f * 128:(f + 1) * 128], idb[:]),
                  reads=(XB, "c_idb"), writes=(BB,), tag=f"s0 trb{i}_{f}")
            pr.op("v", lambda e, bk=bk, sl=sl, d=d, f=f: e.tensor_copy(
                  evb[d][:, f * 128:(f + 1) * 128], pb[bk][:, sl * 128:(sl + 1) * 128]),
                  reads=(BB,), writes=(EB,))
        pr.dma("y", xA_hbm[:, :, i * 128:(i + 1) * 128],
               evf[d][:].rearrange("p (f t) -> p f t", t=128), reads=(EF,), tag=f"s0 stxA{i}")
        pr.dma("y", xnA_hbm[:, :, i * 128:(i + 1) * 128],
               evb[d][:].rearrange("p (f t) -> p f t", t=128), reads=(EB,), tag=f"s0 stxnA{i}")



def emit_gemm(p, pr, tag, W_hbm, Din, Dout, rhs_fn, evac_fn, banks, wname,
              nch=8, chw=512, wq="g"):
    NK = Din // 128
    NM = cdiv(Dout, 128)
    wbuf = [p.sbuf(f"{wname}{i}", [128, NK * 128], BF16) for i in range(2)]
    for m in range(NM):
        mw = min(128, Dout - m * 128)
        ws = m % 2
        src = W_hbm[:, m * 128:m * 128 + mw].rearrange("(nk r) c -> r nk c", r=128)
        pr.dma(wq, wbuf[ws][:].rearrange("r (nk c) -> r nk c", c=128)[:, :, :mw],
               src, writes=(f"{wname}{ws}",), tag=f"{tag} w{m}")
        for ch in range(nch):
            ba, bres = banks[ch % 2]
            for k in range(NK):
                pr.op("t", lambda e, ws=ws, k=k, ch=ch, mw=mw, ba=ba: e.matmul(
                      ba[:mw, :chw], wbuf[ws][:, k * 128:k * 128 + mw],
                      rhs_fn(k, ch)[0], start=(k == 0), stop=(k == NK - 1)),
                      reads=(f"{wname}{ws}", rhs_fn(k, ch)[1]),
                      writes=(bres,), tag=f"{tag} mm{m}_{ch}_{k}")
            evac_fn(m, ch, ba, bres)


def bcast_strip(p, pr, strip_ap, strip_res, w, bank_ap, bank_res, f32=True):
    ones = p.sb["c_ones_k1f" if f32 else "c_ones_k1b"]
    ores = "c_ones_k1f" if f32 else "c_ones_k1b"
    pr.op("t", lambda e: e.matmul(bank_ap[:, :w], ones[:], strip_ap,
                                  start=True, stop=True),
          reads=(strip_res, ores), writes=(bank_res,), tag="bcast")


def rstd_strip(p, pr, strip_ap, strip_res, ln_scale, eps, exp_bias=None, n=1):
    ec = p.const_col(eps)[:n, :]
    pr.op("s", lambda e: e.activation(strip_ap, strip_ap, AF.Ln,
                                      bias=ec, scale=ln_scale),
          reads=(strip_res, "c_scal"), writes=(strip_res,), tag="rstd_ln")
    if exp_bias is None:
        pr.op("s", lambda e: e.activation(strip_ap, strip_ap, AF.Exp, scale=-0.5),
              reads=(strip_res,), writes=(strip_res,), tag="rstd_exp")
    else:
        bc = p.const_col(exp_bias)[:n, :]
        pr.op("s", lambda e: e.activation(strip_ap, strip_ap, AF.Exp,
                                          bias=bc, scale=-0.5),
              reads=(strip_res, "c_scal"), writes=(strip_res,), tag="rstd_exp")


def softplus_block(p, pr, src_ap, src_res, n, w, tmp_a, tmp_r, ra, rr,
                   dtb_col, dtb_res, negA_col, negA_res, out_ap, out_res):
    pr.op("s", lambda e: e.activation(tmp_a, src_ap, AF.Abs, bias=dtb_col),
          reads=(src_res, dtb_res), writes=(tmp_r,), tag="sp_abs")
    pr.op("s", lambda e: e.activation(ra, src_ap, AF.Relu, bias=dtb_col),
          reads=(src_res, dtb_res), writes=(rr,), tag="sp_relu")
    pr.op("s", lambda e: e.activation(tmp_a, tmp_a, AF.Exp, scale=-1.0),
          reads=(tmp_r,), writes=(tmp_r,), tag="sp_exp")
    pr.op("s", lambda e: e.activation(tmp_a, tmp_a, AF.Ln, bias=1.0),
          reads=(tmp_r,), writes=(tmp_r,), tag="sp_ln")
    pr.op("v", lambda e: e.tensor_tensor(tmp_a, tmp_a, ra, ALU.add),
          reads=(tmp_r, rr), writes=(tmp_r,), tag="sp_add")
    pr.op("v", lambda e: e.tensor_scalar(out_ap, tmp_a, negA_col, None, ALU.mult),
          reads=(tmp_r, negA_res), writes=(out_res,), tag="sp_scale")



def emit_s1(p, pr, xnA, qcA, kcA, vcA, zsA, bg, win):
    nc = p.nc
    xn = p.sbuf("s1_xn", [128, NF * T], BF16)
    pr.dma("y", xn[:].rearrange("p (f t) -> p f t", t=T), xnA,
           writes=("s1xn",), tag="s1 ldxn")

    def rhs(k, ch):
        return xn[:, k * T + ch * 512: k * T + ch * 512 + 512], "s1xn"

    banks = [(p.psum_bank("pf0"), "pf0"), (p.psum_bank("pf1"), "pf1")]
    banks2 = [(p.psum_bank("pf2"), "pf2"), (p.psum_bank("pf3"), "pf3")]

    cw = {}
    for sec in ("q", "k", "v"):
        t = p.sbuf(f"s1_cw{sec}", [128, 32], F32)
        cw[sec] = t
        pr.dma("y", t[:].rearrange("r (h j) -> r h j", j=4),
               win[f"l0_cw{sec}"].rearrange("(h r) j -> r h j", r=128),
               writes=(f"s1cw{sec}",), tag=f"s1 ldcw{sec}")
    dtb = p.sbuf("s1_dtb", [8, 1], F32)
    pr.dma("y", dtb[:], win["l0_dtb"], writes=("s1dtb",))
    alog = p.sbuf("s1_alog", [8, 1], F32)
    pr.dma("y", alog[:], win["l0_Alog"], writes=("s1alog",))
    negA = p.sbuf("s1_negA", [8, 1], F32)
    pr.op("s", lambda e: e.activation(negA[:], alog[:], AF.Exp),
          reads=("s1alog",), writes=("s1negA",))
    pr.op("v", lambda e: e.tensor_scalar_mul(negA[:], negA[:], -1.0),
          reads=("s1negA",), writes=("s1negA",))

    raw = [p.sbuf(f"s1_raw{i}", [128, T + 3], BF16) for i in range(2)]
    acc = [p.sbuf(f"s1_acc{i}", [128, T], F32) for i in range(2)]
    outb = [p.sbuf(f"s1_out{i}", [128, 512], BF16) for i in range(2)]

    for sec, Wk, dst in (("q", "l0_Wq", qcA), ("k", "l0_Wk", kcA), ("v", "l0_Wv", vcA)):
        def evac(m, ch, ba, bres, sec=sec, dst=dst):
            ms = m % 2
            R = f"s1raw{ms}"
            pr.op("v", lambda e, ms=ms, ch=ch, ba=ba: e.tensor_copy(
                  raw[ms][:, 3 + ch * 512: 3 + (ch + 1) * 512], ba[:, :512]),
                  reads=(bres,), writes=(R,), tag=f"s1 {sec}ev{m}_{ch}")
            if ch == 0:
                pr.op("v", lambda e, ms=ms: e.memset(raw[ms][:, 0:3], 0.0),
                      writes=(R,), tag="s1 pad0")
            if ch == nchunks - 1:
                A = f"s1acc{ms}"
                pr.op("v", lambda e, ms=ms, sec=sec, m=m: e.tensor_scalar(
                      acc[ms][:], raw[ms][:, 0:T], cw[sec][:, 4 * m:4 * m + 1],
                      None, ALU.mult),
                      reads=(R, f"s1cw{sec}"), writes=(A,), tag="s1 conv0")
                for j in range(1, 4):
                    pr.op("v", lambda e, ms=ms, sec=sec, m=m, j=j: e.scalar_tensor_tensor(
                          acc[ms][:], raw[ms][:, j:j + T], cw[sec][:, 4 * m + j:4 * m + j + 1],
                          acc[ms][:], ALU.mult, ALU.add),
                          reads=(R, f"s1cw{sec}", A), writes=(A,), tag=f"s1 conv{j}")
                for oc in range(8):
                    O = f"s1out{oc % 2}"
                    pr.op("s", lambda e, ms=ms, oc=oc: e.activation(
                          outb[oc % 2][:], acc[ms][:, oc * 512:(oc + 1) * 512], AF.Silu),
                          reads=(A,), writes=(O,), tag="s1 silu")
                    pr.dma("y", dst[:, m, oc * 512:(oc + 1) * 512], outb[oc % 2][:],
                           reads=(O,), tag=f"s1 st{sec}{m}_{oc}")
        nchunks = 8
        emit_gemm(p, pr, f"s1{sec}", win[Wk], C, C, rhs, evac, banks, f"s1w{sec}")

    zst = [p.sbuf(f"s1_z{i}", [128, T], BF16) for i in range(2)]

    def evac_z(m, ch, ba, bres):
        ms = m % 2
        Z = f"s1z{ms}"
        pr.op("s", lambda e, ms=ms, ch=ch, ba=ba: e.activation(
              zst[ms][:, ch * 512:(ch + 1) * 512], ba[:, :512], AF.Silu),
              reads=(bres,), writes=(Z,), tag=f"s1 zev{m}_{ch}")
        if ch == 7:
            pr.dma("y", zsA[:, m, :], zst[ms][:], reads=(Z,), tag=f"s1 stz{m}")
    emit_gemm(p, pr, "s1z", win["l0_Wz"], C, C, rhs, evac_z, banks, "s1wz")

    wb = p.sbuf("s1_wb", [128, NF * 8], BF16)
    wa = p.sbuf("s1_wa", [128, NF * 8], BF16)
    pr.dma("g", wb[:].rearrange("r (nk c) -> r nk c", c=8),
           win["l0_Wb"].rearrange("(nk r) c -> r nk c", r=128), writes=("s1wb",))
    pr.dma("g", wa[:].rearrange("r (nk c) -> r nk c", c=8),
           win["l0_Wa"].rearrange("(nk r) c -> r nk c", r=128), writes=("s1wa",))
    bss = [p.sbuf(f"s1_bs{i}", [8, 512], F32) for i in range(2)]
    gss = [p.sbuf(f"s1_gs{i}", [8, 512], F32) for i in range(2)]
    gcs = [p.sbuf(f"s1_gc{i}", [8, 512], F32) for i in range(2)]
    tt1 = [p.sbuf(f"s1_t1{i}", [8, 512], F32) for i in range(2)]
    tt2 = [p.sbuf(f"s1_t2{i}", [8, 512], F32) for i in range(2)]
    egs = [p.sbuf(f"s1_eg{i}", [8, 512], F32) for i in range(2)]
    ers = [p.sbuf(f"s1_er{i}", [8, 512], F32) for i in range(2)]
    bes = [p.sbuf(f"s1_be{i}", [8, 512], F32) for i in range(2)]
    zcol = p.sbuf("s1_zcol", [8, 1], F32)
    pr.op("v", lambda e: e.memset(zcol[:], 0.0), writes=("s1zcol",))

    for ch in range(8):
        cs = ch % 2
        ba_b, bres_b = banks2[0]
        ba_a, bres_a = banks2[1]
        for k in range(NF):
            pr.op("t", lambda e, k=k, ch=ch, ba=ba_b: e.matmul(
                  ba[:8, :512], wb[:, k * 8:(k + 1) * 8], rhs(k, ch)[0],
                  start=(k == 0), stop=(k == NF - 1)),
                  reads=("s1wb", "s1xn"), writes=(bres_b,), tag=f"s1 bmm{ch}")
            pr.op("t", lambda e, k=k, ch=ch, ba=ba_a: e.matmul(
                  ba[:8, :512], wa[:, k * 8:(k + 1) * 8], rhs(k, ch)[0],
                  start=(k == 0), stop=(k == NF - 1)),
                  reads=("s1wa", "s1xn"), writes=(bres_a,), tag=f"s1 amm{ch}")
        B, G, GC = f"s1bs{cs}", f"s1gs{cs}", f"s1gc{cs}"
        pr.op("s", lambda e, cs=cs, ba=ba_b: e.activation(bss[cs][:], ba[:8, :512], AF.Sigmoid),
              reads=(bres_b,), writes=(B,), tag="s1 beta")
        softplus_block(p, pr, ba_a[:8, :512], bres_a, 8, 512,
                       tt1[cs][:], f"s1t1{cs}", tt2[cs][:], f"s1t2{cs}",
                       dtb[:], "s1dtb", negA[:], "s1negA",
                       gss[cs][:], G)
        for q in range(4):
            sl = slice(q * 128, (q + 1) * 128)
            pr.op("v", lambda e, cs=cs, sl=sl: e.tensor_tensor_scan(
                  gcs[cs][:, sl], gss[cs][:, sl], gss[cs][:, sl], zcol[:],
                  ALU.add, ALU.bypass),
                  reads=(G, "s1zcol"), writes=(GC,), tag="s1 cumsum")
        for q in range(4):
            sl = slice(q * 128, (q + 1) * 128)
            pr.op("v", lambda e, cs=cs, sl=sl, q=q: e.tensor_scalar(
                  gss[cs][:, sl], gcs[cs][:, sl], -1.0,
                  gcs[cs][:, q * 128 + 127: q * 128 + 128], ALU.mult, ALU.add),
                  reads=(GC,), writes=(G,), tag="s1 grev")
        EG, ER, BE = f"s1eg{cs}", f"s1er{cs}", f"s1be{cs}"
        pr.op("s", lambda e, cs=cs: e.activation(egs[cs][:], gcs[cs][:], AF.Exp),
              reads=(GC,), writes=(EG,), tag="s1 Eg")
        pr.op("s", lambda e, cs=cs: e.activation(ers[cs][:], gss[cs][:], AF.Exp),
              reads=(G,), writes=(ER,), tag="s1 Erev")
        pr.op("v", lambda e, cs=cs: e.tensor_tensor(bes[cs][:], bss[cs][:], egs[cs][:], ALU.mult),
              reads=(B, EG), writes=(BE,), tag="s1 bEg")
        sl5 = slice(ch * 512, (ch + 1) * 512)
        pr.dma("y", bg[0, :, sl5], gcs[cs][:], reads=(GC,), tag=f"s1 stg{ch}")
        pr.dma("y", bg[1, :, sl5], egs[cs][:], reads=(EG,), tag=f"s1 steg{ch}")
        pr.dma("y", bg[2, :, sl5], ers[cs][:], reads=(ER,), tag=f"s1 ster{ch}")
        pr.dma("y", bg[3, :, sl5], bss[cs][:], reads=(B,), tag=f"s1 stb{ch}")
        pr.dma("y", bg[4, :, sl5], bes[cs][:], reads=(BE,), tag=f"s1 stbe{ch}")



def emit_delta_head(p, pr, pfx, dk, dv, cstride, cofs, kA_ap, kA_res, qA_ap, qA_res,
                    vA_ap, vA_res, strips, tcols, eglast_bc, S_ap, S_res,
                    core_ap, core_res, nch=NCH, idsl=None):
    nc = p.nc
    idb = p.sb["c_idb"]
    if idsl is None:
        idsl = slice(0, dk)
    PF = [p.psum_bank(f"pf{i}") for i in range(6)]
    BB = [p.psum_bank("pbb0", BF16), p.psum_bank("pbb1", BF16)]
    BANKS = [dict(A=(PF[0], "pf0"), B=(PF[1], "pf1"), C=(PF[2], "pf2"),
                  D=(BB[0], "pbb0")),
             dict(A=(PF[3], "pf3"), B=(PF[4], "pf4"), C=(PF[5], "pf5"),
                  D=(BB[1], "pbb1"))]

    def W(n, par, sh=(128, 128), dt=BF16):
        nm = f"{pfx}_{n}{par}"
        return p.sb[nm] if nm in p.sb else p.sbuf(nm, list(sh), dt)

    gs, gs_r = strips["g"]
    egs, egs_r = strips["Eg"]
    bes, bes_r = strips["be"]
    gTn, gTn_r = tcols["gTn"]
    beT, beT_r = tcols["beT"]
    erT, erT_r = tcols["erT"]
    begT, begT_r = tcols["begT"]
    egl, egl_r = eglast_bc

    def ci(i):
        return slice(i * cstride + cofs, i * cstride + cofs + 1)

    for i in range(nch):
        par = i % 2
        bk = BANKS[par]
        bA, bAr = bk["A"]
        bN, bNr = bk["B"]
        bC, bCr = bk["C"]
        bD, bDr = bk["D"]
        R = lambda n: f"{pfx}_{n}{par}"
        argDT = W("argDT", par, (128, 128), F32)
        DstT = W("DstT", par); DTt = W("DTt", par); DbT = W("DbT", par)
        triT = W("triT", par); MTt = W("MTt", par)
        Xb = W("Xb", par); Zb = W("Zb", par); Pe = W("Pe", par); PeT = W("PeT", par)
        kT = W("kT", par); VbT = W("VbT", par); KtT = W("KtT", par)
        K2T = W("K2T", par); kcn = W("kcn", par); vnA = W("vnA", par)
        vnR = W("vnR", par); qt = W("qt", par); atT = W("atT", par)
        cs = slice(i * CH, (i + 1) * CH)
        bcast_strip(p, pr, gs[:, cs], gs_r, CH, bA, bAr, f32=True)
        pr.op("v", lambda e, i=i, argDT=argDT, bA=bA: e.scalar_tensor_tensor(
              argDT[:], bA[:, :CH], gTn[:, ci(i)], p.sb["c_maskDT"][:],
              ALU.add, ALU.add),
              reads=(bAr, gTn_r, "c_maskDT"), writes=(R("argDT"),), tag=f"{pfx} argDT{i}")
        pr.op("s", lambda e, DstT=DstT, argDT=argDT: e.activation(DstT[:], argDT[:], AF.Exp),
              reads=(R("argDT"),), writes=(R("DstT"),), tag="expD")
        pr.op("v", lambda e, DTt=DTt, DstT=DstT: e.tensor_tensor(
              DTt[:], DstT[:], p.sb["c_eyeb"][:], ALU.add),
              reads=(R("DstT"), "c_eyeb"), writes=(R("DTt"),), tag="DT")
        bcast_strip(p, pr, bes[:, cs], bes_r, CH, bA, bAr, f32=False)
        pr.op("v", lambda e, DbT=DbT, DstT=DstT, bA=bA: e.tensor_tensor(
              DbT[:], DstT[:], bA[:, :CH], ALU.mult),
              reads=(R("DstT"), bAr), writes=(R("DbT"),), tag="DbT")
        pr.op("t", lambda e, cs=cs, bA=bA: e.matmul(bA[:, :CH], kA_ap[:, cs],
                                                    kA_ap[:, cs], start=True, stop=True),
              reads=(kA_res,), writes=(bAr,), tag=f"{pfx} W1{i}")
        pr.op("v", lambda e, triT=triT, DbT=DbT, bA=bA: e.tensor_tensor(
              triT[:], bA[:, :CH], DbT[:], ALU.mult),
              reads=(bAr, R("DbT")), writes=(R("triT"),), tag="triT")
        pr.op("v", lambda e, MTt=MTt, triT=triT: e.tensor_tensor(
              MTt[:], triT[:], p.sb["c_eyeb"][:], ALU.add),
              reads=(R("triT"), "c_eyeb"), writes=(R("MTt"),), tag="MT")
        pr.op("v", lambda e, Xb=Xb, triT=triT: e.tensor_tensor(
              Xb[:], p.sb["c_eyeb"][:], triT[:], ALU.subtract),
              reads=(R("triT"), "c_eyeb"), writes=(R("Xb"),), tag="X0")
        for it in range(6):
            pr.op("t", lambda e, MTt=MTt, Xb=Xb, bN=bN: e.matmul(
                  bN[:, :CH], MTt[:], Xb[:], start=True, stop=True),
                  reads=(R("MTt"), R("Xb")), writes=(bNr,), tag=f"nwP{it}")
            pr.op("s", lambda e, Pe=Pe, bN=bN: e.copy(Pe[:], bN[:, :CH]),
                  reads=(bNr,), writes=(R("Pe"),), tag="nwPe")
            pr.op("t", lambda e, Pe=Pe, bD=bD: e.transpose(bD[:, 0:CH], Pe[:], idb[:]),
                  reads=(R("Pe"), "c_idb"), writes=(bDr,), tag="nwPT")
            pr.op("v", lambda e, PeT=PeT, bD=bD: e.tensor_copy(PeT[:], bD[:, 0:CH]),
                  reads=(bDr,), writes=(R("PeT"),), tag="nwPTe")
            pr.op("t", lambda e, PeT=PeT, Xb=Xb, bN=bN: e.matmul(
                  bN[:, :CH], PeT[:], Xb[:], start=True, stop=True),
                  reads=(R("PeT"), R("Xb")), writes=(bNr,), tag="nwXP")
            pr.op("v", lambda e, Xb=Xb, bN=bN: e.scalar_tensor_tensor(
                  Xb[:], Xb[:], 2.0, bN[:, :CH], ALU.mult, ALU.subtract),
                  reads=(R("Xb"), bNr), writes=(R("Xb"),), tag="nwX")
        pr.op("t", lambda e, Xb=Xb, bD=bD: e.transpose(bD[:, 0:CH], Xb[:], idb[:]),
              reads=(R("Xb"), "c_idb"), writes=(bDr,), tag="trTmT")
        pr.op("s", lambda e, Zb=Zb, bD=bD: e.copy(Zb[:], bD[:, 0:CH]),
              reads=(bDr,), writes=(R("Zb"),), tag="TmT")
        pr.op("t", lambda e, cs=cs, bD=bD: e.transpose(
              bD[:, 128:128 + dv], vA_ap[:dv, cs], idb[idsl, idsl]),
              reads=(vA_res, "c_idb"), writes=(bDr,), tag="trV")
        pr.op("v", lambda e, i=i, VbT=VbT, bD=bD: e.tensor_scalar(
              VbT[:, :dv], bD[:, 128:128 + dv], beT[:, ci(i)], None, ALU.mult),
              reads=(bDr, beT_r), writes=(R("VbT"),), tag="VbT")
        pr.op("t", lambda e, cs=cs, bD=bD: e.transpose(
              bD[:, 256:256 + dk], kA_ap[:dk, cs], idb[idsl, idsl]),
              reads=(kA_res, "c_idb"), writes=(bDr,), tag="trK")
        pr.op("s", lambda e, kT=kT, bD=bD: e.copy(kT[:, :dk], bD[:, 256:256 + dk]),
              reads=(bDr,), writes=(R("kT"),), tag="kT")
        pr.op("v", lambda e, i=i, KtT=KtT, kT=kT: e.tensor_scalar(
              KtT[:, :dk], kT[:, :dk], begT[:, ci(i)], None, ALU.mult),
              reads=(R("kT"), begT_r), writes=(R("KtT"),), tag="KtT")
        pr.op("v", lambda e, i=i, K2T=K2T, kT=kT: e.tensor_scalar(
              K2T[:, :dk], kT[:, :dk], erT[:, ci(i)], None, ALU.mult),
              reads=(R("kT"), erT_r), writes=(R("K2T"),), tag="K2T")
        pr.op("t", lambda e, KtT=KtT, Zb=Zb, bC=bC: e.matmul(
              bC[:dk, :CH], KtT[:, :dk], Zb[:], start=True, stop=True),
              reads=(R("KtT"), R("Zb")), writes=(bCr,), tag="kcum")
        pr.op("s", lambda e, kcn=kcn, bC=bC: e.mul(kcn[:dk, :], bC[:dk, :CH], -1.0),
              reads=(bCr,), writes=(R("kcn"),), tag="kcn")
        pr.op("t", lambda e, VbT=VbT, Zb=Zb, bC=bC: e.matmul(
              bC[:dv, :CH], VbT[:, :dv], Zb[:], start=True, stop=False),
              reads=(R("VbT"), R("Zb")), writes=(bCr,), tag="vn1")
        pr.op("t", lambda e, kcn=kcn, bC=bC: e.matmul(
              bC[:dv, :CH], S_ap, kcn[:dk, :], start=False, stop=True),
              reads=(S_res, R("kcn")), writes=(bCr,), tag="vn2")
        pr.op("s", lambda e, vnA=vnA, bC=bC: e.copy(vnA[:dv, :], bC[:dv, :CH]),
              reads=(bCr,), writes=(R("vnA"),), tag="vnA")
        pr.op("t", lambda e, vnA=vnA, bD=bD: e.transpose(
              bD[:, 384:384 + dv], vnA[:dv, :], idb[:dv, :dv]),
              reads=(R("vnA"), "c_idb"), writes=(bDr,), tag="trVn")
        pr.op("s", lambda e, vnR=vnR, bD=bD: e.copy(vnR[:, :dv], bD[:, 384:384 + dv]),
              reads=(bDr,), writes=(R("vnR"),), tag="vnR")
        bcast_strip(p, pr, egs[:, cs], egs_r, CH, bA, bAr, f32=True)
        pr.op("v", lambda e, cs=cs, qt=qt, bA=bA: e.tensor_tensor(
              qt[:dk, :], qA_ap[:, cs], bA[:dk, :CH], ALU.mult),
              reads=(qA_res, bAr), writes=(R("qt"),), tag="qt")
        pr.op("t", lambda e, cs=cs, bA=bA: e.matmul(
              bA[:, :CH], kA_ap[:, cs], qA_ap[:, cs], start=True, stop=True),
              reads=(kA_res, qA_res), writes=(bAr,), tag="attn")
        pr.op("v", lambda e, atT=atT, DTt=DTt, bA=bA: e.tensor_tensor(
              atT[:], bA[:, :CH], DTt[:], ALU.mult),
              reads=(bAr, R("DTt")), writes=(R("atT"),), tag="atT")
        pr.op("t", lambda e, qt=qt, bC=bC: e.matmul(
              bC[:dv, :CH], S_ap, qt[:dk, :], start=True, stop=False),
              reads=(S_res, R("qt")), writes=(bCr,), tag="out1")
        pr.op("t", lambda e, vnR=vnR, atT=atT, bC=bC: e.matmul(
              bC[:dv, :CH], vnR[:, :dv], atT[:], start=False, stop=True),
              reads=(R("vnR"), R("atT")), writes=(bCr,), tag="out2")
        pr.op("s", lambda e, cs=cs, bC=bC: e.copy(core_ap[:, cs], bC[:dv, :CH]),
              reads=(bCr,), writes=(core_res,), tag=f"{pfx} core{i}")
        pr.op("t", lambda e, K2T=K2T, vnR=vnR, bC=bC: e.matmul(
              bC[:dk, :dv], K2T[:, :dk], vnR[:, :dv], start=True, stop=True),
              reads=(R("K2T"), R("vnR")), writes=(bCr,), tag="U")
        pr.op("v", lambda e, i=i, bC=bC: e.scalar_tensor_tensor(
              S_ap, S_ap, egl[:dk, i:i + 1], bC[:dk, :dv], ALU.mult, ALU.add),
              reads=(S_res, egl_r, bCr), writes=(S_res,), tag="Supd")


def emit_s2a(p, pr, qcA, kcA, vcA, zsA, bg, gatedA, nw_hbm):
    nc = p.nc
    idf = p.sb["c_idf"]
    pf0 = p.psum_bank("pf0")
    pf1 = p.psum_bank("pf1")
    stage = p.sbuf("s2_stage", [8, T], F32)
    gT = p.sbuf("s2_gT", [128, NCH * 8], F32)
    gTn = p.sbuf("s2_gTn", [128, NCH * 8], F32)
    beT = p.sbuf("s2_beT", [128, NCH * 8], F32)
    begT = p.sbuf("s2_begT", [128, NCH * 8], F32)
    erT = p.sbuf("s2_erT", [128, NCH * 8], F32)
    for row, dst, res in ((0, gT, "s2gT"), (3, beT, "s2beT"),
                          (4, begT, "s2begT"), (2, erT, "s2erT")):
        pr.dma("y", stage[:], bg[row], writes=("s2stage",), tag=f"s2 ldbg{row}")
        for i in range(NCH):
            pr.op("t", lambda e, i=i: e.transpose(
                  pf0[:, (i % 4) * 128:(i % 4) * 128 + 8],
                  stage[:, i * CH:(i + 1) * CH], idf[:8, :8]),
                  reads=("s2stage", "c_idf"), writes=("pf0",), tag=f"s2 trbg{row}_{i}")
            pr.op("v", lambda e, i=i, dst=dst, res=res: e.tensor_copy(
                  dst[:, i * 8:(i + 1) * 8], pf0[:, (i % 4) * 128:(i % 4) * 128 + 8]),
                  reads=("pf0",), writes=(res,), tag=f"s2 evbg{row}_{i}")
    pr.op("v", lambda e: e.tensor_scalar_mul(gTn[:], gT[:], -1.0),
          reads=("s2gT",), writes=("s2gTn",))
    nw = p.sbuf("s2_nw", [128, 1], F32)
    pr.dma("y", nw[:], nw_hbm, writes=("s2nw",))

    qh = p.sbuf("s2_qh", [128, T], BF16)
    kh = p.sbuf("s2_kh", [128, T], BF16)
    vh = p.sbuf("s2_vh", [128, T], BF16)
    zh = p.sbuf("s2_zh", [128, T], BF16)
    sqb = p.sbuf("s2_sq", [128, T], BF16)
    core = p.sbuf("s2_core", [128, T], BF16)
    gcore = p.sbuf("s2_gcore", [128, T], BF16)
    qn = p.sbuf("s2_qn", [128, T], BF16)
    kn = p.sbuf("s2_kn", [128, T], BF16)
    ssf = p.sbuf("s2_ssf", [1, T], F32)
    g_st = p.sbuf("s2_gst", [1, T], F32)
    eg_st = p.sbuf("s2_egst", [1, T], F32)
    be_st = p.sbuf("s2_best", [1, T], BF16)
    egl = p.sbuf("s2_egl", [128, NCH], F32)
    Sst = p.sbuf("s2_S", [128, 128], BF16)

    def l2pass(src_ap, src_res, dst, dst_res, exp_bias):
        pr.op("s", lambda e: e.activation(sqb[:], src_ap, AF.Square),
              reads=(src_res,), writes=("s2sq",), tag="sq")
        for ch in range(8):
            pr.op("t", lambda e, ch=ch: e.matmul(
                  pf1[:1, :512],
                  p.sb["c_ones_c1b"][:], sqb[:, ch * 512:(ch + 1) * 512],
                  start=True, stop=True),
                  reads=("s2sq", "c_ones_c1b"), writes=("pf1",), tag=f"ssq{ch}")
            pr.op("v", lambda e, ch=ch: e.tensor_copy(
                  ssf[:, ch * 512:(ch + 1) * 512], pf1[:1, :512]),
                  reads=("pf1",), writes=("s2ssf",), tag=f"ssqe{ch}")
        rstd_strip(p, pr, ssf[:], "s2ssf", 1.0, 1e-6, exp_bias, n=1)
        for ch in range(8):
            sl = slice(ch * 512, (ch + 1) * 512)
            bcast_strip(p, pr, ssf[:, sl], "s2ssf", 512, pf0, "pf0", f32=True)
            pr.op("v", lambda e, sl=sl, src_ap=src_ap: e.tensor_tensor(
                  dst[:, sl], src_ap[:, sl], pf0[:, :512], ALU.mult),
                  reads=(src_res, "pf0"), writes=(dst_res,), tag="l2mul")

    LN128 = float(np.log(128.0 ** -0.5))
    for h in range(8):
        pr.dma("y", qh[:], qcA[:, h, :], writes=("s2qh",), tag=f"s2 ldq{h}")
        pr.dma("y", kh[:], kcA[:, h, :], writes=("s2kh",), tag=f"s2 ldk{h}")
        pr.dma("y", vh[:], vcA[:, h, :], writes=("s2vh",), tag=f"s2 ldv{h}")
        pr.dma("y", zh[:], zsA[:, h, :], writes=("s2zh",), tag=f"s2 ldz{h}")
        pr.dma("y", g_st[:], bg[0, h:h + 1, :], writes=("s2gst",), tag=f"s2 ldg{h}")
        pr.dma("y", eg_st[:], bg[1, h:h + 1, :], writes=("s2egst",), tag=f"s2 ldeg{h}")
        pr.dma("g", be_st[:], bg[3, h:h + 1, :], writes=("s2best",), tag=f"s2 ldbe{h}")
        bcast_strip(p, pr, eg_st[:, CH - 1::CH], "s2egst", NCH, pf0, "pf0", f32=True)
        pr.op("v", lambda e: e.tensor_copy(egl[:], pf0[:, :NCH]),
              reads=("pf0",), writes=("s2egl",), tag="egl")
        l2pass(qh[:], "s2qh", qn, "s2qn", LN128)
        l2pass(kh[:], "s2kh", kn, "s2kn", None)
        if h == 0:
            p.dbg_store(pr, "qn", qn[:], "s2qn", [128, T], BF16)
            p.dbg_store(pr, "kn", kn[:], "s2kn", [128, T], BF16)
            p.dbg_store(pr, "egl", egl[:], "s2egl", [128, NCH])
            p.dbg_store(pr, "gT", gT[:], "s2gT", [128, NCH * 8])
            p.dbg_store(pr, "beT", beT[:], "s2beT", [128, NCH * 8])
            p.dbg_store(pr, "begT", begT[:], "s2begT", [128, NCH * 8])
            p.dbg_store(pr, "erT", erT[:], "s2erT", [128, NCH * 8])
        pr.op("v", lambda e: e.memset(Sst[:], 0.0), writes=("s2S",), tag="S0")
        emit_delta_head(
            p, pr, "s2d", 128, 128, 8, h,
            kn[:], "s2kn", qn[:], "s2qn", vh[:], "s2vh",
            dict(g=(g_st[:], "s2gst"), Eg=(eg_st[:], "s2egst"),
                 be=(be_st[:], "s2best")),
            dict(gTn=(gTn[:, :], "s2gTn"), beT=(beT[:, :], "s2beT"),
                 erT=(erT[:, :], "s2erT"), begT=(begT[:, :], "s2begT")),
            (egl[:], "s2egl"), Sst[:], "s2S", core[:], "s2core")
        pr.op("s", lambda e: e.activation(sqb[:], core[:], AF.Square),
              reads=("s2core",), writes=("s2sq",), tag="csq")
        for ch in range(8):
            pr.op("t", lambda e, ch=ch: e.matmul(
                  pf1[:1, :512], p.sb["c_ones_c1b"][:],
                  sqb[:, ch * 512:(ch + 1) * 512], start=True, stop=True),
                  reads=("s2sq", "c_ones_c1b"), writes=("pf1",), tag=f"cs{ch}")
            pr.op("v", lambda e, ch=ch: e.tensor_copy(
                  ssf[:, ch * 512:(ch + 1) * 512], pf1[:1, :512]),
                  reads=("pf1",), writes=("s2ssf",), tag=f"cse{ch}")
        rstd_strip(p, pr, ssf[:], "s2ssf", 1.0 / 128.0, 1e-6, None, n=1)
        for ch in range(8):
            sl = slice(ch * 512, (ch + 1) * 512)
            bcast_strip(p, pr, ssf[:, sl], "s2ssf", 512, pf0, "pf0", f32=True)
            pr.op("v", lambda e, sl=sl: e.scalar_tensor_tensor(
                  gcore[:, sl], core[:, sl], nw[:], pf0[:, :512], ALU.mult, ALU.mult),
                  reads=("s2core", "s2nw", "pf0"), writes=("s2gcore",), tag="gc1")
            pr.op("v", lambda e, sl=sl: e.tensor_tensor(
                  gcore[:, sl], gcore[:, sl], zh[:, sl], ALU.mult),
                  reads=("s2gcore", "s2zh"), writes=("s2gcore",), tag="gc2")
        pr.dma("y", gatedA[:, h, :], gcore[:], reads=("s2gcore",), tag=f"s2 stg{h}")


def emit_s2b(p, pr, gatedA, WoT_hbm, s0A):
    gt = p.sbuf("s2b_g", [128, NF * T], BF16)
    pr.dma("y", gt[:].rearrange("p (f t) -> p f t", t=T), gatedA,
           writes=("s2bg",), tag="s2b ldg")

    def rhs(k, ch):
        return gt[:, k * T + ch * 512: k * T + ch * 512 + 512], "s2bg"

    banks = [(p.psum_bank("pf0"), "pf0"), (p.psum_bank("pf1"), "pf1")]
    ob = [p.sbuf(f"s2b_o{i}", [128, T], BF16) for i in range(2)]

    def evac(m, ch, ba, bres):
        ms = m % 2
        O = f"s2bo{ms}"
        pr.op("v", lambda e, ms=ms, ch=ch, ba=ba: e.tensor_copy(
              ob[ms][:, ch * 512:(ch + 1) * 512], ba[:, :512]),
              reads=(bres,), writes=(O,), tag=f"s2b ev{m}_{ch}")
        if ch == 7:
            pr.dma("y", s0A[:, m, :], ob[ms][:], reads=(O,), tag=f"s2b st{m}")
    emit_gemm(p, pr, "s2b", WoT_hbm, C, C, rhs, evac, banks, "s2bw")



def emit_rmsA(p, pr, pfx, src_rhs, n_w_col, xr, xr_res, eps=1e-6, pre_ch=None):
    pf0 = p.psum_bank("pf0")
    pf1 = p.psum_bank("pf1")
    sq = p.sbuf(f"{pfx}_sq", [128, 512], BF16)
    ssf = p.sbuf(f"{pfx}_ssf", [1, T], F32)
    SQ, SS = f"{pfx}sq", f"{pfx}ssf"
    for ch in range(8):
        if pre_ch is not None:
            pre_ch(ch, 0)
        for f in range(NF):
            ap, res = src_rhs(f, ch)
            pr.op("s", lambda e, ap=ap: e.activation(sq[:], ap, AF.Square),
                  reads=(res,), writes=(SQ,), tag=f"{pfx} sq{ch}_{f}")
            pr.op("t", lambda e, f=f: e.matmul(pf1[:1, :512], p.sb["c_ones_c1b"][:],
                                               sq[:], start=(f == 0), stop=(f == NF - 1)),
                  reads=(SQ, "c_ones_c1b"), writes=("pf1",), tag=f"{pfx} ssq{ch}_{f}")
        pr.op("v", lambda e, ch=ch: e.tensor_copy(ssf[:, ch * 512:(ch + 1) * 512],
                                                  pf1[:1, :512]),
              reads=("pf1",), writes=(SS,), tag=f"{pfx} ssqe{ch}")
    rstd_strip(p, pr, ssf[:], SS, 1.0 / C, eps, None, n=1)
    for ch in range(8):
        if pre_ch is not None:
            pre_ch(ch, 1)
        sl = slice(ch * 512, (ch + 1) * 512)
        bcast_strip(p, pr, ssf[:, sl], SS, 512, pf0, "pf0", f32=True)
        for f in range(NF):
            ap, res = src_rhs(f, ch)
            if n_w_col is None:
                pr.op("v", lambda e, ap=ap, f=f, sl=sl: e.tensor_tensor(
                      xr[:, f * T:][:, sl], ap, pf0[:, :512], ALU.mult),
                      reads=(res, "pf0"), writes=(xr_res,), tag=f"{pfx} xr{ch}_{f}")
            else:
                wc, wcr = n_w_col
                pr.op("v", lambda e, ap=ap, f=f, sl=sl, wc=wc: e.scalar_tensor_tensor(
                      xr[:, f * T:][:, sl], ap, wc[:, f:f + 1], pf0[:, :512],
                      ALU.mult, ALU.mult),
                      reads=(res, wcr, "pf0"), writes=(xr_res,), tag=f"{pfx} xr{ch}_{f}")


def emit_s3a(p, pr, xnA, qkvA, win):
    n1 = p.sbuf("s3_n1", [128, NF], F32)
    pr.dma("y", n1[:], win["f0_n1"], writes=("s3n1",))
    qb = p.sbuf("s3_qb", [128, 24], F32)
    pr.dma("y", qb[:], win["f0_qkvb"], writes=("s3qb",))
    xr = p.sbuf("s3_xr", [128, NF * T], BF16)
    xn = p.sbuf("s3_xnf", [128, NF * T], BF16)
    pr.dma("y", xn[:].rearrange("p (f t) -> p f t", t=T), xnA,
           writes=("s3xn",), tag="s3 ldxn")

    def src(f, ch):
        return xn[:, f * T + ch * 512: f * T + (ch + 1) * 512], "s3xn"
    emit_rmsA(p, pr, "s3r", src, (n1, "s3n1"), xr[:], "s3xr")

    def rhs(k, ch):
        return xr[:, k * T + ch * 512: k * T + ch * 512 + 512], "s3xr"

    banks = [(p.psum_bank("pf2"), "pf2"), (p.psum_bank("pf3"), "pf3")]
    raw = [p.sbuf(f"s3_raw{i}", [128, T], BF16) for i in range(2)]
    sw = [p.sbuf(f"s3_sw{i}", [128, T], BF16) for i in range(2)]
    cosR = p.sb["c_cosR"]
    sinR = p.sb["c_sinR"]

    def evac(m, ch, ba, bres):
        ms = m % 2
        R = f"s3raw{ms}"
        pr.op("v", lambda e, ms=ms, ch=ch, ba=ba, m=m: e.tensor_scalar(
              raw[ms][:, ch * 512:(ch + 1) * 512], ba[:, :512],
              qb[:, m:m + 1], None, ALU.add),
              reads=(bres, "s3qb"), writes=(R,), tag=f"s3 ev{m}_{ch}")
        if ch == 7:
            if m >= 16:
                pr.dma("y", qkvA[:, m, :], raw[ms][:], reads=(R,), tag=f"s3 stv{m}")
            else:
                S = f"s3sw{ms}"
                for blk in range(4):
                    srcr = [32, 0, 96, 64][blk]
                    pr.dma("y", sw[ms][blk * 32:(blk + 1) * 32, :],
                           raw[ms][srcr:srcr + 32, :], reads=(R,), writes=(S,),
                           tag=f"s3 sw{m}_{blk}")
                pr.op("v", lambda e, ms=ms: e.tensor_tensor(raw[ms][:], raw[ms][:],
                                                            cosR[:], ALU.mult),
                      reads=(R, "c_cosR"), writes=(R,), tag="rope1")
                pr.op("v", lambda e, ms=ms: e.tensor_tensor(sw[ms][:], sw[ms][:],
                                                            sinR[:], ALU.mult),
                      reads=(S, "c_sinR"), writes=(S,), tag="rope2")
                pr.op("v", lambda e, ms=ms: e.tensor_tensor(raw[ms][:], raw[ms][:],
                                                            sw[ms][:], ALU.add),
                      reads=(R, S), writes=(R,), tag="rope3")
                pr.dma("y", qkvA[:, m, :], raw[ms][:], reads=(R,), tag=f"s3 stqk{m}")
    emit_gemm(p, pr, "s3qkv", win["f0_qkvT"], C, 3 * C, rhs, evac, banks, "s3w")



def emit_s3b(p, pr, qkvA, oAh):
    pf_sc = [(p.psum_bank("pf0"), "pf0"), (p.psum_bank("pf1"), "pf1")]
    pf_o = p.psum_bank("pf2")
    pf_b = p.psum_bank("pf3")
    q2 = p.sbuf("s3b_q", [128, T], BF16)
    k2 = p.sbuf("s3b_k", [128, T], BF16)
    v2 = p.sbuf("s3b_v", [128, T], BF16)
    sq = p.sbuf("s3b_sq", [128, T], BF16)
    ssf = p.sbuf("s3b_ssf", [1, T], F32)
    mx = p.sbuf("s3b_mx", [1, 4], F32)
    negB = p.sbuf("s3b_negB", [128, 1], F32)
    vT = p.sbuf("s3b_vT", [128, 32 * 65], BF16)
    PT = [p.sbuf(f"s3b_PT{i}", [128, 512], BF16) for i in range(2)]
    lsb = p.sbuf("s3b_l", [1, 512], F32)
    rb = p.sbuf("s3b_rb", [64, 512], F32)
    oa = p.sbuf("s3b_oa", [128, T], BF16)
    idb = p.sb["c_idb"]

    for hp in range(8):
        pr.dma("y", q2[:], qkvA[:, hp, :], writes=("s3bq",), tag=f"s3b ldq{hp}")
        pr.dma("y", k2[:], qkvA[:, 8 + hp, :], writes=("s3bk",), tag=f"s3b ldk{hp}")
        pr.dma("y", v2[:], qkvA[:, 16 + hp, :], writes=("s3bv",), tag=f"s3b ldv{hp}")
        for hh in range(2):
            rws = slice(hh * 64, hh * 64 + 64)
            for t_, src_, res_ in ((0, q2, "s3bq"), (1, k2, "s3bk")):
                pr.op("s", lambda e, src_=src_, rws=rws: e.activation(
                      sq[rws, :], src_[rws, :], AF.Square),
                      reads=(res_,), writes=("s3bsq",), tag="Bsq")
                for ch in range(8):
                    pr.op("t", lambda e, ch=ch, rws=rws: e.matmul(
                          pf_b[:1, :512], p.sb["c_ones_c1b"][rws, :],
                          sq[rws, ch * 512:(ch + 1) * 512], start=True, stop=True),
                          reads=("s3bsq", "c_ones_c1b"), writes=("pf3",), tag="Bmm")
                    pr.op("v", lambda e, ch=ch: e.tensor_copy(
                          ssf[:, ch * 512:(ch + 1) * 512], pf_b[:1, :512]),
                          reads=("pf3",), writes=("s3bssf",), tag="Bev")
                pr.op("v", lambda e, t_=t_: e.tensor_reduce(
                      mx[:, t_:t_ + 1], ssf[:], AX.X, ALU.max),
                      reads=("s3bssf",), writes=("s3bmx",), tag="Bmax")
            pr.op("s", lambda e: e.activation(mx[:, 0:1], mx[:, 0:1], AF.Sqrt),
                  reads=("s3bmx",), writes=("s3bmx",), tag="Bsq1")
            pr.op("s", lambda e: e.activation(mx[:, 1:2], mx[:, 1:2], AF.Sqrt),
                  reads=("s3bmx",), writes=("s3bmx",), tag="Bsq2")
            pr.op("v", lambda e: e.tensor_tensor(mx[:, 2:3], mx[:, 0:1], mx[:, 1:2],
                                                 ALU.mult),
                  reads=("s3bmx",), writes=("s3bmx",), tag="Bmul")
            pr.op("v", lambda e: e.tensor_scalar_mul(mx[:, 2:3], mx[:, 2:3], -0.125),
                  reads=("s3bmx",), writes=("s3bmx",), tag="Bneg")
            bcast_strip(p, pr, mx[:, 2:3], "s3bmx", 1, pf_b, "pf3", f32=True)
            pr.op("v", lambda e: e.tensor_copy(negB[:], pf_b[:, :1]),
                  reads=("pf3",), writes=("s3bnegB",), tag="Bbc")
            pr.op("v", lambda e: e.memset(vT[:], 1.0), writes=("s3bvT",), tag="vT1")
            bb0 = p.psum_bank("pbb0", BF16)
            for kt in range(32):
                pr.op("t", lambda e, kt=kt, rws=rws: e.transpose(
                      bb0[:, (kt % 8) * 128:(kt % 8) * 128 + 64],
                      v2[rws, kt * 128:(kt + 1) * 128], idb[rws, rws]),
                      reads=("s3bv", "c_idb"), writes=("pbb0",), tag=f"vTt{kt}")
                pr.op("v", lambda e, kt=kt: e.tensor_copy(
                      vT[:, kt * 65:kt * 65 + 64], bb0[:, (kt % 8) * 128:(kt % 8) * 128 + 64]),
                      reads=("pbb0",), writes=("s3bvT",), tag=f"vTe{kt}")
            for qc in range(8):
                qsl = slice(qc * 512, (qc + 1) * 512)

                def sc_mm(kt):
                    sb, sbr = pf_sc[kt % 2]
                    pr.op("t", lambda e, kt=kt, rws=rws, qsl=qsl, sb=sb: e.matmul(
                          sb[:, :512], k2[rws, kt * 128:(kt + 1) * 128],
                          q2[rws, qsl], start=True, stop=True),
                          reads=("s3bk", "s3bq"), writes=(sbr,), tag=f"fs{qc}_{kt}")

                def exp_op(kt):
                    sb, sbr = pf_sc[kt % 2]
                    pt = PT[kt % 2]
                    pr.op("s", lambda e, sb=sb, pt=pt: e.activation(
                          pt[:], sb[:, :512], AF.Exp, bias=negB[:], scale=0.125),
                          reads=(sbr, "s3bnegB"), writes=(f"s3bPT{kt % 2}",),
                          tag=f"fe{qc}_{kt}")

                def pv_mm(kt):
                    pt = PT[kt % 2]
                    pr.op("t", lambda e, kt=kt, pt=pt: e.matmul(
                          pf_o[:65, :512], vT[:, kt * 65:(kt + 1) * 65], pt[:],
                          start=(kt == 0), stop=(kt == 31), skip_group_check=True),
                          reads=("s3bvT", f"s3bPT{kt % 2}"), writes=("pf2",),
                          tag=f"fpv{qc}_{kt}")
                sc_mm(0)
                exp_op(0)
                for kt in range(1, 32):
                    sc_mm(kt)
                    pv_mm(kt - 1)
                    exp_op(kt)
                pv_mm(31)
                pr.op("v", lambda e: e.tensor_copy(lsb[:], pf_o[64:65, :512]),
                      reads=("pf2",), writes=("s3bl",), tag="fl")
                pr.op("v", lambda e: e.reciprocal(lsb[:], lsb[:]),
                      reads=("s3bl",), writes=("s3bl",), tag="fr")
                bcast_strip(p, pr, lsb[:], "s3bl", 512, pf_b, "pf3", f32=True)
                pr.op("v", lambda e: e.tensor_copy(rb[:], pf_b[:64, :512]),
                      reads=("pf3",), writes=("s3brb",), tag="frb")
                pr.op("v", lambda e, rws=rws, qsl=qsl: e.tensor_tensor(
                      oa[rws, qsl], pf_o[:64, :512], rb[:], ALU.mult),
                      reads=("pf2", "s3brb"), writes=("s3boa",), tag="fo")
        pr.dma("y", oAh[:, hp, :], oa[:], reads=("s3boa",), tag=f"s3b st{hp}")



def emit_s3c(p, pr, oAh, xnA, x1fA, win):
    oh = p.sbuf("s3c_o", [128, NF * T], BF16)
    pr.dma("y", oh[:].rearrange("p (f t) -> p f t", t=T), oAh,
           writes=("s3co",), tag="s3c ldo")
    ob = p.sbuf("s3c_ob", [128, NF], F32)
    pr.dma("y", ob[:], win["f0_ob"], writes=("s3cob",))
    xnc = p.sbuf("s3c_xn", [128, 512], BF16)
    tmp = p.sbuf("s3c_tmp", [128, 512], F32)
    x1c = p.sbuf("s3c_x1", [128, 512], F32)

    def rhs(k, ch):
        return oh[:, k * T + ch * 512: k * T + ch * 512 + 512], "s3co"

    banks = [(p.psum_bank("pf0"), "pf0"), (p.psum_bank("pf1"), "pf1")]

    def evac(m, ch, ba, bres):
        pr.dma("y", xnc[:], xnA[:, m, ch * 512:(ch + 1) * 512],
               writes=("s3cxn",), tag=f"s3c ldxn{m}_{ch}")
        pr.op("v", lambda e, ba=ba, m=m: e.tensor_scalar(
              tmp[:], ba[:, :512], ob[:, m:m + 1], None, ALU.add),
              reads=(bres, "s3cob"), writes=("s3ctmp",), tag=f"s3c bias{m}_{ch}")
        pr.op("v", lambda e: e.tensor_tensor(x1c[:], tmp[:], xnc[:], ALU.add),
              reads=("s3ctmp", "s3cxn"), writes=("s3cx1",), tag=f"s3c add{m}_{ch}")
        pr.dma("y", x1fA[:, m, ch * 512:(ch + 1) * 512], x1c[:],
               reads=("s3cx1",), tag=f"s3c st{m}_{ch}")
    emit_gemm(p, pr, "s3c", win["f0_oT"], C, C, rhs, evac, banks, "s3cw")



def emit_top2(p, pr, pfx, lg_psum, psum_res, nexp, wts_ap, wts_res, mode):
    lg = p.sbuf(f"{pfx}_lg", [128, 8], F32) if f"{pfx}_lg" not in p.sb else p.sb[f"{pfx}_lg"]
    st = p.sbuf(f"{pfx}_st", [128, 8], F32) if f"{pfx}_st" not in p.sb else p.sb[f"{pfx}_st"]
    m_ = p.sbuf(f"{pfx}_m", [128, 6], F32) if f"{pfx}_m" not in p.sb else p.sb[f"{pfx}_m"]
    L, Sr, M = f"{pfx}lg", f"{pfx}st", f"{pfx}m"
    ne = nexp
    pr.op("v", lambda e: e.tensor_copy(lg[:, :ne], lg_psum), reads=(psum_res,),
          writes=(L,), tag=f"{pfx} cp")
    pr.op("v", lambda e: e.tensor_reduce(m_[:, 0:1], lg[:, :ne], AX.X, ALU.max),
          reads=(L,), writes=(M,), tag="m1")
    pr.op("v", lambda e: e.tensor_scalar(st[:, :ne], lg[:, :ne], m_[:, 0:1], None, ALU.is_ge),
          reads=(L, M), writes=(Sr,), tag="mask1")
    pr.op("v", lambda e: e.scalar_tensor_tensor(lg[:, :ne], st[:, :ne], float(NEG), lg[:, :ne],
                                                ALU.mult, ALU.add),
          reads=(Sr, L), writes=(L,), tag="lgm")
    pr.op("v", lambda e: e.tensor_reduce(m_[:, 1:2], lg[:, :ne], AX.X, ALU.max),
          reads=(L,), writes=(M,), tag="m2")
    pr.op("v", lambda e: e.tensor_scalar(lg[:, :ne], lg[:, :ne], m_[:, 1:2], None, ALU.is_ge),
          reads=(L, M), writes=(L,), tag="mask2")
    if mode == "softmax":
        pr.op("v", lambda e: e.tensor_tensor(m_[:, 2:3], m_[:, 0:1], m_[:, 1:2], ALU.subtract),
              reads=(M,), writes=(M,), tag="d12")
        pr.op("s", lambda e: e.activation(m_[:, 3:4], m_[:, 2:3], AF.Sigmoid),
              reads=(M,), writes=(M + "x",), tag="s1")
        pr.op("v", lambda e: e.tensor_scalar(m_[:, 4:5], m_[:, 3:4], -1.0, 1.0,
                                             ALU.mult, ALU.add),
              reads=(M + "x",), writes=(M + "y",), tag="s2")
    else:
        pr.op("v", lambda e: e.tensor_tensor(m_[:, 2:3], m_[:, 0:1], m_[:, 1:2], ALU.add),
              reads=(M,), writes=(M,), tag="dsum")
        pr.op("v", lambda e: e.reciprocal(m_[:, 5:6], m_[:, 2:3]),
              reads=(M,), writes=(M + "r",), tag="rec")
        pr.op("v", lambda e: e.tensor_tensor(m_[:, 3:4], m_[:, 0:1], m_[:, 5:6], ALU.mult),
              reads=(M, M + "r"), writes=(M + "x",), tag="w1")
        pr.op("v", lambda e: e.tensor_tensor(m_[:, 4:5], m_[:, 1:2], m_[:, 5:6], ALU.mult),
              reads=(M, M + "r"), writes=(M + "y",), tag="w2")
    pr.op("v", lambda e: e.tensor_scalar(st[:, :ne], st[:, :ne], m_[:, 3:4], None, ALU.mult),
          reads=(Sr, M + "x"), writes=(Sr,), tag="c1")
    pr.op("v", lambda e: e.scalar_tensor_tensor(wts_ap, lg[:, :ne], m_[:, 4:5],
                                                st[:, :ne], ALU.mult, ALU.add),
          reads=(L, M + "y", Sr), writes=(wts_res,), tag="comb")


def emit_s4(p, pr, x1fA, s0A, xA, xb1A, comb_hbm, win):
    idf = p.sb["c_idf"]
    pf0 = p.psum_bank("pf0")
    pf1 = p.psum_bank("pf1")
    pfA = p.psum_bank("pf2")
    pfB = p.psum_bank("pf3")
    pfC = p.psum_bank("pf4")
    pfD = p.psum_bank("pf5")
    n2 = p.sbuf("s4_n2", [128, NF], F32)
    pr.dma("y", n2[:], win["f0_n2"], writes=("s4n2",))
    xr = p.sbuf("s4_xr", [128, NF * T], BF16)
    x1c = p.sbuf("s4_x1c", [128, NF * 512], F32)

    def ldx1(ch, phase):
        pr.dma("y", x1c[:].rearrange("p (f t) -> p f t", t=512),
               x1fA[:, :, ch * 512:(ch + 1) * 512],
               writes=("s4x1c",), tag=f"s4 ldx1_{phase}_{ch}")

    def src(f, ch):
        return x1c[:, f * 512:(f + 1) * 512], "s4x1c"
    emit_rmsA(p, pr, "s4r", src, (n2, "s4n2"), xr[:], "s4xr", pre_ch=ldx1)

    if getattr(p, "dbg", False):
        p.dbg_store(pr, "xr", xr[:], "s4xr", [128, NF * T], BF16)
    gw = p.sbuf("s4_gw", [128, NF * 8], BF16)
    pr.dma("g", gw[:].rearrange("r (nk c) -> r nk c", c=8),
           win["f0_gT"].rearrange("(nk r) c -> r nk c", r=128), writes=("s4gw",))
    comb8 = p.sbuf("s4_comb8", [8, 128], F32)
    wts = p.sbuf("s4_wts", [128, 8], F32)
    for i in range(NT):
        for k in range(NF):
            pr.op("t", lambda e, i=i, k=k: e.matmul(
                  pf0[:, :8], xr[:, k * T + i * 128: k * T + (i + 1) * 128],
                  gw[:, k * 8:(k + 1) * 8], start=(k == 0), stop=(k == NF - 1)),
                  reads=("s4xr", "s4gw"), writes=("pf0",), tag=f"s4 rt{i}_{k}")
        emit_top2(p, pr, "s4t", pf0[:, :8], "pf0", 8, wts[:], "s4wts", "softmax")
        pr.op("t", lambda e, i=i: e.transpose(pf1[:8, :128], wts[:], idf[:]),
              reads=("s4wts", "c_idf"), writes=("pf1",), tag=f"s4 rtt{i}")
        pr.op("v", lambda e, i=i: e.tensor_copy(comb8[:], pf1[:8, :128]),
              reads=("pf1",), writes=("s4comb8",), tag=f"s4 rte{i}")
        pr.dma("y", comb_hbm[:, i * 128:(i + 1) * 128], comb8[:],
               reads=("s4comb8",), writes=("d_comb8",), tag=f"s4 stcomb{i}")

    TC = 1024
    hsb = p.sbuf("s4_h", [128, 16 * TC], BF16)
    acc = p.sbuf("s4_acc", [128, NF * TC], BF16)
    w12 = [p.sbuf(f"s4_w12{i}", [128, NF * 128], BF16) for i in range(4)]
    w3b = [p.sbuf(f"s4_w3{i}", [128, 16 * 128], BF16) for i in range(2)]
    b1c = p.sbuf("s4_b1", [128, 16], F32)
    b2c = p.sbuf("s4_b2", [128, 16], F32)
    b3c = p.sbuf("s4_b3", [128, 8], F32)
    cstr = p.sbuf("s4_cstr", [1, TC], BF16)
    s1t = [p.sbuf(f"s4_s1t{i}", [128, 512], BF16) for i in range(2)]
    u2t = [p.sbuf(f"s4_u2t{i}", [128, 512], BF16) for i in range(2)]
    yt = [p.sbuf(f"s4_yt{i}", [128, 512], F32) for i in range(2)]
    xbuf = [p.sbuf(f"s4_xb{i}", [128, 512], F32) for i in range(2)]
    gbuf = [p.sbuf(f"s4_gb{i}", [128, 512], BF16) for i in range(2)]
    sbuf0 = [p.sbuf(f"s4_sb{i}", [128, 512], BF16) for i in range(2)]
    xab = [p.sbuf(f"s4_xa{i}", [128, 512], F32) for i in range(2)]
    ot = [p.sbuf(f"s4_ot{i}", [128, 512], F32) for i in range(2)]

    for tc in range(4):
        tsl = slice(tc * TC, (tc + 1) * TC)
        if getattr(p, "dbg", False) and tc > 0:
            p.dbg_store(pr, f"acc{tc - 1}", acc[:], "s4acc", [128, NF * TC], BF16)
        pr.op("v", lambda e: e.memset(acc[:], 0.0), writes=("s4acc",), tag=f"s4 z{tc}")
        for ex in range(8):
            pr.dma("g", cstr[:], comb_hbm[ex:ex + 1, tsl],
                   reads=("d_comb8",), writes=("s4cstr",), tag=f"s4 ldc{tc}_{ex}")
            pr.dma("y", b1c[:], win["f0_b1"][ex], writes=("s4b1",))
            pr.dma("y", b2c[:], win["f0_b2"][ex], writes=("s4b2",))
            pr.dma("y", b3c[:], win["f0_b3"][ex], writes=("s4b3",))
            HB = [(pfA, "pf2", pfB, "pf3"), (pfC, "pf4", pfD, "pf5")]
            for m in range(16):
                wp = m % 2
                b1a, b1r, b2a, b2r = HB[wp]
                W1R, W2R = f"s4w12{2 * wp}", f"s4w12{2 * wp + 1}"
                pr.dma("g", w12[2 * wp][:].rearrange("r (nk c) -> r nk c", c=128),
                       win["f0_w1T"][ex][:, m * 128:(m + 1) * 128]
                       .rearrange("(nk r) c -> r nk c", r=128),
                       writes=(W1R,), tag=f"s4 w1{ex}_{m}")
                pr.dma("g", w12[2 * wp + 1][:].rearrange("r (nk c) -> r nk c", c=128),
                       win["f0_w2T"][ex][:, m * 128:(m + 1) * 128]
                       .rearrange("(nk r) c -> r nk c", r=128),
                       writes=(W2R,), tag=f"s4 w2{ex}_{m}")
                for sub in range(2):
                    ssl = slice(tc * TC + sub * 512, tc * TC + sub * 512 + 512)
                    for k in range(NF):
                        pr.op("t", lambda e, k=k, wp=wp, ssl=ssl, b1a=b1a: e.matmul(
                              b1a[:, :512], w12[2 * wp][:, k * 128:(k + 1) * 128],
                              xr[:, k * T:][:, ssl], start=(k == 0), stop=(k == NF - 1)),
                              reads=(W1R, "s4xr"), writes=(b1r,), tag="h1mm")
                        pr.op("t", lambda e, k=k, wp=wp, ssl=ssl, b2a=b2a: e.matmul(
                              b2a[:, :512], w12[2 * wp + 1][:, k * 128:(k + 1) * 128],
                              xr[:, k * T:][:, ssl], start=(k == 0), stop=(k == NF - 1)),
                              reads=(W2R, "s4xr"), writes=(b2r,), tag="h2mm")
                    pr.op("s", lambda e, m=m, wp=wp, b1a=b1a: e.activation(
                          s1t[wp][:], b1a[:, :512], AF.Silu, bias=b1c[:, m:m + 1]),
                          reads=(b1r, "s4b1"), writes=(f"s4s1t{wp}",), tag="silu1")
                    pr.op("v", lambda e, m=m, wp=wp, b2a=b2a: e.tensor_scalar(
                          u2t[wp][:], b2a[:, :512], b2c[:, m:m + 1], None, ALU.add),
                          reads=(b2r, "s4b2"), writes=(f"s4u2t{wp}",), tag="b2add")
                    pr.op("v", lambda e, m=m, sub=sub, wp=wp: e.tensor_tensor(
                          hsb[:, m * TC + sub * 512: m * TC + sub * 512 + 512],
                          s1t[wp][:], u2t[wp][:], ALU.mult),
                          reads=(f"s4s1t{wp}", f"s4u2t{wp}"), writes=("s4h",), tag="hmul")
            if getattr(p, "dbg", False) and tc == 0 and ex == 7:
                p.dbg_store(pr, "hsb", hsb[:], "s4h", [128, 16 * TC], BF16)
            for mo in range(8):
                wp = mo % 2
                ya, yr = (pfA, "pf2") if wp == 0 else (pfB, "pf3")
                W3R = f"s4w3{wp}"
                pr.dma("g", w3b[wp][:].rearrange("r (nk c) -> r nk c", c=128),
                       win["f0_w3T"][ex][:, mo * 128:(mo + 1) * 128]
                       .rearrange("(nk r) c -> r nk c", r=128),
                       writes=(W3R,), tag=f"s4 w3{ex}_{mo}")
                for sub in range(2):
                    cb, cbr = (pfC, "pf4") if sub == 0 else (pfD, "pf5")
                    bcast_strip(p, pr, cstr[:, sub * 512:(sub + 1) * 512], "s4cstr",
                                512, cb, cbr, f32=False)
                    for k in range(16):
                        pr.op("t", lambda e, k=k, wp=wp, sub=sub, ya=ya: e.matmul(
                              ya[:, :512], w3b[wp][:, k * 128:(k + 1) * 128],
                              hsb[:, k * TC + sub * 512: k * TC + sub * 512 + 512],
                              start=(k == 0), stop=(k == 15)),
                              reads=(W3R, "s4h"), writes=(yr,), tag="ymm")
                    pr.op("v", lambda e, mo=mo, ya=ya, wp=wp: e.tensor_scalar(
                          yt[wp][:], ya[:, :512], b3c[:, mo:mo + 1], None, ALU.add),
                          reads=(yr, "s4b3"), writes=(f"s4yt{wp}",), tag="b3add")
                    pr.op("v", lambda e, cb=cb, wp=wp: e.tensor_tensor(
                          yt[wp][:], yt[wp][:], cb[:, :512], ALU.mult),
                          reads=(f"s4yt{wp}", cbr), writes=(f"s4yt{wp}",), tag="cmul")
                    pr.op("v", lambda e, mo=mo, sub=sub, wp=wp: e.tensor_tensor(
                          acc[:, mo * TC + sub * 512: mo * TC + sub * 512 + 512],
                          acc[:, mo * TC + sub * 512: mo * TC + sub * 512 + 512],
                          yt[wp][:], ALU.add),
                          reads=("s4acc", f"s4yt{wp}"), writes=("s4acc",), tag="yacc")
        for f in range(NF):
            for sub in range(2):
                w = (2 * f + sub) % 2
                XB, GB, SB, XAB, OT = (f"s4xb{w}", f"s4gb{w}", f"s4sb{w}",
                                       f"s4xa{w}", f"s4ot{w}")
                ssl = slice(tc * TC + sub * 512, tc * TC + sub * 512 + 512)
                pr.dma("y", xbuf[w][:], x1fA[:, f, ssl], writes=(XB,), tag="ld x1")
                pr.op("v", lambda e, f=f, sub=sub, w=w: e.tensor_tensor(
                      xbuf[w][:], xbuf[w][:], acc[:, f * TC + sub * 512: f * TC + sub * 512 + 512],
                      ALU.add),
                      reads=(XB, "s4acc"), writes=(XB,), tag="x2f")
                pr.op("s", lambda e, w=w: e.activation(gbuf[w][:], xbuf[w][:], AF.Silu),
                      reads=(XB,), writes=(GB,), tag="g0")
                pr.dma("y", sbuf0[w][:], s0A[:, f, ssl], writes=(SB,), tag="ld s0")
                pr.dma("y", xab[w][:], xA[:, f, ssl], writes=(XAB,), tag="ld xA")
                pr.op("v", lambda e, w=w: e.tensor_tensor(ot[w][:], gbuf[w][:], sbuf0[w][:], ALU.mult),
                      reads=(GB, SB), writes=(OT,), tag="sg")
                pr.op("v", lambda e, w=w: e.tensor_tensor(ot[w][:], ot[w][:], xab[w][:], ALU.add),
                      reads=(OT, XAB), writes=(OT,), tag="xb1")
                pr.dma("y", xb1A[:, f, ssl], ot[w][:], reads=(OT,), tag="st xb1")
        if getattr(p, "dbg", False) and tc == 3:
            p.dbg_store(pr, "acc3", acc[:], "s4acc", [128, NF * TC], BF16)



def emit_s5a(p, pr, xb1A, xn1A, q1A, k1A, v1A, gt1A, bg1, win):
    pf0, pf1, pf2 = p.psum_bank("pf0"), p.psum_bank("pf1"), p.psum_bank("pf2")
    banks = [(p.psum_bank("pf3"), "pf3"), (p.psum_bank("pf4"), "pf4")]
    banks2 = [(p.psum_bank("pf5"), "pf5"), (p.psum_bank("pf2"), "pf2")]
    xc = p.sbuf("s5_xc", [128, NF * 512], F32)
    xcb = p.sbuf("s5_xcb", [128, 512], BF16)
    sq = p.sbuf("s5_sq", [128, 512], BF16)
    nm = p.sbuf("s5_nm", [1, T], BF16)
    rs = p.sbuf("s5_rs", [1, T], F32)
    tmpf = p.sbuf("s5_tmpf", [128, 512], F32)
    xn1 = p.sbuf("s5_xn1", [128, NF * T], BF16)

    for ch in range(8):
        sl = slice(ch * 512, (ch + 1) * 512)
        pr.dma("y", xc[:].rearrange("p (f t) -> p f t", t=512),
               xb1A[:, :, sl], writes=("s5xc",), tag=f"s5 ldx{ch}")
        for f in range(NF):
            pr.op("v", lambda e, f=f: e.tensor_copy(xcb[:], xc[:, f * 512:(f + 1) * 512]),
                  reads=("s5xc",), writes=("s5xcb",), tag="cast")
            pr.op("s", lambda e: e.activation(sq[:], xcb[:], AF.Square),
                  reads=("s5xcb",), writes=("s5sq",), tag="sq")
            pr.op("t", lambda e, f=f: e.matmul(pf1[:1, :512], p.sb["c_ones_c1b"][:],
                                               xcb[:], start=(f == 0), stop=(f == NF - 1)),
                  reads=("s5xcb", "c_ones_c1b"), writes=("pf1",), tag="sum")
            pr.op("t", lambda e, f=f: e.matmul(pf0[:1, :512], p.sb["c_ones_c1b"][:],
                                               sq[:], start=(f == 0), stop=(f == NF - 1)),
                  reads=("s5sq", "c_ones_c1b"), writes=("pf0",), tag="ssq")
        pr.op("v", lambda e, sl=sl: e.tensor_scalar(nm[:, sl], pf1[:1, :512],
                                                    -1.0 / C, None, ALU.mult),
              reads=("pf1",), writes=("s5nm",), tag="negmean")
        pr.op("v", lambda e, sl=sl: e.tensor_scalar(rs[:, sl], pf0[:1, :512],
                                                    1.0 / C, None, ALU.mult),
              reads=("pf0",), writes=("s5rs",), tag="msq")
    pr.op("v", lambda e: e.scalar_tensor_tensor(nm[:, T:T] if False else rs[:],
          nm[:], -1.0, rs[:], ALU.mult, ALU.bypass) if False else
          e.tensor_tensor(sq[:1, :1], nm[:1, :1], nm[:1, :1], ALU.mult),
          reads=(), writes=(), tag="dead") if False else None
    mm2 = p.sbuf("s5_mm2", [1, T], BF16)
    pr.op("v", lambda e: e.tensor_tensor(mm2[:], nm[:], nm[:], ALU.mult),
          reads=("s5nm",), writes=("s5mm2",), tag="m2")
    pr.op("v", lambda e: e.tensor_tensor(rs[:], rs[:], mm2[:], ALU.subtract),
          reads=("s5rs", "s5mm2"), writes=("s5rs",), tag="var")
    rstd_strip(p, pr, rs[:], "s5rs", 1.0, 1e-5, None, n=1)
    for ch in range(8):
        sl = slice(ch * 512, (ch + 1) * 512)
        pr.dma("y", xc[:].rearrange("p (f t) -> p f t", t=512),
               xb1A[:, :, sl], writes=("s5xc",), tag=f"s5 ldx2_{ch}")
        bcast_strip(p, pr, nm[:, sl], "s5nm", 512, pf0, "pf0", f32=False)
        bcast_strip(p, pr, rs[:, sl], "s5rs", 512, pf1, "pf1", f32=True)
        for f in range(NF):
            pr.op("v", lambda e, f=f: e.tensor_tensor(tmpf[:], xc[:, f * 512:(f + 1) * 512],
                                                      pf0[:, :512], ALU.add),
                  reads=("s5xc", "pf0"), writes=("s5tmpf",), tag="xmn")
            pr.op("v", lambda e, f=f, sl=sl: e.tensor_tensor(
                  xn1[:, f * T:][:, sl], tmpf[:], pf1[:, :512], ALU.mult),
                  reads=("s5tmpf", "pf1"), writes=("s5xn1",), tag="xn1")
        pr.dma("y", xn1A[:, :, sl],
               xn1[:].rearrange("p (f t) -> p f t", t=T)[:, :, sl],
               reads=("s5xn1",), tag=f"s5 stxn{ch}")

    def rhs(k, ch):
        return xn1[:, k * T + ch * 512: k * T + ch * 512 + 512], "s5xn1"

    ob = [p.sbuf(f"s5_o{i}", [128, T], BF16) for i in range(2)]

    def mk_evac(dst, act=None):
        def evac(m, ch, ba, bres):
            ms = m % 2
            O = f"s5o{ms}"
            if act is None:
                pr.op("v", lambda e, ms=ms, ch=ch, ba=ba: e.tensor_copy(
                      ob[ms][:, ch * 512:(ch + 1) * 512], ba[:, :512]),
                      reads=(bres,), writes=(O,), tag=f"s5 ev{m}_{ch}")
            else:
                pr.op("s", lambda e, ms=ms, ch=ch, ba=ba: e.activation(
                      ob[ms][:, ch * 512:(ch + 1) * 512], ba[:, :512], act),
                      reads=(bres,), writes=(O,), tag=f"s5 evg{m}_{ch}")
            if ch == 7:
                pr.dma("y", dst[:, m, :], ob[ms][:], reads=(O,), tag=f"s5 st{m}")
        return evac
    emit_gemm(p, pr, "s5q", win["l1_Wq"], C, C, rhs, mk_evac(q1A), banks, "s5wq")
    emit_gemm(p, pr, "s5k", win["l1_Wk"], C, C, rhs, mk_evac(k1A), banks, "s5wk")
    emit_gemm(p, pr, "s5v", win["l1_Wv"], C, C, rhs, mk_evac(v1A), banks, "s5wv")
    emit_gemm(p, pr, "s5g", win["l1_Wg"], C, C, rhs, mk_evac(gt1A, AF.Silu), banks, "s5wg")

    wal = p.sbuf("s5_wal", [128, NF * 16], BF16)
    wbe = p.sbuf("s5_wbe", [128, NF * 16], BF16)
    pr.dma("g", wal[:].rearrange("r (nk c) -> r nk c", c=16),
           win["l1_Wal"].rearrange("(nk r) c -> r nk c", r=128), writes=("s5wal",))
    pr.dma("g", wbe[:].rearrange("r (nk c) -> r nk c", c=16),
           win["l1_Wbe"].rearrange("(nk r) c -> r nk c", r=128), writes=("s5wbe",))
    dtb = p.sbuf("s5_dtb", [16, 1], F32)
    pr.dma("y", dtb[:], win["l1_dtb"], writes=("s5dtb",))
    alog = p.sbuf("s5_alog", [16, 1], F32)
    pr.dma("y", alog[:], win["l1_Alog"], writes=("s5alog",))
    negA = p.sbuf("s5_negA", [16, 1], F32)
    pr.op("s", lambda e: e.activation(negA[:], alog[:], AF.Exp),
          reads=("s5alog",), writes=("s5negA",))
    pr.op("v", lambda e: e.tensor_scalar_mul(negA[:], negA[:], -1.0),
          reads=("s5negA",), writes=("s5negA",))
    H1 = 16
    bss = [p.sbuf(f"s5_bs{i}", [H1, 512], F32) for i in range(2)]
    gss = [p.sbuf(f"s5_gs{i}", [H1, 512], F32) for i in range(2)]
    gcs = [p.sbuf(f"s5_gc{i}", [H1, 512], F32) for i in range(2)]
    tt1 = [p.sbuf(f"s5_t1{i}", [H1, 512], F32) for i in range(2)]
    tt2 = [p.sbuf(f"s5_t2{i}", [H1, 512], F32) for i in range(2)]
    egs = [p.sbuf(f"s5_eg{i}", [H1, 512], F32) for i in range(2)]
    ers = [p.sbuf(f"s5_er{i}", [H1, 512], F32) for i in range(2)]
    bes = [p.sbuf(f"s5_be{i}", [H1, 512], F32) for i in range(2)]
    zcol = p.sbuf("s5_zcol", [H1, 1], F32)
    pr.op("v", lambda e: e.memset(zcol[:], 0.0), writes=("s5zcol",))
    for ch in range(8):
        cs = ch % 2
        ba_b, bres_b = banks2[0]
        ba_a, bres_a = banks2[1]
        for k in range(NF):
            pr.op("t", lambda e, k=k, ch=ch, ba=ba_b: e.matmul(
                  ba[:H1, :512], wbe[:, k * 16:(k + 1) * 16], rhs(k, ch)[0],
                  start=(k == 0), stop=(k == NF - 1)),
                  reads=("s5wbe", "s5xn1"), writes=(bres_b,), tag=f"s5 bmm{ch}")
            pr.op("t", lambda e, k=k, ch=ch, ba=ba_a: e.matmul(
                  ba[:H1, :512], wal[:, k * 16:(k + 1) * 16], rhs(k, ch)[0],
                  start=(k == 0), stop=(k == NF - 1)),
                  reads=("s5wal", "s5xn1"), writes=(bres_a,), tag=f"s5 amm{ch}")
        B, G, GC = f"s5bs{cs}", f"s5gs{cs}", f"s5gc{cs}"
        pr.op("s", lambda e, cs=cs, ba=ba_b: e.activation(bss[cs][:], ba[:H1, :512], AF.Sigmoid),
              reads=(bres_b,), writes=(B,), tag="s5 beta")
        softplus_block(p, pr, ba_a[:H1, :512], bres_a, H1, 512,
                       tt1[cs][:], f"s5t1{cs}", tt2[cs][:], f"s5t2{cs}",
                       dtb[:], "s5dtb", negA[:], "s5negA",
                       gss[cs][:], G)
        for q in range(4):
            sl = slice(q * 128, (q + 1) * 128)
            pr.op("v", lambda e, cs=cs, sl=sl: e.tensor_tensor_scan(
                  gcs[cs][:, sl], gss[cs][:, sl], gss[cs][:, sl], zcol[:],
                  ALU.add, ALU.bypass),
                  reads=(G, "s5zcol"), writes=(GC,), tag="s5 cumsum")
        for q in range(4):
            sl = slice(q * 128, (q + 1) * 128)
            pr.op("v", lambda e, cs=cs, sl=sl, q=q: e.tensor_scalar(
                  gss[cs][:, sl], gcs[cs][:, sl], -1.0,
                  gcs[cs][:, q * 128 + 127: q * 128 + 128], ALU.mult, ALU.add),
                  reads=(GC,), writes=(G,), tag="s5 grev")
        EG, ER, BE = f"s5eg{cs}", f"s5er{cs}", f"s5be{cs}"
        pr.op("s", lambda e, cs=cs: e.activation(egs[cs][:], gcs[cs][:], AF.Exp),
              reads=(GC,), writes=(EG,), tag="s5 Eg")
        pr.op("s", lambda e, cs=cs: e.activation(ers[cs][:], gss[cs][:], AF.Exp),
              reads=(G,), writes=(ER,), tag="s5 Erev")
        pr.op("v", lambda e, cs=cs: e.tensor_tensor(bes[cs][:], bss[cs][:], egs[cs][:], ALU.mult),
              reads=(B, EG), writes=(BE,), tag="s5 bEg")
        sl5 = slice(ch * 512, (ch + 1) * 512)
        pr.dma("y", bg1[0, :, sl5], gcs[cs][:], reads=(GC,), tag=f"s5 stg{ch}")
        pr.dma("y", bg1[1, :, sl5], egs[cs][:], reads=(EG,), tag=f"s5 steg{ch}")
        pr.dma("y", bg1[2, :, sl5], ers[cs][:], reads=(ER,), tag=f"s5 ster{ch}")
        pr.dma("y", bg1[3, :, sl5], bss[cs][:], reads=(B,), tag=f"s5 stb{ch}")
        pr.dma("y", bg1[4, :, sl5], bes[cs][:], reads=(BE,), tag=f"s5 stbe{ch}")



def emit_s5b(p, pr, q1A, k1A, v1A, gt1A, bg1, gated1A, win):
    idf = p.sb["c_idf"]
    pf0 = p.psum_bank("pf0")
    pf1 = p.psum_bank("pf1")
    H1 = 16
    stage = p.sbuf("s5b_stage", [H1, T], F32)
    gT = p.sbuf("s5b_gT", [128, NCH * H1], F32)
    gTn = p.sbuf("s5b_gTn", [128, NCH * H1], F32)
    beT = p.sbuf("s5b_beT", [128, NCH * H1], F32)
    begT = p.sbuf("s5b_begT", [128, NCH * H1], F32)
    erT = p.sbuf("s5b_erT", [128, NCH * H1], F32)
    for row, dst, res in ((0, gT, "s5bgT"), (3, beT, "s5bbeT"),
                          (4, begT, "s5bbegT"), (2, erT, "s5berT")):
        pr.dma("y", stage[:], bg1[row], writes=("s5bstage",), tag=f"s5b ldbg{row}")
        for i in range(NCH):
            pr.op("t", lambda e, i=i: e.transpose(
                  pf0[:, (i % 4) * 128:(i % 4) * 128 + H1],
                  stage[:, i * CH:(i + 1) * CH], idf[:H1, :H1]),
                  reads=("s5bstage", "c_idf"), writes=("pf0",), tag=f"s5b tr{row}_{i}")
            pr.op("v", lambda e, i=i, dst=dst, res=res: e.tensor_copy(
                  dst[:, i * H1:(i + 1) * H1], pf0[:, (i % 4) * 128:(i % 4) * 128 + H1]),
                  reads=("pf0",), writes=(res,), tag=f"s5b ev{row}_{i}")
    pr.op("v", lambda e: e.tensor_scalar_mul(gTn[:], gT[:], -1.0),
          reads=("s5bgT",), writes=("s5bgTn",))
    rw = p.sbuf("s5b_rw", [128, 1], F32)
    pr.dma("y", rw[:], win["l1_rmsw"], writes=("s5brw",))

    qp = p.sbuf("s5b_qp", [128, T], BF16)
    kp = p.sbuf("s5b_kp", [128, T], BF16)
    vp = p.sbuf("s5b_vp", [128, T], BF16)
    gp = p.sbuf("s5b_gp", [128, T], BF16)
    sqb = p.sbuf("s5b_sq", [128, T], BF16)
    core = p.sbuf("s5b_core", [128, T], BF16)
    gcore = p.sbuf("s5b_gcore", [128, T], BF16)
    qn = p.sbuf("s5b_qn", [128, T], BF16)
    kn = p.sbuf("s5b_kn", [128, T], BF16)
    ssf = p.sbuf("s5b_ssf", [1, T], F32)
    g_st = p.sbuf("s5b_gst", [1, T], F32)
    eg_st = p.sbuf("s5b_egst", [1, T], F32)
    be_st = p.sbuf("s5b_best", [1, T], BF16)
    egl = p.sbuf("s5b_egl", [128, NCH], F32)
    Sst = p.sbuf("s5b_S", [64, 64], BF16)

    LN64 = float(np.log(64.0 ** -0.5))
    for hp in range(8):
        pr.dma("y", qp[:], q1A[:, hp, :], writes=("s5bqp",), tag=f"s5b ldq{hp}")
        pr.dma("y", kp[:], k1A[:, hp, :], writes=("s5bkp",), tag=f"s5b ldk{hp}")
        pr.dma("y", vp[:], v1A[:, hp, :], writes=("s5bvp",), tag=f"s5b ldv{hp}")
        pr.dma("y", gp[:], gt1A[:, hp, :], writes=("s5bgp",), tag=f"s5b ldg{hp}")
        for hh in range(2):
            h = hp * 2 + hh
            rws = slice(hh * 64, hh * 64 + 64)
            pr.dma("y", g_st[:], bg1[0, h:h + 1, :], writes=("s5bgst",), tag=f"ldgs{h}")
            pr.dma("y", eg_st[:], bg1[1, h:h + 1, :], writes=("s5begst",), tag=f"ldegs{h}")
            pr.dma("g", be_st[:], bg1[3, h:h + 1, :], writes=("s5bbest",), tag=f"ldbes{h}")
            bcast_strip(p, pr, eg_st[:, CH - 1::CH], "s5begst", NCH, pf0, "pf0", f32=True)
            pr.op("v", lambda e: e.tensor_copy(egl[:], pf0[:, :NCH]),
                  reads=("pf0",), writes=("s5begl",), tag="egl")
            for srcn, srcr, dstn, dstr, eb in ((qp, "s5bqp", qn, "s5bqn", LN64),
                                               (kp, "s5bkp", kn, "s5bkn", None)):
                pr.op("s", lambda e, srcn=srcn, rws=rws: e.activation(
                      sqb[rws, :], srcn[rws, :], AF.Square),
                      reads=(srcr,), writes=("s5bsq",), tag="sq")
                for ch in range(8):
                    pr.op("t", lambda e, ch=ch, rws=rws: e.matmul(
                          pf1[:1, :512], p.sb["c_ones_c1b"][rws, :],
                          sqb[rws, ch * 512:(ch + 1) * 512], start=True, stop=True),
                          reads=("s5bsq", "c_ones_c1b"), writes=("pf1",), tag=f"ssq{ch}")
                    pr.op("v", lambda e, ch=ch: e.tensor_copy(
                          ssf[:, ch * 512:(ch + 1) * 512], pf1[:1, :512]),
                          reads=("pf1",), writes=("s5bssf",), tag=f"ssqe{ch}")
                rstd_strip(p, pr, ssf[:], "s5bssf", 1.0, 1e-18, eb, n=1)
                for ch in range(8):
                    sl = slice(ch * 512, (ch + 1) * 512)
                    bcast_strip(p, pr, ssf[:, sl], "s5bssf", 512, pf0, "pf0", f32=True)
                    pr.op("v", lambda e, sl=sl, rws=rws, srcn=srcn, dstn=dstn:
                          e.tensor_tensor(dstn[rws, sl], srcn[rws, sl],
                                          pf0[rws, :512], ALU.mult),
                          reads=(srcr, "pf0"), writes=(dstr,), tag="l2mul")
            pr.op("v", lambda e: e.memset(Sst[:], 0.0), writes=("s5bS",), tag="S0")
            emit_delta_head(
                p, pr, "s5d", 64, 64, H1, h,
                kn[rws, :], "s5bkn", qn[rws, :], "s5bqn", vp[rws, :], "s5bvp",
                dict(g=(g_st[:], "s5bgst"), Eg=(eg_st[:], "s5begst"),
                     be=(be_st[:], "s5bbest")),
                dict(gTn=(gTn[:, :], "s5bgTn"), beT=(beT[:, :], "s5bbeT"),
                     erT=(erT[:, :], "s5berT"), begT=(begT[:, :], "s5bbegT")),
                (egl[:], "s5begl"), Sst[:], "s5bS", core[rws, :], "s5bcore",
                idsl=rws)
            pr.op("s", lambda e, rws=rws: e.activation(sqb[rws, :], core[rws, :], AF.Square),
                  reads=("s5bcore",), writes=("s5bsq",), tag="csq")
            for ch in range(8):
                pr.op("t", lambda e, ch=ch, rws=rws: e.matmul(
                      pf1[:1, :512], p.sb["c_ones_c1b"][rws, :],
                      sqb[rws, ch * 512:(ch + 1) * 512], start=True, stop=True),
                      reads=("s5bsq", "c_ones_c1b"), writes=("pf1",), tag=f"cs{ch}")
                pr.op("v", lambda e, ch=ch: e.tensor_copy(
                      ssf[:, ch * 512:(ch + 1) * 512], pf1[:1, :512]),
                      reads=("pf1",), writes=("s5bssf",), tag=f"cse{ch}")
            rstd_strip(p, pr, ssf[:], "s5bssf", 1.0 / 64.0, 1e-6, None, n=1)
            for ch in range(8):
                sl = slice(ch * 512, (ch + 1) * 512)
                bcast_strip(p, pr, ssf[:, sl], "s5bssf", 512, pf0, "pf0", f32=True)
                pr.op("v", lambda e, sl=sl, rws=rws: e.scalar_tensor_tensor(
                      gcore[rws, sl], core[rws, sl], rw[rws, :], pf0[rws, :512],
                      ALU.mult, ALU.mult),
                      reads=("s5bcore", "s5brw", "pf0"), writes=("s5bgcore",), tag="gc1")
                pr.op("v", lambda e, sl=sl, rws=rws: e.tensor_tensor(
                      gcore[rws, sl], gcore[rws, sl], gp[rws, sl], ALU.mult),
                      reads=("s5bgcore", "s5bgp"), writes=("s5bgcore",), tag="gc2")
        pr.dma("y", gated1A[:, hp, :], gcore[:], reads=("s5bgcore",), tag=f"s5b st{hp}")


def emit_s5c(p, pr, gated1A, Wo_hbm, s1A):
    gt = p.sbuf("s5c_g", [128, NF * T], BF16)
    pr.dma("y", gt[:].rearrange("p (f t) -> p f t", t=T), gated1A,
           writes=("s5cg",), tag="s5c ldg")

    def rhs(k, ch):
        return gt[:, k * T + ch * 512: k * T + ch * 512 + 512], "s5cg"

    banks = [(p.psum_bank("pf0"), "pf0"), (p.psum_bank("pf1"), "pf1")]
    ob = [p.sbuf(f"s5c_o{i}", [128, T], BF16) for i in range(2)]

    def evac(m, ch, ba, bres):
        ms = m % 2
        O = f"s5co{ms}"
        pr.op("v", lambda e, ms=ms, ch=ch, ba=ba: e.tensor_copy(
              ob[ms][:, ch * 512:(ch + 1) * 512], ba[:, :512]),
              reads=(bres,), writes=(O,), tag=f"s5c ev{m}_{ch}")
        if ch == 7:
            pr.dma("y", s1A[:, m, :], ob[ms][:], reads=(O,), tag=f"s5c st{m}")
    emit_gemm(p, pr, "s5c", Wo_hbm, C, C, rhs, evac, banks, "s5cw")



def emit_s6(p, pr, xn1A, xb1A, s1A, y_hbm, comb1_hbm, win):
    idf = p.sb["c_idf"]
    pf0 = p.psum_bank("pf0")
    pf1 = p.psum_bank("pf1")
    pfA = p.psum_bank("pf2")
    pfB = p.psum_bank("pf3")
    pfC = p.psum_bank("pf4")
    pfD = p.psum_bank("pf5")
    HID = 2816
    NH2 = HID // 128
    xn1 = p.sbuf("s6_xn1", [128, NF * T], BF16)
    pr.dma("y", xn1[:].rearrange("p (f t) -> p f t", t=T), xn1A,
           writes=("s6xn",), tag="s6 ldxn")

    wr = p.sbuf("s6_wr", [128, NF * 7], BF16)
    pr.dma("g", wr[:].rearrange("r (nk c) -> r nk c", c=7),
           win["f1_Wr"].rearrange("(nk r) c -> r nk c", r=128), writes=("s6wr",))
    rb = p.sbuf("s6_rb", [128, 7], F32)
    pr.dma("y", rb[:], win["f1_rb"], writes=("s6rb",))
    probs = p.sbuf("s6_probs", [128, 7], F32)
    wts = p.sbuf("s6_wts", [128, 7], F32)
    comb7 = p.sbuf("s6_comb7", [7, 128], F32)
    for i in range(NT):
        for k in range(NF):
            pr.op("t", lambda e, i=i, k=k: e.matmul(
                  pf0[:, :7], xn1[:, k * T + i * 128: k * T + (i + 1) * 128],
                  wr[:, k * 7:(k + 1) * 7], start=(k == 0), stop=(k == NF - 1)),
                  reads=("s6xn", "s6wr"), writes=("pf0",), tag=f"s6 rt{i}_{k}")
        pr.op("v", lambda e: e.tensor_tensor(probs[:], pf0[:, :7], rb[:], ALU.mult),
              reads=("pf0", "s6rb"), writes=("s6probs",), tag="lg")
        pr.op("s", lambda e: e.activation(probs[:], probs[:], AF.Sigmoid),
              reads=("s6probs",), writes=("s6probs",), tag="sig")
        emit_top2(p, pr, "s6t", probs[:], "s6probs", 7, wts[:], "s6wts", "sigmoid")
        pr.op("t", lambda e, i=i: e.transpose(pf1[:7, :128], wts[:], idf[:]),
              reads=("s6wts", "c_idf"), writes=("pf1",), tag=f"s6 rtt{i}")
        pr.op("v", lambda e, i=i: e.tensor_copy(comb7[:], pf1[:7, :128]),
              reads=("pf1",), writes=("s6comb7",), tag=f"s6 rte{i}")
        pr.dma("y", comb1_hbm[:, i * 128:(i + 1) * 128], comb7[:],
               reads=("s6comb7",), writes=("d_comb1",), tag=f"s6 stc{i}")

    TC = 1024
    hsb = p.sbuf("s6_h", [128, NH2 * TC], BF16)
    acc = p.sbuf("s6_acc", [128, NF * TC], BF16)
    w12 = [p.sbuf(f"s6_w12{i}", [128, NF * 128], BF16) for i in range(4)]
    w3b = [p.sbuf(f"s6_w3{i}", [128, NH2 * 128], BF16) for i in range(2)]
    cstr = p.sbuf("s6_cstr", [1, TC], BF16)
    s1t = p.sbuf("s6_s1t", [128, 512], BF16)
    yt = p.sbuf("s6_yt", [128, 512], F32)
    gbuf = p.sbuf("s6_gb", [128, 512], BF16)
    sbuf0 = p.sbuf("s6_sb", [128, 512], BF16)
    xab = p.sbuf("s6_xa", [128, 512], F32)
    ot = p.sbuf("s6_ot", [128, 512], F32)
    asm = p.sbuf("s6_asm", [128, 4 * 1024], F32)

    for tc in range(4):
        tsl = slice(tc * TC, (tc + 1) * TC)
        pr.op("v", lambda e: e.memset(acc[:], 0.0), writes=("s6acc",), tag=f"s6 z{tc}")
        for ex in range(8):
            routed = ex > 0
            if routed:
                wg_h = win["f1_rg"][ex - 1]
                wu_h = win["f1_ru"][ex - 1]
                wd_h = win["f1_rd"][ex - 1]
                pr.dma("g", cstr[:], comb1_hbm[ex - 1:ex, tsl],
                       reads=("d_comb1",), writes=("s6cstr",), tag=f"s6 ldc{tc}_{ex}")
            else:
                wg_h, wu_h, wd_h = win["f1_sg"], win["f1_su"], win["f1_sd"]
            for m in range(NH2):
                wp = m % 2
                W1R, W2R = f"s6w12{2 * wp}", f"s6w12{2 * wp + 1}"
                pr.dma("g", w12[2 * wp][:].rearrange("r (nk c) -> r nk c", c=128),
                       wg_h[:, m * 128:(m + 1) * 128].rearrange("(nk r) c -> r nk c", r=128),
                       writes=(W1R,), tag=f"s6 wg{ex}_{m}")
                pr.dma("g", w12[2 * wp + 1][:].rearrange("r (nk c) -> r nk c", c=128),
                       wu_h[:, m * 128:(m + 1) * 128].rearrange("(nk r) c -> r nk c", r=128),
                       writes=(W2R,), tag=f"s6 wu{ex}_{m}")
                for sub in range(2):
                    ssl = slice(tc * TC + sub * 512, tc * TC + sub * 512 + 512)
                    for k in range(NF):
                        pr.op("t", lambda e, k=k, wp=wp, ssl=ssl: e.matmul(
                              pfA[:, :512], w12[2 * wp][:, k * 128:(k + 1) * 128],
                              xn1[:, k * T:][:, ssl], start=(k == 0), stop=(k == NF - 1)),
                              reads=(W1R, "s6xn"), writes=("pf2",), tag="h1mm")
                        pr.op("t", lambda e, k=k, wp=wp, ssl=ssl: e.matmul(
                              pfB[:, :512], w12[2 * wp + 1][:, k * 128:(k + 1) * 128],
                              xn1[:, k * T:][:, ssl], start=(k == 0), stop=(k == NF - 1)),
                              reads=(W2R, "s6xn"), writes=("pf3",), tag="h2mm")
                    pr.op("s", lambda e: e.activation(s1t[:], pfA[:, :512], AF.Silu),
                          reads=("pf2",), writes=("s6s1t",), tag="silu1")
                    pr.op("v", lambda e, m=m, sub=sub: e.tensor_tensor(
                          hsb[:, m * TC + sub * 512: m * TC + sub * 512 + 512],
                          s1t[:], pfB[:, :512], ALU.mult),
                          reads=("s6s1t", "pf3"), writes=("s6h",), tag="hmul")
            for mo in range(NF):
                wp = mo % 2
                W3R = f"s6w3{wp}"
                pr.dma("g", w3b[wp][:].rearrange("r (nk c) -> r nk c", c=128),
                       wd_h[:, mo * 128:(mo + 1) * 128].rearrange("(nk r) c -> r nk c", r=128),
                       writes=(W3R,), tag=f"s6 wd{ex}_{mo}")
                for sub in range(2):
                    for k in range(NH2):
                        pr.op("t", lambda e, k=k, wp=wp, sub=sub: e.matmul(
                              pfC[:, :512], w3b[wp][:, k * 128:(k + 1) * 128],
                              hsb[:, k * TC + sub * 512: k * TC + sub * 512 + 512],
                              start=(k == 0), stop=(k == NH2 - 1)),
                              reads=(W3R, "s6h"), writes=("pf4",), tag="ymm")
                    if routed:
                        bcast_strip(p, pr, cstr[:, sub * 512:(sub + 1) * 512], "s6cstr",
                                    512, pfD, "pf5", f32=False)
                        pr.op("v", lambda e: e.tensor_tensor(yt[:], pfC[:, :512],
                                                             pfD[:, :512], ALU.mult),
                              reads=("pf4", "pf5"), writes=("s6yt",), tag="cmul") \
                            if False else None
                        pr.op("v", lambda e: e.tensor_copy(yt[:], pfC[:, :512]),
                              reads=("pf4",), writes=("s6yt",), tag="ycp")
                        pr.op("v", lambda e: e.tensor_tensor(yt[:], yt[:], pfD[:, :512],
                                                             ALU.mult),
                              reads=("s6yt", "pf5"), writes=("s6yt",), tag="cmul")
                    else:
                        pr.op("v", lambda e: e.tensor_copy(yt[:], pfC[:, :512]),
                              reads=("pf4",), writes=("s6yt",), tag="ycp")
                    pr.op("v", lambda e, mo=mo, sub=sub: e.tensor_tensor(
                          acc[:, mo * TC + sub * 512: mo * TC + sub * 512 + 512],
                          acc[:, mo * TC + sub * 512: mo * TC + sub * 512 + 512],
                          yt[:], ALU.add),
                          reads=("s6acc", "s6yt"), writes=("s6acc",), tag="yacc")
        for sub in range(2):
            for f in range(NF):
                ssl = slice(tc * TC + sub * 512, tc * TC + sub * 512 + 512)
                pr.op("s", lambda e, f=f, sub=sub: e.activation(
                      gbuf[:], acc[:, f * TC + sub * 512: f * TC + sub * 512 + 512],
                      AF.Silu),
                      reads=("s6acc",), writes=("s6gb",), tag="g1")
                pr.dma("y", sbuf0[:], s1A[:, f, ssl], writes=("s6sb",), tag="ld s1")
                pr.dma("y", xab[:], xb1A[:, f, ssl], writes=("s6xa",), tag="ld xb1")
                pr.op("v", lambda e: e.tensor_tensor(ot[:], gbuf[:], sbuf0[:], ALU.mult),
                      reads=("s6gb", "s6sb"), writes=("s6ot",), tag="sg")
                pr.op("v", lambda e: e.tensor_tensor(ot[:], ot[:], xab[:], ALU.add),
                      reads=("s6ot", "s6xa"), writes=("s6ot",), tag="out")
                for q in range(4):
                    pr.op("t", lambda e, q=q: e.transpose(
                          pf1[:, (q % 1) * 128:128] if False else pf1[:, :128],
                          ot[:, q * 128:(q + 1) * 128], idf[:]),
                          reads=("s6ot", "c_idf"), writes=("pf1",), tag=f"ftr{f}_{q}")
                    pr.op("v", lambda e, q=q, f=f: e.tensor_copy(
                          asm[:, q * 1024 + f * 128: q * 1024 + (f + 1) * 128],
                          pf1[:, :128]),
                          reads=("pf1",), writes=("s6asm",), tag=f"fev{f}_{q}")
            for q in range(4):
                t0 = tc * TC + sub * 512 + q * 128
                pr.dma("y", y_hbm[t0:t0 + 128, :], asm[:, q * 1024:(q + 1) * 1024],
                       reads=("s6asm",), tag=f"sty{t0}")



def _mk_inmaps(p, host, per_core):
    consts = {k: host[k] for k in CONST_DEFS if k in p.dram}
    weights = {k: host[k] for k in host if not k.startswith("c_") and k in p.dram}
    return [dict(consts, **weights, **pc) for pc in per_core]


def run_s0_only(inputs, nt=NT):
    host = host_layout(inputs)
    x = inputs["L0_attn_x"]
    p = P()
    x_hbm = p.dram_t("x", [T, C], F32, kind="ExternalInput")
    xA = p.dram_t("xA", [128, NF, T], F32, kind="ExternalOutput")
    xnA = p.dram_t("xnA", [128, NF, T], BF16, kind="ExternalOutput")
    pr = p.stage("s0")
    load_consts(p, pr)
    emit_s0(p, pr, x_hbm, xA, xnA, nt=nt)
    pr.build()
    in_maps = _mk_inmaps(p, host, [dict(x=np.ascontiguousarray(x[b])) for b in range(8)])
    return run_bass_kernel_spmd(p.nc, in_maps, core_ids=list(range(8)))


def run_s1_only(xnA_np, host):
    p = P()
    xnA = p.dram_t("xnA", [128, NF, T], BF16, kind="ExternalInput")
    qcA = p.dram_t("qcA", [128, 8, T], BF16, kind="ExternalOutput")
    kcA = p.dram_t("kcA", [128, 8, T], BF16, kind="ExternalOutput")
    vcA = p.dram_t("vcA", [128, 8, T], BF16, kind="ExternalOutput")
    zsA = p.dram_t("zsA", [128, 8, T], BF16, kind="ExternalOutput")
    bg = p.dram_t("bg", [5, 8, T], F32, kind="ExternalOutput")
    win = {k: p.dram_t(k, list(host[k].shape), F32, kind="ExternalInput")
           for k in ("l0_Wq", "l0_Wk", "l0_Wv", "l0_Wz", "l0_Wb", "l0_Wa",
                     "l0_cwq", "l0_cwk", "l0_cwv", "l0_dtb", "l0_Alog")}
    pr = p.stage("s1")
    load_consts(p, pr)
    emit_s1(p, pr, xnA, qcA, kcA, vcA, zsA, bg, win)
    pr.build()
    in_maps = _mk_inmaps(p, host, [dict(xnA=xnA_np[b]) for b in range(8)])
    return run_bass_kernel_spmd(p.nc, in_maps, core_ids=list(range(8)))


def run_s2_only(s1res, host, dbg=False):
    p = P()
    p.dbg = dbg
    p.dbg_chunk = (0, 0)
    qcA = p.dram_t("qcA", [128, 8, T], BF16, kind="ExternalInput")
    kcA = p.dram_t("kcA", [128, 8, T], BF16, kind="ExternalInput")
    vcA = p.dram_t("vcA", [128, 8, T], BF16, kind="ExternalInput")
    zsA = p.dram_t("zsA", [128, 8, T], BF16, kind="ExternalInput")
    bg = p.dram_t("bg", [5, 8, T], F32, kind="ExternalInput")
    nw = p.dram_t("l0_nw", [128, 1], F32, kind="ExternalInput")
    WoT = p.dram_t("l0_WoT", [C, C], F32, kind="ExternalInput")
    gatedA = p.dram_t("gatedA", [128, 8, T], BF16, kind="ExternalOutput")
    s0A = p.dram_t("s0A", [128, 8, T], BF16, kind="ExternalOutput")
    with p.scope():
        pr = p.stage("s2a")
        load_consts(p, pr)
        emit_s2a(p, pr, qcA, kcA, vcA, zsA, bg, gatedA, nw)
        pr.build()
    with p.scope():
        pr2 = p.stage("s2b")
        emit_s2b(p, pr2, gatedA, WoT, s0A)
        pr2.build()
    per_core = [dict(qcA=np.asarray(r["qcA"]), kcA=np.asarray(r["kcA"]),
                     vcA=np.asarray(r["vcA"]), zsA=np.asarray(r["zsA"]),
                     bg=np.asarray(r["bg"])) for r in s1res]
    in_maps = _mk_inmaps(p, host, per_core)
    return run_bass_kernel_spmd(p.nc, in_maps, core_ids=list(range(8)))


def run_s3_only(xnA_np, host, stages="abc"):
    p = P()
    xnA = p.dram_t("xnA", [128, NF, T], BF16, kind="ExternalInput")
    qkvA = p.dram_t("qkvA", [128, 24, T], BF16,
                    kind="Internal" if "b" in stages else "ExternalOutput")
    if "b" in stages or "c" in stages:
        oAh = p.dram_t("oAh", [128, 8, T], BF16,
                       kind="ExternalOutput" if "b" in stages else "ExternalInput")
    else:
        oAh = None
    x1fA = (p.dram_t("x1fA", [128, 8, T], F32, kind="ExternalOutput")
            if "c" in stages else None)
    win = {k: p.dram_t(k, list(host[k].shape), F32, kind="ExternalInput")
           for k in ("f0_n1", "f0_qkvT", "f0_qkvb", "f0_oT", "f0_ob")}
    first = [True]

    def st(name, fn):
        with p.scope():
            pr = p.stage(name)
            if first[0]:
                load_consts(p, pr)
                first[0] = False
            fn(pr)
            pr.build()
    if "a" in stages:
        st("s3a", lambda pr: emit_s3a(p, pr, xnA, qkvA, win))
    if "b" in stages:
        st("s3b", lambda pr: emit_s3b(p, pr, qkvA, oAh))
    if "c" in stages:
        st("s3c", lambda pr: emit_s3c(p, pr, oAh, xnA, x1fA, win))
    per = [dict(xnA=xnA_np[b]) for b in range(8)]
    in_maps = _mk_inmaps(p, host, per)
    return run_bass_kernel_spmd(p.nc, in_maps, core_ids=list(range(8)))


def run_s4_only(per_core, host, dbg=False):
    p = P()
    p.dbg = dbg
    x1fA = p.dram_t("x1fA", [128, 8, T], F32, kind="ExternalInput")
    s0A = p.dram_t("s0A", [128, 8, T], BF16, kind="ExternalInput")
    xA = p.dram_t("xA", [128, 8, T], F32, kind="ExternalInput")
    xb1A = p.dram_t("xb1A", [128, 8, T], F32, kind="ExternalOutput")
    comb = p.dram_t("comb8", [8, T], F32, kind="ExternalOutput")
    win = {k: p.dram_t(k, list(host[k].shape), F32, kind="ExternalInput")
           for k in ("f0_n2", "f0_gT", "f0_w1T", "f0_w2T", "f0_w3T",
                     "f0_b1", "f0_b2", "f0_b3")}
    with p.scope():
        pr = p.stage("s4")
        load_consts(p, pr)
        emit_s4(p, pr, x1fA, s0A, xA, xb1A, comb, win)
        pr.build()
    in_maps = _mk_inmaps(p, host, per_core)
    return run_bass_kernel_spmd(p.nc, in_maps, core_ids=list(range(8)))


BOUND = {
    "xA": ([128, NF, T], F32), "xnA": ([128, NF, T], BF16),
    "qcA": ([128, 8, T], BF16), "kcA": ([128, 8, T], BF16),
    "vcA": ([128, 8, T], BF16), "zsA": ([128, 8, T], BF16),
    "bg": ([5, 8, T], F32), "gatedA": ([128, 8, T], BF16),
    "s0A": ([128, 8, T], BF16), "qkvA": ([128, 24, T], BF16),
    "oAh": ([128, 8, T], BF16), "x1fA": ([128, 8, T], F32),
    "comb8": ([8, T], F32), "xb1A": ([128, 8, T], F32),
    "xn1A": ([128, 8, T], BF16), "q1A": ([128, 8, T], BF16),
    "k1A": ([128, 8, T], BF16), "v1A": ([128, 8, T], BF16),
    "gt1A": ([128, 8, T], BF16), "bg1": ([5, 16, T], F32),
    "gated1A": ([128, 8, T], BF16), "s1A": ([128, 8, T], BF16),
    "comb1": ([7, T], F32),
    "x": ([T, C], F32), "y": ([T, C], F32),
}

STAGE_DEFS = [
    ("s0", ["x"], ["xA", "xnA"],
     lambda p, pr, t, w: emit_s0(p, pr, t["x"], t["xA"], t["xnA"])),
    ("s1", ["xnA"], ["qcA", "kcA", "vcA", "zsA", "bg"],
     lambda p, pr, t, w: emit_s1(p, pr, t["xnA"], t["qcA"], t["kcA"], t["vcA"],
                                 t["zsA"], t["bg"], w)),
    ("s2a", ["qcA", "kcA", "vcA", "zsA", "bg"], ["gatedA"],
     lambda p, pr, t, w: emit_s2a(p, pr, t["qcA"], t["kcA"], t["vcA"], t["zsA"],
                                  t["bg"], t["gatedA"], w["l0_nw"])),
    ("s2b", ["gatedA"], ["s0A"],
     lambda p, pr, t, w: emit_s2b(p, pr, t["gatedA"], w["l0_WoT"], t["s0A"])),
    ("s3a", ["xnA"], ["qkvA"],
     lambda p, pr, t, w: emit_s3a(p, pr, t["xnA"], t["qkvA"], w)),
    ("s3b", ["qkvA"], ["oAh"],
     lambda p, pr, t, w: emit_s3b(p, pr, t["qkvA"], t["oAh"])),
    ("s3c", ["oAh", "xnA"], ["x1fA"],
     lambda p, pr, t, w: emit_s3c(p, pr, t["oAh"], t["xnA"], t["x1fA"], w)),
    ("s4", ["x1fA", "s0A", "xA"], ["xb1A", "comb8"],
     lambda p, pr, t, w: emit_s4(p, pr, t["x1fA"], t["s0A"], t["xA"],
                                 t["xb1A"], t["comb8"], w)),
    ("s5a", ["xb1A"], ["xn1A", "q1A", "k1A", "v1A", "gt1A", "bg1"],
     lambda p, pr, t, w: emit_s5a(p, pr, t["xb1A"], t["xn1A"], t["q1A"],
                                  t["k1A"], t["v1A"], t["gt1A"], t["bg1"], w)),
    ("s5b", ["q1A", "k1A", "v1A", "gt1A", "bg1"], ["gated1A"],
     lambda p, pr, t, w: emit_s5b(p, pr, t["q1A"], t["k1A"], t["v1A"],
                                  t["gt1A"], t["bg1"], t["gated1A"], w)),
    ("s5c", ["gated1A"], ["s1A"],
     lambda p, pr, t, w: emit_s5c(p, pr, t["gated1A"], w["l1_Wo"], t["s1A"])),
    ("s6", ["xn1A", "xb1A", "s1A"], ["y", "comb1"],
     lambda p, pr, t, w: emit_s6(p, pr, t["xn1A"], t["xb1A"], t["s1A"],
                                 t["y"], t["comb1"], w)),
]


class Win:
    def __init__(self, p, host):
        self.p = p
        self.host = host

    def __getitem__(self, k):
        if k not in self.p.dram:
            self.p.dram_t(k, list(self.host[k].shape), F32, kind="ExternalInput")
        return self.p.dram[k]


def _np16(a):
    return np.ascontiguousarray(a)


def kernel(**inputs):
    host = host_layout(inputs)
    x = inputs["L0_attn_x"]
    if FUSED:
        p = P()
        win = Win(p, host)
        tens = {}
        for n, (sh, dt) in BOUND.items():
            kind = ("ExternalInput" if n == "x" else
                    "ExternalOutput" if n == "y" else "Internal")
            tens[n] = p.dram_t(n, list(sh), dt, kind=kind)
        first = True
        for name, _ins, _outs, fn in STAGE_DEFS:
            with p.scope(), p.nc.named_scope(name):
                pr = p.stage(name)
                if first:
                    load_consts(p, pr)
                    first = False
                fn(p, pr, tens, win)
                pr.build()
        wnames = [k for k in p.dram if k in host and not k.startswith("c_")]
        consts = {k: host[k] for k in CONST_DEFS}
        in_maps = [dict(consts, **{k: host[k] for k in wnames},
                        x=np.ascontiguousarray(x[b])) for b in range(8)]
        res = run_bass_kernel_spmd(p.nc, in_maps, core_ids=list(range(8)))
        return np.stack([np.asarray(r["y"], dtype=np.float32)
                         for r in res.results])
    vals = [dict(x=np.ascontiguousarray(x[b])) for b in range(8)]
    for name, ins, outs, fn in STAGE_DEFS:
        p = P()
        win = Win(p, host)
        tens = {}
        for n in ins:
            tens[n] = p.dram_t(n, list(BOUND[n][0]), BOUND[n][1],
                               kind="ExternalInput")
        for n in outs:
            tens[n] = p.dram_t(n, list(BOUND[n][0]), BOUND[n][1],
                               kind="ExternalOutput")
        with p.scope():
            pr = p.stage(name)
            load_consts(p, pr)
            fn(p, pr, tens, win)
            pr.build()
        wnames = [k for k in p.dram if k in host and not k.startswith("c_")]
        consts = {k: host[k] for k in CONST_DEFS}
        in_maps = [dict(consts, **{k: host[k] for k in wnames},
                        **{n: vals[b][n] for n in ins}) for b in range(8)]
        res = run_bass_kernel_spmd(p.nc, in_maps, core_ids=list(range(8)))
        for b in range(8):
            for n in outs:
                vals[b][n] = np.asarray(res.results[b][n])
    return np.stack([vals[b]["y"].astype(np.float32) for b in range(8)])


# version 31 (final)
# speedup vs baseline: 1.2204x; 1.0676x over previous
import numpy as np
import ml_dtypes
import concourse.bass as bass
import concourse.mybir as mybir
from concourse.bass_utils import run_bass_kernel_spmd

F32 = mybir.dt.float32
BF16 = mybir.dt.bfloat16
AF = mybir.ActivationFunctionType
ALU = mybir.AluOpType
AX = mybir.AxisListType

T = 4096
C = 1024
NT = T // 128
NF = C // 128
CH = 128
NCH = T // CH
NEG = np.float32(-1e30)
FUSED = True
DMA_CHAIN = True
SCHED_W = 0
SCHED_ON = True
DMA_STRIPES = 12


def cdiv(a, b):
    return (a + b - 1) // b

ENGINES = ("t", "v", "s", "g", "y")
DMA_QUEUES = ("y", "g", "s")


class Op:
    __slots__ = ("eng", "fn", "reads", "writes", "dma", "idx",
                 "eseq", "milestone", "mcount", "tag")

    def __init__(self, eng, fn, reads, writes, dma, tag=""):
        self.eng = eng
        self.fn = fn
        self.reads = reads
        self.writes = writes
        self.dma = dma
        self.tag = tag
        self.milestone = False
        self.mcount = 0


class Prog:

    def __init__(self, p, name):
        self.p = p
        self.name = name
        self.ops = []

    def op(self, eng, fn, reads=(), writes=(), tag=""):
        o = Op(eng, fn, tuple(reads), tuple(writes), False, tag)
        o.idx = len(self.ops)
        self.ops.append(o)
        return o

    def dma(self, eng, out_ap, in_ap, reads=(), writes=(), tag=""):
        assert eng in DMA_QUEUES
        def fn(e, out_ap=out_ap, in_ap=in_ap):
            return e.dma_start(out=out_ap, in_=in_ap)
        o = Op(eng, fn, tuple(reads), tuple(writes), True, tag)
        o.idx = len(self.ops)
        self.ops.append(o)
        return o

    def _schedule(self, deps):
        import heapq
        N = len(self.ops)
        DUR = {"t": 250, "v": 450, "s": 420, "g": 450, "y": 100}
        succ = [[] for _ in range(N)]
        indeg = [0] * N
        for i, ds in enumerate(deps):
            for d in ds:
                succ[d].append(i)
                indeg[i] += 1
        finish = [0] * N
        engine_free = {}
        order = []
        heap = []

        def est(i):
            o = self.ops[i]
            key = ("q" + o.eng) if o.dma else o.eng
            s = engine_free.get(key, 0)
            for d in deps[i]:
                do = self.ops[d]
                lag = 150 if (do.dma or do.eng != o.eng) else 0
                s = max(s, finish[d] + lag)
            return s

        for i in range(N):
            if indeg[i] == 0:
                heapq.heappush(heap, (est(i), i))
        while heap:
            s0, i = heapq.heappop(heap)
            s = est(i)
            if s > s0 and heap and heap[0][0] < s:
                heapq.heappush(heap, (s, i))
                continue
            o = self.ops[i]
            key = ("q" + o.eng) if o.dma else o.eng
            if o.dma:
                engine_free[key] = s + 300
                finish[i] = s + 2500
            else:
                d = DUR[o.eng]
                engine_free[key] = s + d
                finish[i] = s + d
            order.append(i)
            for j in succ[i]:
                indeg[j] -= 1
                if indeg[j] == 0:
                    heapq.heappush(heap, (est(j), j))
        assert len(order) == N
        return order

    def build(self, schedule=True):
        nc = self.p.nc
        last_writer = {}
        readers = {}
        deps = [set() for _ in self.ops]
        for o in self.ops:
            for r in o.reads:
                w = last_writer.get(r)
                if w is not None:
                    deps[o.idx].add(w)
                rl = readers.setdefault(r, [])
                if r.startswith(("pf", "pbb")) and o.eng in ("v", "s"):
                    for rd in rl:
                        if self.ops[rd].eng in ("v", "s") and self.ops[rd].eng != o.eng:
                            deps[o.idx].add(rd)
                rl.append(o.idx)
            for w in o.writes:
                pw = last_writer.get(w)
                if pw is not None:
                    deps[o.idx].add(pw)
                for rd in readers.get(w, ()):
                    deps[o.idx].add(rd)
                readers[w] = []
                last_writer[w] = o.idx
            deps[o.idx].discard(o.idx)
        need = [set() for _ in self.ops]
        need_drain_after = [set() for _ in self.ops]
        for o in self.ops:
            for d in deps[o.idx]:
                do = self.ops[d]
                if do.dma or do.eng != o.eng:
                    need[o.idx].add(d)
                    do.milestone = True
                elif o.eng in ("v", "s", "g"):
                    need_drain_after[o.idx].add(d)
        prev_dma = {}
        dma_stripe = {}
        nq = {}
        for o in self.ops:
            if o.dma:
                o.milestone = True
                st_ = nq.get(o.eng, 0) % DMA_STRIPES
                nq[o.eng] = nq.get(o.eng, 0) + 1
                dma_stripe[o.idx] = st_
                key = (o.eng, st_)
                if DMA_CHAIN and key in prev_dma:
                    deps[o.idx].add(prev_dma[key])
                prev_dma[key] = o.idx
        self.dma_stripe = dma_stripe
        W = SCHED_W
        if W:
            for i in range(W, len(self.ops)):
                deps[i].add(i - W)
        order = self._schedule(deps) if (schedule and SCHED_ON) else list(range(len(self.ops)))
        eng_ops = {e: [] for e in ENGINES}
        counts = self.p.semcnt
        used_keys = set()

        def keyof(o):
            return (("dma", o.eng, dma_stripe[o.idx]) if o.dma
                    else ("c", o.eng))
        for idx in order:
            o = self.ops[idx]
            key = keyof(o)
            if o.milestone:
                self.p.get_sem(key)
                counts[key] = counts.get(key, 0) + (16 if o.dma else 1)
                o.mcount = counts[key]
                used_keys.add(key)
            eng_ops[o.eng].append(o)
        sems = self.p.semreg
        eng_seq = {}
        for e in ENGINES:
            for si, o in enumerate(eng_ops[e]):
                eng_seq[o.idx] = si

        def emit_for(e, eng):
            watermark = {}
            last_drain = -1
            for o in eng_ops[e]:
                if need_drain_after[o.idx]:
                    dep_seq = max(eng_seq[d] for d in need_drain_after[o.idx])
                    if dep_seq > last_drain:
                        eng.drain()
                        last_drain = eng_seq[o.idx] - 1
                for d in sorted(need[o.idx]):
                    do = self.ops[d]
                    key = keyof(do)
                    v = do.mcount
                    if watermark.get(key, 0) < v:
                        eng.wait_ge(sems[key], v)
                        watermark[key] = v
                inst = o.fn(eng)
                if o.milestone:
                    key = keyof(o)
                    assert inst is not None, f"milestone op {o.tag} returned no inst"
                    if o.dma or o.eng == "t":
                        inst.then_inc(sems[key], 16 if o.dma else 1)
                    else:
                        eng.drain().then_inc(sems[key], 1)
                        last_drain = eng_seq[o.idx]

        def emit_end_drain(e, eng):
            emit_for(e, eng)
            eng.drain()

        with nc.Block() as blk:
            if eng_ops["y"]:
                blk.sync(lambda e: emit_for("y", e))
            if eng_ops["g"]:
                blk.gpsimd(lambda e: emit_end_drain("g", e))
            if eng_ops["s"]:
                blk.scalar(lambda e: emit_end_drain("s", e))
            if eng_ops["v"]:
                blk.vector(lambda e: emit_end_drain("v", e))
            if eng_ops["t"]:
                blk.tensor(lambda e: emit_for("t", e))
            lasts = [(k, counts[k]) for k in sorted(used_keys) if k[0] == "dma"]
            if lasts:
                def fin(eng):
                    for key, v in lasts:
                        eng.wait_ge(sems[key], v)
                blk.gpsimd(fin)


class P:

    def __init__(self):
        import contextlib
        self.nc = bass.Bass("TRN2", target_bir_lowering=False)
        self.es = contextlib.ExitStack()
        self.dram = {}
        self.sb = {}
        self._psum = {}
        self._scopes = [self.es]
        self._contextlib = contextlib
        self.semreg = {}
        self.semcnt = {}

    def get_sem(self, key):
        if key not in self.semreg:
            nm = "sem_" + "_".join(str(k) for k in key)
            self.semreg[key] = self.es.enter_context(self.nc.semaphore(nm))
            self.semcnt[key] = 0
        return self.semreg[key]

    def scope(self):
        import contextlib as _cl

        @_cl.contextmanager
        def _s():
            es = _cl.ExitStack()
            self._scopes.append(es)
            es._names = []
            try:
                yield
            finally:
                self._scopes.pop()
                es.close()
                for n in es._names:
                    self.sb.pop(n, None)
        return _s()

    def dram_t(self, name, shape, dt=F32, kind="Internal"):
        ap = self.nc.dram_tensor(name, shape, dt, kind=kind).ap()
        self.dram[name] = ap
        return ap

    def sbuf(self, name, shape, dt=F32):
        sc = self._scopes[-1]
        t = sc.enter_context(self.nc.sbuf_tensor(name, shape, dt))
        self.sb[name] = t
        if hasattr(sc, "_names"):
            sc._names.append(name)
        return t

    def const_col(self, value):
        j = CONST_SCALARS.index(value)
        return self.sb["c_scal"][:, j:j + 1]

    def psum_bank(self, name, dt=F32):
        if name not in self._psum:
            shape = [128, 512] if dt == F32 else [128, 1024]
            self._psum[name] = self.es.enter_context(
                self.nc.psum_tensor(name, shape, dt))
        return self._psum[name]

    def stage(self, name):
        return Prog(self, name)

    def dbg_store(self, pr, name, ap, res, shape, dt=F32):
        if not getattr(self, "dbg", False):
            return
        d = self.dram_t("dbg_" + name, list(shape), dt, kind="ExternalOutput")
        pr.dma("y", d, ap, reads=(res,), tag=f"dbg {name}")



def host_layout(inputs):
    W = dict(inputs)
    out = {}
    qkvz = W["L0_attn_W_qkvz"]
    qk = qkvz.reshape(C, 8, 4, 128)
    out["l0_Wq"] = np.ascontiguousarray(qk[:, :, 0, :].reshape(C, 1024))
    out["l0_Wk"] = np.ascontiguousarray(qk[:, :, 1, :].reshape(C, 1024))
    out["l0_Wv"] = np.ascontiguousarray(qk[:, :, 2, :].reshape(C, 1024))
    out["l0_Wz"] = np.ascontiguousarray(qk[:, :, 3, :].reshape(C, 1024))
    ba = W["L0_attn_W_ba"]
    out["l0_Wb"] = np.ascontiguousarray(ba[:, 0::2])
    out["l0_Wa"] = np.ascontiguousarray(ba[:, 1::2])
    cw = W["L0_attn_conv_w"][:, 0, :]
    out["l0_cwq"] = np.ascontiguousarray(cw[:1024])
    out["l0_cwk"] = np.ascontiguousarray(cw[1024:2048])
    out["l0_cwv"] = np.ascontiguousarray(cw[2048:])
    out["l0_dtb"] = W["L0_attn_dt_bias"].reshape(8, 1)
    out["l0_Alog"] = W["L0_attn_A_log"].reshape(8, 1)
    out["l0_nw"] = W["L0_attn_norm_weight"].reshape(128, 1)
    out["l0_WoT"] = W["L0_attn_W_out"]
    def cols(v):
        return np.ascontiguousarray(v.reshape(-1, 128).T)
    out["f0_n1"] = cols(W["L0_ffn_norm1_w"])
    out["f0_n2"] = cols(W["L0_ffn_norm2_w"])
    out["f0_qkvT"] = np.ascontiguousarray(W["L0_ffn_qkv_w"].T)
    out["f0_qkvb"] = cols(W["L0_ffn_qkv_b"])
    out["f0_oT"] = np.ascontiguousarray(W["L0_ffn_out_w"].T)
    out["f0_ob"] = cols(W["L0_ffn_out_b"])
    out["f0_gT"] = np.ascontiguousarray(W["L0_ffn_gate_w"].T)
    out["f0_w1T"] = np.ascontiguousarray(W["L0_ffn_w1"].transpose(0, 2, 1))
    out["f0_w2T"] = np.ascontiguousarray(W["L0_ffn_w2"].transpose(0, 2, 1))
    out["f0_w3T"] = np.ascontiguousarray(W["L0_ffn_w3"].transpose(0, 2, 1))
    out["f0_b1"] = np.stack([cols(W["L0_ffn_b1"][e]) for e in range(8)])
    out["f0_b2"] = np.stack([cols(W["L0_ffn_b2"][e]) for e in range(8)])
    out["f0_b3"] = np.stack([cols(W["L0_ffn_b3"][e]) for e in range(8)])
    out["l1_Wq"] = W["L1_attn_Wq"]
    out["l1_Wk"] = W["L1_attn_Wk"]
    out["l1_Wv"] = W["L1_attn_Wv"]
    out["l1_Wal"] = W["L1_attn_Walpha"]
    out["l1_Wbe"] = W["L1_attn_Wbeta"]
    out["l1_Alog"] = W["L1_attn_A_log"].reshape(16, 1)
    out["l1_dtb"] = W["L1_attn_dt_bias"].reshape(16, 1)
    out["l1_Wg"] = W["L1_attn_Wgate"]
    out["l1_Wo"] = W["L1_attn_Wout"]
    out["l1_rmsw"] = np.ascontiguousarray(np.tile(W["L1_attn_rms_w"].reshape(64, 1), (2, 1)))
    out["f1_sg"] = W["L1_ffn_shared_gate"]
    out["f1_su"] = W["L1_ffn_shared_up"]
    out["f1_sd"] = W["L1_ffn_shared_down"]
    out["f1_rg"] = W["L1_ffn_routed_gate"]
    out["f1_ru"] = W["L1_ffn_routed_up"]
    out["f1_rd"] = W["L1_ffn_routed_down"]
    out["f1_Wr"] = W["L1_ffn_Wr"]
    out["f1_rb"] = np.ascontiguousarray(
        np.broadcast_to(W["L1_ffn_routing_bias"], (128, 7)))
    out["c_scal"] = np.ascontiguousarray(
        np.broadcast_to(np.array(CONST_SCALARS, np.float32), (128, len(CONST_SCALARS))))
    out["c_idf"] = np.eye(128, dtype=np.float32)
    out["c_idb"] = np.eye(128).astype(ml_dtypes.bfloat16)
    out["c_ones_k1f"] = np.ones((1, 128), np.float32)
    out["c_ones_k1b"] = np.ones((1, 128)).astype(ml_dtypes.bfloat16)
    out["c_ones_c1f"] = np.ones((128, 1), np.float32)
    out["c_ones_c1b"] = np.ones((128, 1)).astype(ml_dtypes.bfloat16)
    out["c_maskDT"] = np.where(np.arange(128)[:, None] >= np.arange(128)[None, :],
                               NEG, np.float32(0)).astype(np.float32)
    out["c_eyeb"] = np.eye(128).astype(ml_dtypes.bfloat16)
    inv = 1.0 / 10000.0 ** (np.arange(0, 64, 2, dtype=np.float32) / 64)
    tpos = np.arange(T, dtype=np.float32)
    fr = tpos[:, None] * inv[None, :]
    emb = np.concatenate([fr, fr], -1)
    cos = np.cos(emb).astype(np.float32).T
    sin = np.sin(emb).astype(np.float32).T
    sinS = np.concatenate([-sin[:32], sin[32:]], 0)
    out["c_cosR"] = np.ascontiguousarray(
        np.concatenate([cos, cos], 0)).astype(ml_dtypes.bfloat16)
    out["c_sinR"] = np.ascontiguousarray(
        np.concatenate([sinS, sinS], 0)).astype(ml_dtypes.bfloat16)
    return out


CONST_SCALARS = [1e-5, 1e-6, float(np.log(128.0 ** -0.5)), float(np.log(64.0 ** -0.5)),
                 1.0, 2.0, 0.5, -1.0, 1e-18]

CONST_DEFS = {
    "c_scal": ([128, len(CONST_SCALARS)], F32),
    "c_idf": ([128, 128], F32), "c_idb": ([128, 128], BF16),
    "c_ones_k1f": ([1, 128], F32), "c_ones_k1b": ([1, 128], BF16),
    "c_ones_c1f": ([128, 1], F32), "c_ones_c1b": ([128, 1], BF16),
    "c_maskDT": ([128, 128], F32), "c_eyeb": ([128, 128], BF16),
    "c_cosR": ([128, T], BF16), "c_sinR": ([128, T], BF16),
}


def alloc_consts(p):
    for name, (shape, dt) in CONST_DEFS.items():
        ap = p.dram_t(name, shape, dt, kind="ExternalInput")
        t = p._scopes[0].enter_context(p.nc.sbuf_tensor("sb_" + name, shape, dt))
        p.sb[name] = t


def load_consts(p, pr):
    if "c_idf" not in p.sb:
        alloc_consts(p)
    for name in CONST_DEFS:
        pr.dma("y", p.sb[name][:], p.dram[name], writes=(name,), tag=f"ld_{name}")



def emit_s0(p, pr, x_hbm, xA_hbm, xnA_hbm, nt=NT, dbg_hbm=None):
    DB = 3
    xt = [p.sbuf(f"s0_x{i}", [128, C], F32) for i in range(DB)]
    sq = [p.sbuf(f"s0_sq{i}", [128, C], F32) for i in range(2)]
    stt = [p.sbuf(f"s0_st{i}", [128, 6], F32) for i in range(2)]
    xnf = [p.sbuf(f"s0_xnf{i}", [128, C], F32) for i in range(2)]
    xnb = [p.sbuf(f"s0_xnb{i}", [128, C], BF16) for i in range(2)]
    evf = [p.sbuf(f"s0_evf{i}", [128, C], F32) for i in range(2)]
    evb = [p.sbuf(f"s0_evb{i}", [128, C], BF16) for i in range(2)]
    pf = [p.psum_bank("pf0"), p.psum_bank("pf1")]
    pb = [p.psum_bank("pbb0", BF16), p.psum_bank("pbb1", BF16)]
    idf = p.sb["c_idf"]
    idb = p.sb["c_idb"]

    for i in range(nt):
        b = i % DB
        d = i % 2
        X, SQ, ST, XF, XB, EF, EB = (f"s0x{b}", f"s0sq{d}", f"s0st{d}",
                                     f"s0xnf{d}", f"s0xnb{d}", f"s0evf{d}", f"s0evb{d}")
        pr.dma("y", xt[b][:], x_hbm[i * 128:(i + 1) * 128, :], writes=(X,), tag=f"s0 ldx{i}")
        pr.op("s", lambda e, b=b, d=d: e.activation(sq[d][:], xt[b][:], AF.Square),
              reads=(X,), writes=(SQ,))
        pr.op("v", lambda e, b=b, d=d: e.tensor_reduce(stt[d][:, 0:1], xt[b][:], AX.X, ALU.add),
              reads=(X,), writes=(ST,))
        pr.op("v", lambda e, d=d: e.tensor_reduce(stt[d][:, 1:2], sq[d][:], AX.X, ALU.add),
              reads=(SQ,), writes=(ST,))
        pr.op("v", lambda e, d=d: e.tensor_scalar(stt[d][:, 2:3], stt[d][:, 0:1], 1.0 / C, -1.0, ALU.mult, ALU.mult),
              reads=(ST,), writes=(ST,))
        pr.op("v", lambda e, d=d: e.tensor_scalar(stt[d][:, 1:2], stt[d][:, 1:2], 1.0 / C, None, ALU.mult),
              reads=(ST,), writes=(ST,))
        pr.op("v", lambda e, d=d: e.tensor_tensor(stt[d][:, 3:4], stt[d][:, 2:3], stt[d][:, 2:3], ALU.mult),
              reads=(ST,), writes=(ST,))
        pr.op("v", lambda e, d=d: e.tensor_tensor(stt[d][:, 3:4], stt[d][:, 1:2], stt[d][:, 3:4], ALU.subtract),
              reads=(ST,), writes=(ST,))
        pr.op("s", lambda e, d=d: e.activation(stt[d][:, 4:5], stt[d][:, 3:4], AF.Ln, bias=p.const_col(1e-5)),
              reads=(ST, "c_scal"), writes=(ST + "r",))
        pr.op("s", lambda e, d=d: e.activation(stt[d][:, 4:5], stt[d][:, 4:5], AF.Exp, scale=-0.5),
              reads=(ST + "r",), writes=(ST + "r",))
        pr.op("v", lambda e, b=b, d=d: e.tensor_scalar(
              xnf[d][:], xt[b][:], stt[d][:, 2:3], stt[d][:, 4:5], ALU.add, ALU.mult),
              reads=(X, ST, ST + "r"), writes=(XF,))
        pr.op("v", lambda e, d=d: e.tensor_copy(xnb[d][:], xnf[d][:]),
              reads=(XF,), writes=(XB,))
        if dbg_hbm is not None:
            pr.dma("y", dbg_hbm[i], stt[d][:], reads=(ST, ST + "r"), tag=f"dbg{i}")
        for f in range(NF):
            bk = f % 2
            sl = (f // 2) % 4
            BA, BB = f"pf{bk}", f"pbb{bk}"
            pr.op("t", lambda e, bk=bk, sl=sl, b=b, f=f: e.transpose(
                  pf[bk][:, sl * 128:(sl + 1) * 128], xt[b][:, f * 128:(f + 1) * 128], idf[:]),
                  reads=(X, "c_idf"), writes=(BA,), tag=f"s0 trf{i}_{f}")
            pr.op("v", lambda e, bk=bk, sl=sl, d=d, f=f: e.tensor_copy(
                  evf[d][:, f * 128:(f + 1) * 128], pf[bk][:, sl * 128:(sl + 1) * 128]),
                  reads=(BA,), writes=(EF,))
            pr.op("t", lambda e, bk=bk, sl=sl, d=d, f=f: e.transpose(
                  pb[bk][:, sl * 128:(sl + 1) * 128], xnb[d][:, f * 128:(f + 1) * 128], idb[:]),
                  reads=(XB, "c_idb"), writes=(BB,), tag=f"s0 trb{i}_{f}")
            pr.op("v", lambda e, bk=bk, sl=sl, d=d, f=f: e.tensor_copy(
                  evb[d][:, f * 128:(f + 1) * 128], pb[bk][:, sl * 128:(sl + 1) * 128]),
                  reads=(BB,), writes=(EB,))
        pr.dma("y", xA_hbm[:, :, i * 128:(i + 1) * 128],
               evf[d][:].rearrange("p (f t) -> p f t", t=128), reads=(EF,), tag=f"s0 stxA{i}")
        pr.dma("y", xnA_hbm[:, :, i * 128:(i + 1) * 128],
               evb[d][:].rearrange("p (f t) -> p f t", t=128), reads=(EB,), tag=f"s0 stxnA{i}")



def emit_gemm(p, pr, tag, W_hbm, Din, Dout, rhs_fn, evac_fn, banks, wname,
              nch=8, chw=512, wq="g"):
    NK = Din // 128
    NM = cdiv(Dout, 128)
    wbuf = [p.sbuf(f"{wname}{i}", [128, NK * 128], BF16) for i in range(2)]
    for m in range(NM):
        mw = min(128, Dout - m * 128)
        ws = m % 2
        src = W_hbm[:, m * 128:m * 128 + mw].rearrange("(nk r) c -> r nk c", r=128)
        pr.dma(wq, wbuf[ws][:].rearrange("r (nk c) -> r nk c", c=128)[:, :, :mw],
               src, writes=(f"{wname}{ws}",), tag=f"{tag} w{m}")
        for ch in range(nch):
            ba, bres = banks[ch % 2]
            for k in range(NK):
                pr.op("t", lambda e, ws=ws, k=k, ch=ch, mw=mw, ba=ba: e.matmul(
                      ba[:mw, :chw], wbuf[ws][:, k * 128:k * 128 + mw],
                      rhs_fn(k, ch)[0], start=(k == 0), stop=(k == NK - 1)),
                      reads=(f"{wname}{ws}", rhs_fn(k, ch)[1]),
                      writes=(bres,), tag=f"{tag} mm{m}_{ch}_{k}")
            evac_fn(m, ch, ba, bres)


def bcast_strip(p, pr, strip_ap, strip_res, w, bank_ap, bank_res, f32=True):
    ones = p.sb["c_ones_k1f" if f32 else "c_ones_k1b"]
    ores = "c_ones_k1f" if f32 else "c_ones_k1b"
    pr.op("t", lambda e: e.matmul(bank_ap[:, :w], ones[:], strip_ap,
                                  start=True, stop=True),
          reads=(strip_res, ores), writes=(bank_res,), tag="bcast")


def rstd_strip(p, pr, strip_ap, strip_res, ln_scale, eps, exp_bias=None, n=1):
    ec = p.const_col(eps)[:n, :]
    pr.op("s", lambda e: e.activation(strip_ap, strip_ap, AF.Ln,
                                      bias=ec, scale=ln_scale),
          reads=(strip_res, "c_scal"), writes=(strip_res,), tag="rstd_ln")
    if exp_bias is None:
        pr.op("s", lambda e: e.activation(strip_ap, strip_ap, AF.Exp, scale=-0.5),
              reads=(strip_res,), writes=(strip_res,), tag="rstd_exp")
    else:
        bc = p.const_col(exp_bias)[:n, :]
        pr.op("s", lambda e: e.activation(strip_ap, strip_ap, AF.Exp,
                                          bias=bc, scale=-0.5),
              reads=(strip_res, "c_scal"), writes=(strip_res,), tag="rstd_exp")


def softplus_block(p, pr, src_ap, src_res, n, w, tmp_a, tmp_r, ra, rr,
                   dtb_col, dtb_res, negA_col, negA_res, out_ap, out_res):
    pr.op("s", lambda e: e.activation(tmp_a, src_ap, AF.Abs, bias=dtb_col),
          reads=(src_res, dtb_res), writes=(tmp_r,), tag="sp_abs")
    pr.op("s", lambda e: e.activation(ra, src_ap, AF.Relu, bias=dtb_col),
          reads=(src_res, dtb_res), writes=(rr,), tag="sp_relu")
    pr.op("s", lambda e: e.activation(tmp_a, tmp_a, AF.Exp, scale=-1.0),
          reads=(tmp_r,), writes=(tmp_r,), tag="sp_exp")
    pr.op("s", lambda e: e.activation(tmp_a, tmp_a, AF.Ln, bias=1.0),
          reads=(tmp_r,), writes=(tmp_r,), tag="sp_ln")
    pr.op("v", lambda e: e.tensor_tensor(tmp_a, tmp_a, ra, ALU.add),
          reads=(tmp_r, rr), writes=(tmp_r,), tag="sp_add")
    pr.op("v", lambda e: e.tensor_scalar(out_ap, tmp_a, negA_col, None, ALU.mult),
          reads=(tmp_r, negA_res), writes=(out_res,), tag="sp_scale")



def emit_s1(p, pr, xnA, qcA, kcA, vcA, zsA, bg, win):
    nc = p.nc
    xn = p.sbuf("s1_xn", [128, NF * T], BF16)
    pr.dma("y", xn[:].rearrange("p (f t) -> p f t", t=T), xnA,
           writes=("s1xn",), tag="s1 ldxn")

    def rhs(k, ch):
        return xn[:, k * T + ch * 512: k * T + ch * 512 + 512], "s1xn"

    banks = [(p.psum_bank("pf0"), "pf0"), (p.psum_bank("pf1"), "pf1")]
    banks2 = [(p.psum_bank("pf2"), "pf2"), (p.psum_bank("pf3"), "pf3")]

    cw = {}
    for sec in ("q", "k", "v"):
        t = p.sbuf(f"s1_cw{sec}", [128, 32], F32)
        cw[sec] = t
        pr.dma("y", t[:].rearrange("r (h j) -> r h j", j=4),
               win[f"l0_cw{sec}"].rearrange("(h r) j -> r h j", r=128),
               writes=(f"s1cw{sec}",), tag=f"s1 ldcw{sec}")
    dtb = p.sbuf("s1_dtb", [8, 1], F32)
    pr.dma("y", dtb[:], win["l0_dtb"], writes=("s1dtb",))
    alog = p.sbuf("s1_alog", [8, 1], F32)
    pr.dma("y", alog[:], win["l0_Alog"], writes=("s1alog",))
    negA = p.sbuf("s1_negA", [8, 1], F32)
    pr.op("s", lambda e: e.activation(negA[:], alog[:], AF.Exp),
          reads=("s1alog",), writes=("s1negA",))
    pr.op("v", lambda e: e.tensor_scalar_mul(negA[:], negA[:], -1.0),
          reads=("s1negA",), writes=("s1negA",))

    raw = [p.sbuf(f"s1_raw{i}", [128, T + 3], BF16) for i in range(2)]
    acc = [p.sbuf(f"s1_acc{i}", [128, T], F32) for i in range(2)]
    outb = [p.sbuf(f"s1_out{i}", [128, 512], BF16) for i in range(2)]

    for sec, Wk, dst in (("q", "l0_Wq", qcA), ("k", "l0_Wk", kcA), ("v", "l0_Wv", vcA)):
        def evac(m, ch, ba, bres, sec=sec, dst=dst):
            ms = m % 2
            R = f"s1raw{ms}"
            pr.op("v", lambda e, ms=ms, ch=ch, ba=ba: e.tensor_copy(
                  raw[ms][:, 3 + ch * 512: 3 + (ch + 1) * 512], ba[:, :512]),
                  reads=(bres,), writes=(R,), tag=f"s1 {sec}ev{m}_{ch}")
            if ch == 0:
                pr.op("v", lambda e, ms=ms: e.memset(raw[ms][:, 0:3], 0.0),
                      writes=(R,), tag="s1 pad0")
            if ch == nchunks - 1:
                A = f"s1acc{ms}"
                pr.op("v", lambda e, ms=ms, sec=sec, m=m: e.tensor_scalar(
                      acc[ms][:], raw[ms][:, 0:T], cw[sec][:, 4 * m:4 * m + 1],
                      None, ALU.mult),
                      reads=(R, f"s1cw{sec}"), writes=(A,), tag="s1 conv0")
                for j in range(1, 4):
                    pr.op("v", lambda e, ms=ms, sec=sec, m=m, j=j: e.scalar_tensor_tensor(
                          acc[ms][:], raw[ms][:, j:j + T], cw[sec][:, 4 * m + j:4 * m + j + 1],
                          acc[ms][:], ALU.mult, ALU.add),
                          reads=(R, f"s1cw{sec}", A), writes=(A,), tag=f"s1 conv{j}")
                for oc in range(8):
                    O = f"s1out{oc % 2}"
                    pr.op("s", lambda e, ms=ms, oc=oc: e.activation(
                          outb[oc % 2][:], acc[ms][:, oc * 512:(oc + 1) * 512], AF.Silu),
                          reads=(A,), writes=(O,), tag="s1 silu")
                    pr.dma("y", dst[:, m, oc * 512:(oc + 1) * 512], outb[oc % 2][:],
                           reads=(O,), tag=f"s1 st{sec}{m}_{oc}")
        nchunks = 8
        emit_gemm(p, pr, f"s1{sec}", win[Wk], C, C, rhs, evac, banks, f"s1w{sec}")

    zst = [p.sbuf(f"s1_z{i}", [128, T], BF16) for i in range(2)]

    def evac_z(m, ch, ba, bres):
        ms = m % 2
        Z = f"s1z{ms}"
        pr.op("s", lambda e, ms=ms, ch=ch, ba=ba: e.activation(
              zst[ms][:, ch * 512:(ch + 1) * 512], ba[:, :512], AF.Silu),
              reads=(bres,), writes=(Z,), tag=f"s1 zev{m}_{ch}")
        if ch == 7:
            pr.dma("y", zsA[:, m, :], zst[ms][:], reads=(Z,), tag=f"s1 stz{m}")
    emit_gemm(p, pr, "s1z", win["l0_Wz"], C, C, rhs, evac_z, banks, "s1wz")

    wb = p.sbuf("s1_wb", [128, NF * 8], BF16)
    wa = p.sbuf("s1_wa", [128, NF * 8], BF16)
    pr.dma("g", wb[:].rearrange("r (nk c) -> r nk c", c=8),
           win["l0_Wb"].rearrange("(nk r) c -> r nk c", r=128), writes=("s1wb",))
    pr.dma("g", wa[:].rearrange("r (nk c) -> r nk c", c=8),
           win["l0_Wa"].rearrange("(nk r) c -> r nk c", r=128), writes=("s1wa",))
    bss = [p.sbuf(f"s1_bs{i}", [8, 512], F32) for i in range(2)]
    gss = [p.sbuf(f"s1_gs{i}", [8, 512], F32) for i in range(2)]
    gcs = [p.sbuf(f"s1_gc{i}", [8, 512], F32) for i in range(2)]
    tt1 = [p.sbuf(f"s1_t1{i}", [8, 512], F32) for i in range(2)]
    tt2 = [p.sbuf(f"s1_t2{i}", [8, 512], F32) for i in range(2)]
    egs = [p.sbuf(f"s1_eg{i}", [8, 512], F32) for i in range(2)]
    ers = [p.sbuf(f"s1_er{i}", [8, 512], F32) for i in range(2)]
    bes = [p.sbuf(f"s1_be{i}", [8, 512], F32) for i in range(2)]
    zcol = p.sbuf("s1_zcol", [8, 1], F32)
    pr.op("v", lambda e: e.memset(zcol[:], 0.0), writes=("s1zcol",))

    for ch in range(8):
        cs = ch % 2
        ba_b, bres_b = banks2[0]
        ba_a, bres_a = banks2[1]
        for k in range(NF):
            pr.op("t", lambda e, k=k, ch=ch, ba=ba_b: e.matmul(
                  ba[:8, :512], wb[:, k * 8:(k + 1) * 8], rhs(k, ch)[0],
                  start=(k == 0), stop=(k == NF - 1)),
                  reads=("s1wb", "s1xn"), writes=(bres_b,), tag=f"s1 bmm{ch}")
            pr.op("t", lambda e, k=k, ch=ch, ba=ba_a: e.matmul(
                  ba[:8, :512], wa[:, k * 8:(k + 1) * 8], rhs(k, ch)[0],
                  start=(k == 0), stop=(k == NF - 1)),
                  reads=("s1wa", "s1xn"), writes=(bres_a,), tag=f"s1 amm{ch}")
        B, G, GC = f"s1bs{cs}", f"s1gs{cs}", f"s1gc{cs}"
        pr.op("s", lambda e, cs=cs, ba=ba_b: e.activation(bss[cs][:], ba[:8, :512], AF.Sigmoid),
              reads=(bres_b,), writes=(B,), tag="s1 beta")
        softplus_block(p, pr, ba_a[:8, :512], bres_a, 8, 512,
                       tt1[cs][:], f"s1t1{cs}", tt2[cs][:], f"s1t2{cs}",
                       dtb[:], "s1dtb", negA[:], "s1negA",
                       gss[cs][:], G)
        for q in range(4):
            sl = slice(q * 128, (q + 1) * 128)
            pr.op("v", lambda e, cs=cs, sl=sl: e.tensor_tensor_scan(
                  gcs[cs][:, sl], gss[cs][:, sl], gss[cs][:, sl], zcol[:],
                  ALU.add, ALU.bypass),
                  reads=(G, "s1zcol"), writes=(GC,), tag="s1 cumsum")
        for q in range(4):
            sl = slice(q * 128, (q + 1) * 128)
            pr.op("v", lambda e, cs=cs, sl=sl, q=q: e.tensor_scalar(
                  gss[cs][:, sl], gcs[cs][:, sl], -1.0,
                  gcs[cs][:, q * 128 + 127: q * 128 + 128], ALU.mult, ALU.add),
                  reads=(GC,), writes=(G,), tag="s1 grev")
        EG, ER, BE = f"s1eg{cs}", f"s1er{cs}", f"s1be{cs}"
        pr.op("s", lambda e, cs=cs: e.activation(egs[cs][:], gcs[cs][:], AF.Exp),
              reads=(GC,), writes=(EG,), tag="s1 Eg")
        pr.op("s", lambda e, cs=cs: e.activation(ers[cs][:], gss[cs][:], AF.Exp),
              reads=(G,), writes=(ER,), tag="s1 Erev")
        pr.op("v", lambda e, cs=cs: e.tensor_tensor(bes[cs][:], bss[cs][:], egs[cs][:], ALU.mult),
              reads=(B, EG), writes=(BE,), tag="s1 bEg")
        sl5 = slice(ch * 512, (ch + 1) * 512)
        pr.dma("y", bg[0, :, sl5], gcs[cs][:], reads=(GC,), tag=f"s1 stg{ch}")
        pr.dma("y", bg[1, :, sl5], egs[cs][:], reads=(EG,), tag=f"s1 steg{ch}")
        pr.dma("y", bg[2, :, sl5], ers[cs][:], reads=(ER,), tag=f"s1 ster{ch}")
        pr.dma("y", bg[3, :, sl5], bss[cs][:], reads=(B,), tag=f"s1 stb{ch}")
        pr.dma("y", bg[4, :, sl5], bes[cs][:], reads=(BE,), tag=f"s1 stbe{ch}")



def emit_delta_head(p, pr, pfx, dk, dv, cstride, cofs, kA_ap, kA_res, qA_ap, qA_res,
                    vA_ap, vA_res, strips, tcols, eglast_bc, S_ap, S_res,
                    core_ap, core_res, nch=NCH, idsl=None):
    nc = p.nc
    idb = p.sb["c_idb"]
    if idsl is None:
        idsl = slice(0, dk)
    PF = [p.psum_bank(f"pf{i}") for i in range(6)]
    BB = [p.psum_bank("pbb0", BF16), p.psum_bank("pbb1", BF16)]
    BANKS = [dict(A=(PF[0], "pf0"), B=(PF[1], "pf1"), C=(PF[2], "pf2"),
                  D=(BB[0], "pbb0")),
             dict(A=(PF[3], "pf3"), B=(PF[4], "pf4"), C=(PF[5], "pf5"),
                  D=(BB[1], "pbb1"))]

    def W(n, par, sh=(128, 128), dt=BF16):
        nm = f"{pfx}_{n}{par}"
        return p.sb[nm] if nm in p.sb else p.sbuf(nm, list(sh), dt)

    gs, gs_r = strips["g"]
    egs, egs_r = strips["Eg"]
    bes, bes_r = strips["be"]
    gTn, gTn_r = tcols["gTn"]
    beT, beT_r = tcols["beT"]
    erT, erT_r = tcols["erT"]
    begT, begT_r = tcols["begT"]
    egl, egl_r = eglast_bc

    def ci(i):
        return slice(i * cstride + cofs, i * cstride + cofs + 1)

    for i in range(nch):
        par = i % 2
        bk = BANKS[par]
        bA, bAr = bk["A"]
        bN, bNr = bk["B"]
        bC, bCr = bk["C"]
        bD, bDr = bk["D"]
        R = lambda n: f"{pfx}_{n}{par}"
        argDT = W("argDT", par, (128, 128), F32)
        DstT = W("DstT", par); DTt = W("DTt", par); DbT = W("DbT", par)
        triT = W("triT", par); MTt = W("MTt", par)
        Xb = W("Xb", par); Zb = W("Zb", par); Pe = W("Pe", par); PeT = W("PeT", par)
        kT = W("kT", par); VbT = W("VbT", par); KtT = W("KtT", par)
        K2T = W("K2T", par); kcn = W("kcn", par); vnA = W("vnA", par)
        vnR = W("vnR", par); qt = W("qt", par); atT = W("atT", par)
        cs = slice(i * CH, (i + 1) * CH)
        bcast_strip(p, pr, gs[:, cs], gs_r, CH, bA, bAr, f32=True)
        pr.op("v", lambda e, i=i, argDT=argDT, bA=bA: e.scalar_tensor_tensor(
              argDT[:], bA[:, :CH], gTn[:, ci(i)], p.sb["c_maskDT"][:],
              ALU.add, ALU.add),
              reads=(bAr, gTn_r, "c_maskDT"), writes=(R("argDT"),), tag=f"{pfx} argDT{i}")
        pr.op("s", lambda e, DstT=DstT, argDT=argDT: e.activation(DstT[:], argDT[:], AF.Exp),
              reads=(R("argDT"),), writes=(R("DstT"),), tag="expD")
        pr.op("v", lambda e, DTt=DTt, DstT=DstT: e.tensor_tensor(
              DTt[:], DstT[:], p.sb["c_eyeb"][:], ALU.add),
              reads=(R("DstT"), "c_eyeb"), writes=(R("DTt"),), tag="DT")
        bcast_strip(p, pr, bes[:, cs], bes_r, CH, bA, bAr, f32=False)
        pr.op("v", lambda e, DbT=DbT, DstT=DstT, bA=bA: e.tensor_tensor(
              DbT[:], DstT[:], bA[:, :CH], ALU.mult),
              reads=(R("DstT"), bAr), writes=(R("DbT"),), tag="DbT")
        pr.op("t", lambda e, cs=cs, bA=bA: e.matmul(bA[:, :CH], kA_ap[:, cs],
                                                    kA_ap[:, cs], start=True, stop=True),
              reads=(kA_res,), writes=(bAr,), tag=f"{pfx} W1{i}")
        pr.op("v", lambda e, triT=triT, DbT=DbT, bA=bA: e.tensor_tensor(
              triT[:], bA[:, :CH], DbT[:], ALU.mult),
              reads=(bAr, R("DbT")), writes=(R("triT"),), tag="triT")
        pr.op("v", lambda e, MTt=MTt, triT=triT: e.tensor_tensor(
              MTt[:], triT[:], p.sb["c_eyeb"][:], ALU.add),
              reads=(R("triT"), "c_eyeb"), writes=(R("MTt"),), tag="MT")
        pr.op("v", lambda e, Xb=Xb, triT=triT: e.tensor_tensor(
              Xb[:], p.sb["c_eyeb"][:], triT[:], ALU.subtract),
              reads=(R("triT"), "c_eyeb"), writes=(R("Xb"),), tag="X0")
        for it in range(6):
            pr.op("t", lambda e, MTt=MTt, Xb=Xb, bN=bN: e.matmul(
                  bN[:, :CH], MTt[:], Xb[:], start=True, stop=True),
                  reads=(R("MTt"), R("Xb")), writes=(bNr,), tag=f"nwP{it}")
            if it % 2:
                pr.op("v", lambda e, Pe=Pe, bN=bN: e.tensor_copy(Pe[:], bN[:, :CH]),
                      reads=(bNr,), writes=(R("Pe"),), tag="nwPe")
            else:
                pr.op("s", lambda e, Pe=Pe, bN=bN: e.copy(Pe[:], bN[:, :CH]),
                      reads=(bNr,), writes=(R("Pe"),), tag="nwPe")
            pr.op("t", lambda e, Pe=Pe, bD=bD: e.transpose(bD[:, 0:CH], Pe[:], idb[:]),
                  reads=(R("Pe"), "c_idb"), writes=(bDr,), tag="nwPT")
            pr.op("s", lambda e, PeT=PeT, bD=bD: e.copy(PeT[:], bD[:, 0:CH]),
                  reads=(bDr,), writes=(R("PeT"),), tag="nwPTe")
            pr.op("t", lambda e, PeT=PeT, Xb=Xb, bN=bN: e.matmul(
                  bN[:, :CH], PeT[:], Xb[:], start=True, stop=True),
                  reads=(R("PeT"), R("Xb")), writes=(bNr,), tag="nwXP")
            pr.op("v", lambda e, Xb=Xb, bN=bN: e.scalar_tensor_tensor(
                  Xb[:], Xb[:], 2.0, bN[:, :CH], ALU.mult, ALU.subtract),
                  reads=(R("Xb"), bNr), writes=(R("Xb"),), tag="nwX")
        pr.op("t", lambda e, Xb=Xb, bD=bD: e.transpose(bD[:, 0:CH], Xb[:], idb[:]),
              reads=(R("Xb"), "c_idb"), writes=(bDr,), tag="trTmT")
        pr.op("s", lambda e, Zb=Zb, bD=bD: e.copy(Zb[:], bD[:, 0:CH]),
              reads=(bDr,), writes=(R("Zb"),), tag="TmT")
        pr.op("t", lambda e, cs=cs, bD=bD: e.transpose(
              bD[:, 128:128 + dv], vA_ap[:dv, cs], idb[idsl, idsl]),
              reads=(vA_res, "c_idb"), writes=(bDr,), tag="trV")
        pr.op("v", lambda e, i=i, VbT=VbT, bD=bD: e.tensor_scalar(
              VbT[:, :dv], bD[:, 128:128 + dv], beT[:, ci(i)], None, ALU.mult),
              reads=(bDr, beT_r), writes=(R("VbT"),), tag="VbT")
        pr.op("t", lambda e, cs=cs, bD=bD: e.transpose(
              bD[:, 256:256 + dk], kA_ap[:dk, cs], idb[idsl, idsl]),
              reads=(kA_res, "c_idb"), writes=(bDr,), tag="trK")
        pr.op("s", lambda e, kT=kT, bD=bD: e.copy(kT[:, :dk], bD[:, 256:256 + dk]),
              reads=(bDr,), writes=(R("kT"),), tag="kT")
        pr.op("v", lambda e, i=i, KtT=KtT, kT=kT: e.tensor_scalar(
              KtT[:, :dk], kT[:, :dk], begT[:, ci(i)], None, ALU.mult),
              reads=(R("kT"), begT_r), writes=(R("KtT"),), tag="KtT")
        pr.op("v", lambda e, i=i, K2T=K2T, kT=kT: e.tensor_scalar(
              K2T[:, :dk], kT[:, :dk], erT[:, ci(i)], None, ALU.mult),
              reads=(R("kT"), erT_r), writes=(R("K2T"),), tag="K2T")
        pr.op("t", lambda e, KtT=KtT, Zb=Zb, bC=bC: e.matmul(
              bC[:dk, :CH], KtT[:, :dk], Zb[:], start=True, stop=True),
              reads=(R("KtT"), R("Zb")), writes=(bCr,), tag="kcum")
        pr.op("s", lambda e, kcn=kcn, bC=bC: e.mul(kcn[:dk, :], bC[:dk, :CH], -1.0),
              reads=(bCr,), writes=(R("kcn"),), tag="kcn")
        pr.op("t", lambda e, VbT=VbT, Zb=Zb, bC=bC: e.matmul(
              bC[:dv, :CH], VbT[:, :dv], Zb[:], start=True, stop=False),
              reads=(R("VbT"), R("Zb")), writes=(bCr,), tag="vn1")
        pr.op("t", lambda e, kcn=kcn, bC=bC: e.matmul(
              bC[:dv, :CH], S_ap, kcn[:dk, :], start=False, stop=True),
              reads=(S_res, R("kcn")), writes=(bCr,), tag="vn2")
        pr.op("s", lambda e, vnA=vnA, bC=bC: e.copy(vnA[:dv, :], bC[:dv, :CH]),
              reads=(bCr,), writes=(R("vnA"),), tag="vnA")
        pr.op("t", lambda e, vnA=vnA, bD=bD: e.transpose(
              bD[:, 384:384 + dv], vnA[:dv, :], idb[:dv, :dv]),
              reads=(R("vnA"), "c_idb"), writes=(bDr,), tag="trVn")
        pr.op("s", lambda e, vnR=vnR, bD=bD: e.copy(vnR[:, :dv], bD[:, 384:384 + dv]),
              reads=(bDr,), writes=(R("vnR"),), tag="vnR")
        bcast_strip(p, pr, egs[:, cs], egs_r, CH, bA, bAr, f32=True)
        pr.op("v", lambda e, cs=cs, qt=qt, bA=bA: e.tensor_tensor(
              qt[:dk, :], qA_ap[:, cs], bA[:dk, :CH], ALU.mult),
              reads=(qA_res, bAr), writes=(R("qt"),), tag="qt")
        pr.op("t", lambda e, cs=cs, bA=bA: e.matmul(
              bA[:, :CH], kA_ap[:, cs], qA_ap[:, cs], start=True, stop=True),
              reads=(kA_res, qA_res), writes=(bAr,), tag="attn")
        pr.op("v", lambda e, atT=atT, DTt=DTt, bA=bA: e.tensor_tensor(
              atT[:], bA[:, :CH], DTt[:], ALU.mult),
              reads=(bAr, R("DTt")), writes=(R("atT"),), tag="atT")
        pr.op("t", lambda e, qt=qt, bC=bC: e.matmul(
              bC[:dv, :CH], S_ap, qt[:dk, :], start=True, stop=False),
              reads=(S_res, R("qt")), writes=(bCr,), tag="out1")
        pr.op("t", lambda e, vnR=vnR, atT=atT, bC=bC: e.matmul(
              bC[:dv, :CH], vnR[:, :dv], atT[:], start=False, stop=True),
              reads=(R("vnR"), R("atT")), writes=(bCr,), tag="out2")
        pr.op("s", lambda e, cs=cs, bC=bC: e.copy(core_ap[:, cs], bC[:dv, :CH]),
              reads=(bCr,), writes=(core_res,), tag=f"{pfx} core{i}")
        pr.op("t", lambda e, K2T=K2T, vnR=vnR, bC=bC: e.matmul(
              bC[:dk, :dv], K2T[:, :dk], vnR[:, :dv], start=True, stop=True),
              reads=(R("K2T"), R("vnR")), writes=(bCr,), tag="U")
        pr.op("v", lambda e, i=i, bC=bC: e.scalar_tensor_tensor(
              S_ap, S_ap, egl[:dk, i:i + 1], bC[:dk, :dv], ALU.mult, ALU.add),
              reads=(S_res, egl_r, bCr), writes=(S_res,), tag="Supd")


def emit_s2a(p, pr, qcA, kcA, vcA, zsA, bg, gatedA, nw_hbm):
    nc = p.nc
    idf = p.sb["c_idf"]
    pf0 = p.psum_bank("pf0")
    pf1 = p.psum_bank("pf1")
    stage = p.sbuf("s2_stage", [8, T], F32)
    gT = p.sbuf("s2_gT", [128, NCH * 8], F32)
    gTn = p.sbuf("s2_gTn", [128, NCH * 8], F32)
    beT = p.sbuf("s2_beT", [128, NCH * 8], F32)
    begT = p.sbuf("s2_begT", [128, NCH * 8], F32)
    erT = p.sbuf("s2_erT", [128, NCH * 8], F32)
    for row, dst, res in ((0, gT, "s2gT"), (3, beT, "s2beT"),
                          (4, begT, "s2begT"), (2, erT, "s2erT")):
        pr.dma("y", stage[:], bg[row], writes=("s2stage",), tag=f"s2 ldbg{row}")
        for i in range(NCH):
            pr.op("t", lambda e, i=i: e.transpose(
                  pf0[:, (i % 4) * 128:(i % 4) * 128 + 8],
                  stage[:, i * CH:(i + 1) * CH], idf[:8, :8]),
                  reads=("s2stage", "c_idf"), writes=("pf0",), tag=f"s2 trbg{row}_{i}")
            pr.op("v", lambda e, i=i, dst=dst, res=res: e.tensor_copy(
                  dst[:, i * 8:(i + 1) * 8], pf0[:, (i % 4) * 128:(i % 4) * 128 + 8]),
                  reads=("pf0",), writes=(res,), tag=f"s2 evbg{row}_{i}")
    pr.op("v", lambda e: e.tensor_scalar_mul(gTn[:], gT[:], -1.0),
          reads=("s2gT",), writes=("s2gTn",))
    nw = p.sbuf("s2_nw", [128, 1], F32)
    pr.dma("y", nw[:], nw_hbm, writes=("s2nw",))

    qh = p.sbuf("s2_qh", [128, T], BF16)
    kh = p.sbuf("s2_kh", [128, T], BF16)
    vh = p.sbuf("s2_vh", [128, T], BF16)
    zh = p.sbuf("s2_zh", [128, T], BF16)
    sqb = p.sbuf("s2_sq", [128, T], BF16)
    core = p.sbuf("s2_core", [128, T], BF16)
    gcore = p.sbuf("s2_gcore", [128, T], BF16)
    qn = p.sbuf("s2_qn", [128, T], BF16)
    kn = p.sbuf("s2_kn", [128, T], BF16)
    ssf = p.sbuf("s2_ssf", [1, T], F32)
    g_st = p.sbuf("s2_gst", [1, T], F32)
    eg_st = p.sbuf("s2_egst", [1, T], F32)
    be_st = p.sbuf("s2_best", [1, T], BF16)
    egl = p.sbuf("s2_egl", [128, NCH], F32)
    Sst = p.sbuf("s2_S", [128, 128], BF16)

    def l2pass(src_ap, src_res, dst, dst_res, exp_bias):
        pr.op("s", lambda e: e.activation(sqb[:], src_ap, AF.Square),
              reads=(src_res,), writes=("s2sq",), tag="sq")
        for ch in range(8):
            pr.op("t", lambda e, ch=ch: e.matmul(
                  pf1[:1, :512],
                  p.sb["c_ones_c1b"][:], sqb[:, ch * 512:(ch + 1) * 512],
                  start=True, stop=True),
                  reads=("s2sq", "c_ones_c1b"), writes=("pf1",), tag=f"ssq{ch}")
            pr.op("v", lambda e, ch=ch: e.tensor_copy(
                  ssf[:, ch * 512:(ch + 1) * 512], pf1[:1, :512]),
                  reads=("pf1",), writes=("s2ssf",), tag=f"ssqe{ch}")
        rstd_strip(p, pr, ssf[:], "s2ssf", 1.0, 1e-6, exp_bias, n=1)
        for ch in range(8):
            sl = slice(ch * 512, (ch + 1) * 512)
            bcast_strip(p, pr, ssf[:, sl], "s2ssf", 512, pf0, "pf0", f32=True)
            pr.op("v", lambda e, sl=sl, src_ap=src_ap: e.tensor_tensor(
                  dst[:, sl], src_ap[:, sl], pf0[:, :512], ALU.mult),
                  reads=(src_res, "pf0"), writes=(dst_res,), tag="l2mul")

    LN128 = float(np.log(128.0 ** -0.5))
    for h in range(8):
        pr.dma("y", qh[:], qcA[:, h, :], writes=("s2qh",), tag=f"s2 ldq{h}")
        pr.dma("y", kh[:], kcA[:, h, :], writes=("s2kh",), tag=f"s2 ldk{h}")
        pr.dma("y", vh[:], vcA[:, h, :], writes=("s2vh",), tag=f"s2 ldv{h}")
        pr.dma("y", zh[:], zsA[:, h, :], writes=("s2zh",), tag=f"s2 ldz{h}")
        pr.dma("y", g_st[:], bg[0, h:h + 1, :], writes=("s2gst",), tag=f"s2 ldg{h}")
        pr.dma("y", eg_st[:], bg[1, h:h + 1, :], writes=("s2egst",), tag=f"s2 ldeg{h}")
        pr.dma("g", be_st[:], bg[3, h:h + 1, :], writes=("s2best",), tag=f"s2 ldbe{h}")
        bcast_strip(p, pr, eg_st[:, CH - 1::CH], "s2egst", NCH, pf0, "pf0", f32=True)
        pr.op("v", lambda e: e.tensor_copy(egl[:], pf0[:, :NCH]),
              reads=("pf0",), writes=("s2egl",), tag="egl")
        l2pass(qh[:], "s2qh", qn, "s2qn", LN128)
        l2pass(kh[:], "s2kh", kn, "s2kn", None)
        if h == 0:
            p.dbg_store(pr, "qn", qn[:], "s2qn", [128, T], BF16)
            p.dbg_store(pr, "kn", kn[:], "s2kn", [128, T], BF16)
            p.dbg_store(pr, "egl", egl[:], "s2egl", [128, NCH])
            p.dbg_store(pr, "gT", gT[:], "s2gT", [128, NCH * 8])
            p.dbg_store(pr, "beT", beT[:], "s2beT", [128, NCH * 8])
            p.dbg_store(pr, "begT", begT[:], "s2begT", [128, NCH * 8])
            p.dbg_store(pr, "erT", erT[:], "s2erT", [128, NCH * 8])
        pr.op("v", lambda e: e.memset(Sst[:], 0.0), writes=("s2S",), tag="S0")
        emit_delta_head(
            p, pr, "s2d", 128, 128, 8, h,
            kn[:], "s2kn", qn[:], "s2qn", vh[:], "s2vh",
            dict(g=(g_st[:], "s2gst"), Eg=(eg_st[:], "s2egst"),
                 be=(be_st[:], "s2best")),
            dict(gTn=(gTn[:, :], "s2gTn"), beT=(beT[:, :], "s2beT"),
                 erT=(erT[:, :], "s2erT"), begT=(begT[:, :], "s2begT")),
            (egl[:], "s2egl"), Sst[:], "s2S", core[:], "s2core")
        pr.op("s", lambda e: e.activation(sqb[:], core[:], AF.Square),
              reads=("s2core",), writes=("s2sq",), tag="csq")
        for ch in range(8):
            pr.op("t", lambda e, ch=ch: e.matmul(
                  pf1[:1, :512], p.sb["c_ones_c1b"][:],
                  sqb[:, ch * 512:(ch + 1) * 512], start=True, stop=True),
                  reads=("s2sq", "c_ones_c1b"), writes=("pf1",), tag=f"cs{ch}")
            pr.op("v", lambda e, ch=ch: e.tensor_copy(
                  ssf[:, ch * 512:(ch + 1) * 512], pf1[:1, :512]),
                  reads=("pf1",), writes=("s2ssf",), tag=f"cse{ch}")
        rstd_strip(p, pr, ssf[:], "s2ssf", 1.0 / 128.0, 1e-6, None, n=1)
        for ch in range(8):
            sl = slice(ch * 512, (ch + 1) * 512)
            bcast_strip(p, pr, ssf[:, sl], "s2ssf", 512, pf0, "pf0", f32=True)
            pr.op("v", lambda e, sl=sl: e.scalar_tensor_tensor(
                  gcore[:, sl], core[:, sl], nw[:], pf0[:, :512], ALU.mult, ALU.mult),
                  reads=("s2core", "s2nw", "pf0"), writes=("s2gcore",), tag="gc1")
            pr.op("v", lambda e, sl=sl: e.tensor_tensor(
                  gcore[:, sl], gcore[:, sl], zh[:, sl], ALU.mult),
                  reads=("s2gcore", "s2zh"), writes=("s2gcore",), tag="gc2")
        pr.dma("y", gatedA[:, h, :], gcore[:], reads=("s2gcore",), tag=f"s2 stg{h}")


def emit_s2b(p, pr, gatedA, WoT_hbm, s0A):
    gt = p.sbuf("s2b_g", [128, NF * T], BF16)
    pr.dma("y", gt[:].rearrange("p (f t) -> p f t", t=T), gatedA,
           writes=("s2bg",), tag="s2b ldg")

    def rhs(k, ch):
        return gt[:, k * T + ch * 512: k * T + ch * 512 + 512], "s2bg"

    banks = [(p.psum_bank("pf0"), "pf0"), (p.psum_bank("pf1"), "pf1")]
    ob = [p.sbuf(f"s2b_o{i}", [128, T], BF16) for i in range(2)]

    def evac(m, ch, ba, bres):
        ms = m % 2
        O = f"s2bo{ms}"
        pr.op("v", lambda e, ms=ms, ch=ch, ba=ba: e.tensor_copy(
              ob[ms][:, ch * 512:(ch + 1) * 512], ba[:, :512]),
              reads=(bres,), writes=(O,), tag=f"s2b ev{m}_{ch}")
        if ch == 7:
            pr.dma("y", s0A[:, m, :], ob[ms][:], reads=(O,), tag=f"s2b st{m}")
    emit_gemm(p, pr, "s2b", WoT_hbm, C, C, rhs, evac, banks, "s2bw")



def emit_rmsA(p, pr, pfx, src_rhs, n_w_col, xr, xr_res, eps=1e-6, pre_ch=None):
    pf0 = p.psum_bank("pf0")
    pf1 = p.psum_bank("pf1")
    sq = p.sbuf(f"{pfx}_sq", [128, 512], BF16)
    ssf = p.sbuf(f"{pfx}_ssf", [1, T], F32)
    SQ, SS = f"{pfx}sq", f"{pfx}ssf"
    for ch in range(8):
        if pre_ch is not None:
            pre_ch(ch, 0)
        for f in range(NF):
            ap, res = src_rhs(f, ch)
            pr.op("s", lambda e, ap=ap: e.activation(sq[:], ap, AF.Square),
                  reads=(res,), writes=(SQ,), tag=f"{pfx} sq{ch}_{f}")
            pr.op("t", lambda e, f=f: e.matmul(pf1[:1, :512], p.sb["c_ones_c1b"][:],
                                               sq[:], start=(f == 0), stop=(f == NF - 1)),
                  reads=(SQ, "c_ones_c1b"), writes=("pf1",), tag=f"{pfx} ssq{ch}_{f}")
        pr.op("v", lambda e, ch=ch: e.tensor_copy(ssf[:, ch * 512:(ch + 1) * 512],
                                                  pf1[:1, :512]),
              reads=("pf1",), writes=(SS,), tag=f"{pfx} ssqe{ch}")
    rstd_strip(p, pr, ssf[:], SS, 1.0 / C, eps, None, n=1)
    for ch in range(8):
        if pre_ch is not None:
            pre_ch(ch, 1)
        sl = slice(ch * 512, (ch + 1) * 512)
        bcast_strip(p, pr, ssf[:, sl], SS, 512, pf0, "pf0", f32=True)
        for f in range(NF):
            ap, res = src_rhs(f, ch)
            if n_w_col is None:
                pr.op("v", lambda e, ap=ap, f=f, sl=sl: e.tensor_tensor(
                      xr[:, f * T:][:, sl], ap, pf0[:, :512], ALU.mult),
                      reads=(res, "pf0"), writes=(xr_res,), tag=f"{pfx} xr{ch}_{f}")
            else:
                wc, wcr = n_w_col
                pr.op("v", lambda e, ap=ap, f=f, sl=sl, wc=wc: e.scalar_tensor_tensor(
                      xr[:, f * T:][:, sl], ap, wc[:, f:f + 1], pf0[:, :512],
                      ALU.mult, ALU.mult),
                      reads=(res, wcr, "pf0"), writes=(xr_res,), tag=f"{pfx} xr{ch}_{f}")


def emit_s3a(p, pr, xnA, qkvA, win):
    n1 = p.sbuf("s3_n1", [128, NF], F32)
    pr.dma("y", n1[:], win["f0_n1"], writes=("s3n1",))
    qb = p.sbuf("s3_qb", [128, 24], F32)
    pr.dma("y", qb[:], win["f0_qkvb"], writes=("s3qb",))
    xr = p.sbuf("s3_xr", [128, NF * T], BF16)
    xn = p.sbuf("s3_xnf", [128, NF * T], BF16)
    pr.dma("y", xn[:].rearrange("p (f t) -> p f t", t=T), xnA,
           writes=("s3xn",), tag="s3 ldxn")

    def src(f, ch):
        return xn[:, f * T + ch * 512: f * T + (ch + 1) * 512], "s3xn"
    emit_rmsA(p, pr, "s3r", src, (n1, "s3n1"), xr[:], "s3xr")

    def rhs(k, ch):
        return xr[:, k * T + ch * 512: k * T + ch * 512 + 512], "s3xr"

    banks = [(p.psum_bank("pf2"), "pf2"), (p.psum_bank("pf3"), "pf3")]
    raw = [p.sbuf(f"s3_raw{i}", [128, T], BF16) for i in range(2)]
    sw = [p.sbuf(f"s3_sw{i}", [128, T], BF16) for i in range(2)]
    cosR = p.sb["c_cosR"]
    sinR = p.sb["c_sinR"]

    def evac(m, ch, ba, bres):
        ms = m % 2
        R = f"s3raw{ms}"
        pr.op("v", lambda e, ms=ms, ch=ch, ba=ba, m=m: e.tensor_scalar(
              raw[ms][:, ch * 512:(ch + 1) * 512], ba[:, :512],
              qb[:, m:m + 1], None, ALU.add),
              reads=(bres, "s3qb"), writes=(R,), tag=f"s3 ev{m}_{ch}")
        if ch == 7:
            if m >= 16:
                pr.dma("y", qkvA[:, m, :], raw[ms][:], reads=(R,), tag=f"s3 stv{m}")
            else:
                S = f"s3sw{ms}"
                for blk in range(4):
                    srcr = [32, 0, 96, 64][blk]
                    pr.dma("y", sw[ms][blk * 32:(blk + 1) * 32, :],
                           raw[ms][srcr:srcr + 32, :], reads=(R,), writes=(S,),
                           tag=f"s3 sw{m}_{blk}")
                pr.op("v", lambda e, ms=ms: e.tensor_tensor(raw[ms][:], raw[ms][:],
                                                            cosR[:], ALU.mult),
                      reads=(R, "c_cosR"), writes=(R,), tag="rope1")
                pr.op("v", lambda e, ms=ms: e.tensor_tensor(sw[ms][:], sw[ms][:],
                                                            sinR[:], ALU.mult),
                      reads=(S, "c_sinR"), writes=(S,), tag="rope2")
                pr.op("v", lambda e, ms=ms: e.tensor_tensor(raw[ms][:], raw[ms][:],
                                                            sw[ms][:], ALU.add),
                      reads=(R, S), writes=(R,), tag="rope3")
                pr.dma("y", qkvA[:, m, :], raw[ms][:], reads=(R,), tag=f"s3 stqk{m}")
    emit_gemm(p, pr, "s3qkv", win["f0_qkvT"], C, 3 * C, rhs, evac, banks, "s3w")



def emit_s3b(p, pr, qkvA, oAh):
    pf_sc = [(p.psum_bank("pf0"), "pf0"), (p.psum_bank("pf1"), "pf1")]
    pf_o = p.psum_bank("pf2")
    pf_b = p.psum_bank("pf3")
    q2 = p.sbuf("s3b_q", [128, T], BF16)
    k2 = p.sbuf("s3b_k", [128, T], BF16)
    v2 = p.sbuf("s3b_v", [128, T], BF16)
    sq = p.sbuf("s3b_sq", [128, T], BF16)
    ssf = p.sbuf("s3b_ssf", [1, T], F32)
    mx = p.sbuf("s3b_mx", [1, 4], F32)
    negB = p.sbuf("s3b_negB", [128, 1], F32)
    vT = p.sbuf("s3b_vT", [128, 32 * 65], BF16)
    PT = [p.sbuf(f"s3b_PT{i}", [128, 512], BF16) for i in range(2)]
    lsb = p.sbuf("s3b_l", [1, 512], F32)
    rb = p.sbuf("s3b_rb", [64, 512], F32)
    oa = p.sbuf("s3b_oa", [128, T], BF16)
    idb = p.sb["c_idb"]

    for hp in range(8):
        pr.dma("y", q2[:], qkvA[:, hp, :], writes=("s3bq",), tag=f"s3b ldq{hp}")
        pr.dma("y", k2[:], qkvA[:, 8 + hp, :], writes=("s3bk",), tag=f"s3b ldk{hp}")
        pr.dma("y", v2[:], qkvA[:, 16 + hp, :], writes=("s3bv",), tag=f"s3b ldv{hp}")
        for hh in range(2):
            rws = slice(hh * 64, hh * 64 + 64)
            for t_, src_, res_ in ((0, q2, "s3bq"), (1, k2, "s3bk")):
                pr.op("s", lambda e, src_=src_, rws=rws: e.activation(
                      sq[rws, :], src_[rws, :], AF.Square),
                      reads=(res_,), writes=("s3bsq",), tag="Bsq")
                for ch in range(8):
                    pr.op("t", lambda e, ch=ch, rws=rws: e.matmul(
                          pf_b[:1, :512], p.sb["c_ones_c1b"][rws, :],
                          sq[rws, ch * 512:(ch + 1) * 512], start=True, stop=True),
                          reads=("s3bsq", "c_ones_c1b"), writes=("pf3",), tag="Bmm")
                    pr.op("v", lambda e, ch=ch: e.tensor_copy(
                          ssf[:, ch * 512:(ch + 1) * 512], pf_b[:1, :512]),
                          reads=("pf3",), writes=("s3bssf",), tag="Bev")
                pr.op("v", lambda e, t_=t_: e.tensor_reduce(
                      mx[:, t_:t_ + 1], ssf[:], AX.X, ALU.max),
                      reads=("s3bssf",), writes=("s3bmx",), tag="Bmax")
            pr.op("s", lambda e: e.activation(mx[:, 0:1], mx[:, 0:1], AF.Sqrt),
                  reads=("s3bmx",), writes=("s3bmx",), tag="Bsq1")
            pr.op("s", lambda e: e.activation(mx[:, 1:2], mx[:, 1:2], AF.Sqrt),
                  reads=("s3bmx",), writes=("s3bmx",), tag="Bsq2")
            pr.op("v", lambda e: e.tensor_tensor(mx[:, 2:3], mx[:, 0:1], mx[:, 1:2],
                                                 ALU.mult),
                  reads=("s3bmx",), writes=("s3bmx",), tag="Bmul")
            pr.op("v", lambda e: e.tensor_scalar_mul(mx[:, 2:3], mx[:, 2:3], -0.125),
                  reads=("s3bmx",), writes=("s3bmx",), tag="Bneg")
            bcast_strip(p, pr, mx[:, 2:3], "s3bmx", 1, pf_b, "pf3", f32=True)
            pr.op("v", lambda e: e.tensor_copy(negB[:], pf_b[:, :1]),
                  reads=("pf3",), writes=("s3bnegB",), tag="Bbc")
            pr.op("v", lambda e: e.memset(vT[:], 1.0), writes=("s3bvT",), tag="vT1")
            bb0 = p.psum_bank("pbb0", BF16)
            for kt in range(32):
                pr.op("t", lambda e, kt=kt, rws=rws: e.transpose(
                      bb0[:, (kt % 8) * 128:(kt % 8) * 128 + 64],
                      v2[rws, kt * 128:(kt + 1) * 128], idb[rws, rws]),
                      reads=("s3bv", "c_idb"), writes=("pbb0",), tag=f"vTt{kt}")
                pr.op("v", lambda e, kt=kt: e.tensor_copy(
                      vT[:, kt * 65:kt * 65 + 64], bb0[:, (kt % 8) * 128:(kt % 8) * 128 + 64]),
                      reads=("pbb0",), writes=("s3bvT",), tag=f"vTe{kt}")
            for qc in range(8):
                qsl = slice(qc * 512, (qc + 1) * 512)

                def sc_mm(kt):
                    sb, sbr = pf_sc[kt % 2]
                    pr.op("t", lambda e, kt=kt, rws=rws, qsl=qsl, sb=sb: e.matmul(
                          sb[:, :512], k2[rws, kt * 128:(kt + 1) * 128],
                          q2[rws, qsl], start=True, stop=True),
                          reads=("s3bk", "s3bq"), writes=(sbr,), tag=f"fs{qc}_{kt}")

                def exp_op(kt):
                    sb, sbr = pf_sc[kt % 2]
                    pt = PT[kt % 2]
                    pr.op("s", lambda e, sb=sb, pt=pt: e.activation(
                          pt[:], sb[:, :512], AF.Exp, bias=negB[:], scale=0.125),
                          reads=(sbr, "s3bnegB"), writes=(f"s3bPT{kt % 2}",),
                          tag=f"fe{qc}_{kt}")

                def pv_mm(kt):
                    pt = PT[kt % 2]
                    pr.op("t", lambda e, kt=kt, pt=pt: e.matmul(
                          pf_o[:65, :512], vT[:, kt * 65:(kt + 1) * 65], pt[:],
                          start=(kt == 0), stop=(kt == 31), skip_group_check=True),
                          reads=("s3bvT", f"s3bPT{kt % 2}"), writes=("pf2",),
                          tag=f"fpv{qc}_{kt}")
                sc_mm(0)
                exp_op(0)
                for kt in range(1, 32):
                    sc_mm(kt)
                    pv_mm(kt - 1)
                    exp_op(kt)
                pv_mm(31)
                pr.op("v", lambda e: e.tensor_copy(lsb[:], pf_o[64:65, :512]),
                      reads=("pf2",), writes=("s3bl",), tag="fl")
                pr.op("v", lambda e: e.reciprocal(lsb[:], lsb[:]),
                      reads=("s3bl",), writes=("s3bl",), tag="fr")
                bcast_strip(p, pr, lsb[:], "s3bl", 512, pf_b, "pf3", f32=True)
                pr.op("v", lambda e: e.tensor_copy(rb[:], pf_b[:64, :512]),
                      reads=("pf3",), writes=("s3brb",), tag="frb")
                pr.op("v", lambda e, rws=rws, qsl=qsl: e.tensor_tensor(
                      oa[rws, qsl], pf_o[:64, :512], rb[:], ALU.mult),
                      reads=("pf2", "s3brb"), writes=("s3boa",), tag="fo")
        pr.dma("y", oAh[:, hp, :], oa[:], reads=("s3boa",), tag=f"s3b st{hp}")



def emit_s3c(p, pr, oAh, xnA, x1fA, win):
    oh = p.sbuf("s3c_o", [128, NF * T], BF16)
    pr.dma("y", oh[:].rearrange("p (f t) -> p f t", t=T), oAh,
           writes=("s3co",), tag="s3c ldo")
    ob = p.sbuf("s3c_ob", [128, NF], F32)
    pr.dma("y", ob[:], win["f0_ob"], writes=("s3cob",))
    xnc = p.sbuf("s3c_xn", [128, 512], BF16)
    tmp = p.sbuf("s3c_tmp", [128, 512], F32)
    x1c = p.sbuf("s3c_x1", [128, 512], F32)

    def rhs(k, ch):
        return oh[:, k * T + ch * 512: k * T + ch * 512 + 512], "s3co"

    banks = [(p.psum_bank("pf0"), "pf0"), (p.psum_bank("pf1"), "pf1")]

    def evac(m, ch, ba, bres):
        pr.dma("y", xnc[:], xnA[:, m, ch * 512:(ch + 1) * 512],
               writes=("s3cxn",), tag=f"s3c ldxn{m}_{ch}")
        pr.op("v", lambda e, ba=ba, m=m: e.tensor_scalar(
              tmp[:], ba[:, :512], ob[:, m:m + 1], None, ALU.add),
              reads=(bres, "s3cob"), writes=("s3ctmp",), tag=f"s3c bias{m}_{ch}")
        pr.op("v", lambda e: e.tensor_tensor(x1c[:], tmp[:], xnc[:], ALU.add),
              reads=("s3ctmp", "s3cxn"), writes=("s3cx1",), tag=f"s3c add{m}_{ch}")
        pr.dma("y", x1fA[:, m, ch * 512:(ch + 1) * 512], x1c[:],
               reads=("s3cx1",), tag=f"s3c st{m}_{ch}")
    emit_gemm(p, pr, "s3c", win["f0_oT"], C, C, rhs, evac, banks, "s3cw")



def emit_top2(p, pr, pfx, lg_psum, psum_res, nexp, wts_ap, wts_res, mode):
    lg = p.sbuf(f"{pfx}_lg", [128, 8], F32) if f"{pfx}_lg" not in p.sb else p.sb[f"{pfx}_lg"]
    st = p.sbuf(f"{pfx}_st", [128, 8], F32) if f"{pfx}_st" not in p.sb else p.sb[f"{pfx}_st"]
    m_ = p.sbuf(f"{pfx}_m", [128, 6], F32) if f"{pfx}_m" not in p.sb else p.sb[f"{pfx}_m"]
    L, Sr, M = f"{pfx}lg", f"{pfx}st", f"{pfx}m"
    ne = nexp
    pr.op("v", lambda e: e.tensor_copy(lg[:, :ne], lg_psum), reads=(psum_res,),
          writes=(L,), tag=f"{pfx} cp")
    pr.op("v", lambda e: e.tensor_reduce(m_[:, 0:1], lg[:, :ne], AX.X, ALU.max),
          reads=(L,), writes=(M,), tag="m1")
    pr.op("v", lambda e: e.tensor_scalar(st[:, :ne], lg[:, :ne], m_[:, 0:1], None, ALU.is_ge),
          reads=(L, M), writes=(Sr,), tag="mask1")
    pr.op("v", lambda e: e.scalar_tensor_tensor(lg[:, :ne], st[:, :ne], float(NEG), lg[:, :ne],
                                                ALU.mult, ALU.add),
          reads=(Sr, L), writes=(L,), tag="lgm")
    pr.op("v", lambda e: e.tensor_reduce(m_[:, 1:2], lg[:, :ne], AX.X, ALU.max),
          reads=(L,), writes=(M,), tag="m2")
    pr.op("v", lambda e: e.tensor_scalar(lg[:, :ne], lg[:, :ne], m_[:, 1:2], None, ALU.is_ge),
          reads=(L, M), writes=(L,), tag="mask2")
    if mode == "softmax":
        pr.op("v", lambda e: e.tensor_tensor(m_[:, 2:3], m_[:, 0:1], m_[:, 1:2], ALU.subtract),
              reads=(M,), writes=(M,), tag="d12")
        pr.op("s", lambda e: e.activation(m_[:, 3:4], m_[:, 2:3], AF.Sigmoid),
              reads=(M,), writes=(M + "x",), tag="s1")
        pr.op("v", lambda e: e.tensor_scalar(m_[:, 4:5], m_[:, 3:4], -1.0, 1.0,
                                             ALU.mult, ALU.add),
              reads=(M + "x",), writes=(M + "y",), tag="s2")
    else:
        pr.op("v", lambda e: e.tensor_tensor(m_[:, 2:3], m_[:, 0:1], m_[:, 1:2], ALU.add),
              reads=(M,), writes=(M,), tag="dsum")
        pr.op("v", lambda e: e.reciprocal(m_[:, 5:6], m_[:, 2:3]),
              reads=(M,), writes=(M + "r",), tag="rec")
        pr.op("v", lambda e: e.tensor_tensor(m_[:, 3:4], m_[:, 0:1], m_[:, 5:6], ALU.mult),
              reads=(M, M + "r"), writes=(M + "x",), tag="w1")
        pr.op("v", lambda e: e.tensor_tensor(m_[:, 4:5], m_[:, 1:2], m_[:, 5:6], ALU.mult),
              reads=(M, M + "r"), writes=(M + "y",), tag="w2")
    pr.op("v", lambda e: e.tensor_scalar(st[:, :ne], st[:, :ne], m_[:, 3:4], None, ALU.mult),
          reads=(Sr, M + "x"), writes=(Sr,), tag="c1")
    pr.op("v", lambda e: e.scalar_tensor_tensor(wts_ap, lg[:, :ne], m_[:, 4:5],
                                                st[:, :ne], ALU.mult, ALU.add),
          reads=(L, M + "y", Sr), writes=(wts_res,), tag="comb")


def emit_s4(p, pr, x1fA, s0A, xA, xb1A, comb_hbm, win):
    idf = p.sb["c_idf"]
    pf0 = p.psum_bank("pf0")
    pf1 = p.psum_bank("pf1")
    pfA = p.psum_bank("pf2")
    pfB = p.psum_bank("pf3")
    pfC = p.psum_bank("pf4")
    pfD = p.psum_bank("pf5")
    n2 = p.sbuf("s4_n2", [128, NF], F32)
    pr.dma("y", n2[:], win["f0_n2"], writes=("s4n2",))
    xr = p.sbuf("s4_xr", [128, NF * T], BF16)
    x1c = p.sbuf("s4_x1c", [128, NF * 512], F32)

    def ldx1(ch, phase):
        pr.dma("y", x1c[:].rearrange("p (f t) -> p f t", t=512),
               x1fA[:, :, ch * 512:(ch + 1) * 512],
               writes=("s4x1c",), tag=f"s4 ldx1_{phase}_{ch}")

    def src(f, ch):
        return x1c[:, f * 512:(f + 1) * 512], "s4x1c"
    emit_rmsA(p, pr, "s4r", src, (n2, "s4n2"), xr[:], "s4xr", pre_ch=ldx1)

    if getattr(p, "dbg", False):
        p.dbg_store(pr, "xr", xr[:], "s4xr", [128, NF * T], BF16)
    gw = p.sbuf("s4_gw", [128, NF * 8], BF16)
    pr.dma("g", gw[:].rearrange("r (nk c) -> r nk c", c=8),
           win["f0_gT"].rearrange("(nk r) c -> r nk c", r=128), writes=("s4gw",))
    comb8 = p.sbuf("s4_comb8", [8, 128], F32)
    wts = p.sbuf("s4_wts", [128, 8], F32)
    for i in range(NT):
        for k in range(NF):
            pr.op("t", lambda e, i=i, k=k: e.matmul(
                  pf0[:, :8], xr[:, k * T + i * 128: k * T + (i + 1) * 128],
                  gw[:, k * 8:(k + 1) * 8], start=(k == 0), stop=(k == NF - 1)),
                  reads=("s4xr", "s4gw"), writes=("pf0",), tag=f"s4 rt{i}_{k}")
        emit_top2(p, pr, "s4t", pf0[:, :8], "pf0", 8, wts[:], "s4wts", "softmax")
        pr.op("t", lambda e, i=i: e.transpose(pf1[:8, :128], wts[:], idf[:]),
              reads=("s4wts", "c_idf"), writes=("pf1",), tag=f"s4 rtt{i}")
        pr.op("v", lambda e, i=i: e.tensor_copy(comb8[:], pf1[:8, :128]),
              reads=("pf1",), writes=("s4comb8",), tag=f"s4 rte{i}")
        pr.dma("y", comb_hbm[:, i * 128:(i + 1) * 128], comb8[:],
               reads=("s4comb8",), writes=("d_comb8",), tag=f"s4 stcomb{i}")

    TC = 1024
    hsb = p.sbuf("s4_h", [128, 16 * TC], BF16)
    acc = p.sbuf("s4_acc", [128, NF * TC], BF16)
    w12 = [p.sbuf(f"s4_w12{i}", [128, NF * 128], BF16) for i in range(6)]
    w3b = [p.sbuf(f"s4_w3{i}", [128, 16 * 128], BF16) for i in range(2)]
    b1c = p.sbuf("s4_b1", [128, 16], F32)
    b2c = p.sbuf("s4_b2", [128, 16], F32)
    b3c = p.sbuf("s4_b3", [128, 8], F32)
    cstr = p.sbuf("s4_cstr", [1, TC], BF16)
    s1t = [p.sbuf(f"s4_s1t{i}", [128, 512], BF16) for i in range(2)]
    u2t = [p.sbuf(f"s4_u2t{i}", [128, 512], BF16) for i in range(2)]
    yt = [p.sbuf(f"s4_yt{i}", [128, 512], BF16) for i in range(2)]
    _xb = p.sbuf("s4_xb0", [128, 512], F32)
    _gb = p.sbuf("s4_gb0", [128, 512], BF16)
    _sb = p.sbuf("s4_sb0", [128, 512], BF16)
    _xa = p.sbuf("s4_xa0", [128, 512], F32)
    _ot = p.sbuf("s4_ot0", [128, 512], F32)
    xbuf = [_xb, _xb]
    gbuf = [_gb, _gb]
    sbuf0 = [_sb, _sb]
    xab = [_xa, _xa]
    ot = [_ot, _ot]

    for tc in range(4):
        tsl = slice(tc * TC, (tc + 1) * TC)
        if getattr(p, "dbg", False) and tc > 0:
            p.dbg_store(pr, f"acc{tc - 1}", acc[:], "s4acc", [128, NF * TC], BF16)
        pr.op("v", lambda e: e.memset(acc[:], 0.0), writes=("s4acc",), tag=f"s4 z{tc}")
        for ex in range(8):
            pr.dma("g", cstr[:], comb_hbm[ex:ex + 1, tsl],
                   reads=("d_comb8",), writes=("s4cstr",), tag=f"s4 ldc{tc}_{ex}")
            pr.dma("y", b1c[:], win["f0_b1"][ex], writes=("s4b1",))
            pr.dma("y", b2c[:], win["f0_b2"][ex], writes=("s4b2",))
            pr.dma("y", b3c[:], win["f0_b3"][ex], writes=("s4b3",))
            HB = [(pfA, "pf2", pfB, "pf3"), (pfC, "pf4", pfD, "pf5")]
            for m in range(16):
                wp = m % 3
                b1a, b1r, b2a, b2r = HB[m % 2]
                W1R, W2R = f"s4w12{2 * wp}", f"s4w12{2 * wp + 1}"
                pr.dma("g", w12[2 * wp][:].rearrange("r (nk c) -> r nk c", c=128),
                       win["f0_w1T"][ex][:, m * 128:(m + 1) * 128]
                       .rearrange("(nk r) c -> r nk c", r=128),
                       writes=(W1R,), tag=f"s4 w1{ex}_{m}")
                pr.dma("g", w12[2 * wp + 1][:].rearrange("r (nk c) -> r nk c", c=128),
                       win["f0_w2T"][ex][:, m * 128:(m + 1) * 128]
                       .rearrange("(nk r) c -> r nk c", r=128),
                       writes=(W2R,), tag=f"s4 w2{ex}_{m}")
                for sub in range(2):
                    ssl = slice(tc * TC + sub * 512, tc * TC + sub * 512 + 512)
                    for k in range(NF):
                        pr.op("t", lambda e, k=k, wp=wp, ssl=ssl, b1a=b1a: e.matmul(
                              b1a[:, :512], w12[2 * wp][:, k * 128:(k + 1) * 128],
                              xr[:, k * T:][:, ssl], start=(k == 0), stop=(k == NF - 1)),
                              reads=(W1R, "s4xr"), writes=(b1r,), tag="h1mm")
                        pr.op("t", lambda e, k=k, wp=wp, ssl=ssl, b2a=b2a: e.matmul(
                              b2a[:, :512], w12[2 * wp + 1][:, k * 128:(k + 1) * 128],
                              xr[:, k * T:][:, ssl], start=(k == 0), stop=(k == NF - 1)),
                              reads=(W2R, "s4xr"), writes=(b2r,), tag="h2mm")
                    pr.op("s", lambda e, m=m, wp=wp, b1a=b1a: e.activation(
                          s1t[wp % 2][:], b1a[:, :512], AF.Silu, bias=b1c[:, m:m + 1]),
                          reads=(b1r, "s4b1"), writes=(f"s4s1t{wp % 2}",), tag="silu1")
                    pr.op("v", lambda e, m=m, wp=wp, b2a=b2a: e.tensor_scalar(
                          u2t[wp % 2][:], b2a[:, :512], b2c[:, m:m + 1], None, ALU.add),
                          reads=(b2r, "s4b2"), writes=(f"s4u2t{wp % 2}",), tag="b2add")
                    pr.op("v", lambda e, m=m, sub=sub, wp=wp: e.tensor_tensor(
                          hsb[:, m * TC + sub * 512: m * TC + sub * 512 + 512],
                          s1t[wp % 2][:], u2t[wp % 2][:], ALU.mult),
                          reads=(f"s4s1t{wp % 2}", f"s4u2t{wp % 2}"), writes=("s4h",), tag="hmul")
            if getattr(p, "dbg", False) and tc == 0 and ex == 7:
                p.dbg_store(pr, "hsb", hsb[:], "s4h", [128, 16 * TC], BF16)
            for mo in range(8):
                wp = mo % 2
                ya, yr = (pfA, "pf2") if wp == 0 else (pfB, "pf3")
                W3R = f"s4w3{wp}"
                pr.dma("g", w3b[wp][:].rearrange("r (nk c) -> r nk c", c=128),
                       win["f0_w3T"][ex][:, mo * 128:(mo + 1) * 128]
                       .rearrange("(nk r) c -> r nk c", r=128),
                       writes=(W3R,), tag=f"s4 w3{ex}_{mo}")
                for sub in range(2):
                    cb, cbr = (pfC, "pf4") if sub == 0 else (pfD, "pf5")
                    bcast_strip(p, pr, cstr[:, sub * 512:(sub + 1) * 512], "s4cstr",
                                512, cb, cbr, f32=False)
                    for k in range(16):
                        pr.op("t", lambda e, k=k, wp=wp, sub=sub, ya=ya: e.matmul(
                              ya[:, :512], w3b[wp][:, k * 128:(k + 1) * 128],
                              hsb[:, k * TC + sub * 512: k * TC + sub * 512 + 512],
                              start=(k == 0), stop=(k == 15)),
                              reads=(W3R, "s4h"), writes=(yr,), tag="ymm")
                    pr.op("v", lambda e, mo=mo, ya=ya, wp=wp: e.tensor_scalar(
                          yt[wp][:], ya[:, :512], b3c[:, mo:mo + 1], None, ALU.add),
                          reads=(yr, "s4b3"), writes=(f"s4yt{wp}",), tag="b3add")
                    pr.op("v", lambda e, cb=cb, wp=wp: e.tensor_tensor(
                          yt[wp][:], yt[wp][:], cb[:, :512], ALU.mult),
                          reads=(f"s4yt{wp}", cbr), writes=(f"s4yt{wp}",), tag="cmul")
                    pr.op("v", lambda e, mo=mo, sub=sub, wp=wp: e.tensor_tensor(
                          acc[:, mo * TC + sub * 512: mo * TC + sub * 512 + 512],
                          acc[:, mo * TC + sub * 512: mo * TC + sub * 512 + 512],
                          yt[wp][:], ALU.add),
                          reads=("s4acc", f"s4yt{wp}"), writes=("s4acc",), tag="yacc")
        for f in range(NF):
            for sub in range(2):
                w = 0
                XB, GB, SB, XAB, OT = ("s4xb0", "s4gb0", "s4sb0",
                                       "s4xa0", "s4ot0")
                ssl = slice(tc * TC + sub * 512, tc * TC + sub * 512 + 512)
                pr.dma("y", xbuf[w][:], x1fA[:, f, ssl], writes=(XB,), tag="ld x1")
                pr.op("v", lambda e, f=f, sub=sub, w=w: e.tensor_tensor(
                      xbuf[w][:], xbuf[w][:], acc[:, f * TC + sub * 512: f * TC + sub * 512 + 512],
                      ALU.add),
                      reads=(XB, "s4acc"), writes=(XB,), tag="x2f")
                pr.op("s", lambda e, w=w: e.activation(gbuf[w][:], xbuf[w][:], AF.Silu),
                      reads=(XB,), writes=(GB,), tag="g0")
                pr.dma("y", sbuf0[w][:], s0A[:, f, ssl], writes=(SB,), tag="ld s0")
                pr.dma("y", xab[w][:], xA[:, f, ssl], writes=(XAB,), tag="ld xA")
                pr.op("v", lambda e, w=w: e.tensor_tensor(ot[w][:], gbuf[w][:], sbuf0[w][:], ALU.mult),
                      reads=(GB, SB), writes=(OT,), tag="sg")
                pr.op("v", lambda e, w=w: e.tensor_tensor(ot[w][:], ot[w][:], xab[w][:], ALU.add),
                      reads=(OT, XAB), writes=(OT,), tag="xb1")
                pr.dma("y", xb1A[:, f, ssl], ot[w][:], reads=(OT,), tag="st xb1")
        if getattr(p, "dbg", False) and tc == 3:
            p.dbg_store(pr, "acc3", acc[:], "s4acc", [128, NF * TC], BF16)



def emit_s5a(p, pr, xb1A, xn1A, q1A, k1A, v1A, gt1A, bg1, win):
    pf0, pf1, pf2 = p.psum_bank("pf0"), p.psum_bank("pf1"), p.psum_bank("pf2")
    banks = [(p.psum_bank("pf3"), "pf3"), (p.psum_bank("pf4"), "pf4")]
    banks2 = [(p.psum_bank("pf5"), "pf5"), (p.psum_bank("pf2"), "pf2")]
    xc = p.sbuf("s5_xc", [128, NF * 512], F32)
    xcb = p.sbuf("s5_xcb", [128, 512], BF16)
    sq = p.sbuf("s5_sq", [128, 512], BF16)
    nm = p.sbuf("s5_nm", [1, T], BF16)
    rs = p.sbuf("s5_rs", [1, T], F32)
    tmpf = p.sbuf("s5_tmpf", [128, 512], F32)
    xn1 = p.sbuf("s5_xn1", [128, NF * T], BF16)

    for ch in range(8):
        sl = slice(ch * 512, (ch + 1) * 512)
        pr.dma("y", xc[:].rearrange("p (f t) -> p f t", t=512),
               xb1A[:, :, sl], writes=("s5xc",), tag=f"s5 ldx{ch}")
        for f in range(NF):
            pr.op("v", lambda e, f=f: e.tensor_copy(xcb[:], xc[:, f * 512:(f + 1) * 512]),
                  reads=("s5xc",), writes=("s5xcb",), tag="cast")
            pr.op("s", lambda e: e.activation(sq[:], xcb[:], AF.Square),
                  reads=("s5xcb",), writes=("s5sq",), tag="sq")
            pr.op("t", lambda e, f=f: e.matmul(pf1[:1, :512], p.sb["c_ones_c1b"][:],
                                               xcb[:], start=(f == 0), stop=(f == NF - 1)),
                  reads=("s5xcb", "c_ones_c1b"), writes=("pf1",), tag="sum")
            pr.op("t", lambda e, f=f: e.matmul(pf0[:1, :512], p.sb["c_ones_c1b"][:],
                                               sq[:], start=(f == 0), stop=(f == NF - 1)),
                  reads=("s5sq", "c_ones_c1b"), writes=("pf0",), tag="ssq")
        pr.op("v", lambda e, sl=sl: e.tensor_scalar(nm[:, sl], pf1[:1, :512],
                                                    -1.0 / C, None, ALU.mult),
              reads=("pf1",), writes=("s5nm",), tag="negmean")
        pr.op("v", lambda e, sl=sl: e.tensor_scalar(rs[:, sl], pf0[:1, :512],
                                                    1.0 / C, None, ALU.mult),
              reads=("pf0",), writes=("s5rs",), tag="msq")
    pr.op("v", lambda e: e.scalar_tensor_tensor(nm[:, T:T] if False else rs[:],
          nm[:], -1.0, rs[:], ALU.mult, ALU.bypass) if False else
          e.tensor_tensor(sq[:1, :1], nm[:1, :1], nm[:1, :1], ALU.mult),
          reads=(), writes=(), tag="dead") if False else None
    mm2 = p.sbuf("s5_mm2", [1, T], BF16)
    pr.op("v", lambda e: e.tensor_tensor(mm2[:], nm[:], nm[:], ALU.mult),
          reads=("s5nm",), writes=("s5mm2",), tag="m2")
    pr.op("v", lambda e: e.tensor_tensor(rs[:], rs[:], mm2[:], ALU.subtract),
          reads=("s5rs", "s5mm2"), writes=("s5rs",), tag="var")
    rstd_strip(p, pr, rs[:], "s5rs", 1.0, 1e-5, None, n=1)
    for ch in range(8):
        sl = slice(ch * 512, (ch + 1) * 512)
        pr.dma("y", xc[:].rearrange("p (f t) -> p f t", t=512),
               xb1A[:, :, sl], writes=("s5xc",), tag=f"s5 ldx2_{ch}")
        bcast_strip(p, pr, nm[:, sl], "s5nm", 512, pf0, "pf0", f32=False)
        bcast_strip(p, pr, rs[:, sl], "s5rs", 512, pf1, "pf1", f32=True)
        for f in range(NF):
            pr.op("v", lambda e, f=f: e.tensor_tensor(tmpf[:], xc[:, f * 512:(f + 1) * 512],
                                                      pf0[:, :512], ALU.add),
                  reads=("s5xc", "pf0"), writes=("s5tmpf",), tag="xmn")
            pr.op("v", lambda e, f=f, sl=sl: e.tensor_tensor(
                  xn1[:, f * T:][:, sl], tmpf[:], pf1[:, :512], ALU.mult),
                  reads=("s5tmpf", "pf1"), writes=("s5xn1",), tag="xn1")
        pr.dma("y", xn1A[:, :, sl],
               xn1[:].rearrange("p (f t) -> p f t", t=T)[:, :, sl],
               reads=("s5xn1",), tag=f"s5 stxn{ch}")

    def rhs(k, ch):
        return xn1[:, k * T + ch * 512: k * T + ch * 512 + 512], "s5xn1"

    ob = [p.sbuf(f"s5_o{i}", [128, T], BF16) for i in range(2)]

    def mk_evac(dst, act=None):
        def evac(m, ch, ba, bres):
            ms = m % 2
            O = f"s5o{ms}"
            if act is None:
                pr.op("v", lambda e, ms=ms, ch=ch, ba=ba: e.tensor_copy(
                      ob[ms][:, ch * 512:(ch + 1) * 512], ba[:, :512]),
                      reads=(bres,), writes=(O,), tag=f"s5 ev{m}_{ch}")
            else:
                pr.op("s", lambda e, ms=ms, ch=ch, ba=ba: e.activation(
                      ob[ms][:, ch * 512:(ch + 1) * 512], ba[:, :512], act),
                      reads=(bres,), writes=(O,), tag=f"s5 evg{m}_{ch}")
            if ch == 7:
                pr.dma("y", dst[:, m, :], ob[ms][:], reads=(O,), tag=f"s5 st{m}")
        return evac
    emit_gemm(p, pr, "s5q", win["l1_Wq"], C, C, rhs, mk_evac(q1A), banks, "s5wq")
    emit_gemm(p, pr, "s5k", win["l1_Wk"], C, C, rhs, mk_evac(k1A), banks, "s5wk")
    emit_gemm(p, pr, "s5v", win["l1_Wv"], C, C, rhs, mk_evac(v1A), banks, "s5wv")
    emit_gemm(p, pr, "s5g", win["l1_Wg"], C, C, rhs, mk_evac(gt1A, AF.Silu), banks, "s5wg")

    wal = p.sbuf("s5_wal", [128, NF * 16], BF16)
    wbe = p.sbuf("s5_wbe", [128, NF * 16], BF16)
    pr.dma("g", wal[:].rearrange("r (nk c) -> r nk c", c=16),
           win["l1_Wal"].rearrange("(nk r) c -> r nk c", r=128), writes=("s5wal",))
    pr.dma("g", wbe[:].rearrange("r (nk c) -> r nk c", c=16),
           win["l1_Wbe"].rearrange("(nk r) c -> r nk c", r=128), writes=("s5wbe",))
    dtb = p.sbuf("s5_dtb", [16, 1], F32)
    pr.dma("y", dtb[:], win["l1_dtb"], writes=("s5dtb",))
    alog = p.sbuf("s5_alog", [16, 1], F32)
    pr.dma("y", alog[:], win["l1_Alog"], writes=("s5alog",))
    negA = p.sbuf("s5_negA", [16, 1], F32)
    pr.op("s", lambda e: e.activation(negA[:], alog[:], AF.Exp),
          reads=("s5alog",), writes=("s5negA",))
    pr.op("v", lambda e: e.tensor_scalar_mul(negA[:], negA[:], -1.0),
          reads=("s5negA",), writes=("s5negA",))
    H1 = 16
    bss = [p.sbuf(f"s5_bs{i}", [H1, 512], F32) for i in range(2)]
    gss = [p.sbuf(f"s5_gs{i}", [H1, 512], F32) for i in range(2)]
    gcs = [p.sbuf(f"s5_gc{i}", [H1, 512], F32) for i in range(2)]
    tt1 = [p.sbuf(f"s5_t1{i}", [H1, 512], F32) for i in range(2)]
    tt2 = [p.sbuf(f"s5_t2{i}", [H1, 512], F32) for i in range(2)]
    egs = [p.sbuf(f"s5_eg{i}", [H1, 512], F32) for i in range(2)]
    ers = [p.sbuf(f"s5_er{i}", [H1, 512], F32) for i in range(2)]
    bes = [p.sbuf(f"s5_be{i}", [H1, 512], F32) for i in range(2)]
    zcol = p.sbuf("s5_zcol", [H1, 1], F32)
    pr.op("v", lambda e: e.memset(zcol[:], 0.0), writes=("s5zcol",))
    for ch in range(8):
        cs = ch % 2
        ba_b, bres_b = banks2[0]
        ba_a, bres_a = banks2[1]
        for k in range(NF):
            pr.op("t", lambda e, k=k, ch=ch, ba=ba_b: e.matmul(
                  ba[:H1, :512], wbe[:, k * 16:(k + 1) * 16], rhs(k, ch)[0],
                  start=(k == 0), stop=(k == NF - 1)),
                  reads=("s5wbe", "s5xn1"), writes=(bres_b,), tag=f"s5 bmm{ch}")
            pr.op("t", lambda e, k=k, ch=ch, ba=ba_a: e.matmul(
                  ba[:H1, :512], wal[:, k * 16:(k + 1) * 16], rhs(k, ch)[0],
                  start=(k == 0), stop=(k == NF - 1)),
                  reads=("s5wal", "s5xn1"), writes=(bres_a,), tag=f"s5 amm{ch}")
        B, G, GC = f"s5bs{cs}", f"s5gs{cs}", f"s5gc{cs}"
        pr.op("s", lambda e, cs=cs, ba=ba_b: e.activation(bss[cs][:], ba[:H1, :512], AF.Sigmoid),
              reads=(bres_b,), writes=(B,), tag="s5 beta")
        softplus_block(p, pr, ba_a[:H1, :512], bres_a, H1, 512,
                       tt1[cs][:], f"s5t1{cs}", tt2[cs][:], f"s5t2{cs}",
                       dtb[:], "s5dtb", negA[:], "s5negA",
                       gss[cs][:], G)
        for q in range(4):
            sl = slice(q * 128, (q + 1) * 128)
            pr.op("v", lambda e, cs=cs, sl=sl: e.tensor_tensor_scan(
                  gcs[cs][:, sl], gss[cs][:, sl], gss[cs][:, sl], zcol[:],
                  ALU.add, ALU.bypass),
                  reads=(G, "s5zcol"), writes=(GC,), tag="s5 cumsum")
        for q in range(4):
            sl = slice(q * 128, (q + 1) * 128)
            pr.op("v", lambda e, cs=cs, sl=sl, q=q: e.tensor_scalar(
                  gss[cs][:, sl], gcs[cs][:, sl], -1.0,
                  gcs[cs][:, q * 128 + 127: q * 128 + 128], ALU.mult, ALU.add),
                  reads=(GC,), writes=(G,), tag="s5 grev")
        EG, ER, BE = f"s5eg{cs}", f"s5er{cs}", f"s5be{cs}"
        pr.op("s", lambda e, cs=cs: e.activation(egs[cs][:], gcs[cs][:], AF.Exp),
              reads=(GC,), writes=(EG,), tag="s5 Eg")
        pr.op("s", lambda e, cs=cs: e.activation(ers[cs][:], gss[cs][:], AF.Exp),
              reads=(G,), writes=(ER,), tag="s5 Erev")
        pr.op("v", lambda e, cs=cs: e.tensor_tensor(bes[cs][:], bss[cs][:], egs[cs][:], ALU.mult),
              reads=(B, EG), writes=(BE,), tag="s5 bEg")
        sl5 = slice(ch * 512, (ch + 1) * 512)
        pr.dma("y", bg1[0, :, sl5], gcs[cs][:], reads=(GC,), tag=f"s5 stg{ch}")
        pr.dma("y", bg1[1, :, sl5], egs[cs][:], reads=(EG,), tag=f"s5 steg{ch}")
        pr.dma("y", bg1[2, :, sl5], ers[cs][:], reads=(ER,), tag=f"s5 ster{ch}")
        pr.dma("y", bg1[3, :, sl5], bss[cs][:], reads=(B,), tag=f"s5 stb{ch}")
        pr.dma("y", bg1[4, :, sl5], bes[cs][:], reads=(BE,), tag=f"s5 stbe{ch}")



def emit_s5b(p, pr, q1A, k1A, v1A, gt1A, bg1, gated1A, win):
    idf = p.sb["c_idf"]
    pf0 = p.psum_bank("pf0")
    pf1 = p.psum_bank("pf1")
    H1 = 16
    stage = p.sbuf("s5b_stage", [H1, T], F32)
    gT = p.sbuf("s5b_gT", [128, NCH * H1], F32)
    gTn = p.sbuf("s5b_gTn", [128, NCH * H1], F32)
    beT = p.sbuf("s5b_beT", [128, NCH * H1], F32)
    begT = p.sbuf("s5b_begT", [128, NCH * H1], F32)
    erT = p.sbuf("s5b_erT", [128, NCH * H1], F32)
    for row, dst, res in ((0, gT, "s5bgT"), (3, beT, "s5bbeT"),
                          (4, begT, "s5bbegT"), (2, erT, "s5berT")):
        pr.dma("y", stage[:], bg1[row], writes=("s5bstage",), tag=f"s5b ldbg{row}")
        for i in range(NCH):
            pr.op("t", lambda e, i=i: e.transpose(
                  pf0[:, (i % 4) * 128:(i % 4) * 128 + H1],
                  stage[:, i * CH:(i + 1) * CH], idf[:H1, :H1]),
                  reads=("s5bstage", "c_idf"), writes=("pf0",), tag=f"s5b tr{row}_{i}")
            pr.op("v", lambda e, i=i, dst=dst, res=res: e.tensor_copy(
                  dst[:, i * H1:(i + 1) * H1], pf0[:, (i % 4) * 128:(i % 4) * 128 + H1]),
                  reads=("pf0",), writes=(res,), tag=f"s5b ev{row}_{i}")
    pr.op("v", lambda e: e.tensor_scalar_mul(gTn[:], gT[:], -1.0),
          reads=("s5bgT",), writes=("s5bgTn",))
    rw = p.sbuf("s5b_rw", [128, 1], F32)
    pr.dma("y", rw[:], win["l1_rmsw"], writes=("s5brw",))

    qp = p.sbuf("s5b_qp", [128, T], BF16)
    kp = p.sbuf("s5b_kp", [128, T], BF16)
    vp = p.sbuf("s5b_vp", [128, T], BF16)
    gp = p.sbuf("s5b_gp", [128, T], BF16)
    sqb = p.sbuf("s5b_sq", [128, T], BF16)
    core = p.sbuf("s5b_core", [128, T], BF16)
    gcore = p.sbuf("s5b_gcore", [128, T], BF16)
    qn = p.sbuf("s5b_qn", [128, T], BF16)
    kn = p.sbuf("s5b_kn", [128, T], BF16)
    ssf = p.sbuf("s5b_ssf", [1, T], F32)
    g_st = p.sbuf("s5b_gst", [1, T], F32)
    eg_st = p.sbuf("s5b_egst", [1, T], F32)
    be_st = p.sbuf("s5b_best", [1, T], BF16)
    egl = p.sbuf("s5b_egl", [128, NCH], F32)
    Sst = p.sbuf("s5b_S", [64, 64], BF16)

    LN64 = float(np.log(64.0 ** -0.5))
    for hp in range(8):
        pr.dma("y", qp[:], q1A[:, hp, :], writes=("s5bqp",), tag=f"s5b ldq{hp}")
        pr.dma("y", kp[:], k1A[:, hp, :], writes=("s5bkp",), tag=f"s5b ldk{hp}")
        pr.dma("y", vp[:], v1A[:, hp, :], writes=("s5bvp",), tag=f"s5b ldv{hp}")
        pr.dma("y", gp[:], gt1A[:, hp, :], writes=("s5bgp",), tag=f"s5b ldg{hp}")
        for hh in range(2):
            h = hp * 2 + hh
            rws = slice(hh * 64, hh * 64 + 64)
            pr.dma("y", g_st[:], bg1[0, h:h + 1, :], writes=("s5bgst",), tag=f"ldgs{h}")
            pr.dma("y", eg_st[:], bg1[1, h:h + 1, :], writes=("s5begst",), tag=f"ldegs{h}")
            pr.dma("g", be_st[:], bg1[3, h:h + 1, :], writes=("s5bbest",), tag=f"ldbes{h}")
            bcast_strip(p, pr, eg_st[:, CH - 1::CH], "s5begst", NCH, pf0, "pf0", f32=True)
            pr.op("v", lambda e: e.tensor_copy(egl[:], pf0[:, :NCH]),
                  reads=("pf0",), writes=("s5begl",), tag="egl")
            for srcn, srcr, dstn, dstr, eb in ((qp, "s5bqp", qn, "s5bqn", LN64),
                                               (kp, "s5bkp", kn, "s5bkn", None)):
                pr.op("s", lambda e, srcn=srcn, rws=rws: e.activation(
                      sqb[rws, :], srcn[rws, :], AF.Square),
                      reads=(srcr,), writes=("s5bsq",), tag="sq")
                for ch in range(8):
                    pr.op("t", lambda e, ch=ch, rws=rws: e.matmul(
                          pf1[:1, :512], p.sb["c_ones_c1b"][rws, :],
                          sqb[rws, ch * 512:(ch + 1) * 512], start=True, stop=True),
                          reads=("s5bsq", "c_ones_c1b"), writes=("pf1",), tag=f"ssq{ch}")
                    pr.op("v", lambda e, ch=ch: e.tensor_copy(
                          ssf[:, ch * 512:(ch + 1) * 512], pf1[:1, :512]),
                          reads=("pf1",), writes=("s5bssf",), tag=f"ssqe{ch}")
                rstd_strip(p, pr, ssf[:], "s5bssf", 1.0, 1e-18, eb, n=1)
                for ch in range(8):
                    sl = slice(ch * 512, (ch + 1) * 512)
                    bcast_strip(p, pr, ssf[:, sl], "s5bssf", 512, pf0, "pf0", f32=True)
                    pr.op("v", lambda e, sl=sl, rws=rws, srcn=srcn, dstn=dstn:
                          e.tensor_tensor(dstn[rws, sl], srcn[rws, sl],
                                          pf0[rws, :512], ALU.mult),
                          reads=(srcr, "pf0"), writes=(dstr,), tag="l2mul")
            pr.op("v", lambda e: e.memset(Sst[:], 0.0), writes=("s5bS",), tag="S0")
            emit_delta_head(
                p, pr, "s5d", 64, 64, H1, h,
                kn[rws, :], "s5bkn", qn[rws, :], "s5bqn", vp[rws, :], "s5bvp",
                dict(g=(g_st[:], "s5bgst"), Eg=(eg_st[:], "s5begst"),
                     be=(be_st[:], "s5bbest")),
                dict(gTn=(gTn[:, :], "s5bgTn"), beT=(beT[:, :], "s5bbeT"),
                     erT=(erT[:, :], "s5berT"), begT=(begT[:, :], "s5bbegT")),
                (egl[:], "s5begl"), Sst[:], "s5bS", core[rws, :], "s5bcore",
                idsl=rws)
            pr.op("s", lambda e, rws=rws: e.activation(sqb[rws, :], core[rws, :], AF.Square),
                  reads=("s5bcore",), writes=("s5bsq",), tag="csq")
            for ch in range(8):
                pr.op("t", lambda e, ch=ch, rws=rws: e.matmul(
                      pf1[:1, :512], p.sb["c_ones_c1b"][rws, :],
                      sqb[rws, ch * 512:(ch + 1) * 512], start=True, stop=True),
                      reads=("s5bsq", "c_ones_c1b"), writes=("pf1",), tag=f"cs{ch}")
                pr.op("v", lambda e, ch=ch: e.tensor_copy(
                      ssf[:, ch * 512:(ch + 1) * 512], pf1[:1, :512]),
                      reads=("pf1",), writes=("s5bssf",), tag=f"cse{ch}")
            rstd_strip(p, pr, ssf[:], "s5bssf", 1.0 / 64.0, 1e-6, None, n=1)
            for ch in range(8):
                sl = slice(ch * 512, (ch + 1) * 512)
                bcast_strip(p, pr, ssf[:, sl], "s5bssf", 512, pf0, "pf0", f32=True)
                pr.op("v", lambda e, sl=sl, rws=rws: e.scalar_tensor_tensor(
                      gcore[rws, sl], core[rws, sl], rw[rws, :], pf0[rws, :512],
                      ALU.mult, ALU.mult),
                      reads=("s5bcore", "s5brw", "pf0"), writes=("s5bgcore",), tag="gc1")
                pr.op("v", lambda e, sl=sl, rws=rws: e.tensor_tensor(
                      gcore[rws, sl], gcore[rws, sl], gp[rws, sl], ALU.mult),
                      reads=("s5bgcore", "s5bgp"), writes=("s5bgcore",), tag="gc2")
        pr.dma("y", gated1A[:, hp, :], gcore[:], reads=("s5bgcore",), tag=f"s5b st{hp}")


def emit_s5c(p, pr, gated1A, Wo_hbm, s1A):
    gt = p.sbuf("s5c_g", [128, NF * T], BF16)
    pr.dma("y", gt[:].rearrange("p (f t) -> p f t", t=T), gated1A,
           writes=("s5cg",), tag="s5c ldg")

    def rhs(k, ch):
        return gt[:, k * T + ch * 512: k * T + ch * 512 + 512], "s5cg"

    banks = [(p.psum_bank("pf0"), "pf0"), (p.psum_bank("pf1"), "pf1")]
    ob = [p.sbuf(f"s5c_o{i}", [128, T], BF16) for i in range(2)]

    def evac(m, ch, ba, bres):
        ms = m % 2
        O = f"s5co{ms}"
        pr.op("v", lambda e, ms=ms, ch=ch, ba=ba: e.tensor_copy(
              ob[ms][:, ch * 512:(ch + 1) * 512], ba[:, :512]),
              reads=(bres,), writes=(O,), tag=f"s5c ev{m}_{ch}")
        if ch == 7:
            pr.dma("y", s1A[:, m, :], ob[ms][:], reads=(O,), tag=f"s5c st{m}")
    emit_gemm(p, pr, "s5c", Wo_hbm, C, C, rhs, evac, banks, "s5cw")



def emit_s6(p, pr, xn1A, xb1A, s1A, y_hbm, comb1_hbm, win):
    idf = p.sb["c_idf"]
    pf0 = p.psum_bank("pf0")
    pf1 = p.psum_bank("pf1")
    pfA = p.psum_bank("pf2")
    pfB = p.psum_bank("pf3")
    pfC = p.psum_bank("pf4")
    pfD = p.psum_bank("pf5")
    HID = 2816
    NH2 = HID // 128
    xn1 = p.sbuf("s6_xn1", [128, NF * T], BF16)
    pr.dma("y", xn1[:].rearrange("p (f t) -> p f t", t=T), xn1A,
           writes=("s6xn",), tag="s6 ldxn")

    wr = p.sbuf("s6_wr", [128, NF * 7], BF16)
    pr.dma("g", wr[:].rearrange("r (nk c) -> r nk c", c=7),
           win["f1_Wr"].rearrange("(nk r) c -> r nk c", r=128), writes=("s6wr",))
    rb = p.sbuf("s6_rb", [128, 7], F32)
    pr.dma("y", rb[:], win["f1_rb"], writes=("s6rb",))
    probs = p.sbuf("s6_probs", [128, 7], F32)
    wts = p.sbuf("s6_wts", [128, 7], F32)
    comb7 = p.sbuf("s6_comb7", [7, 128], F32)
    for i in range(NT):
        for k in range(NF):
            pr.op("t", lambda e, i=i, k=k: e.matmul(
                  pf0[:, :7], xn1[:, k * T + i * 128: k * T + (i + 1) * 128],
                  wr[:, k * 7:(k + 1) * 7], start=(k == 0), stop=(k == NF - 1)),
                  reads=("s6xn", "s6wr"), writes=("pf0",), tag=f"s6 rt{i}_{k}")
        pr.op("v", lambda e: e.tensor_tensor(probs[:], pf0[:, :7], rb[:], ALU.mult),
              reads=("pf0", "s6rb"), writes=("s6probs",), tag="lg")
        pr.op("s", lambda e: e.activation(probs[:], probs[:], AF.Sigmoid),
              reads=("s6probs",), writes=("s6probs",), tag="sig")
        emit_top2(p, pr, "s6t", probs[:], "s6probs", 7, wts[:], "s6wts", "sigmoid")
        pr.op("t", lambda e, i=i: e.transpose(pf1[:7, :128], wts[:], idf[:]),
              reads=("s6wts", "c_idf"), writes=("pf1",), tag=f"s6 rtt{i}")
        pr.op("v", lambda e, i=i: e.tensor_copy(comb7[:], pf1[:7, :128]),
              reads=("pf1",), writes=("s6comb7",), tag=f"s6 rte{i}")
        pr.dma("y", comb1_hbm[:, i * 128:(i + 1) * 128], comb7[:],
               reads=("s6comb7",), writes=("d_comb1",), tag=f"s6 stc{i}")

    TC = 1024
    hsb = p.sbuf("s6_h", [128, NH2 * TC], BF16)
    acc = p.sbuf("s6_acc", [128, NF * TC], BF16)
    w12 = [p.sbuf(f"s6_w12{i}", [128, NF * 128], BF16) for i in range(6)]
    w3b = [p.sbuf(f"s6_w3{i}", [128, NH2 * 128], BF16) for i in range(3)]
    cstr = p.sbuf("s6_cstr", [1, TC], BF16)
    s1t = p.sbuf("s6_s1t", [128, 512], BF16)
    yt = p.sbuf("s6_yt", [128, 512], F32)
    gbuf = p.sbuf("s6_gb", [128, 512], BF16)
    sbuf0 = p.sbuf("s6_sb", [128, 512], BF16)
    xab = p.sbuf("s6_xa", [128, 512], F32)
    ot = p.sbuf("s6_ot", [128, 512], F32)
    asm = p.sbuf("s6_asm", [128, 4 * 1024], F32)

    for tc in range(4):
        tsl = slice(tc * TC, (tc + 1) * TC)
        pr.op("v", lambda e: e.memset(acc[:], 0.0), writes=("s6acc",), tag=f"s6 z{tc}")
        for ex in range(8):
            routed = ex > 0
            if routed:
                wg_h = win["f1_rg"][ex - 1]
                wu_h = win["f1_ru"][ex - 1]
                wd_h = win["f1_rd"][ex - 1]
                pr.dma("g", cstr[:], comb1_hbm[ex - 1:ex, tsl],
                       reads=("d_comb1",), writes=("s6cstr",), tag=f"s6 ldc{tc}_{ex}")
            else:
                wg_h, wu_h, wd_h = win["f1_sg"], win["f1_su"], win["f1_sd"]
            for m in range(NH2):
                wp = m % 3
                W1R, W2R = f"s6w12{2 * wp}", f"s6w12{2 * wp + 1}"
                pr.dma("g", w12[2 * wp][:].rearrange("r (nk c) -> r nk c", c=128),
                       wg_h[:, m * 128:(m + 1) * 128].rearrange("(nk r) c -> r nk c", r=128),
                       writes=(W1R,), tag=f"s6 wg{ex}_{m}")
                pr.dma("g", w12[2 * wp + 1][:].rearrange("r (nk c) -> r nk c", c=128),
                       wu_h[:, m * 128:(m + 1) * 128].rearrange("(nk r) c -> r nk c", r=128),
                       writes=(W2R,), tag=f"s6 wu{ex}_{m}")
                for sub in range(2):
                    ssl = slice(tc * TC + sub * 512, tc * TC + sub * 512 + 512)
                    for k in range(NF):
                        pr.op("t", lambda e, k=k, wp=wp, ssl=ssl: e.matmul(
                              pfA[:, :512], w12[2 * wp][:, k * 128:(k + 1) * 128],
                              xn1[:, k * T:][:, ssl], start=(k == 0), stop=(k == NF - 1)),
                              reads=(W1R, "s6xn"), writes=("pf2",), tag="h1mm")
                        pr.op("t", lambda e, k=k, wp=wp, ssl=ssl: e.matmul(
                              pfB[:, :512], w12[2 * wp + 1][:, k * 128:(k + 1) * 128],
                              xn1[:, k * T:][:, ssl], start=(k == 0), stop=(k == NF - 1)),
                              reads=(W2R, "s6xn"), writes=("pf3",), tag="h2mm")
                    pr.op("s", lambda e: e.activation(s1t[:], pfA[:, :512], AF.Silu),
                          reads=("pf2",), writes=("s6s1t",), tag="silu1")
                    pr.op("v", lambda e, m=m, sub=sub: e.tensor_tensor(
                          hsb[:, m * TC + sub * 512: m * TC + sub * 512 + 512],
                          s1t[:], pfB[:, :512], ALU.mult),
                          reads=("s6s1t", "pf3"), writes=("s6h",), tag="hmul")
            for mo in range(NF):
                wp = mo % 3
                W3R = f"s6w3{wp}"
                pr.dma("g", w3b[wp][:].rearrange("r (nk c) -> r nk c", c=128),
                       wd_h[:, mo * 128:(mo + 1) * 128].rearrange("(nk r) c -> r nk c", r=128),
                       writes=(W3R,), tag=f"s6 wd{ex}_{mo}")
                for sub in range(2):
                    for k in range(NH2):
                        pr.op("t", lambda e, k=k, wp=wp, sub=sub: e.matmul(
                              pfC[:, :512], w3b[wp][:, k * 128:(k + 1) * 128],
                              hsb[:, k * TC + sub * 512: k * TC + sub * 512 + 512],
                              start=(k == 0), stop=(k == NH2 - 1)),
                              reads=(W3R, "s6h"), writes=("pf4",), tag="ymm")
                    if routed:
                        bcast_strip(p, pr, cstr[:, sub * 512:(sub + 1) * 512], "s6cstr",
                                    512, pfD, "pf5", f32=False)
                        pr.op("v", lambda e: e.tensor_tensor(yt[:], pfC[:, :512],
                                                             pfD[:, :512], ALU.mult),
                              reads=("pf4", "pf5"), writes=("s6yt",), tag="cmul") \
                            if False else None
                        pr.op("v", lambda e: e.tensor_copy(yt[:], pfC[:, :512]),
                              reads=("pf4",), writes=("s6yt",), tag="ycp")
                        pr.op("v", lambda e: e.tensor_tensor(yt[:], yt[:], pfD[:, :512],
                                                             ALU.mult),
                              reads=("s6yt", "pf5"), writes=("s6yt",), tag="cmul")
                    else:
                        pr.op("v", lambda e: e.tensor_copy(yt[:], pfC[:, :512]),
                              reads=("pf4",), writes=("s6yt",), tag="ycp")
                    pr.op("v", lambda e, mo=mo, sub=sub: e.tensor_tensor(
                          acc[:, mo * TC + sub * 512: mo * TC + sub * 512 + 512],
                          acc[:, mo * TC + sub * 512: mo * TC + sub * 512 + 512],
                          yt[:], ALU.add),
                          reads=("s6acc", "s6yt"), writes=("s6acc",), tag="yacc")
        for sub in range(2):
            for f in range(NF):
                ssl = slice(tc * TC + sub * 512, tc * TC + sub * 512 + 512)
                pr.op("s", lambda e, f=f, sub=sub: e.activation(
                      gbuf[:], acc[:, f * TC + sub * 512: f * TC + sub * 512 + 512],
                      AF.Silu),
                      reads=("s6acc",), writes=("s6gb",), tag="g1")
                pr.dma("y", sbuf0[:], s1A[:, f, ssl], writes=("s6sb",), tag="ld s1")
                pr.dma("y", xab[:], xb1A[:, f, ssl], writes=("s6xa",), tag="ld xb1")
                pr.op("v", lambda e: e.tensor_tensor(ot[:], gbuf[:], sbuf0[:], ALU.mult),
                      reads=("s6gb", "s6sb"), writes=("s6ot",), tag="sg")
                pr.op("v", lambda e: e.tensor_tensor(ot[:], ot[:], xab[:], ALU.add),
                      reads=("s6ot", "s6xa"), writes=("s6ot",), tag="out")
                for q in range(4):
                    pr.op("t", lambda e, q=q: e.transpose(
                          pf1[:, (q % 1) * 128:128] if False else pf1[:, :128],
                          ot[:, q * 128:(q + 1) * 128], idf[:]),
                          reads=("s6ot", "c_idf"), writes=("pf1",), tag=f"ftr{f}_{q}")
                    pr.op("v", lambda e, q=q, f=f: e.tensor_copy(
                          asm[:, q * 1024 + f * 128: q * 1024 + (f + 1) * 128],
                          pf1[:, :128]),
                          reads=("pf1",), writes=("s6asm",), tag=f"fev{f}_{q}")
            for q in range(4):
                t0 = tc * TC + sub * 512 + q * 128
                pr.dma("y", y_hbm[t0:t0 + 128, :], asm[:, q * 1024:(q + 1) * 1024],
                       reads=("s6asm",), tag=f"sty{t0}")



def _mk_inmaps(p, host, per_core):
    consts = {k: host[k] for k in CONST_DEFS if k in p.dram}
    weights = {k: host[k] for k in host if not k.startswith("c_") and k in p.dram}
    return [dict(consts, **weights, **pc) for pc in per_core]


def run_s0_only(inputs, nt=NT):
    host = host_layout(inputs)
    x = inputs["L0_attn_x"]
    p = P()
    x_hbm = p.dram_t("x", [T, C], F32, kind="ExternalInput")
    xA = p.dram_t("xA", [128, NF, T], F32, kind="ExternalOutput")
    xnA = p.dram_t("xnA", [128, NF, T], BF16, kind="ExternalOutput")
    pr = p.stage("s0")
    load_consts(p, pr)
    emit_s0(p, pr, x_hbm, xA, xnA, nt=nt)
    pr.build()
    in_maps = _mk_inmaps(p, host, [dict(x=np.ascontiguousarray(x[b])) for b in range(8)])
    return run_bass_kernel_spmd(p.nc, in_maps, core_ids=list(range(8)))


def run_s1_only(xnA_np, host):
    p = P()
    xnA = p.dram_t("xnA", [128, NF, T], BF16, kind="ExternalInput")
    qcA = p.dram_t("qcA", [128, 8, T], BF16, kind="ExternalOutput")
    kcA = p.dram_t("kcA", [128, 8, T], BF16, kind="ExternalOutput")
    vcA = p.dram_t("vcA", [128, 8, T], BF16, kind="ExternalOutput")
    zsA = p.dram_t("zsA", [128, 8, T], BF16, kind="ExternalOutput")
    bg = p.dram_t("bg", [5, 8, T], F32, kind="ExternalOutput")
    win = {k: p.dram_t(k, list(host[k].shape), F32, kind="ExternalInput")
           for k in ("l0_Wq", "l0_Wk", "l0_Wv", "l0_Wz", "l0_Wb", "l0_Wa",
                     "l0_cwq", "l0_cwk", "l0_cwv", "l0_dtb", "l0_Alog")}
    pr = p.stage("s1")
    load_consts(p, pr)
    emit_s1(p, pr, xnA, qcA, kcA, vcA, zsA, bg, win)
    pr.build()
    in_maps = _mk_inmaps(p, host, [dict(xnA=xnA_np[b]) for b in range(8)])
    return run_bass_kernel_spmd(p.nc, in_maps, core_ids=list(range(8)))


def run_s2_only(s1res, host, dbg=False):
    p = P()
    p.dbg = dbg
    p.dbg_chunk = (0, 0)
    qcA = p.dram_t("qcA", [128, 8, T], BF16, kind="ExternalInput")
    kcA = p.dram_t("kcA", [128, 8, T], BF16, kind="ExternalInput")
    vcA = p.dram_t("vcA", [128, 8, T], BF16, kind="ExternalInput")
    zsA = p.dram_t("zsA", [128, 8, T], BF16, kind="ExternalInput")
    bg = p.dram_t("bg", [5, 8, T], F32, kind="ExternalInput")
    nw = p.dram_t("l0_nw", [128, 1], F32, kind="ExternalInput")
    WoT = p.dram_t("l0_WoT", [C, C], F32, kind="ExternalInput")
    gatedA = p.dram_t("gatedA", [128, 8, T], BF16, kind="ExternalOutput")
    s0A = p.dram_t("s0A", [128, 8, T], BF16, kind="ExternalOutput")
    with p.scope():
        pr = p.stage("s2a")
        load_consts(p, pr)
        emit_s2a(p, pr, qcA, kcA, vcA, zsA, bg, gatedA, nw)
        pr.build()
    with p.scope():
        pr2 = p.stage("s2b")
        emit_s2b(p, pr2, gatedA, WoT, s0A)
        pr2.build()
    per_core = [dict(qcA=np.asarray(r["qcA"]), kcA=np.asarray(r["kcA"]),
                     vcA=np.asarray(r["vcA"]), zsA=np.asarray(r["zsA"]),
                     bg=np.asarray(r["bg"])) for r in s1res]
    in_maps = _mk_inmaps(p, host, per_core)
    return run_bass_kernel_spmd(p.nc, in_maps, core_ids=list(range(8)))


def run_s3_only(xnA_np, host, stages="abc"):
    p = P()
    xnA = p.dram_t("xnA", [128, NF, T], BF16, kind="ExternalInput")
    qkvA = p.dram_t("qkvA", [128, 24, T], BF16,
                    kind="Internal" if "b" in stages else "ExternalOutput")
    if "b" in stages or "c" in stages:
        oAh = p.dram_t("oAh", [128, 8, T], BF16,
                       kind="ExternalOutput" if "b" in stages else "ExternalInput")
    else:
        oAh = None
    x1fA = (p.dram_t("x1fA", [128, 8, T], F32, kind="ExternalOutput")
            if "c" in stages else None)
    win = {k: p.dram_t(k, list(host[k].shape), F32, kind="ExternalInput")
           for k in ("f0_n1", "f0_qkvT", "f0_qkvb", "f0_oT", "f0_ob")}
    first = [True]

    def st(name, fn):
        with p.scope():
            pr = p.stage(name)
            if first[0]:
                load_consts(p, pr)
                first[0] = False
            fn(pr)
            pr.build()
    if "a" in stages:
        st("s3a", lambda pr: emit_s3a(p, pr, xnA, qkvA, win))
    if "b" in stages:
        st("s3b", lambda pr: emit_s3b(p, pr, qkvA, oAh))
    if "c" in stages:
        st("s3c", lambda pr: emit_s3c(p, pr, oAh, xnA, x1fA, win))
    per = [dict(xnA=xnA_np[b]) for b in range(8)]
    in_maps = _mk_inmaps(p, host, per)
    return run_bass_kernel_spmd(p.nc, in_maps, core_ids=list(range(8)))


def run_s4_only(per_core, host, dbg=False):
    p = P()
    p.dbg = dbg
    x1fA = p.dram_t("x1fA", [128, 8, T], F32, kind="ExternalInput")
    s0A = p.dram_t("s0A", [128, 8, T], BF16, kind="ExternalInput")
    xA = p.dram_t("xA", [128, 8, T], F32, kind="ExternalInput")
    xb1A = p.dram_t("xb1A", [128, 8, T], F32, kind="ExternalOutput")
    comb = p.dram_t("comb8", [8, T], F32, kind="ExternalOutput")
    win = {k: p.dram_t(k, list(host[k].shape), F32, kind="ExternalInput")
           for k in ("f0_n2", "f0_gT", "f0_w1T", "f0_w2T", "f0_w3T",
                     "f0_b1", "f0_b2", "f0_b3")}
    with p.scope():
        pr = p.stage("s4")
        load_consts(p, pr)
        emit_s4(p, pr, x1fA, s0A, xA, xb1A, comb, win)
        pr.build()
    in_maps = _mk_inmaps(p, host, per_core)
    return run_bass_kernel_spmd(p.nc, in_maps, core_ids=list(range(8)))


BOUND = {
    "xA": ([128, NF, T], F32), "xnA": ([128, NF, T], BF16),
    "qcA": ([128, 8, T], BF16), "kcA": ([128, 8, T], BF16),
    "vcA": ([128, 8, T], BF16), "zsA": ([128, 8, T], BF16),
    "bg": ([5, 8, T], F32), "gatedA": ([128, 8, T], BF16),
    "s0A": ([128, 8, T], BF16), "qkvA": ([128, 24, T], BF16),
    "oAh": ([128, 8, T], BF16), "x1fA": ([128, 8, T], F32),
    "comb8": ([8, T], F32), "xb1A": ([128, 8, T], F32),
    "xn1A": ([128, 8, T], BF16), "q1A": ([128, 8, T], BF16),
    "k1A": ([128, 8, T], BF16), "v1A": ([128, 8, T], BF16),
    "gt1A": ([128, 8, T], BF16), "bg1": ([5, 16, T], F32),
    "gated1A": ([128, 8, T], BF16), "s1A": ([128, 8, T], BF16),
    "comb1": ([7, T], F32),
    "x": ([T, C], F32), "y": ([T, C], F32),
}

STAGE_DEFS = [
    ("s0", ["x"], ["xA", "xnA"],
     lambda p, pr, t, w: emit_s0(p, pr, t["x"], t["xA"], t["xnA"])),
    ("s1", ["xnA"], ["qcA", "kcA", "vcA", "zsA", "bg"],
     lambda p, pr, t, w: emit_s1(p, pr, t["xnA"], t["qcA"], t["kcA"], t["vcA"],
                                 t["zsA"], t["bg"], w)),
    ("s2a", ["qcA", "kcA", "vcA", "zsA", "bg"], ["gatedA"],
     lambda p, pr, t, w: emit_s2a(p, pr, t["qcA"], t["kcA"], t["vcA"], t["zsA"],
                                  t["bg"], t["gatedA"], w["l0_nw"])),
    ("s2b", ["gatedA"], ["s0A"],
     lambda p, pr, t, w: emit_s2b(p, pr, t["gatedA"], w["l0_WoT"], t["s0A"])),
    ("s3a", ["xnA"], ["qkvA"],
     lambda p, pr, t, w: emit_s3a(p, pr, t["xnA"], t["qkvA"], w)),
    ("s3b", ["qkvA"], ["oAh"],
     lambda p, pr, t, w: emit_s3b(p, pr, t["qkvA"], t["oAh"])),
    ("s3c", ["oAh", "xnA"], ["x1fA"],
     lambda p, pr, t, w: emit_s3c(p, pr, t["oAh"], t["xnA"], t["x1fA"], w)),
    ("s4", ["x1fA", "s0A", "xA"], ["xb1A", "comb8"],
     lambda p, pr, t, w: emit_s4(p, pr, t["x1fA"], t["s0A"], t["xA"],
                                 t["xb1A"], t["comb8"], w)),
    ("s5a", ["xb1A"], ["xn1A", "q1A", "k1A", "v1A", "gt1A", "bg1"],
     lambda p, pr, t, w: emit_s5a(p, pr, t["xb1A"], t["xn1A"], t["q1A"],
                                  t["k1A"], t["v1A"], t["gt1A"], t["bg1"], w)),
    ("s5b", ["q1A", "k1A", "v1A", "gt1A", "bg1"], ["gated1A"],
     lambda p, pr, t, w: emit_s5b(p, pr, t["q1A"], t["k1A"], t["v1A"],
                                  t["gt1A"], t["bg1"], t["gated1A"], w)),
    ("s5c", ["gated1A"], ["s1A"],
     lambda p, pr, t, w: emit_s5c(p, pr, t["gated1A"], w["l1_Wo"], t["s1A"])),
    ("s6", ["xn1A", "xb1A", "s1A"], ["y", "comb1"],
     lambda p, pr, t, w: emit_s6(p, pr, t["xn1A"], t["xb1A"], t["s1A"],
                                 t["y"], t["comb1"], w)),
]


class Win:
    def __init__(self, p, host):
        self.p = p
        self.host = host

    def __getitem__(self, k):
        if k not in self.p.dram:
            self.p.dram_t(k, list(self.host[k].shape), F32, kind="ExternalInput")
        return self.p.dram[k]


def _np16(a):
    return np.ascontiguousarray(a)


def kernel(**inputs):
    host = host_layout(inputs)
    x = inputs["L0_attn_x"]
    if FUSED:
        p = P()
        win = Win(p, host)
        tens = {}
        for n, (sh, dt) in BOUND.items():
            kind = ("ExternalInput" if n == "x" else
                    "ExternalOutput" if n == "y" else "Internal")
            tens[n] = p.dram_t(n, list(sh), dt, kind=kind)
        first = True
        for name, _ins, _outs, fn in STAGE_DEFS:
            with p.scope(), p.nc.named_scope(name):
                pr = p.stage(name)
                if first:
                    load_consts(p, pr)
                    first = False
                fn(p, pr, tens, win)
                pr.build()
        wnames = [k for k in p.dram if k in host and not k.startswith("c_")]
        consts = {k: host[k] for k in CONST_DEFS}
        in_maps = [dict(consts, **{k: host[k] for k in wnames},
                        x=np.ascontiguousarray(x[b])) for b in range(8)]
        res = run_bass_kernel_spmd(p.nc, in_maps, core_ids=list(range(8)))
        return np.stack([np.asarray(r["y"], dtype=np.float32)
                         for r in res.results])
    vals = [dict(x=np.ascontiguousarray(x[b])) for b in range(8)]
    for name, ins, outs, fn in STAGE_DEFS:
        p = P()
        win = Win(p, host)
        tens = {}
        for n in ins:
            tens[n] = p.dram_t(n, list(BOUND[n][0]), BOUND[n][1],
                               kind="ExternalInput")
        for n in outs:
            tens[n] = p.dram_t(n, list(BOUND[n][0]), BOUND[n][1],
                               kind="ExternalOutput")
        with p.scope():
            pr = p.stage(name)
            load_consts(p, pr)
            fn(p, pr, tens, win)
            pr.build()
        wnames = [k for k in p.dram if k in host and not k.startswith("c_")]
        consts = {k: host[k] for k in CONST_DEFS}
        in_maps = [dict(consts, **{k: host[k] for k in wnames},
                        **{n: vals[b][n] for n in ins}) for b in range(8)]
        res = run_bass_kernel_spmd(p.nc, in_maps, core_ids=list(range(8)))
        for b in range(8):
            for n in outs:
                vals[b][n] = np.asarray(res.results[b][n])
    return np.stack([vals[b]["y"].astype(np.float32) for b in range(8)])
